# version 04
# baseline (speedup 1.0000x reference)
import math
import numpy as np
import ml_dtypes
from contextlib import ExitStack
import concourse.bass as bass
import concourse.mybir as mybir
from concourse.bass_utils import run_bass_kernel_spmd

F32 = mybir.dt.float32
BF16 = mybir.dt.bfloat16
AF = mybir.ActivationFunctionType
ALU = mybir.AluOpType
AX = mybir.AxisListType

NCORES = 8
D = 2048
TOK = 2048
NTC = TOK // 128
NFC = D // 128
QT = 512
NQT = TOK // QT
NEG = -1e10

ENGINES = ("pe", "dve", "act", "pool", "sp")
RG_OVERRIDE = None



class Region:
    __slots__ = ("name", "last_w", "readers", "dma_sem", "is_psum")

    def __init__(self, name, is_psum=False):
        self.name = name
        self.last_w = None
        self.readers = []
        self.dma_sem = None
        self.is_psum = is_psum


class Op:
    __slots__ = ("eng", "fn", "idx", "waits", "inc_sem", "inc_amt", "inc_val",
                 "is_dma", "region_sem", "dbg")

    def __init__(self, eng, fn, idx, is_dma, dbg=""):
        self.eng = eng
        self.fn = fn
        self.idx = idx
        self.waits = []
        self.inc_sem = None
        self.inc_amt = 0
        self.inc_val = None
        self.is_dma = is_dma
        self.region_sem = None
        self.dbg = dbg


class _DmaSem:
    __slots__ = ("sem",)

    def __init__(self, sem):
        self.sem = sem


class Ctx:
    def __init__(self, nc, es):
        self.nc = nc
        self.es = es
        self.ops = {e: [] for e in ENGINES}
        self.eng_sem = {e: self.sem("prog_" + e) for e in ENGINES}
        self.covered = {}
        self.n_sems = 5

    def sem(self, name):
        self.n_sems = getattr(self, "n_sems", 0) + 1
        return self.es.enter_context(self.nc.semaphore(name))

    def _dep(self, op, producer):
        if producer is None or producer is op:
            return
        if producer.eng == op.eng and op.eng == "pe" and not producer.is_dma:
            return
        key = (op.eng, producer.eng, producer.is_dma,
               id(producer.region_sem) if producer.is_dma else 0)
        cov = self.covered.get(key, -1)
        if producer.idx <= cov:
            return
        self.covered[key] = producer.idx
        if producer.inc_amt == 0:
            if producer.is_dma:
                producer.inc_amt = 16
            else:
                producer.inc_amt = 1
                producer.inc_sem = self.eng_sem[producer.eng]
        op.waits.append(producer)

    @staticmethod
    def _newest_by_engine(ops):
        best = {}
        for rd in ops:
            k = (rd.eng, rd.is_dma, id(rd.region_sem) if rd.is_dma else 0)
            cur = best.get(k)
            if cur is None or rd.idx > cur.idx:
                best[k] = rd
        return best.values()

    def _track(self, o, reads, writes):
        for r in reads:
            self._dep(o, r.last_w)
            if r.is_psum:
                for rd in self._newest_by_engine(r.readers):
                    if rd.eng != o.eng:
                        self._dep(o, rd)
        for w in writes:
            self._dep(o, w.last_w)
            for rd in self._newest_by_engine(w.readers):
                self._dep(o, rd)
        for w in writes:
            w.readers = []
            w.last_w = o
        for r in reads:
            r.readers.append(o)

    def op(self, eng, fn, reads=(), writes=(), dbg=""):
        o = Op(eng, fn, len(self.ops[eng]), False, dbg)
        self._track(o, reads, writes)
        self.ops[eng].append(o)
        return o

    def dma(self, queue, out_ap_fn, in_ap_fn, reads=(), writes=(), dbg="", **kw):
        assert writes, "DMA needs a written region"
        w0 = writes[0]
        if w0.dma_sem is None:
            w0.dma_sem = {}
        if queue not in w0.dma_sem:
            w0.dma_sem[queue] = _DmaSem(self.sem(f"d{queue[0]}_" + w0.name[:23]))

        qeng = {"sp": self.nc.sync, "pool": self.nc.gpsimd, "act": self.nc.scalar}[queue]

        def fn(_out=out_ap_fn, _in=in_ap_fn, _kw=kw, _q=qeng):
            return _q.dma_start(out=_out(), in_=_in(), **_kw)

        o = Op(queue, fn, len(self.ops[queue]), True, dbg)
        o.region_sem = w0.dma_sem[queue]
        o.inc_sem = w0.dma_sem[queue].sem
        o.inc_amt = 16
        self._track(o, reads, writes)
        self.ops[queue].append(o)
        return o

    def collective(self, in_buf, out_buf, replica_groups, dbg="cc"):
        csem = self.sem("cc_" + dbg[:20])

        def fn(_i=in_buf, _o=out_buf, _rg=replica_groups, _g=self.nc.gpsimd):
            return _g.collective_compute(
                "AllGather", ALU.bypass, replica_groups=_rg,
                ins=[_i.ap.opt()], outs=[_o.ap.opt()])

        o = Op("pool", fn, len(self.ops["pool"]), False, dbg)
        o.inc_sem = csem
        o.inc_amt = 1
        self._track(o, [in_buf.r()], [out_buf.r()])
        self.ops["pool"].append(o)
        return o

    def finalize(self):
        run = {}
        for e in ENGINES:
            for o in self.ops[e]:
                if o.inc_amt:
                    run[o.inc_sem] = run.get(o.inc_sem, 0) + o.inc_amt
                    o.inc_val = run[o.inc_sem]

    def replay(self, block):
        def body(eng_name):
            def _f(eng):
                for o in self.ops[eng_name]:
                    for p in o.waits:
                        assert p.inc_val is not None, f"unresolved dep {p.dbg}"
                    for p in o.waits:
                        eng.wait_ge(p.inc_sem, p.inc_val)
                    try:
                        ins = o.fn()
                    except Exception as exc:
                        raise RuntimeError(f"emit failed at [{eng_name}] {o.dbg}: {exc}") from exc
                    if o.inc_amt and ins is not None:
                        ins.then_inc(o.inc_sem, o.inc_amt)
            return _f
        block.tensor(body("pe"))
        block.vector(body("dve"))
        block.scalar(body("act"))
        block.gpsimd(body("pool"))
        block.sync(body("sp"))


class Buf:
    def __init__(self, ap, name):
        self.ap_ = ap
        self.name = name
        self._regions = {}

    def r(self, key="*"):
        if key not in self._regions:
            self._regions[key] = Region(f"{self.name}.{key}", getattr(self, "psum", False))
        return self._regions[key]

    @property
    def ap(self):
        return self.ap_



def core_token_blocks(parity):
    return [2 * j + parity for j in range(NQT)]


def gathered_key_global_pos():
    pos = np.zeros(2 * TOK, np.int64)
    for j in range(NQT):
        pos[j * 512:(j + 1) * 512] = np.arange(512) + (2 * j) * 512
        pos[TOK + j * 512:TOK + (j + 1) * 512] = np.arange(512) + (2 * j + 1) * 512
    return pos


def attn_plan(kind):
    plan = []
    for j in range(NQT):
        if kind == "causal":
            cs = list(range(0, 4 * (j + 1))) + [16 + c for c in range(0, 4 * (j + 1))]
        elif kind == "swa":
            tiles = sorted(set(t for t in (j - 1, j) if t >= 0))
            cs = []
            for t in tiles:
                cs += list(range(4 * t, 4 * t + 4))
            for t in tiles:
                cs += [16 + c for c in range(4 * t, 4 * t + 4)]
        else:
            cs = list(range(32))
        plan.append(cs)
    return plan


def mask_tables(kind, parity):
    kpos = gathered_key_global_pos()
    plan = attn_plan(kind)
    cls = {}
    p_masks = {}
    row_masks = {}
    for j in range(NQT):
        for s, kc in enumerate(plan[j]):
            kk = kpos[kc * 128:(kc + 1) * 128]
            union_tt = False
            union_row = False
            my_valid = None
            for par in (0, 1):
                qq = np.arange(512) + (2 * j + par) * 512
                if kind == "full":
                    valid = np.ones((128, 512), bool)
                else:
                    valid = kk[:, None] <= qq[None, :]
                    if kind == "swa":
                        valid &= (qq[None, :] - kk[:, None]) < 1024
                row_const = np.all(valid == valid[:, :1], axis=1).all()
                if not row_const:
                    union_tt = True
                elif not valid.all():
                    union_row = True
                if par == parity:
                    my_valid = valid
            if union_tt:
                cls[(j, s)] = "tt"
                p_masks[(j, s)] = my_valid.astype(np.float32)
            elif union_row:
                cls[(j, s)] = "row"
                row_masks[(j, s)] = np.where(my_valid[:, 0], 0.0, NEG).astype(np.float32)
            else:
                cls[(j, s)] = None
    return plan, cls, p_masks, row_masks


def build_constants(core_id):
    parity = core_id % 2
    c = {}
    c["ident_bf"] = np.eye(128).astype(ml_dtypes.bfloat16)
    c["ident_f32"] = np.eye(128).astype(np.float32)
    c["ones_bf"] = np.ones((128, 1), ml_dtypes.bfloat16)
    c["ones_f32"] = np.ones((128, 1), np.float32)
    c["ones_row"] = np.ones((1, 128), np.float32)

    blocks = core_token_blocks(parity)
    pos = np.concatenate([np.arange(512) + b * 512 for b in blocks]).astype(np.float64)
    ts = 10000.0 ** (np.arange(0, 128, 2).astype(np.float64) / 128.0)
    sin128 = pos[:, None] / ts[None, :]
    c["cos128"] = np.cos(sin128).astype(ml_dtypes.bfloat16)
    c["sin128"] = np.sin(sin128).astype(ml_dtypes.bfloat16)
    inv = 1.0 / (10000.0 ** (np.arange(0, 64, 2).astype(np.float64) / 64.0))
    fr = pos[:, None] * inv[None, :]
    c["cos64"] = np.cos(fr).astype(ml_dtypes.bfloat16)
    c["sin64"] = np.sin(fr).astype(ml_dtypes.bfloat16)

    for kind in ("causal", "swa", "full"):
        plan, cls, pm, rm = mask_tables(kind, parity)
        tts, rows = [], []
        for j in range(NQT):
            for s in range(len(plan[j])):
                if cls[(j, s)] == "tt":
                    tts.append(pm[(j, s)])
                elif cls[(j, s)] == "row":
                    rows.append(rm[(j, s)])
        c[f"pmask_{kind}"] = (np.stack(tts).astype(ml_dtypes.bfloat16) if tts
                              else np.zeros((1, 128, 512), ml_dtypes.bfloat16))
        c[f"rmask_{kind}"] = (np.stack(rows).astype(np.float32)[:, :, None] if rows
                              else np.zeros((1, 128, 1), np.float32))
    return c



class Builder:
    def __init__(self, nc, es):
        self.nc = nc
        self.es = es
        self.ctx = Ctx(nc, es)
        self.dram = {}
        _NC_FOR_MM.clear()
        _NC_FOR_MM.append(nc)

        def sb(name, cols, dtype):
            h = es.enter_context(nc.sbuf_tensor(name, [128, cols], dtype))
            return Buf(h, name)

        self.XNT = sb("XNT", NFC * TOK, BF16)
        self.QH = sb("QH", 32768, BF16)
        self.WB = sb("WB", 16384, BF16)
        self.WF = sb("WF", 2048, F32)
        self.TMA = sb("TMA", 4096, F32)
        self.XNB = sb("XNB", 2048, BF16)
        self.KVT = sb("KVT", 3072, BF16)
        self.PB = sb("PB", 2048, BF16)
        self.RS = sb("RS", 512, F32)
        self.SM = sb("SM", 1024, F32)
        self.CC = sb("CC", 288, F32)
        self.CB = sb("CB", 288, BF16)
        self.SC = sb("SC", 256, BF16)
        self.RW = sb("RW", 160, F32)
        self.GT = sb("GT", 256, F32)

        self.banks = []
        for i in range(8):
            bb = Buf(es.enter_context(nc.psum_tensor(f"bank{i}", [128, 512], F32)), f"bank{i}")
            bb.psum = True
            self.banks.append(bb)
        self.const_ap = {}
        self.const_r = {}

    def dram_buf(self, name, shape, dtype, kind):
        h = self.nc.dram_tensor(name, list(shape), dtype, kind=kind)
        b = Buf(h.ap(), name)
        self.dram[name] = b
        return b

    def load_consts(self, cdict):
        ctx = self.ctx
        layout = {"ident_f32": (self.CC, 0, 128, F32),
                  "ones_f32": (self.CC, 128, 1, F32),
                  "ones_row": (self.CC, 129, 128, F32),
                  "ident_bf": (self.CB, 0, 128, BF16),
                  "ones_bf": (self.CB, 128, 1, BF16)}
        for name, arr in cdict.items():
            dt = BF16 if arr.dtype == ml_dtypes.bfloat16 else F32
            d = self.dram_buf("c_" + name, arr.shape, dt, "ExternalInput")
            if name in layout:
                buf, col, w, _ = layout[name]
                ap = buf.ap[:arr.shape[0], col:col + w]
                self.const_ap[name] = ap
                self.const_r[name] = buf.r(name)
                ctx.dma("sp", lambda ap=ap: ap, lambda d=d: d.ap[:, :],
                        reads=[d.r()], writes=[buf.r(name)], dbg="const_" + name)
            else:
                self.const_ap[name] = None

    def cap(self, name):
        return self.const_ap[name]

    def cdram(self, name):
        return self.dram["c_" + name]


_NC_FOR_MM = []


def MM(out_ap, lhsT_ap, rhs_ap, start=True, stop=True):
    return lambda: _NC_FOR_MM[0].tensor.matmul(out_ap, lhsT_ap, rhs_ap, start=start, stop=stop)


def bf_half(bank, n0=0, n=128):
    return bank.ap[:, n0 // 2:(n0 + n) // 2].bitcast(BF16)



def emit_ln(b, X, dbg="ln", gate_hook=None):
    ctx, nc = b.ctx, b.nc
    XNT, TMA, XNB, SM, WF = b.XNT, b.TMA, b.XNB, b.SM, b.WF
    tb = b.banks[6]
    for tc in range(NTC):
        s = tc % 2
        xa = TMA.ap[:, s * 2048:(s + 1) * 2048]
        xr = TMA.r(f"h{s}")
        c0 = s * 8
        ctx.dma("sp", lambda xa=xa: xa,
                lambda tc=tc: X.ap[tc * 128:(tc + 1) * 128, :],
                reads=[X.r()], writes=[xr], dbg=f"{dbg}.load{tc}")
        ctx.op("dve", lambda xa=xa, c0=c0: nc.vector.reduce_sum(
            SM.ap[:, c0:c0 + 1], xa, axis=AX.X),
            reads=[xr], writes=[SM.r(f"s{s}")], dbg=f"{dbg}.sum{tc}")
        ctx.op("act", lambda xa=xa, c0=c0: nc.scalar.activation(
            out=WF.ap[:, 0:2048], in_=xa, func=AF.Square,
            accum_out=SM.ap[:, c0 + 1:c0 + 2]),
            reads=[xr], writes=[WF.r(), SM.r(f"q{s}")], dbg=f"{dbg}.sq{tc}")
        ctx.op("dve", lambda c0=c0: nc.vector.tensor_scalar(
            SM.ap[:, c0:c0 + 1], SM.ap[:, c0:c0 + 1], 1.0 / D, None, ALU.mult),
            reads=[SM.r(f"s{s}")], writes=[SM.r(f"s{s}")], dbg=f"{dbg}.mean{tc}")
        ctx.op("dve", lambda c0=c0: nc.vector.tensor_scalar(
            SM.ap[:, c0 + 1:c0 + 2], SM.ap[:, c0 + 1:c0 + 2], 1.0 / D, None, ALU.mult),
            reads=[SM.r(f"q{s}")], writes=[SM.r(f"q{s}")], dbg=f"{dbg}.msq{tc}")
        ctx.op("dve", lambda c0=c0: nc.vector.tensor_tensor(
            SM.ap[:, c0 + 2:c0 + 3], SM.ap[:, c0:c0 + 1], SM.ap[:, c0:c0 + 1], op=ALU.mult),
            reads=[SM.r(f"s{s}")], writes=[SM.r(f"m2{s}")], dbg=f"{dbg}.m2{tc}")
        ctx.op("dve", lambda c0=c0: nc.vector.tensor_tensor(
            SM.ap[:, c0 + 3:c0 + 4], SM.ap[:, c0 + 1:c0 + 2], SM.ap[:, c0 + 2:c0 + 3],
            op=ALU.subtract),
            reads=[SM.r(f"q{s}"), SM.r(f"m2{s}")], writes=[SM.r(f"v{s}")], dbg=f"{dbg}.var{tc}")
        ctx.op("dve", lambda c0=c0: nc.vector.tensor_scalar(
            SM.ap[:, c0 + 3:c0 + 4], SM.ap[:, c0 + 3:c0 + 4], 1e-5, None, ALU.add),
            reads=[SM.r(f"v{s}")], writes=[SM.r(f"v{s}")], dbg=f"{dbg}.veps{tc}")
        ctx.op("act", lambda c0=c0: nc.scalar.activation(
            out=SM.ap[:, c0 + 4:c0 + 5], in_=SM.ap[:, c0 + 3:c0 + 4], func=AF.Sqrt),
            reads=[SM.r(f"v{s}")], writes=[SM.r(f"sd{s}")], dbg=f"{dbg}.sd{tc}")
        ctx.op("dve", lambda c0=c0: nc.vector.reciprocal(
            SM.ap[:, c0 + 5:c0 + 6], SM.ap[:, c0 + 4:c0 + 5]),
            reads=[SM.r(f"sd{s}")], writes=[SM.r(f"rs{s}")], dbg=f"{dbg}.rs{tc}")
        if gate_hook is None:
            ctx.op("dve", lambda xa=xa, c0=c0: nc.vector.tensor_scalar(
                XNB.ap[:, :], xa, SM.ap[:, c0:c0 + 1], SM.ap[:, c0 + 5:c0 + 6],
                ALU.subtract, ALU.mult),
                reads=[xr, SM.r(f"s{s}"), SM.r(f"rs{s}")], writes=[XNB.r()], dbg=f"{dbg}.n{tc}")
        else:
            ctx.op("dve", lambda xa=xa, c0=c0: nc.vector.tensor_scalar(
                WF.ap[:, 0:2048], xa, SM.ap[:, c0:c0 + 1], SM.ap[:, c0 + 5:c0 + 6],
                ALU.subtract, ALU.mult),
                reads=[xr, SM.r(f"s{s}"), SM.r(f"rs{s}")], writes=[WF.r()], dbg=f"{dbg}.nf{tc}")
            ctx.op("dve", lambda: nc.vector.tensor_copy(XNB.ap[:, :], WF.ap[:, 0:2048]),
                   reads=[WF.r()], writes=[XNB.r()], dbg=f"{dbg}.nc{tc}")
        for g in range(4):
            tbv = bf_half(tb, 0, 512)
            for q in range(4):
                fc = g * 4 + q
                ctx.op("pe", lambda tbv=tbv, q=q, fc=fc: nc.tensor.transpose(
                    tbv[:, q * 128:(q + 1) * 128], XNB.ap[:, fc * 128:(fc + 1) * 128],
                    b.cap("ident_bf")),
                    reads=[XNB.r(), b.const_r["ident_bf"]], writes=[tb.r()],
                    dbg=f"{dbg}.T{tc}_{fc}")
            ctx.op("dve", lambda tc=tc, g=g, tbv=tbv: nc.vector.tensor_copy(
                XNT.ap.rearrange("p (f t) -> p f t", t=TOK)[:, g * 4:(g + 1) * 4,
                                                            tc * 128:(tc + 1) * 128],
                tbv.rearrange("p (f t) -> p f t", t=128)),
                reads=[tb.r()], writes=[XNT.r()], dbg=f"{dbg}.Tev{tc}_{g}")
        if gate_hook is not None:
            t5 = b.banks[5]
            for fc in range(16):
                ctx.op("pe", lambda fc=fc: nc.tensor.transpose(
                    t5.ap[:, 0:128], WF.ap[:, fc * 128:(fc + 1) * 128],
                    b.cap("ident_f32")),
                    reads=[WF.r(), b.const_r["ident_f32"]], writes=[t5.r()],
                    dbg=f"{dbg}.gT{tc}_{fc}")
                ctx.op("dve", lambda: nc.vector.tensor_copy(
                    b.GT.ap[:, 128:256], t5.ap[:, 0:128]),
                    reads=[t5.r()], writes=[b.GT.r("s")], dbg=f"{dbg}.gTe{tc}_{fc}")
                gate_hook(tc, fc)



def rhs_tile(rhs_buf, ic, tt, w=QT):
    return rhs_buf.ap[:, ic * TOK + tt * w: ic * TOK + tt * w + w]


def emit_proj(b, W, d_in, d_out, epilogue, rhs_buf=None, rhs_read=None,
              transposed_w=False, tok_tiles=NQT, dbg="proj",
              oc_list=None, tt_list=None, rhs_col_fn=None):
    ctx, nc = b.ctx, b.nc
    n_ic = d_in // 128
    tw = min(128, d_out)
    n_oc = (d_out + 127) // 128
    oc_list = oc_list if oc_list is not None else list(range(n_oc))
    tt_list = tt_list if tt_list is not None else list(range(tok_tiles))
    if rhs_col_fn is None:
        rhs_col_fn = lambda ic, tt: ic * TOK + tt * QT
    cols_per_oc = n_ic * tw
    grp = max(1, min(8192 // cols_per_oc, len(oc_list)))
    mm_banks = (b.banks[0], b.banks[1])
    slot_cols = (0, 8192)
    groups = [oc_list[i:i + grp] for i in range(0, len(oc_list), grp)]
    w_bf = W.ap.dtype == BF16
    granule = getattr(W, "granule", None)
    if granule:
        assert granule == (n_ic, grp), (granule, n_ic, grp)
    for gi, g in enumerate(groups):
        sc = slot_cols[gi % 2]
        wr = b.WB.r(f"w{2 * (gi % 2)}")
        wr2 = b.WB.r(f"w{2 * (gi % 2) + 1}")
        if granule:
            ctx.dma("sp", lambda sc=sc, n=len(g), n_ic=n_ic: b.WB.ap[:, sc:sc + n_ic * n * 128],
                    lambda W=W, g=g, n=len(g), n_ic=n_ic: W.ap[
                        :, g[0] * n_ic * 128:(g[0] + n) * n_ic * 128],
                    reads=[W.r()], writes=[wr, wr2], dbg=f"{dbg}.w{gi}")
        elif not transposed_w:
            ctx.dma("sp" if w_bf else "pool",
                lambda sc=sc, n=len(g), n_ic=n_ic: b.WB.ap[:, sc:sc + n_ic * n * tw].rearrange(
                "p (i c) -> p i c", c=n * tw),
                lambda W=W, g=g, n=len(g): W.ap.rearrange("(i p) o -> p i o", p=128)[:, :, g[0] * tw:g[0] * tw + n * tw],
                reads=[W.r()], writes=[wr, wr2], dbg=f"{dbg}.w{gi}")
        else:
            assert d_in == 2048
            tb = b.banks[5]
            for oi, oc in enumerate(g):
                ldr = b.WF.r()
                ctx.dma("sp", lambda: b.WF.ap[:, :],
                        lambda W=W, oc=oc: W.ap[oc * 128:(oc + 1) * 128, :],
                        reads=[W.r()], writes=[ldr], dbg=f"{dbg}.wl{oc}")
                for g4 in range(n_ic // 4):
                    for q in range(4):
                        ic = g4 * 4 + q
                        ctx.op("pe", lambda q=q, ic=ic: nc.tensor.transpose(
                            tb.ap[:, q * 128:(q + 1) * 128],
                            b.WF.ap[:, ic * 128:(ic + 1) * 128],
                            b.cap("ident_f32")),
                            reads=[ldr, b.const_r["ident_f32"]], writes=[tb.r()],
                            dbg=f"{dbg}.wT{oc}_{ic}")
                    ctx.op("dve", lambda sc=sc, oi=oi, g4=g4, n=len(g): nc.vector.tensor_copy(
                        b.WB.ap[:, sc:sc + n_ic * n * 128].rearrange(
                            "p (i c) -> p i c", c=n * 128)[:, g4 * 4:(g4 + 1) * 4,
                                                           oi * 128:(oi + 1) * 128],
                        tb.ap.rearrange("p (i c) -> p i c", c=128)[:, 0:4, :]),
                        reads=[tb.r()], writes=[wr, wr2], dbg=f"{dbg}.wTe{oc}_{g4}")
        for oi, oc in enumerate(g):
            for ti, tt in enumerate(tt_list):
                bank = mm_banks[(gi * len(g) + oi + ti) % 2]
                for ic in range(n_ic):
                    if granule:
                        col = sc + oi * n_ic * 128 + ic * 128
                    else:
                        col = sc + ic * len(g) * tw + oi * tw
                    ctx.op("pe", MM(bank.ap[0:tw, :],
                                    b.WB.ap[:, col:col + tw],
                                    rhs_buf.ap[:, rhs_col_fn(ic, tt):rhs_col_fn(ic, tt) + QT],
                                    start=(ic == 0), stop=(ic == n_ic - 1)),
                           reads=[wr, wr2, rhs_read], writes=[bank.r()],
                           dbg=f"{dbg}.mm{oc}_{tt}_{ic}")
                epilogue(oc, tt, bank)



def ep_to_sbuf_bf16(b, dst_buf, dst_region, col_fn, bias_ap=None, bias_reads=(), dbg="ep"):
    ctx, nc = b.ctx, b.nc

    def ep(oc, tt, bank):
        col = col_fn(oc, tt)
        if bias_ap is None:
            ctx.op("dve", lambda col=col, bank=bank: nc.vector.tensor_copy(
                dst_buf.ap[:, col:col + QT], bank.ap[:, :]),
                reads=[bank.r()], writes=[dst_region], dbg=f"{dbg}.ev{oc}_{tt}")
        else:
            ctx.op("dve", lambda col=col, bank=bank: nc.vector.tensor_scalar(
                dst_buf.ap[:, col:col + QT], bank.ap[:, :], bias_ap(oc), None, ALU.add),
                reads=[bank.r(), *bias_reads], writes=[dst_region], dbg=f"{dbg}.ev{oc}_{tt}")
    return ep


def ep_residual(b, X, scale_col=None, scale_reads=(), dbg="res", sm_bias=None):
    ctx, nc = b.ctx, b.nc
    tb = b.banks[7]

    def ep(oc, tt, bank):
        s = (oc + tt) % 2
        stg = b.PB.ap[:, s * 512:(s + 1) * 512]
        stg_r = b.PB.r(f"{s}")
        if sm_bias is None:
            ctx.op("dve", lambda bank=bank, stg=stg: nc.vector.tensor_copy(stg, bank.ap[:, :]),
                   reads=[bank.r()], writes=[stg_r], dbg=f"{dbg}.ev{oc}_{tt}")
        else:
            ctx.op("dve", lambda bank=bank, stg=stg, oc=oc: nc.vector.tensor_scalar(
                stg, bank.ap[:, :], bias_ap(b, sm_bias, oc), None, ALU.add),
                reads=[bank.r(), b.SM.r(f"bias{sm_bias}")], writes=[stg_r],
                dbg=f"{dbg}.ev{oc}_{tt}")
        if RES_LVL < 2:
            return
        tbv = bf_half(tb, 0, 512)
        for q in range(4):
            ctx.op("pe", lambda tbv=tbv, stg=stg, q=q: nc.tensor.transpose(
                tbv[:, q * 128:(q + 1) * 128], stg[:, q * 128:(q + 1) * 128],
                b.cap("ident_bf")),
                reads=[stg_r, b.const_r["ident_bf"]], writes=[tb.r()],
                dbg=f"{dbg}.T{oc}_{tt}_{q}")
        if RES_LVL < 3:
            return
        rs = b.RS.ap[:, 0:512]
        rs_r = b.RS.r()
        if scale_col is None:
            ctx.op("dve", lambda tbv=tbv, rs=rs: nc.vector.tensor_copy(
                rs.rearrange("p (q v) -> p q v", v=128), tbv.rearrange("p (q v) -> p q v", v=128)),
                reads=[tb.r()], writes=[rs_r], dbg=f"{dbg}.Te{oc}_{tt}")
        else:
            for q in range(4):
                ctx.op("dve", lambda tbv=tbv, rs=rs, q=q, tt=tt: nc.vector.tensor_scalar(
                    rs[:, q * 128:(q + 1) * 128], tbv[:, q * 128:(q + 1) * 128],
                    scale_col(tt, q), None, ALU.mult),
                    reads=[tb.r(), *scale_reads], writes=[rs_r], dbg=f"{dbg}.Ts{oc}_{tt}_{q}")
        if RES_LVL < 4:
            return
        kw = {"accum_op": ALU.add} if RES_ACCUM else {}
        ctx.dma("pool", lambda X=X, tt=tt, oc=oc: X.ap[tt * 512:(tt + 1) * 512,
                                                       oc * 128:(oc + 1) * 128].rearrange(
            "(q p) v -> p q v", p=128),
            lambda rs=rs: rs.rearrange("p (q v) -> p q v", v=128),
            reads=[rs_r], writes=[X.r()], dbg=f"{dbg}.acc{oc}_{tt}", **kw)
    return ep


SM_LN = 0
SM_STAGE = 32
SM_QMAX = 64
SM_KMAX = 128
SM_LINV = 144
SM_B1 = 152
SM_NEGB = 156
SM_NEGBR = 160
SM_ROWM = 176
SM_TSC = 192
SM_BIAS = 208
SM_LROW = 512



def emit_v_repack(b, ccVo, vg2, n_kvh, featV, dbg):
    ctx, nc = b.ctx, b.nc
    for h in range(n_kvh):
        for rank in range(2):
            for t4 in range(4):
                s = t4 % 2
                stg = b.PB.ap[:, s * 512:(s + 1) * 512]
                stg_r = b.PB.r(f"{s}")
                ctx.dma("sp", lambda stg=stg: stg.rearrange("p (c v) -> p c v", v=128),
                        lambda h=h, rank=rank, t4=t4: ccVo.ap[
                            rank * TOK + t4 * 512:rank * TOK + (t4 + 1) * 512,
                            h * 128:(h + 1) * 128].rearrange("(c p) v -> p c v", p=128),
                        reads=[ccVo.r()], writes=[stg_r], dbg=f"{dbg}.vrl{h}_{rank}_{t4}")
                ctx.dma("sp", lambda h=h, rank=rank, t4=t4: vg2.ap[
                    :, h * 4096 + rank * 2048 + t4 * 512:h * 4096 + rank * 2048 + (t4 + 1) * 512],
                    lambda stg=stg: stg,
                    reads=[stg_r], writes=[vg2.r()], dbg=f"{dbg}.vrs{h}_{rank}_{t4}")


def emit_attn(b, *, kind, n_heads, scale, q_parts, k_parts, v_dram, vcol,
              kmax_col, qmax_col, o_sink, dbg):
    ctx, nc = b.ctx, b.nc
    plan, cls, _, _ = mask_tables(kind, 0)
    S0, S1, OB, LB, TB = b.banks[2], b.banks[3], b.banks[4], b.banks[5], b.banks[6]
    tt_idx, row_idx = {}, {}
    for j in range(NQT):
        for s in range(len(plan[j])):
            if cls[(j, s)] == "tt":
                tt_idx[(j, s)] = len(tt_idx)
            elif cls[(j, s)] == "row":
                row_idx[(j, s)] = len(row_idx)
    pmask_d = b.cdram(f"pmask_{kind}")
    rmask_d = b.cdram(f"rmask_{kind}")

    for j in range(NQT):
        mask_slots = {}
        rowm_slots = {}
        for s in range(len(plan[j])):
            if (j, s) in tt_idx:
                sl = len(mask_slots)
                assert sl < 16
                mask_slots[s] = sl
                ctx.dma("sp", lambda sl=sl: b.WB.ap[:, sl * 512:(sl + 1) * 512],
                        lambda i=tt_idx[(j, s)]: pmask_d.ap[i, :, :],
                        reads=[pmask_d.r()], writes=[b.WB.r("w0"), b.WB.r("w1")],
                        dbg=f"{dbg}.mask{j}_{s}")
            elif (j, s) in row_idx:
                sl = len(rowm_slots)
                assert sl < 16
                rowm_slots[s] = sl
                ctx.dma("sp", lambda sl=sl: b.SM.ap[:, SM_ROWM + sl:SM_ROWM + sl + 1],
                        lambda i=row_idx[(j, s)]: rmask_d.ap[i, :, :],
                        reads=[rmask_d.r()], writes=[b.SM.r(f"rowm{sl}")],
                        dbg=f"{dbg}.rowm{j}_{s}")

        for h in range(n_heads):
            qc = qmax_col(h, j)
            kc_ = kmax_col(h)
            ctx.op("dve", lambda qc=qc, kc_=kc_: nc.vector.tensor_tensor(
                b.SM.ap[0:1, SM_B1:SM_B1 + 1], b.SM.ap[0:1, qc:qc + 1],
                b.SM.ap[0:1, kc_:kc_ + 1], op=ALU.mult),
                reads=[b.SM.r("qmax"), b.SM.r("kmax")], writes=[b.SM.r("negB1")],
                dbg=f"{dbg}.b1_{h}_{j}")
            ctx.op("dve", lambda: nc.vector.tensor_scalar(
                b.SM.ap[0:1, SM_B1 + 1:SM_B1 + 2], b.SM.ap[0:1, SM_B1:SM_B1 + 1],
                -1.02 * scale, None, ALU.mult),
                reads=[b.SM.r("negB1")], writes=[b.SM.r("negB2")], dbg=f"{dbg}.b2_{h}_{j}")
            ctx.op("pe", MM(TB.ap[:, 0:1], b.cap("ones_row"),
                            b.SM.ap[0:1, SM_B1 + 1:SM_B1 + 2]),
                   reads=[b.SM.r("negB2"), b.const_r["ones_row"]], writes=[TB.r()],
                   dbg=f"{dbg}.bb_{h}_{j}")
            ctx.op("dve", lambda: nc.vector.tensor_copy(
                b.SM.ap[:, SM_NEGB:SM_NEGB + 1], TB.ap[:, 0:1]),
                reads=[TB.r()], writes=[b.SM.r("negB")], dbg=f"{dbg}.bc_{h}_{j}")
            for s, sl in rowm_slots.items():
                ctx.op("dve", lambda sl=sl: nc.vector.tensor_tensor(
                    b.SM.ap[:, SM_NEGBR + sl:SM_NEGBR + sl + 1],
                    b.SM.ap[:, SM_NEGB:SM_NEGB + 1],
                    b.SM.ap[:, SM_ROWM + sl:SM_ROWM + sl + 1], op=ALU.add),
                    reads=[b.SM.r("negB"), b.SM.r(f"rowm{sl}")],
                    writes=[b.SM.r(f"negBr{sl}")], dbg=f"{dbg}.bv_{h}_{j}_{sl}")

            chunks = plan[j]
            n_ch = len(chunks)
            assert n_ch % 4 == 0 or n_ch < 4
            kp = k_parts(h)
            qp = q_parts(h)
            for si, kchunk in enumerate(chunks):
                rank, kc16 = kchunk // 16, kchunk % 16
                sb_ = (S0, S1)[si % 2]
                kv_s = (si // 4) % 2
                sub = si % 4
                kreads = []
                if sub == 0:
                    nw = min(4, n_ch - si) * 128
                    for pi, (kd, row0f, col0f, prt) in enumerate(kp):
                        qbase = qp[pi][3] if len(qp[pi]) > 3 else 0
                        kr = b.KVT.r(f"k{kv_s}_{pi}")
                        ctx.dma("sp", lambda kv_s=kv_s, pi=pi, prt=prt, qbase=qbase, nw=nw:
                                b.KVT.ap[qbase:qbase + prt,
                                         kv_s * 1536 + pi * 512:kv_s * 1536 + pi * 512 + nw],
                                lambda kd=kd, row0f=row0f, col0f=col0f, rank=rank,
                                kc16=kc16, prt=prt, nw=nw:
                                kd.ap[row0f(rank):row0f(rank) + prt,
                                      col0f(rank) + kc16 * 128:col0f(rank) + kc16 * 128 + nw],
                                reads=[kd.r()], writes=[kr], dbg=f"{dbg}.kld{j}_{h}_{si}")
                    vr = b.KVT.r(f"v{kv_s}")
                    ctx.dma("sp", lambda kv_s=kv_s, nw=nw: b.KVT.ap[:, kv_s * 1536 + 1024:
                                                                    kv_s * 1536 + 1024 + nw],
                            lambda rank=rank, kc16=kc16, h=h, nw=nw: v_dram.ap[
                                :, vcol(h) * 4096 // 128 + rank * 2048 + kc16 * 128:
                                vcol(h) * 4096 // 128 + rank * 2048 + kc16 * 128 + nw],
                            reads=[v_dram.r()], writes=[vr], dbg=f"{dbg}.vld{j}_{h}_{si}")
                kreads = [b.KVT.r(f"k{kv_s}_{pi}") for pi in range(len(kp))]
                vr = b.KVT.r(f"v{kv_s}")
                for pi, (kd, row0f, col0f, prt) in enumerate(kp):
                    tup = qp[pi]
                    qa, qprt, qread = tup[0], tup[1], tup[2]
                    qbase = tup[3] if len(tup) > 3 else 0
                    ctx.op("pe", MM(sb_.ap[:, :],
                                    b.KVT.ap[qbase:qbase + prt,
                                             kv_s * 1536 + pi * 512 + sub * 128:
                                             kv_s * 1536 + pi * 512 + (sub + 1) * 128],
                                    qa[:, j * QT:(j + 1) * QT],
                                    start=(pi == 0), stop=(pi == len(kp) - 1)),
                           reads=[kreads[pi], qread], writes=[sb_.r()],
                           dbg=f"{dbg}.smm{j}_{h}_{si}")
                if si in rowm_slots:
                    bias = b.SM.ap[:, SM_NEGBR + rowm_slots[si]:SM_NEGBR + rowm_slots[si] + 1]
                    bias_r = b.SM.r(f"negBr{rowm_slots[si]}")
                else:
                    bias = b.SM.ap[:, SM_NEGB:SM_NEGB + 1]
                    bias_r = b.SM.r("negB")
                ps = si % 2
                pb = b.PB.ap[:, ps * 512:(ps + 1) * 512]
                pb_r = b.PB.r(f"{ps}")
                ctx.op("act", lambda sb_=sb_, pb=pb, bias=bias: nc.scalar.activation(
                    out=pb, in_=sb_.ap[:, :], func=AF.Exp, bias=bias, scale=scale),
                    reads=[sb_.r(), bias_r], writes=[pb_r], dbg=f"{dbg}.exp{j}_{h}_{si}")
                if si in mask_slots:
                    sl = mask_slots[si]
                    ctx.op("dve", lambda pb=pb, sl=sl: nc.vector.tensor_mul(
                        pb, pb, b.WB.ap[:, sl * 512:(sl + 1) * 512]),
                        reads=[pb_r, b.WB.r("w0"), b.WB.r("w1")], writes=[pb_r],
                        dbg=f"{dbg}.pm{j}_{h}_{si}")
                ctx.op("pe", MM(LB.ap[0:1, 0:512], b.cap("ones_bf"), pb,
                                start=(si == 0), stop=(si == n_ch - 1)),
                       reads=[pb_r, b.const_r["ones_bf"]], writes=[LB.r()],
                       dbg=f"{dbg}.lmm{j}_{h}_{si}")
                ctx.op("pe", MM(OB.ap[:, :],
                                b.KVT.ap[:, kv_s * 1536 + 1024 + sub * 128:
                                         kv_s * 1536 + 1024 + (sub + 1) * 128],
                                pb,
                                start=(si == 0), stop=(si == n_ch - 1)),
                       reads=[pb_r, vr], writes=[OB.r()],
                       dbg=f"{dbg}.av{j}_{h}_{si}")
            ctx.op("dve", lambda: nc.vector.tensor_copy(
                b.SM.ap[0:1, SM_LROW:SM_LROW + 512], LB.ap[0:1, 0:512]),
                reads=[LB.r()], writes=[b.SM.r("lrow")], dbg=f"{dbg}.lc{j}_{h}")
            ctx.op("dve", lambda: nc.vector.reciprocal(
                b.SM.ap[0:1, SM_LROW:SM_LROW + 512], b.SM.ap[0:1, SM_LROW:SM_LROW + 512]),
                reads=[b.SM.r("lrow")], writes=[b.SM.r("lrow")], dbg=f"{dbg}.li{j}_{h}")
            ctx.op("pe", MM(TB.ap[:, 0:512], b.cap("ones_row"),
                            b.SM.ap[0:1, SM_LROW:SM_LROW + 512]),
                   reads=[b.SM.r("lrow"), b.const_r["ones_row"]], writes=[TB.r()],
                   dbg=f"{dbg}.lb{j}_{h}")
            ctx.op("dve", lambda: nc.vector.tensor_copy(
                b.WF.ap[:, 0:512], TB.ap[:, 0:512]),
                reads=[TB.r()], writes=[b.WF.r()], dbg=f"{dbg}.lbe{j}_{h}")
            o_sink(h, j, OB, b.WF.ap[:, 0:512], b.WF.r())


def osink_to_xnt(b, dbg="osink"):
    ctx, nc = b.ctx, b.nc

    def sink(h, j, ob_bank, linv_ap, linv_r):
        ctx.op("dve", lambda h=h, j=j, ob_bank=ob_bank, linv_ap=linv_ap:
               nc.vector.tensor_mul(
                   b.XNT.ap[:, h * TOK + j * 512:h * TOK + (j + 1) * 512],
                   ob_bank.ap[:, :], linv_ap),
               reads=[ob_bank.r(), linv_r], writes=[b.XNT.r()], dbg=f"{dbg}.ev{h}_{j}")
    return sink



QN_LVL = 9
RES_ACCUM = True
RES_LVL = 9


def emit_sq_norm(b, src_ap, src_r, nb_bank, dbg):
    ctx, nc = b.ctx, b.nc
    sq = b.XNB.ap[:, 0:512]
    sq_r = b.XNB.r()
    ctx.op("act", lambda src_ap=src_ap, sq=sq: nc.scalar.activation(
        out=sq, in_=src_ap, func=AF.Square),
        reads=[src_r], writes=[sq_r], dbg=dbg + ".sq")
    if QN_LVL < 2:
        return
    ctx.op("pe", MM(nb_bank.ap[0:1, 0:512], b.cap("ones_bf"), sq),
           reads=[sq_r, b.const_r["ones_bf"]], writes=[nb_bank.r()], dbg=dbg + ".nmm")


def emit_qmax_from_nb(b, nb_bank, h, j, scale_hint, dbg):
    ctx, nc = b.ctx, b.nc
    col = SM_QMAX + h * 4 + j
    if QN_LVL < 3:
        return
    ctx.op("dve", lambda: nc.vector.reduce_max(
        b.SM.ap[0:1, SM_STAGE:SM_STAGE + 1], nb_bank.ap[0:1, 0:512], axis=AX.X),
        reads=[nb_bank.r()], writes=[b.SM.r("stg0")], dbg=dbg + ".max")
    if QN_LVL < 4:
        return
    ctx.op("act", lambda col=col: nc.scalar.activation(
        out=b.SM.ap[0:1, col:col + 1], in_=b.SM.ap[0:1, SM_STAGE:SM_STAGE + 1],
        func=AF.Sqrt),
        reads=[b.SM.r("stg0")], writes=[b.SM.r("qmax")], dbg=dbg + ".sqrt")



def load_bias_cols(b, bias_dram, sm_col, dbg):
    ctx, nc = b.ctx, b.nc
    n_oc = bias_dram.ap.shape[0] // 128
    tb = b.banks[6]
    ctx.dma("sp", lambda n_oc=n_oc: b.WF.ap[0:n_oc, 0:128],
            lambda n_oc=n_oc: bias_dram.ap.rearrange("(o p) -> o p", p=128),
            reads=[bias_dram.r()], writes=[b.WF.r()], dbg=dbg + ".l")
    ctx.op("pe", lambda n_oc=n_oc: nc.tensor.transpose(
        tb.ap[:, 32:32 + n_oc], b.WF.ap[0:n_oc, 0:128],
        b.cap("ident_f32")[0:n_oc, 0:n_oc]),
        reads=[b.WF.r(), b.const_r["ident_f32"]], writes=[tb.r()], dbg=dbg + ".T")
    ctx.op("dve", lambda n_oc=n_oc: nc.vector.tensor_copy(
        b.SM.ap[:, sm_col:sm_col + n_oc], tb.ap[:, 32:32 + n_oc]),
        reads=[tb.r()], writes=[b.SM.r(f"bias{sm_col}")], dbg=dbg + ".e")


def bias_ap(b, sm_col, oc):
    if sm_col is None:
        return None
    return b.SM.ap[:, sm_col + oc:sm_col + oc + 1]


def ep_q_path(b, NB, dbg, sm_bias=None, qh_col=None):
    ctx, nc = b.ctx, b.nc
    qh_col = qh_col or (lambda h, tt: h * TOK + tt * 512)

    def ep(h, tt, bank):
        col = qh_col(h, tt)
        qr = b.QH.r()
        if sm_bias is None:
            ctx.op("dve", lambda bank=bank, col=col: nc.vector.tensor_copy(
                b.QH.ap[:, col:col + 512], bank.ap[:, :]),
                reads=[bank.r()], writes=[qr], dbg=f"{dbg}.qev{h}_{tt}")
        else:
            ctx.op("dve", lambda bank=bank, col=col, h=h: nc.vector.tensor_scalar(
                b.QH.ap[:, col:col + 512], bank.ap[:, :], bias_ap(b, sm_bias, h), None, ALU.add),
                reads=[bank.r(), b.SM.r(f"bias{sm_bias}")], writes=[qr], dbg=f"{dbg}.qev{h}_{tt}")
        emit_sq_norm(b, b.QH.ap[:, col:col + 512], qr, NB, f"{dbg}.qn{h}_{tt}")
        emit_qmax_from_nb(b, NB, h, tt, None, f"{dbg}.qm{h}_{tt}")
    return ep


def ep_k_path(b, NB, ccK, nrow0, dbg, sm_bias=None):
    ctx, nc = b.ctx, b.nc

    def ep(h, tt, bank):
        s = tt % 2
        stg = b.PB.ap[:, s * 512:(s + 1) * 512]
        stg_r = b.PB.r(f"{s}")
        if sm_bias is None:
            ctx.op("dve", lambda bank=bank, stg=stg: nc.vector.tensor_copy(stg, bank.ap[:, :]),
                   reads=[bank.r()], writes=[stg_r], dbg=f"{dbg}.kev{h}_{tt}")
        else:
            ctx.op("dve", lambda bank=bank, stg=stg, h=h: nc.vector.tensor_scalar(
                stg, bank.ap[:, :], bias_ap(b, sm_bias, h), None, ALU.add),
                reads=[bank.r(), b.SM.r(f"bias{sm_bias}")], writes=[stg_r], dbg=f"{dbg}.kev{h}_{tt}")
        ctx.dma("sp", lambda h=h, tt=tt: ccK.ap[h * 128:(h + 1) * 128, tt * 512:(tt + 1) * 512],
                lambda stg=stg: stg,
                reads=[stg_r], writes=[ccK.r()], dbg=f"{dbg}.kst{h}_{tt}")
        emit_sq_norm(b, stg, stg_r, NB, f"{dbg}.kn{h}_{tt}")
        ctx.op("dve", lambda: nc.vector.tensor_copy(
            b.XNB.ap[0:1, 512:1024], NB.ap[0:1, 0:512]),
            reads=[NB.r()], writes=[b.XNB.r()], dbg=f"{dbg}.kne{h}_{tt}")
        ctx.dma("sp", lambda h=h, tt=tt, nrow0=nrow0: ccK.ap[nrow0 + h:nrow0 + h + 1,
                                                            tt * 512:(tt + 1) * 512],
                lambda: b.XNB.ap[0:1, 512:1024],
                reads=[b.XNB.r()], writes=[ccK.r()], dbg=f"{dbg}.knst{h}_{tt}")
    return ep


def ep_v_path(b, ccV, dbg, sm_bias=None, vcol=None):
    ctx, nc = b.ctx, b.nc
    tb = b.banks[7]
    vcol = vcol or (lambda h: h * 128)

    def ep(h, tt, bank):
        s = tt % 2
        stg = b.PB.ap[:, s * 512:(s + 1) * 512]
        stg_r = b.PB.r(f"{s}")
        if sm_bias is None:
            ctx.op("dve", lambda bank=bank, stg=stg: nc.vector.tensor_copy(stg, bank.ap[:, :]),
                   reads=[bank.r()], writes=[stg_r], dbg=f"{dbg}.vev{h}_{tt}")
        else:
            ctx.op("dve", lambda bank=bank, stg=stg, h=h: nc.vector.tensor_scalar(
                stg, bank.ap[:, :], bias_ap(b, sm_bias, h), None, ALU.add),
                reads=[bank.r(), b.SM.r(f"bias{sm_bias}")], writes=[stg_r], dbg=f"{dbg}.vev{h}_{tt}")
        tbv = bf_half(tb, 0, 512)
        for q in range(4):
            ctx.op("pe", lambda tbv=tbv, stg=stg, q=q: nc.tensor.transpose(
                tbv[:, q * 128:(q + 1) * 128], stg[:, q * 128:(q + 1) * 128],
                b.cap("ident_bf")),
                reads=[stg_r, b.const_r["ident_bf"]], writes=[tb.r()],
                dbg=f"{dbg}.vT{h}_{tt}_{q}")
        s2 = 1 - s
        stg2 = b.PB.ap[:, s2 * 512:(s2 + 1) * 512]
        stg2_r = b.PB.r(f"{s2}")
        ctx.op("dve", lambda tbv=tbv, stg2=stg2: nc.vector.tensor_copy(stg2, tbv),
               reads=[tb.r()], writes=[stg2_r], dbg=f"{dbg}.vTe{h}_{tt}")
        ctx.dma("sp", lambda h=h, tt=tt: ccV.ap[tt * 512:(tt + 1) * 512,
                                                vcol(h):vcol(h) + 128].rearrange(
            "(sq p) v -> p sq v", p=128),
            lambda stg2=stg2: stg2.rearrange("p (sq v) -> p sq v", v=128),
            reads=[stg2_r], writes=[ccV.r()], dbg=f"{dbg}.vst{h}_{tt}")
    return ep



L0AA_SUB = 4
L0AA_PARTS = 5


def sub_l0aa(b):
    ctx, nc = b.ctx, b.nc
    X = b.X
    Wqkv = b.dram["L0_attn_a_w_qkv"]
    Wout = b.dram["L0_attn_a_w_out"]
    KR = 2064
    ccK = b.dram_buf("ccK_l0aa", [KR, TOK], BF16, "Internal")
    ccKo = b.dram_buf("ccKo_l0aa", [2 * KR, TOK], BF16, "Internal")
    ccV = b.dram_buf("ccV_l0aa", [TOK, D], BF16, "Internal")
    ccVo = b.dram_buf("ccVo_l0aa", [2 * TOK, D], BF16, "Internal")
    vg2 = b.dram_buf("vg2_l0aa", [128, 16 * 4096], BF16, "Internal")

    emit_ln(b, X, dbg="l0aa.ln")
    if L0AA_PARTS < 2:
        return
    NB = b.banks[2]
    epq = ep_q_path(b, NB, "l0aa")
    epk = ep_k_path(b, NB, ccK, 2048, "l0aa")
    epv = ep_v_path(b, ccV, "l0aa")

    def ep(oc, tt, bank):
        if oc < 16:
            epq(oc, tt, bank)
        elif oc < 32:
            epk(oc - 16, tt, bank)
        else:
            epv(oc - 32, tt, bank)

    emit_proj(b, Wqkv, D, 3 * D, ep, rhs_buf=b.XNT, rhs_read=b.XNT.r(), dbg="l0aa.qkv")

    if L0AA_PARTS < 3:
        return
    rg = RG_OVERRIDE or [[0, 1], [2, 3], [4, 5], [6, 7]]
    ctx.collective(ccK, ccKo, rg, dbg="l0aa_k")
    ctx.collective(ccV, ccVo, rg, dbg="l0aa_v")
    emit_v_repack(b, ccVo, vg2, 16, D, "l0aa.vr")

    emit_kmax(b, ccKo, KR, 2048, 16, "l0aa.kmax")

    if L0AA_PARTS < 4:
        return
    emit_attn(b, kind="causal", n_heads=16, scale=1.0 / math.sqrt(128.0),
              q_parts=lambda h: [(b.QH.ap[:, h * TOK:(h + 1) * TOK], 128, b.QH.r())],
              k_parts=lambda h: [(ccKo, lambda rank: rank * KR + h * 128, lambda rank: 0, 128)],
              v_dram=vg2,
              vcol=lambda h: h * 128,
              kmax_col=lambda h: SM_KMAX + h,
              qmax_col=lambda h, j: SM_QMAX + h * 4 + j,
              o_sink=osink_to_xnt(b, "l0aa.os"), dbg="l0aa.at")

    if L0AA_PARTS < 5:
        return
    emit_proj(b, Wout, D, D, ep_residual(b, X, dbg="l0aa.res"),
              rhs_buf=b.XNT, rhs_read=b.XNT.r(), dbg="l0aa.o")


def emit_kmax(b, ccKo, KR, nrow0, n_heads, dbg):
    ctx, nc = b.ctx, b.nc
    tb = b.banks[6]
    ctx.dma("sp", lambda n=n_heads: b.XNB.ap[0:n, 0:TOK],
            lambda: ccKo.ap[nrow0:nrow0 + n_heads, :],
            reads=[ccKo.r()], writes=[b.XNB.r()], dbg=dbg + ".knl0")
    ctx.op("dve", lambda: nc.vector.reduce_max(
        b.SM.ap[0:n_heads, SM_STAGE:SM_STAGE + 1], b.XNB.ap[0:n_heads, 0:TOK], axis=AX.X),
        reads=[b.XNB.r()], writes=[b.SM.r("kms0")], dbg=dbg + ".m0")
    ctx.dma("sp", lambda n=n_heads: b.XNB.ap[0:n, 0:TOK],
            lambda: ccKo.ap[KR + nrow0:KR + nrow0 + n_heads, :],
            reads=[ccKo.r()], writes=[b.XNB.r()], dbg=dbg + ".knl1")
    ctx.op("dve", lambda: nc.vector.reduce_max(
        b.SM.ap[0:n_heads, SM_STAGE + 1:SM_STAGE + 2], b.XNB.ap[0:n_heads, 0:TOK], axis=AX.X),
        reads=[b.XNB.r()], writes=[b.SM.r("kms1")], dbg=dbg + ".m1")
    ctx.op("dve", lambda: nc.vector.tensor_tensor(
        b.SM.ap[0:n_heads, SM_STAGE + 2:SM_STAGE + 3],
        b.SM.ap[0:n_heads, SM_STAGE:SM_STAGE + 1],
        b.SM.ap[0:n_heads, SM_STAGE + 1:SM_STAGE + 2], op=ALU.max),
        reads=[b.SM.r("kms0"), b.SM.r("kms1")], writes=[b.SM.r("kms2")], dbg=dbg + ".mx")
    ctx.op("act", lambda: nc.scalar.activation(
        out=b.SM.ap[0:n_heads, SM_STAGE + 3:SM_STAGE + 4],
        in_=b.SM.ap[0:n_heads, SM_STAGE + 2:SM_STAGE + 3], func=AF.Sqrt),
        reads=[b.SM.r("kms2")], writes=[b.SM.r("kms3")], dbg=dbg + ".sq")
    ctx.op("pe", lambda: nc.tensor.transpose(
        tb.ap[0:1, 16:16 + n_heads],
        b.SM.ap[0:n_heads, SM_STAGE + 3:SM_STAGE + 4],
        b.cap("ident_f32")[0:n_heads, 0:n_heads]),
        reads=[b.SM.r("kms3"), b.const_r["ident_f32"]], writes=[tb.r()], dbg=dbg + ".T")
    ctx.op("dve", lambda: nc.vector.tensor_copy(
        b.SM.ap[0:1, SM_KMAX:SM_KMAX + n_heads], tb.ap[0:1, 16:16 + n_heads]),
        reads=[tb.r()], writes=[b.SM.r("kmax")], dbg=dbg + ".Te")



SUB_WEIGHTS = {
    "l0aa": ["L0_attn_a_w_qkv", "L0_attn_a_w_out"],
    "l0ffn1": ["L0_ffn_w_gate_up", "L0_ffn_w_proj"],
    "l0ab": ["L0_attn_b_q_kernel", "L0_attn_b_kv_kernel", "L0_attn_b_out_kernel",
             "L0_attn_b_q_norm_scale", "L0_attn_b_k_norm_scale"],
    "l0ffn2": ["L0_ffn_w_gate_up", "L0_ffn_w_proj"],
    "l1aa": ["L1_attn_a_w_dq", "L1_attn_a_g_q", "L1_attn_a_w_uq", "L1_attn_a_w_dkv",
             "L1_attn_a_w_kr", "L1_attn_a_g_kv", "L1_attn_a_w_uk", "L1_attn_a_w_uv",
             "L1_attn_a_w_o"],
    "l1ffn1": ["L1_ffn_gate_w", "L1_ffn_w_gate", "L1_ffn_w_up", "L1_ffn_w_down",
               "L1_ffn_sw_gate", "L1_ffn_sw_up", "L1_ffn_sw_down", "L1_ffn_shared_gate_w"],
    "l1ab": ["L1_attn_b_Wq", "L1_attn_b_bq", "L1_attn_b_Wk", "L1_attn_b_bk",
             "L1_attn_b_Wv", "L1_attn_b_bv", "L1_attn_b_Wo", "L1_attn_b_bo"],
    "l1ffn2": ["L1_ffn_gate_w", "L1_ffn_w_gate", "L1_ffn_w_up", "L1_ffn_w_down",
               "L1_ffn_sw_gate", "L1_ffn_sw_up", "L1_ffn_sw_down", "L1_ffn_shared_gate_w"],
}

WEIGHT_SHAPES = {
    "L0_attn_a_w_qkv": (2048, 6144), "L0_attn_a_w_out": (2048, 2048),
    "L0_attn_b_q_kernel": (16, 2048, 128), "L0_attn_b_kv_kernel": (2, 8, 2048, 128),
    "L0_attn_b_out_kernel": (16, 128, 2048), "L0_attn_b_q_norm_scale": (128,),
    "L0_attn_b_k_norm_scale": (128,),
    "L0_ffn_w_gate_up": (16384, 2048), "L0_ffn_w_proj": (2048, 8192),
    "L1_attn_a_w_dq": (2048, 768), "L1_attn_a_g_q": (768,),
    "L1_attn_a_w_uq": (768, 3072), "L1_attn_a_w_dkv": (2048, 512),
    "L1_attn_a_w_kr": (2048, 64), "L1_attn_a_g_kv": (512,),
    "L1_attn_a_w_uk": (512, 2048), "L1_attn_a_w_uv": (512, 2048),
    "L1_attn_a_w_o": (2048, 2048),
    "L1_attn_b_Wq": (2048, 2048), "L1_attn_b_bq": (2048,),
    "L1_attn_b_Wk": (2048, 2048), "L1_attn_b_bk": (2048,),
    "L1_attn_b_Wv": (2048, 2048), "L1_attn_b_bv": (2048,),
    "L1_attn_b_Wo": (2048, 2048), "L1_attn_b_bo": (2048,),
    "L1_ffn_gate_w": (8, 2048), "L1_ffn_w_gate": (8, 2048, 1408),
    "L1_ffn_w_up": (8, 2048, 1408), "L1_ffn_w_down": (8, 1408, 2048),
    "L1_ffn_sw_gate": (2048, 5632), "L1_ffn_sw_up": (2048, 5632),
    "L1_ffn_sw_down": (5632, 2048), "L1_ffn_shared_gate_w": (2048, 1),
}


def build_nc(sub_names, verbose=False):
    nc = bass.Bass("TRN2", target_bir_lowering=False, debug=False)
    es = ExitStack()
    b = Builder(nc, es)
    b.parity = None
    x_in = b.dram_buf("x_in", [TOK, D], F32, "ExternalInput")
    x_out = b.dram_buf("x_out", [TOK, D], F32, "ExternalOutput")
    X = b.dram_buf("Xres", [TOK, D], F32, "Internal")
    b.X = X

    wnames = []
    for s in sub_names:
        for w in SUB_WEIGHTS[s]:
            if w not in wnames:
                wnames.append(w)
    for w in wnames:
        b.dram_buf(w, WEIGHT_SHAPES[w], F32, "ExternalInput")

    cshapes = build_constants(0)
    b.load_consts(cshapes)

    ctx = b.ctx
    for i in range(4):
        ctx.dma("pool", lambda i=i: X.ap[i * 512:(i + 1) * 512, :],
                lambda i=i: x_in.ap[i * 512:(i + 1) * 512, :],
                reads=[x_in.r()], writes=[X.r()], dbg=f"xin{i}")

    seen = {}
    for s in sub_names:
        seen[s] = seen.get(s, 0) + 1
        SUB_EMITTERS[s](b)

    for i in range(4):
        ctx.dma("pool", lambda i=i: x_out.ap[i * 512:(i + 1) * 512, :],
                lambda i=i: X.ap[i * 512:(i + 1) * 512, :],
                reads=[X.r()], writes=[x_out.r()], dbg=f"xout{i}")

    ctx.finalize()
    if verbose:
        tot = {e: len(ctx.ops[e]) for e in ENGINES}
        nw = sum(len(o.waits) for e in ENGINES for o in ctx.ops[e])
        print(f"[build] ops={tot} waits={nw} sems={ctx.n_sems}")
    with nc.Block() as block:
        ctx.replay(block)
    es.close()
    return nc



SUB_EMITTERS = {
    "l0aa": sub_l0aa,
}


def shard_x(x):
    out = []
    for c in range(NCORES):
        s, p = c // 2, c % 2
        xc = np.ascontiguousarray(x[s].reshape(8, 512, D)[p::2].reshape(TOK, D))
        out.append(xc)
    return out


def unshard_x(parts):
    x = np.zeros((4, T_GLOBAL_, D), parts[0].dtype)
    for c in range(NCORES):
        s, p = c // 2, c % 2
        x[s].reshape(8, 512, D)[p::2] = parts[c].reshape(NQT, 512, D)
    return x


T_GLOBAL_ = 4096

_NC_CACHE = {}


def _get_nc(subs):
    key = tuple(subs)
    if key not in _NC_CACHE:
        _NC_CACHE[key] = build_nc(list(subs), verbose=False)
    return _NC_CACHE[key]


def run_subs(subs, x, weights):
    nc = _get_nc(subs)
    shards = shard_x(np.asarray(x, np.float32))
    in_maps = []
    for c in range(NCORES):
        consts = build_constants(c)
        m = {"x_in": shards[c]}
        for s in subs:
            for w in SUB_WEIGHTS[s]:
                m[w] = np.asarray(weights[w], np.float32)
        for k, v in consts.items():
            m["c_" + k] = v
        in_maps.append(m)
    res = run_bass_kernel_spmd(nc, in_maps, core_ids=list(range(NCORES)))
    return unshard_x([r["x_out"] for r in res.results])


def kernel(**inputs):
    x = inputs["L0_attn_a_x"]
    subs = ["l0aa", "l0ffn1", "l0ab", "l0ffn2", "l1aa", "l1ffn1", "l1ab", "l1ffn2"]
    return run_subs(subs, x, inputs).astype(np.float32)



def sub_l1ab(b):
    ctx, nc = b.ctx, b.nc
    X = b.X
    KR = 2064
    ccK = b.dram_buf("ccK_l1ab", [KR, TOK], BF16, "Internal")
    ccKo = b.dram_buf("ccKo_l1ab", [2 * KR, TOK], BF16, "Internal")
    ccV = b.dram_buf("ccV_l1ab", [TOK, D], BF16, "Internal")
    ccVo = b.dram_buf("ccVo_l1ab", [2 * TOK, D], BF16, "Internal")
    vg2 = b.dram_buf("vg2_l1ab", [128, 16 * 4096], BF16, "Internal")

    emit_ln(b, X, dbg="l1ab.ln")
    NB = b.banks[2]
    load_bias_cols(b, b.dram["L1_attn_b_bq"], SM_BIAS, "l1ab.bq")
    load_bias_cols(b, b.dram["L1_attn_b_bk"], SM_BIAS + 16, "l1ab.bk")
    load_bias_cols(b, b.dram["L1_attn_b_bv"], SM_BIAS + 32, "l1ab.bv")
    load_bias_cols(b, b.dram["L1_attn_b_bo"], SM_BIAS + 48, "l1ab.bo")

    emit_proj(b, b.dram["L1_attn_b_Wq"], D, D, ep_q_path(b, NB, "l1ab", sm_bias=SM_BIAS),
              rhs_buf=b.XNT, rhs_read=b.XNT.r(), transposed_w=True, dbg="l1ab.q")
    emit_proj(b, b.dram["L1_attn_b_Wk"], D, D,
              ep_k_path(b, NB, ccK, 2048, "l1ab", sm_bias=SM_BIAS + 16),
              rhs_buf=b.XNT, rhs_read=b.XNT.r(), transposed_w=True, dbg="l1ab.k")
    emit_proj(b, b.dram["L1_attn_b_Wv"], D, D,
              ep_v_path(b, ccV, "l1ab", sm_bias=SM_BIAS + 32),
              rhs_buf=b.XNT, rhs_read=b.XNT.r(), transposed_w=True, dbg="l1ab.v")

    rg = RG_OVERRIDE or [[0, 1], [2, 3], [4, 5], [6, 7]]
    ctx.collective(ccK, ccKo, rg, dbg="l1ab_k")
    ctx.collective(ccV, ccVo, rg, dbg="l1ab_v")
    emit_v_repack(b, ccVo, vg2, 16, D, "l1ab.vr")
    emit_kmax(b, ccKo, KR, 2048, 16, "l1ab.kmax")

    emit_attn(b, kind="full", n_heads=16, scale=1.0 / math.sqrt(128.0),
              q_parts=lambda h: [(b.QH.ap[:, h * TOK:(h + 1) * TOK], 128, b.QH.r())],
              k_parts=lambda h: [(ccKo, lambda rank: rank * KR + h * 128, lambda rank: 0, 128)],
              v_dram=vg2,
              vcol=lambda h: h * 128,
              kmax_col=lambda h: SM_KMAX + h,
              qmax_col=lambda h, j: SM_QMAX + h * 4 + j,
              o_sink=osink_to_xnt(b, "l1ab.os"), dbg="l1ab.at")

    emit_proj(b, b.dram["L1_attn_b_Wo"], D, D,
              ep_residual(b, X, dbg="l1ab.res", sm_bias=SM_BIAS + 48),
              rhs_buf=b.XNT, rhs_read=b.XNT.r(), transposed_w=True, dbg="l1ab.o")


SUB_EMITTERS.update({
    "l1ab": sub_l1ab,
})



def emit_quant_cache(b, W, n_row_chunks, width, cache, cache_col_of, dbg):
    ctx, nc = b.ctx, b.nc
    n_cs = (width + 2047) // 2048
    t1, t2 = b.banks[5], b.banks[6]
    for r in range(n_row_chunks):
        for cs in range(n_cs):
            w0 = cs * 2048
            wd = min(2048, width - w0)
            ng = wd // 64
            nf = wd // 128
            wb = b.QH.ap[:, 0:wd]
            wb_r = b.QH.r()
            ctx.dma("pool", lambda wb=wb: wb,
                    lambda W=W, r=r, w0=w0, wd=wd: W.ap[r * 128:(r + 1) * 128, w0:w0 + wd],
                    reads=[W.r()], writes=[wb_r], dbg=f"{dbg}.wl{r}_{cs}")
            sgn = b.TMA.ap[:, 0:wd]
            a2 = b.TMA.ap[:, 2048:2048 + wd]
            ctx.op("act", lambda wb=wb, sgn=sgn: nc.scalar.activation(
                out=sgn, in_=wb, func=AF.Sign),
                reads=[wb_r], writes=[b.TMA.r("0")], dbg=f"{dbg}.sgn{r}_{cs}")
            ctx.op("act", lambda wb=wb, a2=a2: nc.scalar.activation(
                out=a2, in_=wb, func=AF.Abs, scale=2.0),
                reads=[wb_r], writes=[b.TMA.r("1")], dbg=f"{dbg}.abs{r}_{cs}")
            ctx.op("dve", lambda a2=a2, ng=ng: nc.vector.tensor_reduce(
                b.WF.ap[:, 0:ng], a2.rearrange("p (g k) -> p g k", k=64),
                axis=AX.X, op=ALU.add),
                reads=[b.TMA.r("1")], writes=[b.WF.r()], dbg=f"{dbg}.gs{r}_{cs}")
            ctx.op("dve", lambda ng=ng: nc.vector.tensor_scalar(
                b.WF.ap[:, 64:64 + ng], b.WF.ap[:, 0:ng], 1.0 / 128.0, 1e-8,
                ALU.mult, ALU.max),
                reads=[b.WF.r()], writes=[b.WF.r()], dbg=f"{dbg}.sc{r}_{cs}")
            ctx.op("dve", lambda ng=ng: nc.vector.tensor_copy(
                b.XNB.ap[:, 0:ng], b.WF.ap[:, 64:64 + ng]),
                reads=[b.WF.r()], writes=[b.XNB.r()], dbg=f"{dbg}.sb{r}_{cs}")
            ctx.op("dve", lambda ng=ng: nc.vector.tensor_copy(
                b.WF.ap[:, 128:128 + ng], b.XNB.ap[:, 0:ng]),
                reads=[b.XNB.r()], writes=[b.WF.r()], dbg=f"{dbg}.srt{r}_{cs}")
            ctx.op("dve", lambda a2=a2, ng=ng: nc.vector.tensor_tensor(
                a2.rearrange("p (g k) -> p g k", k=64),
                a2.rearrange("p (g k) -> p g k", k=64),
                b.WF.ap[:, 128:128 + ng].rearrange("p (g one) -> p g one", one=1).to_broadcast(
                    [128, ng, 64]), op=ALU.is_gt),
                reads=[b.TMA.r("1"), b.WF.r()], writes=[b.TMA.r("1")], dbg=f"{dbg}.msk{r}_{cs}")
            ctx.op("dve", lambda sgn=sgn, a2=a2: nc.vector.tensor_mul(sgn, a2, sgn),
                   reads=[b.TMA.r("0"), b.TMA.r("1")], writes=[b.TMA.r("0")],
                   dbg=f"{dbg}.qs{r}_{cs}")
            ctx.op("dve", lambda wb=wb, sgn=sgn, ng=ng: nc.vector.tensor_tensor(
                wb.rearrange("p (g k) -> p g k", k=64),
                sgn.rearrange("p (g k) -> p g k", k=64),
                b.WF.ap[:, 128:128 + ng].rearrange("p (g one) -> p g one", one=1).to_broadcast(
                    [128, ng, 64]), op=ALU.mult),
                reads=[b.TMA.r("0"), b.WF.r()], writes=[wb_r], dbg=f"{dbg}.wq{r}_{cs}")
            for g4 in range(nf // 4):
                tb = (t1, t2)[g4 % 2]
                tbv = bf_half(tb, 0, 512)
                for q in range(4):
                    ic = g4 * 4 + q
                    ctx.op("pe", lambda tbv=tbv, wb=wb, q=q, ic=ic: nc.tensor.transpose(
                        tbv[:, q * 128:(q + 1) * 128], wb[:, ic * 128:(ic + 1) * 128],
                        b.cap("ident_bf")),
                        reads=[wb_r, b.const_r["ident_bf"]], writes=[tb.r()],
                        dbg=f"{dbg}.T{r}_{cs}_{ic}")
                ctx.op("dve", lambda tbv=tbv, g4=g4: nc.vector.tensor_copy(
                    b.QH.ap[:, 2048 + g4 * 512:2048 + (g4 + 1) * 512], tbv),
                       reads=[tb.r()], writes=[b.QH.r()], dbg=f"{dbg}.Te{r}_{cs}_{g4}")
            ctx.dma("sp", lambda r=r, cs=cs, nf=nf: cache.ap[:, cache_col_of(r) + cs * 2048:
                                                             cache_col_of(r) + cs * 2048 + nf * 128],
                    lambda nf=nf: b.QH.ap[:, 2048:2048 + nf * 128],
                    reads=[b.QH.r()], writes=[cache.r()], dbg=f"{dbg}.cs{r}_{cs}")


def sub_l0ffn(b):
    ctx, nc = b.ctx, b.nc
    X = b.X
    Wgu = b.dram["L0_ffn_w_gate_up"]
    Wp = b.dram["L0_ffn_w_proj"]
    if not hasattr(b, "l0ffn_cache"):
        cgu = b.dram_buf("cache_gu", [128, 262144], BF16, "Internal")
        cgu.granule = (16, 4)
        cp = b.dram_buf("cache_p", [128, 131072], BF16, "Internal")
        cp.granule = (64, 1)
        emit_quant_cache(b, Wgu, 128, 2048, cgu,
                         lambda r: (2 * r if r < 64 else 2 * (r - 64) + 1) * 2048, "l0f.qgu")
        emit_quant_cache(b, Wp, 16, 8192, cp, lambda r: r * 8192, "l0f.qp")
        b.l0ffn_cache = (cgu, cp)
    cgu, cp = b.l0ffn_cache

    emit_ln(b, X, dbg="l0f.ln")
    NB = b.banks[2]
    EPS = 1.1920928955078125e-07

    for tt in range(NQT):
        state = {}

        def ep_gu(oc, tt_, bank, state=state):
            hc = oc // 2
            if oc % 2 == 0:
                s = hc % 2
                hg = b.PB.ap[:, s * 512:(s + 1) * 512]
                ctx.op("act", lambda bank=bank, hg=hg: nc.scalar.activation(
                    out=hg, in_=bank.ap[:, :], func=AF.Silu),
                    reads=[bank.r()], writes=[b.PB.r(f"{s}")], dbg=f"l0f.silu{hc}_{tt_}")
                state["hg"] = (hg, b.PB.r(f"{s}"))
            else:
                hg, hg_r = state["hg"]
                s2 = 1 - (hc % 2)
                hu = b.XNB.ap[:, s2 * 512:(s2 + 1) * 512]
                ctx.op("dve", lambda bank=bank, hu=hu: nc.vector.tensor_copy(hu, bank.ap[:, :]),
                       reads=[bank.r()], writes=[b.XNB.r()], dbg=f"l0f.uev{hc}_{tt_}")
                hcol = hc * 512
                ctx.op("dve", lambda hg=hg, hu=hu, hcol=hcol: nc.vector.tensor_mul(
                    b.QH.ap[:, hcol:hcol + 512], hg, hu),
                    reads=[hg_r, b.XNB.r()], writes=[b.QH.r()], dbg=f"l0f.h{hc}_{tt_}")
                sq = b.WF.ap[:, 0:512]
                ctx.op("act", lambda hcol=hcol, sq=sq: nc.scalar.activation(
                    out=sq, in_=b.QH.ap[:, hcol:hcol + 512], func=AF.Square),
                    reads=[b.QH.r()], writes=[b.WF.r()], dbg=f"l0f.h2{hc}_{tt_}")
                sqb = b.XNB.ap[:, 1024:1536]
                ctx.op("dve", lambda sq=sq, sqb=sqb: nc.vector.tensor_copy(sqb, sq),
                       reads=[b.WF.r()], writes=[b.XNB.r()], dbg=f"l0f.h2b{hc}_{tt_}")
                ctx.op("pe", MM(NB.ap[0:1, 0:512], b.cap("ones_bf"), sqb,
                                start=(hc == 0), stop=(hc == 63)),
                       reads=[b.XNB.r(), b.const_r["ones_bf"]], writes=[NB.r()],
                       dbg=f"l0f.nmm{hc}_{tt_}")

        emit_proj(b, cgu, D, 16384, ep_gu, rhs_buf=b.XNT, rhs_read=b.XNT.r(),
                  tt_list=[tt], dbg=f"l0f.gu{tt}")

        ctx.op("dve", lambda: nc.vector.tensor_scalar(
            b.SM.ap[0:1, SM_LROW:SM_LROW + 512], NB.ap[0:1, 0:512],
            1.0 / 8192.0, EPS, ALU.mult, ALU.add),
            reads=[NB.r()], writes=[b.SM.r("lrow")], dbg=f"l0f.ms{tt}")
        ctx.op("act", lambda: nc.scalar.activation(
            out=b.SM.ap[0:1, SM_LROW:SM_LROW + 512],
            in_=b.SM.ap[0:1, SM_LROW:SM_LROW + 512], func=AF.Sqrt),
            reads=[b.SM.r("lrow")], writes=[b.SM.r("lrow")], dbg=f"l0f.sq{tt}")
        ctx.op("dve", lambda: nc.vector.reciprocal(
            b.SM.ap[0:1, SM_LROW:SM_LROW + 512], b.SM.ap[0:1, SM_LROW:SM_LROW + 512]),
            reads=[b.SM.r("lrow")], writes=[b.SM.r("lrow")], dbg=f"l0f.rc{tt}")
        TB = b.banks[6]
        for qs in range(4):
            ctx.op("pe", lambda qs=qs: nc.tensor.transpose(
                TB.ap[:, 8 + qs:9 + qs],
                b.SM.ap[0:1, SM_LROW + qs * 128:SM_LROW + (qs + 1) * 128],
                b.cap("ident_f32")[0:1, 0:1]),
                reads=[b.SM.r("lrow"), b.const_r["ident_f32"]], writes=[TB.r()],
                dbg=f"l0f.rT{tt}_{qs}")
            ctx.op("dve", lambda qs=qs: nc.vector.tensor_copy(
                b.SM.ap[:, SM_TSC + qs:SM_TSC + qs + 1], TB.ap[:, 8 + qs:9 + qs]),
                reads=[TB.r()], writes=[b.SM.r("tsc")], dbg=f"l0f.rTe{tt}_{qs}")

        emit_proj(b, cp, 8192, D,
                  ep_residual(b, X, scale_col=lambda tt_, qs: b.SM.ap[:, SM_TSC + qs:SM_TSC + qs + 1],
                              scale_reads=(b.SM.r("tsc"),), dbg=f"l0f.res{tt}"),
                  rhs_buf=b.QH, rhs_read=b.QH.r(),
                  rhs_col_fn=lambda ic, tt_: ic * 512,
                  tt_list=[tt], dbg=f"l0f.dn{tt}")


SUB_EMITTERS.update({"l0ffn1": sub_l0ffn, "l0ffn2": sub_l0ffn})



def sub_l0ab(b):
    ctx, nc = b.ctx, b.nc
    X = b.X
    QK = b.dram["L0_attn_b_q_kernel"]
    KVK = b.dram["L0_attn_b_kv_kernel"]
    OK = b.dram["L0_attn_b_out_kernel"]
    KR = 1032
    ccK = b.dram_buf("ccK_l0ab", [KR, TOK], BF16, "Internal")
    ccKo = b.dram_buf("ccKo_l0ab", [2 * KR, TOK], BF16, "Internal")
    ccV = b.dram_buf("ccV_l0ab", [TOK, 1024], BF16, "Internal")
    ccVo = b.dram_buf("ccVo_l0ab", [2 * TOK, 1024], BF16, "Internal")
    vg2 = b.dram_buf("vg2_l0ab", [128, 8 * 4096], BF16, "Internal")

    emit_ln(b, X, dbg="l0ab.ln")
    ctx.dma("sp", lambda: b.SM.ap[:, SM_BIAS:SM_BIAS + 1],
            lambda: b.dram["L0_attn_b_q_norm_scale"].ap.unsqueeze(1),
            reads=[b.dram["L0_attn_b_q_norm_scale"].r()], writes=[b.SM.r("biasq")],
            dbg="l0ab.gq")
    ctx.dma("sp", lambda: b.SM.ap[:, SM_BIAS + 1:SM_BIAS + 2],
            lambda: b.dram["L0_attn_b_k_norm_scale"].ap.unsqueeze(1),
            reads=[b.dram["L0_attn_b_k_norm_scale"].r()], writes=[b.SM.r("biask")],
            dbg="l0ab.gk")
    NB = b.banks[2]

    def broadcast_gain2(sm_col, kv_col, reg, dbgn):
        T1 = b.banks[6]
        ctx.op("pe", lambda: nc.tensor.transpose(
            T1.ap[0:1, 0:128], b.SM.ap[:, sm_col:sm_col + 1], b.cap("ident_f32")),
            reads=[b.SM.r(reg), b.const_r["ident_f32"]], writes=[T1.r()], dbg=dbgn + ".T")
        ctx.op("dve", lambda: nc.vector.tensor_copy(
            b.SM.ap[0:1, SM_LROW:SM_LROW + 128], T1.ap[0:1, 0:128]),
            reads=[T1.r()], writes=[b.SM.r("grow")], dbg=dbgn + ".e")
        ctx.op("pe", lambda: nc.tensor.matmul(
            T1.ap[:, 0:128], b.cap("ones_row"), b.SM.ap[0:1, SM_LROW:SM_LROW + 128],
            start=True, stop=True),
            reads=[b.SM.r("grow"), b.const_r["ones_row"]], writes=[T1.r()], dbg=dbgn + ".b")
        ctx.op("dve", lambda kv_col=kv_col: nc.vector.tensor_copy(
            b.KVT.ap[:, kv_col:kv_col + 128], T1.ap[:, 0:128]),
            reads=[T1.r()], writes=[b.KVT.r(f"g{kv_col}")], dbg=dbgn + ".c")

    broadcast_gain2(SM_BIAS, 0, "biasq", "l0ab.gbq")
    broadcast_gain2(SM_BIAS + 1, 128, "biask", "l0ab.gbk")

    def q_fm_sink(h, tt):
        def sink(tbv2, treg):
            ctx.op("dve", lambda tbv2=tbv2, h=h, tt=tt: nc.vector.tensor_copy(
                b.QH.ap[:, h * TOK + tt * 512:h * TOK + (tt + 1) * 512], tbv2),
                reads=[treg], writes=[b.QH.r()], dbg=f"l0ab.qf{h}_{tt}")
        return sink

    def q_norm_sink(h, tt):
        def sink(row_ap, row_r):
            col = SM_QMAX + h * 4 + tt
            ctx.op("dve", lambda row_ap=row_ap: nc.vector.reduce_max(
                b.SM.ap[0:1, SM_STAGE:SM_STAGE + 1], row_ap, axis=AX.X),
                reads=[row_r], writes=[b.SM.r("stg0")], dbg=f"l0ab.qm{h}_{tt}")
            ctx.op("act", lambda col=col: nc.scalar.activation(
                out=b.SM.ap[0:1, col:col + 1], in_=b.SM.ap[0:1, SM_STAGE:SM_STAGE + 1],
                func=AF.Sqrt),
                reads=[b.SM.r("stg0")], writes=[b.SM.r("qmax")], dbg=f"l0ab.qs{h}_{tt}")
        return sink

    def k_fm_sink(h, tt):
        def sink(tbv2, treg):
            s2 = (tt + 1) % 2
            stg2 = b.PB.ap[:, s2 * 512:(s2 + 1) * 512]
            stg2_r = b.PB.r(f"{s2}")
            ctx.op("dve", lambda tbv2=tbv2, stg2=stg2: nc.vector.tensor_copy(stg2, tbv2),
                   reads=[treg], writes=[stg2_r], dbg=f"l0ab.kf{h}_{tt}")
            ctx.dma("sp", lambda h=h, tt=tt: ccK.ap[h * 128:(h + 1) * 128,
                                                    tt * 512:(tt + 1) * 512],
                    lambda stg2=stg2: stg2,
                    reads=[stg2_r], writes=[ccK.r()], dbg=f"l0ab.kst{h}_{tt}")
        return sink

    def k_norm_sink(h, tt):
        def sink(row_ap, row_r):
            ctx.op("dve", lambda row_ap=row_ap: nc.vector.tensor_copy(
                b.XNB.ap[0:1, 1024:1536], row_ap),
                reads=[row_r], writes=[b.XNB.r()], dbg=f"l0ab.knb{h}_{tt}")
            ctx.dma("sp", lambda h=h, tt=tt: ccK.ap[1024 + h:1025 + h,
                                                    tt * 512:(tt + 1) * 512],
                    lambda: b.XNB.ap[0:1, 1024:1536],
                    reads=[b.XNB.r()], writes=[ccK.r()], dbg=f"l0ab.knst{h}_{tt}")
        return sink

    def ep_q(h, tt, bank):
        emit_rms_rope_tm2(b, bank, None, None, tt, 128, "128",
                          q_fm_sink(h, tt), norm_sink=q_norm_sink(h, tt),
                          gain_col=0, dbg=f"l0ab.q{h}_{tt}")

    def ep_k(h, tt, bank):
        emit_rms_rope_tm2(b, bank, None, None, tt, 128, "128",
                          k_fm_sink(h, tt), norm_sink=k_norm_sink(h, tt),
                          gain_col=128, dbg=f"l0ab.k{h}_{tt}")

    for h in range(16):
        emit_proj(b, Buf(QK.ap[h], f"qk{h}"), D, 128,
                  lambda oc, tt, bank, h=h: ep_q(h, tt, bank),
                  rhs_buf=b.XNT, rhs_read=b.XNT.r(), dbg=f"l0ab.qp{h}")
    for h in range(8):
        emit_proj(b, Buf(KVK.ap[0, h], f"kk{h}"), D, 128,
                  lambda oc, tt, bank, h=h: ep_k(h, tt, bank),
                  rhs_buf=b.XNT, rhs_read=b.XNT.r(), dbg=f"l0ab.kp{h}")
    for h in range(8):
        emit_proj(b, Buf(KVK.ap[1, h], f"vk{h}"), D, 128,
                  lambda oc, tt, bank, h=h: ep_v_path(b, ccV, "l0ab", vcol=lambda _: h * 128)(h, tt, bank),
                  rhs_buf=b.XNT, rhs_read=b.XNT.r(), dbg=f"l0ab.vp{h}")

    rg = RG_OVERRIDE or [[0, 1], [2, 3], [4, 5], [6, 7]]
    ctx.collective(ccK, ccKo, rg, dbg="l0ab_k")
    ctx.collective(ccV, ccVo, rg, dbg="l0ab_v")
    emit_v_repack(b, ccVo, vg2, 8, 1024, "l0ab.vr")
    emit_kmax(b, ccKo, KR, 1024, 8, "l0ab.kmax")

    emit_attn(b, kind="swa", n_heads=16, scale=0.08838834764831845,
              q_parts=lambda h: [(b.QH.ap[:, h * TOK:(h + 1) * TOK], 128, b.QH.r())],
              k_parts=lambda h: [(ccKo, lambda rank, h=h: rank * KR + (h // 2) * 128, lambda rank: 0, 128)],
              v_dram=vg2,
              vcol=lambda h: (h // 2) * 128,
              kmax_col=lambda h: SM_KMAX + h // 2,
              qmax_col=lambda h, j: SM_QMAX + h * 4 + j,
              o_sink=osink_to_xnt(b, "l0ab.os"), dbg="l0ab.at")

    OKf = Buf(OK.ap.rearrange("n h d -> (n h) d"), "okf")
    emit_proj(b, OKf, D, D, ep_residual(b, X, dbg="l0ab.res"),
              rhs_buf=b.XNT, rhs_read=b.XNT.r(), dbg="l0ab.o")


def emit_rms_rope_tm2(b, bank, scale_col_ap, scale_read, tt, rope_dim, rope_name,
                      fm_sink, norm_sink=None, rms_eps=1e-6, gain_col=None,
                      do_rms=True, dbg=""):
    ctx, nc = b.ctx, b.nc
    T1 = b.banks[6]
    half = rope_dim // 2
    s = tt % 2
    stg = b.PB.ap[:, s * 512:(s + 1) * 512]
    stg_r = b.PB.r(f"{s}")
    ctx.op("dve", lambda bank=bank, stg=stg: nc.vector.tensor_copy(stg, bank.ap[:, :]),
           reads=[bank.r()], writes=[stg_r], dbg=f"{dbg}.ev")
    tbv = bf_half(T1, 0, 512)
    for q in range(4):
        ctx.op("pe", lambda tbv=tbv, stg=stg, q=q: nc.tensor.transpose(
            tbv[:, q * 128:(q + 1) * 128], stg[:, q * 128:(q + 1) * 128],
            b.cap("ident_bf")),
            reads=[stg_r, b.const_r["ident_bf"]], writes=[T1.r()], dbg=f"{dbg}.T{q}")
    tm = b.XNB.ap[:, 0:512]
    tm_r = b.XNB.r()
    ctx.op("dve", lambda tbv=tbv, tm=tm: nc.vector.tensor_copy(tm, tbv),
           reads=[T1.r()], writes=[tm_r], dbg=f"{dbg}.Te")
    c0 = SM_STAGE + 8
    for s4 in range(4):
        tile = tm[:, s4 * 128:s4 * 128 + 128]
        if do_rms:
            ctx.op("act", lambda tile=tile: nc.scalar.activation(
                out=b.WF.ap[:, 1536:1664], in_=tile, func=AF.Square,
                accum_out=b.SM.ap[:, c0:c0 + 1]),
                reads=[tm_r], writes=[b.WF.r(), b.SM.r("rr0")], dbg=f"{dbg}.sq{s4}")
            ctx.op("dve", lambda: nc.vector.tensor_scalar(
                b.SM.ap[:, c0 + 1:c0 + 2], b.SM.ap[:, c0:c0 + 1], 1.0 / 128.0, rms_eps,
                ALU.mult, ALU.add),
                reads=[b.SM.r("rr0")], writes=[b.SM.r("rr1")], dbg=f"{dbg}.ms{s4}")
            ctx.op("act", lambda: nc.scalar.activation(
                out=b.SM.ap[:, c0 + 2:c0 + 3], in_=b.SM.ap[:, c0 + 1:c0 + 2], func=AF.Sqrt),
                reads=[b.SM.r("rr1")], writes=[b.SM.r("rr2")], dbg=f"{dbg}.sd{s4}")
            ctx.op("dve", lambda: nc.vector.reciprocal(
                b.SM.ap[:, c0 + 3:c0 + 4], b.SM.ap[:, c0 + 2:c0 + 3]),
                reads=[b.SM.r("rr2")], writes=[b.SM.r("rr3")], dbg=f"{dbg}.rs{s4}")
            ctx.op("dve", lambda tile=tile: nc.vector.tensor_scalar(
                tile, tile, b.SM.ap[:, c0 + 3:c0 + 4], None, ALU.mult),
                reads=[tm_r, b.SM.r("rr3")], writes=[tm_r], dbg=f"{dbg}.n{s4}")
        if gain_col is not None:
            ctx.op("dve", lambda tile=tile, gain_col=gain_col: nc.vector.tensor_mul(
                tile, tile, b.KVT.ap[:, gain_col:gain_col + 128]),
                reads=[tm_r, b.KVT.r(f"g{gain_col}")], writes=[tm_r], dbg=f"{dbg}.g{s4}")
        if rope_dim > 0:
            cs = b.SC.ap[:, 0:half]
            sn = b.SC.ap[:, 128:128 + half]
            ctx.dma("sp", lambda cs=cs: cs,
                    lambda tt=tt, s4=s4: b.cdram("cos" + rope_name).ap[
                        tt * 512 + s4 * 128:tt * 512 + (s4 + 1) * 128, :],
                    reads=[b.cdram("cos" + rope_name).r()], writes=[b.SC.r("c")],
                    dbg=f"{dbg}.cl{s4}")
            ctx.dma("sp", lambda sn=sn: sn,
                    lambda tt=tt, s4=s4: b.cdram("sin" + rope_name).ap[
                        tt * 512 + s4 * 128:tt * 512 + (s4 + 1) * 128, :],
                    reads=[b.cdram("sin" + rope_name).r()], writes=[b.SC.r("s")],
                    dbg=f"{dbg}.sl{s4}")
            x1 = tile[:, 0:half]
            x2 = tile[:, half:2 * half]
            sc1 = b.XNB.ap[:, 512:512 + half]
            sc2 = b.XNB.ap[:, 640:640 + half]
            sc3 = b.XNB.ap[:, 768:768 + half]
            sc4 = b.XNB.ap[:, 896:896 + half]
            scr = b.XNB.r()
            ctx.op("dve", lambda sc1=sc1, x1=x1, cs=cs: nc.vector.tensor_mul(sc1, x1, cs),
                   reads=[tm_r, b.SC.r("c")], writes=[scr], dbg=f"{dbg}.r1{s4}")
            ctx.op("dve", lambda sc2=sc2, x2=x2, sn=sn: nc.vector.tensor_mul(sc2, x2, sn),
                   reads=[tm_r, b.SC.r("s")], writes=[scr], dbg=f"{dbg}.r2{s4}")
            ctx.op("dve", lambda sc3=sc3, x2=x2, cs=cs: nc.vector.tensor_mul(sc3, x2, cs),
                   reads=[tm_r, b.SC.r("c")], writes=[scr], dbg=f"{dbg}.r3{s4}")
            ctx.op("dve", lambda sc4=sc4, x1=x1, sn=sn: nc.vector.tensor_mul(sc4, x1, sn),
                   reads=[tm_r, b.SC.r("s")], writes=[scr], dbg=f"{dbg}.r4{s4}")
            ctx.op("dve", lambda x1=x1, sc1=sc1, sc2=sc2: nc.vector.tensor_sub(x1, sc1, sc2),
                   reads=[scr], writes=[tm_r], dbg=f"{dbg}.r5{s4}")
            ctx.op("dve", lambda x2=x2, sc3=sc3, sc4=sc4: nc.vector.tensor_add(x2, sc3, sc4),
                   reads=[scr], writes=[tm_r], dbg=f"{dbg}.r6{s4}")
        if norm_sink is not None:
            ctx.op("act", lambda tile=tile: nc.scalar.activation(
                out=b.WF.ap[:, 1536:1664], in_=tile, func=AF.Square,
                accum_out=b.SM.ap[:, c0 + 4:c0 + 5]),
                reads=[tm_r], writes=[b.WF.r(), b.SM.r("rr4")], dbg=f"{dbg}.nq{s4}")
            ctx.op("pe", lambda s4=s4: nc.tensor.transpose(
                b.banks[7].ap[0:1, s4 * 128:(s4 + 1) * 128],
                b.SM.ap[:, c0 + 4:c0 + 5], b.cap("ident_f32")),
                reads=[b.SM.r("rr4"), b.const_r["ident_f32"]], writes=[b.banks[7].r()],
                dbg=f"{dbg}.nT{s4}")
    if norm_sink is not None:
        ctx.op("dve", lambda: nc.vector.tensor_copy(
            b.SM.ap[0:1, SM_LROW:SM_LROW + 512], b.banks[7].ap[0:1, 0:512]),
            reads=[b.banks[7].r()], writes=[b.SM.r("lrow")], dbg=f"{dbg}.nr")
        norm_sink(b.SM.ap[0:1, SM_LROW:SM_LROW + 512], b.SM.r("lrow"))
    T2 = b.banks[7]
    tbv2 = bf_half(T2, 0, 512)
    for q in range(4):
        ctx.op("pe", lambda tbv2=tbv2, tm=tm, q=q: nc.tensor.transpose(
            tbv2[:, q * 128:(q + 1) * 128], tm[:, q * 128:(q + 1) * 128],
            b.cap("ident_bf")),
            reads=[tm_r, b.const_r["ident_bf"]], writes=[T2.r()], dbg=f"{dbg}.U{q}")
    fm_sink(tbv2, T2.r())


SUB_EMITTERS.update({"l0ab": sub_l0ab})



def emit_rope64(b, bank, prt0, tt, fm_sink, norm_max_col=None, dbg=""):
    ctx, nc = b.ctx, b.nc
    T1, T2 = b.banks[6], b.banks[7]
    s = tt % 2
    stg = b.PB.ap[0:64, s * 512:(s + 1) * 512]
    stg_r = b.PB.r(f"{s}")
    ctx.op("dve", lambda bank=bank, stg=stg: nc.vector.tensor_copy(
        stg, bank.ap[prt0:prt0 + 64, :]),
        reads=[bank.r()], writes=[stg_r], dbg=f"{dbg}.ev")
    tbv = bf_half(T1, 0, 512)
    for q in range(4):
        ctx.op("pe", lambda tbv=tbv, stg=stg, q=q: nc.tensor.transpose(
            tbv[:, q * 64:(q + 1) * 64], stg[:, q * 128:(q + 1) * 128],
            b.cap("ident_bf")[0:64, 0:64]),
            reads=[stg_r, b.const_r["ident_bf"]], writes=[T1.r()], dbg=f"{dbg}.T{q}")
    tm = b.XNB.ap[:, 0:256]
    tm_r = b.XNB.r()
    ctx.op("dve", lambda tbv=tbv, tm=tm: nc.vector.tensor_copy(tm, tbv[:, 0:256]),
           reads=[T1.r()], writes=[tm_r], dbg=f"{dbg}.Te")
    c0 = SM_STAGE + 8
    for s4 in range(4):
        tile = tm[:, s4 * 64:s4 * 64 + 64]
        half = 32
        cs = b.SC.ap[:, 0:half]
        sn = b.SC.ap[:, 128:128 + half]
        ctx.dma("sp", lambda cs=cs: cs,
                lambda tt=tt, s4=s4: b.cdram("cos64").ap[
                    tt * 512 + s4 * 128:tt * 512 + (s4 + 1) * 128, :],
                reads=[b.cdram("cos64").r()], writes=[b.SC.r("c")], dbg=f"{dbg}.cl{s4}")
        ctx.dma("sp", lambda sn=sn: sn,
                lambda tt=tt, s4=s4: b.cdram("sin64").ap[
                    tt * 512 + s4 * 128:tt * 512 + (s4 + 1) * 128, :],
                reads=[b.cdram("sin64").r()], writes=[b.SC.r("s")], dbg=f"{dbg}.sl{s4}")
        x1 = tile[:, 0:half]
        x2 = tile[:, half:2 * half]
        sc1 = b.XNB.ap[:, 512:512 + half]
        sc2 = b.XNB.ap[:, 640:640 + half]
        sc3 = b.XNB.ap[:, 768:768 + half]
        sc4 = b.XNB.ap[:, 896:896 + half]
        scr = b.XNB.r()
        ctx.op("dve", lambda sc1=sc1, x1=x1, cs=cs: nc.vector.tensor_mul(sc1, x1, cs),
               reads=[tm_r, b.SC.r("c")], writes=[scr], dbg=f"{dbg}.r1{s4}")
        ctx.op("dve", lambda sc2=sc2, x2=x2, sn=sn: nc.vector.tensor_mul(sc2, x2, sn),
               reads=[tm_r, b.SC.r("s")], writes=[scr], dbg=f"{dbg}.r2{s4}")
        ctx.op("dve", lambda sc3=sc3, x2=x2, cs=cs: nc.vector.tensor_mul(sc3, x2, cs),
               reads=[tm_r, b.SC.r("c")], writes=[scr], dbg=f"{dbg}.r3{s4}")
        ctx.op("dve", lambda sc4=sc4, x1=x1, sn=sn: nc.vector.tensor_mul(sc4, x1, sn),
               reads=[tm_r, b.SC.r("s")], writes=[scr], dbg=f"{dbg}.r4{s4}")
        ctx.op("dve", lambda x1=x1, sc1=sc1, sc2=sc2: nc.vector.tensor_sub(x1, sc1, sc2),
               reads=[scr], writes=[tm_r], dbg=f"{dbg}.r5{s4}")
        ctx.op("dve", lambda x2=x2, sc3=sc3, sc4=sc4: nc.vector.tensor_add(x2, sc3, sc4),
               reads=[scr], writes=[tm_r], dbg=f"{dbg}.r6{s4}")
        if norm_max_col is not None:
            ctx.op("act", lambda tile=tile: nc.scalar.activation(
                out=b.WF.ap[:, 1536:1600], in_=tile, func=AF.Square,
                accum_out=b.SM.ap[:, c0 + 4:c0 + 5]),
                reads=[tm_r], writes=[b.WF.r(), b.SM.r("rr4")], dbg=f"{dbg}.nq{s4}")
            ctx.op("pe", lambda s4=s4: nc.tensor.transpose(
                T2.ap[0:1, s4 * 128:(s4 + 1) * 128],
                b.SM.ap[:, c0 + 4:c0 + 5], b.cap("ident_f32")),
                reads=[b.SM.r("rr4"), b.const_r["ident_f32"]], writes=[T2.r()],
                dbg=f"{dbg}.nT{s4}")
    if norm_max_col is not None:
        ctx.op("dve", lambda: nc.vector.reduce_max(
            b.SM.ap[0:1, norm_max_col:norm_max_col + 1], T2.ap[0:1, 0:512], axis=AX.X),
            reads=[T2.r()], writes=[b.SM.r("pemax")], dbg=f"{dbg}.nx")
    tbv2 = bf_half(T1, 0, 512)
    for q in range(4):
        ctx.op("pe", lambda tbv2=tbv2, tm=tm, q=q: nc.tensor.transpose(
            tbv2[:, q * 128:(q + 1) * 128][0:64, :], tm[:, q * 64:(q + 1) * 64],
            b.cap("ident_bf")),
            reads=[tm_r, b.const_r["ident_bf"]], writes=[T1.r()], dbg=f"{dbg}.U{q}")
    fm_sink(tbv2, T1.r())


def emit_lora_norm(b, n_chunks, d_lora, gain_sm_col, gain_reg, qh_chunk0, W, dbg):
    ctx, nc = b.ctx, b.nc
    NB = b.banks[2]
    T1, T2 = b.banks[6], b.banks[7]
    for tt in range(NQT):
        banks = {}

        def ep(oc, tt_, bank, banks=banks):
            col = (qh_chunk0 + oc) * TOK + tt_ * 512
            ctx.op("dve", lambda bank=bank, col=col, oc=oc: nc.vector.tensor_scalar(
                b.QH.ap[:, col:col + 512], bank.ap[:, :],
                b.SM.ap[:, gain_sm_col + oc:gain_sm_col + oc + 1], None, ALU.mult),
                reads=[bank.r(), b.SM.r(gain_reg)], writes=[b.QH.r()],
                dbg=f"{dbg}.ev{oc}_{tt_}")
            sq = b.XNB.ap[:, 1024:1536]
            ctx.op("act", lambda bank=bank, sq=sq: nc.scalar.activation(
                out=sq, in_=bank.ap[:, :], func=AF.Square),
                reads=[bank.r()], writes=[b.XNB.r()], dbg=f"{dbg}.sq{oc}_{tt_}")
            ctx.op("pe", MM(NB.ap[0:1, 0:512], b.cap("ones_bf"), sq,
                            start=(oc == 0), stop=(oc == n_chunks - 1)),
                   reads=[b.XNB.r(), b.const_r["ones_bf"]], writes=[NB.r()],
                   dbg=f"{dbg}.nmm{oc}_{tt_}")

        emit_proj(b, W, D, d_lora, ep, rhs_buf=b.XNT, rhs_read=b.XNT.r(),
                  tt_list=[tt], dbg=f"{dbg}.p{tt}")
        ctx.op("dve", lambda: nc.vector.tensor_scalar(
            b.SM.ap[0:1, SM_LROW:SM_LROW + 512], NB.ap[0:1, 0:512],
            1.0 / d_lora, 1e-6, ALU.mult, ALU.add),
            reads=[NB.r()], writes=[b.SM.r("lrow")], dbg=f"{dbg}.ms{tt}")
        ctx.op("act", lambda: nc.scalar.activation(
            out=b.SM.ap[0:1, SM_LROW:SM_LROW + 512],
            in_=b.SM.ap[0:1, SM_LROW:SM_LROW + 512], func=AF.Sqrt),
            reads=[b.SM.r("lrow")], writes=[b.SM.r("lrow")], dbg=f"{dbg}.sd{tt}")
        ctx.op("dve", lambda: nc.vector.reciprocal(
            b.SM.ap[0:1, SM_LROW:SM_LROW + 512], b.SM.ap[0:1, SM_LROW:SM_LROW + 512]),
            reads=[b.SM.r("lrow")], writes=[b.SM.r("lrow")], dbg=f"{dbg}.rc{tt}")
        for qs in range(4):
            ctx.op("pe", lambda qs=qs: nc.tensor.transpose(
                T1.ap[:, qs:qs + 1],
                b.SM.ap[0:1, SM_LROW + qs * 128:SM_LROW + (qs + 1) * 128],
                b.cap("ident_f32")[0:1, 0:1]),
                reads=[b.SM.r("lrow"), b.const_r["ident_f32"]], writes=[T1.r()],
                dbg=f"{dbg}.rT{tt}_{qs}")
            ctx.op("dve", lambda qs=qs: nc.vector.tensor_copy(
                b.SM.ap[:, SM_TSC + qs:SM_TSC + qs + 1], T1.ap[:, qs:qs + 1]),
                reads=[T1.r()], writes=[b.SM.r("tsc")], dbg=f"{dbg}.rTe{tt}_{qs}")
        for oc in range(n_chunks):
            col = (qh_chunk0 + oc) * TOK + tt * 512
            s = oc % 2
            tbv = bf_half(T1, 0, 512)
            for q in range(4):
                ctx.op("pe", lambda tbv=tbv, col=col, q=q: nc.tensor.transpose(
                    tbv[:, q * 128:(q + 1) * 128], b.QH.ap[:, col + q * 128:col + (q + 1) * 128],
                    b.cap("ident_bf")),
                    reads=[b.QH.r(), b.const_r["ident_bf"]], writes=[T1.r()],
                    dbg=f"{dbg}.tT{tt}_{oc}_{q}")
            tmb = b.XNB.ap[:, 0:512]
            for q in range(4):
                ctx.op("dve", lambda tbv=tbv, tmb=tmb, q=q: nc.vector.tensor_scalar(
                    tmb[:, q * 128:(q + 1) * 128], tbv[:, q * 128:(q + 1) * 128],
                    b.SM.ap[:, SM_TSC + q:SM_TSC + q + 1], None, ALU.mult),
                    reads=[T1.r(), b.SM.r("tsc")], writes=[b.XNB.r()],
                    dbg=f"{dbg}.ts{tt}_{oc}_{q}")
            tbv2 = bf_half(T2, 0, 512)
            for q in range(4):
                ctx.op("pe", lambda tbv2=tbv2, tmb=tmb, q=q: nc.tensor.transpose(
                    tbv2[:, q * 128:(q + 1) * 128], tmb[:, q * 128:(q + 1) * 128],
                    b.cap("ident_bf")),
                    reads=[b.XNB.r(), b.const_r["ident_bf"]], writes=[T2.r()],
                    dbg=f"{dbg}.uT{tt}_{oc}_{q}")
            ctx.op("dve", lambda tbv2=tbv2, col=col: nc.vector.tensor_copy(
                b.QH.ap[:, col:col + 512], tbv2),
                reads=[T2.r()], writes=[b.QH.r()], dbg=f"{dbg}.uTe{tt}_{oc}")


L1AA_STOP = False
L1AA_EXP = 7


def sub_l1aa(b):
    ctx, nc = b.ctx, b.nc
    X = b.X
    KR = 576
    ccK = b.dram_buf("ccK_l1aa", [KR, TOK], BF16, "Internal")
    ccKo = b.dram_buf("ccKo_l1aa", [2 * KR, TOK], BF16, "Internal")
    kfull = b.dram_buf("kfull_l1aa", [2048, 2 * TOK], BF16, "Internal")
    vfull = b.dram_buf("vfull_l1aa", [128, 16 * 4096], BF16, "Internal")

    emit_ln(b, X, dbg="l1aa.ln")
    load_bias_cols(b, b.dram["L1_attn_a_g_q"], SM_BIAS, "l1aa.gq")
    load_bias_cols(b, b.dram["L1_attn_a_g_kv"], SM_BIAS + 8, "l1aa.gkv")

    emit_lora_norm(b, 6, 768, SM_BIAS, f"bias{SM_BIAS}", 0,
                   b.dram["L1_attn_a_w_dq"], "l1aa.cq")
    emit_lora_norm(b, 4, 512, SM_BIAS + 8, f"bias{SM_BIAS + 8}", 6,
                   b.dram["L1_attn_a_w_dkv"], "l1aa.ckv")

    for oc in range(4):
        for tt in range(NQT):
            ctx.dma("sp", lambda oc=oc, tt=tt: ccK.ap[oc * 128:(oc + 1) * 128,
                                                      tt * 512:(tt + 1) * 512],
                    lambda oc=oc, tt=tt: b.QH.ap[:, (6 + oc) * TOK + tt * 512:
                                                 (6 + oc) * TOK + (tt + 1) * 512],
                    reads=[b.QH.r()], writes=[ccK.r()], dbg=f"l1aa.kvst{oc}_{tt}")

    def kpe_sink(tt):
        def sink(tbv2, treg):
            s2 = (tt + 1) % 2
            stg2 = b.PB.ap[0:64, s2 * 512:(s2 + 1) * 512]
            stg2_r = b.PB.r(f"{s2}")
            ctx.op("dve", lambda tbv2=tbv2, stg2=stg2: nc.vector.tensor_copy(
                stg2, tbv2[0:64, :]),
                reads=[treg], writes=[stg2_r], dbg=f"l1aa.kpf{tt}")
            ctx.dma("sp", lambda tt=tt: ccK.ap[512:576, tt * 512:(tt + 1) * 512],
                    lambda stg2=stg2: stg2,
                    reads=[stg2_r], writes=[ccK.r()], dbg=f"l1aa.kpst{tt}")
        return sink

    def ep_kpe(oc, tt, bank):
        emit_rope64(b, bank, 0, tt, kpe_sink(tt), norm_max_col=None, dbg=f"l1aa.kpe{tt}")

    emit_proj(b, b.dram["L1_attn_a_w_kr"], D, 64, ep_kpe,
              rhs_buf=b.XNT, rhs_read=b.XNT.r(), dbg="l1aa.kr")

    NB = b.banks[2]

    def ep_qnope(h, tt, bank):
        col = h * TOK + tt * 512
        ctx.op("dve", lambda bank=bank, col=col: nc.vector.tensor_copy(
            b.XNT.ap[:, col:col + 512], bank.ap[:, :]),
            reads=[bank.r()], writes=[b.XNT.r()], dbg=f"l1aa.qnev{h}_{tt}")
        sq = b.XNB.ap[:, 1024:1536]
        ctx.op("act", lambda bank=bank, sq=sq: nc.scalar.activation(
            out=sq, in_=bank.ap[:, :], func=AF.Square),
            reads=[bank.r()], writes=[b.XNB.r()], dbg=f"l1aa.qn2{h}_{tt}")
        ctx.op("pe", MM(NB.ap[0:1, 0:512], b.cap("ones_bf"), sq),
               reads=[b.XNB.r(), b.const_r["ones_bf"]], writes=[NB.r()],
               dbg=f"l1aa.qnm{h}_{tt}")
        ctx.op("dve", lambda tt=tt: nc.vector.reduce_max(
            b.SM.ap[0:1, SM_STAGE + 2 + tt:SM_STAGE + 3 + tt], NB.ap[0:1, 0:512],
            axis=AX.X),
            reads=[NB.r()], writes=[b.SM.r("qnmax")], dbg=f"l1aa.qnr{h}_{tt}")

    def cqn_col(ic, tt):
        return ic * TOK + tt * 512

    for h in range(16):
        Wn = Buf(b.dram["L1_attn_a_w_uq"].ap[:, h * 192:h * 192 + 128], f"wuqn{h}")
        emit_proj(b, Wn, 768, 128,
                  lambda oc, tt, bank, h=h: ep_qnope(h, tt, bank),
                  rhs_buf=b.QH, rhs_read=b.QH.r(), rhs_col_fn=cqn_col, dbg=f"l1aa.uqn{h}")
        def qpe_sink(h=h):
            def sink2(tt):
                def sink(tbv2, treg):
                    chunk = 6 + h // 2
                    p0 = (h % 2) * 64
                    ctx.op("dve", lambda tbv2=tbv2, tt=tt, chunk=chunk, p0=p0:
                           nc.vector.tensor_copy(
                               b.QH.ap[p0:p0 + 64, chunk * TOK + tt * 512:
                                       chunk * TOK + (tt + 1) * 512], tbv2[0:64, :]),
                           reads=[treg], writes=[b.QH.r()], dbg=f"l1aa.qpf{h}_{tt}")
                return sink
            return sink2

        def ep_qpe(oc, tt, bank, h=h):
            emit_rope64(b, bank, 0, tt, qpe_sink(h)(tt),
                        norm_max_col=SM_STAGE + 6, dbg=f"l1aa.qpe{h}_{tt}")
            col = SM_QMAX + h * 4 + tt
            ctx.op("dve", lambda tt=tt: nc.vector.tensor_add(
                b.SM.ap[0:1, SM_STAGE:SM_STAGE + 1],
                b.SM.ap[0:1, SM_STAGE + 2 + tt:SM_STAGE + 3 + tt],
                b.SM.ap[0:1, SM_STAGE + 6:SM_STAGE + 7]),
                reads=[b.SM.r("qnmax"), b.SM.r("pemax")], writes=[b.SM.r("stg0")],
                dbg=f"l1aa.qm{h}_{tt}")
            ctx.op("act", lambda col=col: nc.scalar.activation(
                out=b.SM.ap[0:1, col:col + 1], in_=b.SM.ap[0:1, SM_STAGE:SM_STAGE + 1],
                func=AF.Sqrt),
                reads=[b.SM.r("stg0")], writes=[b.SM.r("qmax")], dbg=f"l1aa.qms{h}_{tt}")

        Wp = Buf(b.dram["L1_attn_a_w_uq"].ap[:, h * 192 + 128:h * 192 + 192], f"wuqp{h}")
        emit_proj(b, Wp, 768, 64, ep_qpe,
                  rhs_buf=b.QH, rhs_read=b.QH.r(), rhs_col_fn=cqn_col, dbg=f"l1aa.uqp{h}")

    rg = RG_OVERRIDE or [[0, 1], [2, 3], [4, 5], [6, 7]]
    ctx.collective(ccK, ccKo, rg, dbg="l1aa_k")

    NB2 = b.banks[3]
    for wname, sc in (("L1_attn_a_w_uk", 0), ("L1_attn_a_w_uv", 8192)):
        ctx.dma("pool", lambda sc=sc: b.WB.ap[:, sc:sc + 8192].rearrange(
            "p (i o) -> p i o", o=2048),
            lambda wname=wname: b.dram[wname].ap.rearrange("(i p) o -> p i o", p=128),
            reads=[b.dram[wname].r()],
            writes=[b.WB.r("w0" if sc == 0 else "w2"), b.WB.r("w1" if sc == 0 else "w3")],
            dbg=f"l1aa.wld{sc}")

    tb7 = b.banks[7]
    for rank in range(2):
        for tt8 in range(4):
            rhs_r = b.XNB.r()
            rhs3 = b.XNB.ap[:, 0:2048].rearrange("p (i t) -> p i t", t=512)
            ctx.dma("sp", lambda: b.XNB.ap[:, 0:2048].rearrange(
                "p (i t) -> p i t", t=512),
                lambda rank=rank, tt8=tt8: ccKo.ap[rank * KR:rank * KR + 512,
                                                   tt8 * 512:(tt8 + 1) * 512].rearrange(
                    "(i p) t -> p i t", p=128),
                reads=[ccKo.r()], writes=[rhs_r], dbg=f"l1aa.kvg{rank}_{tt8}")
            kpe_stg = b.PB.ap[0:64, 0:512]
            ctx.dma("sp", lambda rank=rank, tt8=tt8: b.PB.ap[0:64, 0:512],
                    lambda rank=rank, tt8=tt8: ccKo.ap[rank * KR + 512:rank * KR + 576,
                                                       tt8 * 512:(tt8 + 1) * 512],
                    reads=[ccKo.r()], writes=[b.PB.r("0")], dbg=f"l1aa.kpeg{rank}_{tt8}")
            sqp = b.PB.ap[0:64, 512:1024]
            ctx.op("act", lambda kpe_stg=kpe_stg, sqp=sqp: nc.scalar.activation(
                out=sqp, in_=kpe_stg, func=AF.Square),
                reads=[b.PB.r("0")], writes=[b.PB.r("1")], dbg=f"l1aa.kpq{rank}_{tt8}")
            ctx.op("pe", MM(NB2.ap[0:1, 0:512], b.cap("ones_bf")[0:64, :], sqp),
                   reads=[b.PB.r("1"), b.const_r["ones_bf"]], writes=[NB2.r()],
                   dbg=f"l1aa.kpm{rank}_{tt8}")
            ctx.op("dve", lambda: nc.vector.reduce_max(
                b.SM.ap[0:1, SM_STAGE + 6:SM_STAGE + 7], NB2.ap[0:1, 0:512], axis=AX.X),
                reads=[NB2.r()], writes=[b.SM.r("pemax")], dbg=f"l1aa.kpr{rank}_{tt8}")
            for oc in range(16):
                bank = b.banks[oc % 2]
                for ic in range(4):
                    ctx.op("pe", MM(bank.ap[:, :],
                                    b.WB.ap[:, (ic * 16 + oc) * 128:(ic * 16 + oc + 1) * 128],
                                    rhs3[:, ic, :],
                                    start=(ic == 0), stop=(ic == 3)),
                           reads=[b.WB.r("w0"), b.WB.r("w1"), rhs_r], writes=[bank.r()],
                           dbg=f"l1aa.kfm{rank}_{tt8}_{oc}_{ic}")
                s = oc % 2
                stg = b.PB.ap[:, s * 512:(s + 1) * 512]
                stg_r = b.PB.r(f"{s}")
                ctx.op("dve", lambda bank=bank, stg=stg: nc.vector.tensor_copy(stg, bank.ap[:, :]),
                       reads=[bank.r()], writes=[stg_r], dbg=f"l1aa.kfe{rank}_{tt8}_{oc}")
                ctx.dma("sp", lambda rank=rank, tt8=tt8, oc=oc: kfull.ap[
                    oc * 128:(oc + 1) * 128,
                    rank * TOK + tt8 * 512:rank * TOK + (tt8 + 1) * 512],
                    lambda stg=stg: stg,
                    reads=[stg_r], writes=[kfull.r()], dbg=f"l1aa.kfs{rank}_{tt8}_{oc}")
                if not (L1AA_EXP & 2):
                    continue
                sq = b.PB.ap[:, (1 - s) * 512:(2 - s) * 512]
                ctx.op("act", lambda stg=stg, sq=sq: nc.scalar.activation(
                    out=sq, in_=stg, func=AF.Square),
                    reads=[stg_r], writes=[b.PB.r(f"{1 - s}")], dbg=f"l1aa.knq{rank}_{tt8}_{oc}")
                ctx.op("pe", MM(NB.ap[0:1, 0:512], b.cap("ones_bf"), sq),
                       reads=[b.PB.r(f"{1 - s}"), b.const_r["ones_bf"]], writes=[NB.r()],
                       dbg=f"l1aa.knm{rank}_{tt8}_{oc}")
                ctx.op("dve", lambda: nc.vector.reduce_max(
                    b.SM.ap[0:1, SM_STAGE + 7:SM_STAGE + 8], NB.ap[0:1, 0:512], axis=AX.X),
                    reads=[NB.r()], writes=[b.SM.r("knmax")], dbg=f"l1aa.knx{rank}_{tt8}_{oc}")
                ctx.op("dve", lambda: nc.vector.tensor_add(
                    b.SM.ap[0:1, SM_STAGE:SM_STAGE + 1],
                    b.SM.ap[0:1, SM_STAGE + 7:SM_STAGE + 8],
                    b.SM.ap[0:1, SM_STAGE + 6:SM_STAGE + 7]),
                    reads=[b.SM.r("knmax"), b.SM.r("pemax")], writes=[b.SM.r("stg0")],
                    dbg=f"l1aa.kna{rank}_{tt8}_{oc}")
                first = (rank == 0 and tt8 == 0)
                if first:
                    ctx.op("dve", lambda oc=oc: nc.vector.tensor_copy(
                        b.SM.ap[0:1, SM_KMAX + oc:SM_KMAX + oc + 1],
                        b.SM.ap[0:1, SM_STAGE:SM_STAGE + 1]),
                        reads=[b.SM.r("stg0")], writes=[b.SM.r("kmax")],
                        dbg=f"l1aa.kni{oc}")
                else:
                    ctx.op("dve", lambda oc=oc: nc.vector.tensor_tensor(
                        b.SM.ap[0:1, SM_KMAX + oc:SM_KMAX + oc + 1],
                        b.SM.ap[0:1, SM_KMAX + oc:SM_KMAX + oc + 1],
                        b.SM.ap[0:1, SM_STAGE:SM_STAGE + 1], op=ALU.max),
                        reads=[b.SM.r("stg0"), b.SM.r("kmax")], writes=[b.SM.r("kmax")],
                        dbg=f"l1aa.knu{rank}_{tt8}_{oc}")
                if not (L1AA_EXP & 4):
                    continue
                bank2 = b.banks[4 + (oc % 2)]
                for ic in range(4):
                    ctx.op("pe", MM(bank2.ap[:, :],
                                    b.WB.ap[:, 8192 + (ic * 16 + oc) * 128:
                                            8192 + (ic * 16 + oc + 1) * 128],
                                    rhs3[:, ic, :],
                                    start=(ic == 0), stop=(ic == 3)),
                           reads=[b.WB.r("w2"), b.WB.r("w3"), rhs_r], writes=[bank2.r()],
                           dbg=f"l1aa.vfm{rank}_{tt8}_{oc}_{ic}")
                s2 = 1 - s
                stg2 = b.PB.ap[:, s2 * 512:(s2 + 1) * 512]
                stg2_r = b.PB.r(f"{s2}")
                ctx.op("dve", lambda bank2=bank2, stg2=stg2: nc.vector.tensor_copy(
                    stg2, bank2.ap[:, :]),
                    reads=[bank2.r()], writes=[stg2_r], dbg=f"l1aa.vfe{rank}_{tt8}_{oc}")
                tbv = bf_half(tb7, 0, 512)
                for q in range(4):
                    ctx.op("pe", lambda tbv=tbv, stg2=stg2, q=q: nc.tensor.transpose(
                        tbv[:, q * 128:(q + 1) * 128], stg2[:, q * 128:(q + 1) * 128],
                        b.cap("ident_bf")),
                        reads=[stg2_r, b.const_r["ident_bf"]], writes=[tb7.r()],
                        dbg=f"l1aa.vfT{rank}_{tt8}_{oc}_{q}")
                vst = b.QH.ap[:, 14 * TOK:14 * TOK + 512]
                ctx.op("dve", lambda tbv=tbv, vst=vst: nc.vector.tensor_copy(vst, tbv),
                       reads=[tb7.r()], writes=[b.QH.r()], dbg=f"l1aa.vfTe{rank}_{tt8}_{oc}")
                ctx.dma("sp", lambda rank=rank, tt8=tt8, oc=oc: vfull.ap[
                    :, oc * 4096 + rank * 2048 + tt8 * 512:
                    oc * 4096 + rank * 2048 + (tt8 + 1) * 512],
                    lambda vst=vst: vst,
                    reads=[b.QH.r()], writes=[vfull.r()], dbg=f"l1aa.vfs{rank}_{tt8}_{oc}")
    if L1AA_STOP:
        return
    ctx.op("act", lambda: nc.scalar.activation(
        out=b.SM.ap[0:1, SM_KMAX:SM_KMAX + 16], in_=b.SM.ap[0:1, SM_KMAX:SM_KMAX + 16],
        func=AF.Sqrt),
        reads=[b.SM.r("kmax")], writes=[b.SM.r("kmax")], dbg="l1aa.kms")

    emit_attn(b, kind="causal", n_heads=16, scale=192.0 ** -0.5,
              q_parts=lambda h: [
                  (b.XNT.ap[:, h * TOK:(h + 1) * TOK], 128, b.XNT.r(), 0),
                  (b.QH.ap[(h % 2) * 64:(h % 2) * 64 + 64,
                           (6 + h // 2) * TOK:(7 + h // 2) * TOK], 64, b.QH.r(),
                   (h % 2) * 64)],
              k_parts=lambda h: [
                  (kfull, lambda rank, h=h: h * 128, lambda rank: rank * TOK, 128),
                  (ccKo, lambda rank: rank * KR + 512, lambda rank: 0, 64)],
              v_dram=vfull,
              vcol=lambda h: h * 128,
              kmax_col=lambda h: SM_KMAX + h,
              qmax_col=lambda h, j: SM_QMAX + h * 4 + j,
              o_sink=osink_to_xnt(b, "l1aa.os"), dbg="l1aa.at")

    emit_proj(b, b.dram["L1_attn_a_w_o"], D, D, ep_residual(b, X, dbg="l1aa.res"),
              rhs_buf=b.XNT, rhs_read=b.XNT.r(), dbg="l1aa.o")


SUB_EMITTERS.update({"l1aa": sub_l1aa})



def sub_l1ffn(b):
    ctx, nc = b.ctx, b.nc
    X = b.X
    NB = b.banks[2]
    T1 = b.banks[6]
    LG = b.banks[3]

    gw = b.dram["L1_ffn_gate_w"]
    for ic in range(16):
        ctx.dma("sp", lambda ic=ic: b.WF.ap[0:8, 0:128],
                lambda ic=ic: gw.ap[:, ic * 128:(ic + 1) * 128],
                reads=[gw.r()], writes=[b.WF.r()], dbg=f"l1f.gwl{ic}")
        ctx.op("pe", lambda ic=ic: nc.tensor.transpose(
            T1.ap[:, 0:8], b.WF.ap[0:8, 0:128], b.cap("ident_f32")[0:8, 0:8]),
            reads=[b.WF.r(), b.const_r["ident_f32"]], writes=[T1.r()],
            dbg=f"l1f.gwT{ic}")
        ctx.op("dve", lambda ic=ic: nc.vector.tensor_copy(
            b.GT.ap[:, ic * 8:(ic + 1) * 8], T1.ap[:, 0:8]),
            reads=[T1.r()], writes=[b.GT.r("w")], dbg=f"l1f.gwTe{ic}")

    def routing_math(tc):
        c0 = SM_STAGE + 8
        L = b.SM.ap[:, c0 + 8:c0 + 16]
        ctx.op("dve", lambda L=L: nc.vector.tensor_copy(L, LG.ap[:, 0:8]),
               reads=[LG.r()], writes=[b.SM.r("rlog")], dbg=f"l1f.lc{tc}")
        ctx.op("dve", lambda L=L: nc.vector.reduce_max(
            b.SM.ap[:, c0:c0 + 1], L, axis=AX.X),
            reads=[b.SM.r("rlog")], writes=[b.SM.r("rm1")], dbg=f"l1f.m1{tc}")
        ctx.op("dve", lambda L=L: nc.vector.tensor_scalar(
            b.SM.ap[:, c0 + 16:c0 + 24], L, b.SM.ap[:, c0:c0 + 1], None, ALU.subtract),
            reads=[b.SM.r("rlog"), b.SM.r("rm1")], writes=[b.SM.r("rP0")], dbg=f"l1f.ls{tc}")
        ctx.op("act", lambda: nc.scalar.activation(
            out=b.SM.ap[:, c0 + 24:c0 + 32], in_=b.SM.ap[:, c0 + 16:c0 + 24], func=AF.Exp),
            reads=[b.SM.r("rP0")], writes=[b.SM.r("rP")], dbg=f"l1f.le{tc}")
        ctx.op("dve", lambda L=L: nc.vector.tensor_scalar(
            b.SM.ap[:, c0 + 32:c0 + 40], L, b.SM.ap[:, c0:c0 + 1], None, ALU.is_ge),
            reads=[b.SM.r("rlog"), b.SM.r("rm1")], writes=[b.SM.r("rind")], dbg=f"l1f.li{tc}")
        ctx.op("dve", lambda L=L: nc.vector.scalar_tensor_tensor(
            b.SM.ap[:, c0 + 40:c0 + 48], b.SM.ap[:, c0 + 32:c0 + 40], -1e30, L,
            ALU.mult, ALU.add),
            reads=[b.SM.r("rind"), b.SM.r("rlog")], writes=[b.SM.r("rmask")],
            dbg=f"l1f.lm{tc}")
        ctx.op("dve", lambda: nc.vector.reduce_max(
            b.SM.ap[:, c0 + 1:c0 + 2], b.SM.ap[:, c0 + 40:c0 + 48], axis=AX.X),
            reads=[b.SM.r("rmask")], writes=[b.SM.r("rm2")], dbg=f"l1f.m2{tc}")
        ctx.op("dve", lambda: nc.vector.tensor_tensor(
            b.SM.ap[:, c0 + 2:c0 + 3], b.SM.ap[:, c0 + 1:c0 + 2], b.SM.ap[:, c0:c0 + 1],
            op=ALU.subtract),
            reads=[b.SM.r("rm2"), b.SM.r("rm1")], writes=[b.SM.r("rd0")], dbg=f"l1f.d0{tc}")
        ctx.op("act", lambda: nc.scalar.activation(
            out=b.SM.ap[:, c0 + 3:c0 + 4], in_=b.SM.ap[:, c0 + 2:c0 + 3], func=AF.Exp),
            reads=[b.SM.r("rd0")], writes=[b.SM.r("rd1")], dbg=f"l1f.d1{tc}")
        ctx.op("dve", lambda: nc.vector.tensor_scalar(
            b.SM.ap[:, c0 + 4:c0 + 5], b.SM.ap[:, c0 + 3:c0 + 4], 1.0, None, ALU.add),
            reads=[b.SM.r("rd1")], writes=[b.SM.r("rd2")], dbg=f"l1f.d2{tc}")
        ctx.op("dve", lambda: nc.vector.reciprocal(
            b.SM.ap[:, c0 + 5:c0 + 6], b.SM.ap[:, c0 + 4:c0 + 5]),
            reads=[b.SM.r("rd2")], writes=[b.SM.r("rd3")], dbg=f"l1f.d3{tc}")
        ctx.op("dve", lambda L=L: nc.vector.tensor_scalar(
            b.SM.ap[:, c0 + 32:c0 + 40], L, b.SM.ap[:, c0 + 1:c0 + 2], None, ALU.is_ge),
            reads=[b.SM.r("rlog"), b.SM.r("rm2")], writes=[b.SM.r("rind")], dbg=f"l1f.se{tc}")
        ctx.op("dve", lambda: nc.vector.tensor_mul(
            b.SM.ap[:, c0 + 16:c0 + 24], b.SM.ap[:, c0 + 24:c0 + 32],
            b.SM.ap[:, c0 + 32:c0 + 40]),
            reads=[b.SM.r("rP"), b.SM.r("rind")], writes=[b.SM.r("rP0")], dbg=f"l1f.w0{tc}")
        ctx.op("dve", lambda tc=tc: nc.vector.tensor_scalar(
            b.RW.ap[:, tc * 8:(tc + 1) * 8], b.SM.ap[:, c0 + 16:c0 + 24],
            b.SM.ap[:, c0 + 5:c0 + 6], None, ALU.mult),
            reads=[b.SM.r("rP0"), b.SM.r("rd3")], writes=[b.RW.r()], dbg=f"l1f.w1{tc}")

    def gate_hook(tc, fc):
        ctx.op("pe", MM(LG.ap[:, 0:8],
                        b.GT.ap[:, 128:256],
                        b.GT.ap[:, fc * 8:(fc + 1) * 8],
                        start=(fc == 0), stop=(fc == 15)),
               reads=[b.GT.r("s"), b.GT.r("w")], writes=[LG.r()],
               dbg=f"l1f.lgm{tc}_{fc}")
        if fc == 15:
            routing_math(tc)

    emit_ln(b, X, dbg="l1f.ln", gate_hook=gate_hook)

    sgw = b.dram["L1_ffn_shared_gate_w"]
    sgwf = Buf(sgw.ap.rearrange("d one -> (d one)"), "sgwf")
    sgwf._regions = sgw._regions
    load_bias_cols(b, sgwf, SM_BIAS + 32, "l1f.sgl")
    ctx.op("dve", lambda: nc.vector.tensor_copy(
        b.CB.ap[:, 256:272], b.SM.ap[:, SM_BIAS + 32:SM_BIAS + 48]),
        reads=[b.SM.r(f"bias{SM_BIAS + 32}")], writes=[b.CB.r("sgw")], dbg="l1f.sgc")
    for tt in range(NQT):
        bank = b.banks[3]
        for ic in range(NFC):
            ctx.op("pe", MM(bank.ap[0:1, 0:512],
                            b.CB.ap[:, 256 + ic:257 + ic],
                            b.XNT.ap[:, ic * TOK + tt * 512:ic * TOK + (tt + 1) * 512],
                            start=(ic == 0), stop=(ic == NFC - 1)),
                   reads=[b.CB.r("sgw"), b.XNT.r()], writes=[bank.r()],
                   dbg=f"l1f.sg{tt}_{ic}")
        ctx.op("act", lambda bank=bank: nc.scalar.activation(
            out=b.SM.ap[0:1, SM_LROW:SM_LROW + 512], in_=bank.ap[0:1, 0:512],
            func=AF.Sigmoid),
            reads=[bank.r()], writes=[b.SM.r("lrow")], dbg=f"l1f.sgs{tt}")
        for qs in range(4):
            ctx.op("pe", lambda qs=qs: nc.tensor.transpose(
                T1.ap[:, 8 + qs:9 + qs],
                b.SM.ap[0:1, SM_LROW + qs * 128:SM_LROW + (qs + 1) * 128],
                b.cap("ident_f32")[0:1, 0:1]),
                reads=[b.SM.r("lrow"), b.const_r["ident_f32"]], writes=[T1.r()],
                dbg=f"l1f.sgT{tt}_{qs}")
            ctx.op("dve", lambda tt=tt, qs=qs: nc.vector.tensor_copy(
                b.RW.ap[:, 128 + tt * 4 + qs:129 + tt * 4 + qs], T1.ap[:, 8 + qs:9 + qs]),
                reads=[T1.r()], writes=[b.RW.r()], dbg=f"l1f.sgTe{tt}_{qs}")

    WG, WU, WD = b.dram["L1_ffn_w_gate"], b.dram["L1_ffn_w_up"], b.dram["L1_ffn_w_down"]
    SWG, SWU, SWD = b.dram["L1_ffn_sw_gate"], b.dram["L1_ffn_sw_up"], b.dram["L1_ffn_sw_down"]
    for e in range(12):
        routed = e < 8
        q4 = e - 8
        if routed:
            wg_ap = WG.ap[e]
            wu_ap = WU.ap[e]
            wd_ap = WD.ap[e]
            wg_r, wu_r, wd_r = WG.r(), WU.r(), WD.r()
        else:
            wg_ap = SWG.ap[:, q4 * 1408:(q4 + 1) * 1408]
            wu_ap = SWU.ap[:, q4 * 1408:(q4 + 1) * 1408]
            wd_ap = SWD.ap[q4 * 1408:(q4 + 1) * 1408, :]
            wg_r, wu_r, wd_r = SWG.r(), SWU.r(), SWD.r()
        for ioc in range(11):
            sl = (ioc // 2) % 2
            io2 = ioc % 2
            wgr = b.WB.r(f"w{sl}")
            wur = b.WB.r(f"w{2 + sl}")
            if io2 == 0:
                nio = min(2, 11 - ioc) * 128
                ctx.dma("pool", lambda sl=sl, nio=nio: b.WB.ap[:, sl * 4096:sl * 4096 + 16 * nio].rearrange(
                    "p (i c) -> p i c", c=nio),
                    lambda wg_ap=wg_ap, ioc=ioc, nio=nio: wg_ap.rearrange("(i p) o -> p i o", p=128)[
                        :, :, ioc * 128:ioc * 128 + nio],
                    reads=[wg_r], writes=[wgr], dbg=f"l1f.wg{e}_{ioc}")
                ctx.dma("pool", lambda sl=sl, nio=nio: b.WB.ap[:, 8192 + sl * 4096:8192 + sl * 4096 + 16 * nio].rearrange(
                    "p (i c) -> p i c", c=nio),
                    lambda wu_ap=wu_ap, ioc=ioc, nio=nio: wu_ap.rearrange("(i p) o -> p i o", p=128)[
                        :, :, ioc * 128:ioc * 128 + nio],
                    reads=[wu_r], writes=[wur], dbg=f"l1f.wu{e}_{ioc}")
            nio_cur = 256 if (ioc % 2 == 0 and ioc < 10) or (ioc % 2 == 1) else 128
            nio_of_load = 256 if (ioc - io2) < 10 else 128
            for tt in range(NQT):
                bg = b.banks[0]
                bu = b.banks[1]
                for ic in range(NFC):
                    col = sl * 4096 + ic * nio_of_load + io2 * 128
                    ctx.op("pe", MM(bg.ap[:, :],
                                    b.WB.ap[:, col:col + 128],
                                    rhs_tile(b.XNT, ic, tt),
                                    start=(ic == 0), stop=(ic == NFC - 1)),
                           reads=[wgr, b.XNT.r()], writes=[bg.r()],
                           dbg=f"l1f.gm{e}_{ioc}_{tt}_{ic}")
                for ic in range(NFC):
                    col = 8192 + sl * 4096 + ic * nio_of_load + io2 * 128
                    ctx.op("pe", MM(bu.ap[:, :],
                                    b.WB.ap[:, col:col + 128],
                                    rhs_tile(b.XNT, ic, tt),
                                    start=(ic == 0), stop=(ic == NFC - 1)),
                           reads=[wur, b.XNT.r()], writes=[bu.r()],
                           dbg=f"l1f.um{e}_{ioc}_{tt}_{ic}")
                s = tt % 2
                hg = b.PB.ap[:, s * 512:(s + 1) * 512]
                hg_r = b.PB.r(f"{s}")
                ctx.op("act", lambda bg=bg, hg=hg: nc.scalar.activation(
                    out=hg, in_=bg.ap[:, :], func=AF.Silu),
                    reads=[bg.r()], writes=[hg_r], dbg=f"l1f.si{e}_{ioc}_{tt}")
                s2 = 1 - s
                hu = b.PB.ap[:, s2 * 512:(s2 + 1) * 512]
                hu_r = b.PB.r(f"{s2}")
                ctx.op("dve", lambda bu=bu, hu=hu: nc.vector.tensor_copy(hu, bu.ap[:, :]),
                       reads=[bu.r()], writes=[hu_r], dbg=f"l1f.ue{e}_{ioc}_{tt}")
                hcol = ioc * TOK + tt * 512
                ctx.op("dve", lambda hg=hg, hu=hu, hcol=hcol: nc.vector.tensor_mul(
                    b.QH.ap[:, hcol:hcol + 512], hg, hu),
                    reads=[hg_r, hu_r], writes=[b.QH.r()], dbg=f"l1f.h{e}_{ioc}_{tt}")
        if routed:
            def scol(tt, qs, e=e):
                return b.RW.ap[:, (tt * 4 + qs) * 8 + e:(tt * 4 + qs) * 8 + e + 1]
        else:
            def scol(tt, qs, e=e):
                return b.RW.ap[:, 128 + tt * 4 + qs:129 + tt * 4 + qs]
        scale_rd = b.RW.r()
        WdBuf = Buf(wd_ap, f"wd{e}")
        emit_proj(b, WdBuf, 1408, D,
                  ep_residual(b, X, scale_col=scol, scale_reads=(scale_rd,),
                              dbg=f"l1f.res{e}"),
                  rhs_buf=b.QH, rhs_read=b.QH.r(),
                  rhs_col_fn=lambda ic, tt: ic * TOK + tt * 512,
                  dbg=f"l1f.dn{e}")


SUB_EMITTERS.update({"l1ffn1": sub_l1ffn, "l1ffn2": sub_l1ffn})


# version 05
# speedup vs baseline: 1.0015x; 1.0015x over previous
import math
import numpy as np
import ml_dtypes
from contextlib import ExitStack
import concourse.bass as bass
import concourse.mybir as mybir
from concourse.bass_utils import run_bass_kernel_spmd

F32 = mybir.dt.float32
BF16 = mybir.dt.bfloat16
AF = mybir.ActivationFunctionType
ALU = mybir.AluOpType
AX = mybir.AxisListType

NCORES = 8
D = 2048
TOK = 2048
NTC = TOK // 128
NFC = D // 128
QT = 512
NQT = TOK // QT
NEG = -1e10

ENGINES = ("pe", "dve", "act", "pool", "sp")
RG_OVERRIDE = None



class Region:
    __slots__ = ("name", "last_w", "readers", "dma_sem", "is_psum")

    def __init__(self, name, is_psum=False):
        self.name = name
        self.last_w = None
        self.readers = []
        self.dma_sem = None
        self.is_psum = is_psum


class Op:
    __slots__ = ("eng", "fn", "idx", "waits", "inc_sem", "inc_amt", "inc_val",
                 "is_dma", "region_sem", "dbg")

    def __init__(self, eng, fn, idx, is_dma, dbg=""):
        self.eng = eng
        self.fn = fn
        self.idx = idx
        self.waits = []
        self.inc_sem = None
        self.inc_amt = 0
        self.inc_val = None
        self.is_dma = is_dma
        self.region_sem = None
        self.dbg = dbg


class _DmaSem:
    __slots__ = ("sem",)

    def __init__(self, sem):
        self.sem = sem


class Ctx:
    def __init__(self, nc, es):
        self.nc = nc
        self.es = es
        self.ops = {e: [] for e in ENGINES}
        self.eng_sem = {e: self.sem("prog_" + e) for e in ENGINES}
        self.covered = {}
        self.n_sems = 5

    def sem(self, name):
        self.n_sems = getattr(self, "n_sems", 0) + 1
        return self.es.enter_context(self.nc.semaphore(name))

    def _dep(self, op, producer):
        if producer is None or producer is op:
            return
        if producer.eng == op.eng and op.eng == "pe" and not producer.is_dma:
            return
        key = (op.eng, producer.eng, producer.is_dma,
               id(producer.region_sem) if producer.is_dma else 0)
        cov = self.covered.get(key, -1)
        if producer.idx <= cov:
            return
        self.covered[key] = producer.idx
        if producer.inc_amt == 0:
            if producer.is_dma:
                producer.inc_amt = 16
            else:
                producer.inc_amt = 1
                producer.inc_sem = self.eng_sem[producer.eng]
        op.waits.append(producer)

    @staticmethod
    def _newest_by_engine(ops):
        best = {}
        for rd in ops:
            k = (rd.eng, rd.is_dma, id(rd.region_sem) if rd.is_dma else 0)
            cur = best.get(k)
            if cur is None or rd.idx > cur.idx:
                best[k] = rd
        return best.values()

    def _track(self, o, reads, writes):
        for r in reads:
            self._dep(o, r.last_w)
            if r.is_psum:
                for rd in self._newest_by_engine(r.readers):
                    if rd.eng != o.eng:
                        self._dep(o, rd)
        for w in writes:
            self._dep(o, w.last_w)
            for rd in self._newest_by_engine(w.readers):
                self._dep(o, rd)
        for w in writes:
            w.readers = []
            w.last_w = o
        for r in reads:
            r.readers.append(o)

    def op(self, eng, fn, reads=(), writes=(), dbg=""):
        o = Op(eng, fn, len(self.ops[eng]), False, dbg)
        self._track(o, reads, writes)
        self.ops[eng].append(o)
        return o

    def dma(self, queue, out_ap_fn, in_ap_fn, reads=(), writes=(), dbg="", **kw):
        assert writes, "DMA needs a written region"
        w0 = writes[0]
        if w0.dma_sem is None:
            w0.dma_sem = {}
        if queue not in w0.dma_sem:
            w0.dma_sem[queue] = _DmaSem(self.sem(f"d{queue[0]}_" + w0.name[:23]))

        qeng = {"sp": self.nc.sync, "pool": self.nc.gpsimd, "act": self.nc.scalar}[queue]

        def fn(_out=out_ap_fn, _in=in_ap_fn, _kw=kw, _q=qeng):
            return _q.dma_start(out=_out(), in_=_in(), **_kw)

        o = Op(queue, fn, len(self.ops[queue]), True, dbg)
        o.region_sem = w0.dma_sem[queue]
        o.inc_sem = w0.dma_sem[queue].sem
        o.inc_amt = 16
        self._track(o, reads, writes)
        self.ops[queue].append(o)
        return o

    def collective(self, in_buf, out_buf, replica_groups, dbg="cc"):
        csem = self.sem("cc_" + dbg[:20])

        def fn(_i=in_buf, _o=out_buf, _rg=replica_groups, _g=self.nc.gpsimd):
            return _g.collective_compute(
                "AllGather", ALU.bypass, replica_groups=_rg,
                ins=[_i.ap.opt()], outs=[_o.ap.opt()])

        o = Op("pool", fn, len(self.ops["pool"]), False, dbg)
        o.inc_sem = csem
        o.inc_amt = 1
        self._track(o, [in_buf.r()], [out_buf.r()])
        self.ops["pool"].append(o)
        return o

    def finalize(self):
        run = {}
        for e in ENGINES:
            for o in self.ops[e]:
                if o.inc_amt:
                    run[o.inc_sem] = run.get(o.inc_sem, 0) + o.inc_amt
                    o.inc_val = run[o.inc_sem]

    def replay(self, block):
        def body(eng_name):
            def _f(eng):
                for o in self.ops[eng_name]:
                    for p in o.waits:
                        assert p.inc_val is not None, f"unresolved dep {p.dbg}"
                    for p in o.waits:
                        eng.wait_ge(p.inc_sem, p.inc_val)
                    try:
                        ins = o.fn()
                    except Exception as exc:
                        raise RuntimeError(f"emit failed at [{eng_name}] {o.dbg}: {exc}") from exc
                    if o.inc_amt and ins is not None:
                        ins.then_inc(o.inc_sem, o.inc_amt)
            return _f
        block.tensor(body("pe"))
        block.vector(body("dve"))
        block.scalar(body("act"))
        block.gpsimd(body("pool"))
        block.sync(body("sp"))


class Buf:
    def __init__(self, ap, name):
        self.ap_ = ap
        self.name = name
        self._regions = {}

    def r(self, key="*"):
        if key not in self._regions:
            self._regions[key] = Region(f"{self.name}.{key}", getattr(self, "psum", False))
        return self._regions[key]

    @property
    def ap(self):
        return self.ap_



def core_token_blocks(parity):
    return [2 * j + parity for j in range(NQT)]


def gathered_key_global_pos():
    pos = np.zeros(2 * TOK, np.int64)
    for j in range(NQT):
        pos[j * 512:(j + 1) * 512] = np.arange(512) + (2 * j) * 512
        pos[TOK + j * 512:TOK + (j + 1) * 512] = np.arange(512) + (2 * j + 1) * 512
    return pos


def attn_plan(kind):
    plan = []
    for j in range(NQT):
        if kind == "causal":
            cs = list(range(0, 4 * (j + 1))) + [16 + c for c in range(0, 4 * (j + 1))]
        elif kind == "swa":
            tiles = sorted(set(t for t in (j - 1, j) if t >= 0))
            cs = []
            for t in tiles:
                cs += list(range(4 * t, 4 * t + 4))
            for t in tiles:
                cs += [16 + c for c in range(4 * t, 4 * t + 4)]
        else:
            cs = list(range(32))
        plan.append(cs)
    return plan


def mask_tables(kind, parity):
    kpos = gathered_key_global_pos()
    plan = attn_plan(kind)
    cls = {}
    p_masks = {}
    row_masks = {}
    for j in range(NQT):
        for s, kc in enumerate(plan[j]):
            kk = kpos[kc * 128:(kc + 1) * 128]
            union_tt = False
            union_row = False
            my_valid = None
            for par in (0, 1):
                qq = np.arange(512) + (2 * j + par) * 512
                if kind == "full":
                    valid = np.ones((128, 512), bool)
                else:
                    valid = kk[:, None] <= qq[None, :]
                    if kind == "swa":
                        valid &= (qq[None, :] - kk[:, None]) < 1024
                row_const = np.all(valid == valid[:, :1], axis=1).all()
                if not row_const:
                    union_tt = True
                elif not valid.all():
                    union_row = True
                if par == parity:
                    my_valid = valid
            if union_tt:
                cls[(j, s)] = "tt"
                p_masks[(j, s)] = my_valid.astype(np.float32)
            elif union_row:
                cls[(j, s)] = "row"
                row_masks[(j, s)] = np.where(my_valid[:, 0], 0.0, NEG).astype(np.float32)
            else:
                cls[(j, s)] = None
    return plan, cls, p_masks, row_masks


def build_constants(core_id):
    parity = core_id % 2
    c = {}
    c["ident_bf"] = np.eye(128).astype(ml_dtypes.bfloat16)
    c["ident_f32"] = np.eye(128).astype(np.float32)
    c["ones_bf"] = np.ones((128, 1), ml_dtypes.bfloat16)
    c["ones_f32"] = np.ones((128, 1), np.float32)
    c["ones_row"] = np.ones((1, 128), np.float32)

    blocks = core_token_blocks(parity)
    pos = np.concatenate([np.arange(512) + b * 512 for b in blocks]).astype(np.float64)
    ts = 10000.0 ** (np.arange(0, 128, 2).astype(np.float64) / 128.0)
    sin128 = pos[:, None] / ts[None, :]
    c["cos128"] = np.cos(sin128).astype(ml_dtypes.bfloat16)
    c["sin128"] = np.sin(sin128).astype(ml_dtypes.bfloat16)
    inv = 1.0 / (10000.0 ** (np.arange(0, 64, 2).astype(np.float64) / 64.0))
    fr = pos[:, None] * inv[None, :]
    c["cos64"] = np.cos(fr).astype(ml_dtypes.bfloat16)
    c["sin64"] = np.sin(fr).astype(ml_dtypes.bfloat16)

    for kind in ("causal", "swa", "full"):
        plan, cls, pm, rm = mask_tables(kind, parity)
        tts, rows = [], []
        for j in range(NQT):
            for s in range(len(plan[j])):
                if cls[(j, s)] == "tt":
                    tts.append(pm[(j, s)])
                elif cls[(j, s)] == "row":
                    rows.append(rm[(j, s)])
        c[f"pmask_{kind}"] = (np.stack(tts).astype(ml_dtypes.bfloat16) if tts
                              else np.zeros((1, 128, 512), ml_dtypes.bfloat16))
        c[f"rmask_{kind}"] = (np.stack(rows).astype(np.float32)[:, :, None] if rows
                              else np.zeros((1, 128, 1), np.float32))
    return c



class Builder:
    def __init__(self, nc, es):
        self.nc = nc
        self.es = es
        self.ctx = Ctx(nc, es)
        self.dram = {}
        _NC_FOR_MM.clear()
        _NC_FOR_MM.append(nc)

        def sb(name, cols, dtype):
            h = es.enter_context(nc.sbuf_tensor(name, [128, cols], dtype))
            return Buf(h, name)

        self.XNT = sb("XNT", NFC * TOK, BF16)
        self.QH = sb("QH", 32768, BF16)
        self.WB = sb("WB", 16384, BF16)
        self.WF = sb("WF", 2048, F32)
        self.TMA = sb("TMA", 4096, F32)
        self.XNB = sb("XNB", 2048, BF16)
        self.KVT = sb("KVT", 3072, BF16)
        self.PB = sb("PB", 2048, BF16)
        self.RS = sb("RS", 512, F32)
        self.SM = sb("SM", 1024, F32)
        self.CC = sb("CC", 288, F32)
        self.CB = sb("CB", 288, BF16)
        self.SC = sb("SC", 256, BF16)
        self.RW = sb("RW", 160, F32)
        self.GT = sb("GT", 256, F32)

        self.banks = []
        for i in range(8):
            bb = Buf(es.enter_context(nc.psum_tensor(f"bank{i}", [128, 512], F32)), f"bank{i}")
            bb.psum = True
            self.banks.append(bb)
        self.const_ap = {}
        self.const_r = {}

    def dram_buf(self, name, shape, dtype, kind):
        h = self.nc.dram_tensor(name, list(shape), dtype, kind=kind)
        b = Buf(h.ap(), name)
        self.dram[name] = b
        return b

    def load_consts(self, cdict):
        ctx = self.ctx
        layout = {"ident_f32": (self.CC, 0, 128, F32),
                  "ones_f32": (self.CC, 128, 1, F32),
                  "ones_row": (self.CC, 129, 128, F32),
                  "ident_bf": (self.CB, 0, 128, BF16),
                  "ones_bf": (self.CB, 128, 1, BF16)}
        for name, arr in cdict.items():
            dt = BF16 if arr.dtype == ml_dtypes.bfloat16 else F32
            d = self.dram_buf("c_" + name, arr.shape, dt, "ExternalInput")
            if name in layout:
                buf, col, w, _ = layout[name]
                ap = buf.ap[:arr.shape[0], col:col + w]
                self.const_ap[name] = ap
                self.const_r[name] = buf.r(name)
                ctx.dma("sp", lambda ap=ap: ap, lambda d=d: d.ap[:, :],
                        reads=[d.r()], writes=[buf.r(name)], dbg="const_" + name)
            else:
                self.const_ap[name] = None

    def cap(self, name):
        return self.const_ap[name]

    def cdram(self, name):
        return self.dram["c_" + name]


_NC_FOR_MM = []


def MM(out_ap, lhsT_ap, rhs_ap, start=True, stop=True):
    return lambda: _NC_FOR_MM[0].tensor.matmul(out_ap, lhsT_ap, rhs_ap, start=start, stop=stop)


def bf_half(bank, n0=0, n=128):
    return bank.ap[:, n0 // 2:(n0 + n) // 2].bitcast(BF16)



def emit_ln(b, X, dbg="ln", gate_hook=None):
    ctx, nc = b.ctx, b.nc
    XNT, TMA, XNB, SM, WF = b.XNT, b.TMA, b.XNB, b.SM, b.WF
    tb = b.banks[6]
    for tc in range(NTC):
        s = tc % 2
        xa = TMA.ap[:, s * 2048:(s + 1) * 2048]
        xr = TMA.r(f"h{s}")
        c0 = s * 8
        ctx.dma("sp", lambda xa=xa: xa,
                lambda tc=tc: X.ap[tc * 128:(tc + 1) * 128, :],
                reads=[X.r()], writes=[xr], dbg=f"{dbg}.load{tc}")
        ctx.op("dve", lambda xa=xa, c0=c0: nc.vector.reduce_sum(
            SM.ap[:, c0:c0 + 1], xa, axis=AX.X),
            reads=[xr], writes=[SM.r(f"s{s}")], dbg=f"{dbg}.sum{tc}")
        ctx.op("act", lambda xa=xa, c0=c0: nc.scalar.activation(
            out=WF.ap[:, 0:2048], in_=xa, func=AF.Square,
            accum_out=SM.ap[:, c0 + 1:c0 + 2]),
            reads=[xr], writes=[WF.r(), SM.r(f"q{s}")], dbg=f"{dbg}.sq{tc}")
        ctx.op("dve", lambda c0=c0: nc.vector.tensor_scalar(
            SM.ap[:, c0:c0 + 1], SM.ap[:, c0:c0 + 1], 1.0 / D, None, ALU.mult),
            reads=[SM.r(f"s{s}")], writes=[SM.r(f"s{s}")], dbg=f"{dbg}.mean{tc}")
        ctx.op("dve", lambda c0=c0: nc.vector.tensor_scalar(
            SM.ap[:, c0 + 1:c0 + 2], SM.ap[:, c0 + 1:c0 + 2], 1.0 / D, None, ALU.mult),
            reads=[SM.r(f"q{s}")], writes=[SM.r(f"q{s}")], dbg=f"{dbg}.msq{tc}")
        ctx.op("dve", lambda c0=c0: nc.vector.tensor_tensor(
            SM.ap[:, c0 + 2:c0 + 3], SM.ap[:, c0:c0 + 1], SM.ap[:, c0:c0 + 1], op=ALU.mult),
            reads=[SM.r(f"s{s}")], writes=[SM.r(f"m2{s}")], dbg=f"{dbg}.m2{tc}")
        ctx.op("dve", lambda c0=c0: nc.vector.tensor_tensor(
            SM.ap[:, c0 + 3:c0 + 4], SM.ap[:, c0 + 1:c0 + 2], SM.ap[:, c0 + 2:c0 + 3],
            op=ALU.subtract),
            reads=[SM.r(f"q{s}"), SM.r(f"m2{s}")], writes=[SM.r(f"v{s}")], dbg=f"{dbg}.var{tc}")
        ctx.op("dve", lambda c0=c0: nc.vector.tensor_scalar(
            SM.ap[:, c0 + 3:c0 + 4], SM.ap[:, c0 + 3:c0 + 4], 1e-5, None, ALU.add),
            reads=[SM.r(f"v{s}")], writes=[SM.r(f"v{s}")], dbg=f"{dbg}.veps{tc}")
        ctx.op("act", lambda c0=c0: nc.scalar.activation(
            out=SM.ap[:, c0 + 4:c0 + 5], in_=SM.ap[:, c0 + 3:c0 + 4], func=AF.Sqrt),
            reads=[SM.r(f"v{s}")], writes=[SM.r(f"sd{s}")], dbg=f"{dbg}.sd{tc}")
        ctx.op("dve", lambda c0=c0: nc.vector.reciprocal(
            SM.ap[:, c0 + 5:c0 + 6], SM.ap[:, c0 + 4:c0 + 5]),
            reads=[SM.r(f"sd{s}")], writes=[SM.r(f"rs{s}")], dbg=f"{dbg}.rs{tc}")
        if gate_hook is None:
            ctx.op("dve", lambda xa=xa, c0=c0: nc.vector.tensor_scalar(
                XNB.ap[:, :], xa, SM.ap[:, c0:c0 + 1], SM.ap[:, c0 + 5:c0 + 6],
                ALU.subtract, ALU.mult),
                reads=[xr, SM.r(f"s{s}"), SM.r(f"rs{s}")], writes=[XNB.r()], dbg=f"{dbg}.n{tc}")
        else:
            ctx.op("dve", lambda xa=xa, c0=c0: nc.vector.tensor_scalar(
                WF.ap[:, 0:2048], xa, SM.ap[:, c0:c0 + 1], SM.ap[:, c0 + 5:c0 + 6],
                ALU.subtract, ALU.mult),
                reads=[xr, SM.r(f"s{s}"), SM.r(f"rs{s}")], writes=[WF.r()], dbg=f"{dbg}.nf{tc}")
            ctx.op("dve", lambda: nc.vector.tensor_copy(XNB.ap[:, :], WF.ap[:, 0:2048]),
                   reads=[WF.r()], writes=[XNB.r()], dbg=f"{dbg}.nc{tc}")
        for g in range(4):
            tbv = bf_half(tb, 0, 512)
            for q in range(4):
                fc = g * 4 + q
                ctx.op("pe", lambda tbv=tbv, q=q, fc=fc: nc.tensor.transpose(
                    tbv[:, q * 128:(q + 1) * 128], XNB.ap[:, fc * 128:(fc + 1) * 128],
                    b.cap("ident_bf")),
                    reads=[XNB.r(), b.const_r["ident_bf"]], writes=[tb.r()],
                    dbg=f"{dbg}.T{tc}_{fc}")
            ctx.op("dve", lambda tc=tc, g=g, tbv=tbv: nc.vector.tensor_copy(
                XNT.ap.rearrange("p (f t) -> p f t", t=TOK)[:, g * 4:(g + 1) * 4,
                                                            tc * 128:(tc + 1) * 128],
                tbv.rearrange("p (f t) -> p f t", t=128)),
                reads=[tb.r()], writes=[XNT.r()], dbg=f"{dbg}.Tev{tc}_{g}")
        if gate_hook is not None:
            t5 = b.banks[5]
            for fc in range(16):
                ctx.op("pe", lambda fc=fc: nc.tensor.transpose(
                    t5.ap[:, 0:128], WF.ap[:, fc * 128:(fc + 1) * 128],
                    b.cap("ident_f32")),
                    reads=[WF.r(), b.const_r["ident_f32"]], writes=[t5.r()],
                    dbg=f"{dbg}.gT{tc}_{fc}")
                ctx.op("dve", lambda: nc.vector.tensor_copy(
                    b.GT.ap[:, 128:256], t5.ap[:, 0:128]),
                    reads=[t5.r()], writes=[b.GT.r("s")], dbg=f"{dbg}.gTe{tc}_{fc}")
                gate_hook(tc, fc)



def rhs_tile(rhs_buf, ic, tt, w=QT):
    return rhs_buf.ap[:, ic * TOK + tt * w: ic * TOK + tt * w + w]


def emit_proj(b, W, d_in, d_out, epilogue, rhs_buf=None, rhs_read=None,
              transposed_w=False, tok_tiles=NQT, dbg="proj",
              oc_list=None, tt_list=None, rhs_col_fn=None):
    ctx, nc = b.ctx, b.nc
    n_ic = d_in // 128
    tw = min(128, d_out)
    n_oc = (d_out + 127) // 128
    oc_list = oc_list if oc_list is not None else list(range(n_oc))
    tt_list = tt_list if tt_list is not None else list(range(tok_tiles))
    if rhs_col_fn is None:
        rhs_col_fn = lambda ic, tt: ic * TOK + tt * QT
    cols_per_oc = n_ic * tw
    grp = max(1, min(8192 // cols_per_oc, len(oc_list)))
    mm_banks = (b.banks[0], b.banks[1], b.banks[4], b.banks[5])
    slot_cols = (0, 8192)
    groups = [oc_list[i:i + grp] for i in range(0, len(oc_list), grp)]
    w_bf = W.ap.dtype == BF16
    granule = getattr(W, "granule", None)
    if granule:
        assert granule == (n_ic, grp), (granule, n_ic, grp)
    for gi, g in enumerate(groups):
        sc = slot_cols[gi % 2]
        wr = b.WB.r(f"w{2 * (gi % 2)}")
        wr2 = b.WB.r(f"w{2 * (gi % 2) + 1}")
        if granule:
            ctx.dma("sp", lambda sc=sc, n=len(g), n_ic=n_ic: b.WB.ap[:, sc:sc + n_ic * n * 128],
                    lambda W=W, g=g, n=len(g), n_ic=n_ic: W.ap[
                        :, g[0] * n_ic * 128:(g[0] + n) * n_ic * 128],
                    reads=[W.r()], writes=[wr, wr2], dbg=f"{dbg}.w{gi}")
        elif not transposed_w:
            ctx.dma("sp" if w_bf else "pool",
                lambda sc=sc, n=len(g), n_ic=n_ic: b.WB.ap[:, sc:sc + n_ic * n * tw].rearrange(
                "p (i c) -> p i c", c=n * tw),
                lambda W=W, g=g, n=len(g): W.ap.rearrange("(i p) o -> p i o", p=128)[:, :, g[0] * tw:g[0] * tw + n * tw],
                reads=[W.r()], writes=[wr, wr2], dbg=f"{dbg}.w{gi}")
        else:
            assert d_in == 2048
            tb = b.banks[5]
            for oi, oc in enumerate(g):
                ldr = b.WF.r()
                ctx.dma("sp", lambda: b.WF.ap[:, :],
                        lambda W=W, oc=oc: W.ap[oc * 128:(oc + 1) * 128, :],
                        reads=[W.r()], writes=[ldr], dbg=f"{dbg}.wl{oc}")
                for g4 in range(n_ic // 4):
                    for q in range(4):
                        ic = g4 * 4 + q
                        ctx.op("pe", lambda q=q, ic=ic: nc.tensor.transpose(
                            tb.ap[:, q * 128:(q + 1) * 128],
                            b.WF.ap[:, ic * 128:(ic + 1) * 128],
                            b.cap("ident_f32")),
                            reads=[ldr, b.const_r["ident_f32"]], writes=[tb.r()],
                            dbg=f"{dbg}.wT{oc}_{ic}")
                    ctx.op("dve", lambda sc=sc, oi=oi, g4=g4, n=len(g): nc.vector.tensor_copy(
                        b.WB.ap[:, sc:sc + n_ic * n * 128].rearrange(
                            "p (i c) -> p i c", c=n * 128)[:, g4 * 4:(g4 + 1) * 4,
                                                           oi * 128:(oi + 1) * 128],
                        tb.ap.rearrange("p (i c) -> p i c", c=128)[:, 0:4, :]),
                        reads=[tb.r()], writes=[wr, wr2], dbg=f"{dbg}.wTe{oc}_{g4}")
        for oi, oc in enumerate(g):
            for ti, tt in enumerate(tt_list):
                bank = mm_banks[(gi * len(g) * len(tt_list) + oi * len(tt_list) + ti) % 4]
                for ic in range(n_ic):
                    if granule:
                        col = sc + oi * n_ic * 128 + ic * 128
                    else:
                        col = sc + ic * len(g) * tw + oi * tw
                    ctx.op("pe", MM(bank.ap[0:tw, :],
                                    b.WB.ap[:, col:col + tw],
                                    rhs_buf.ap[:, rhs_col_fn(ic, tt):rhs_col_fn(ic, tt) + QT],
                                    start=(ic == 0), stop=(ic == n_ic - 1)),
                           reads=[wr, wr2, rhs_read], writes=[bank.r()],
                           dbg=f"{dbg}.mm{oc}_{tt}_{ic}")
                epilogue(oc, tt, bank)



def ep_to_sbuf_bf16(b, dst_buf, dst_region, col_fn, bias_ap=None, bias_reads=(), dbg="ep"):
    ctx, nc = b.ctx, b.nc

    def ep(oc, tt, bank):
        col = col_fn(oc, tt)
        if bias_ap is None:
            ctx.op("dve", lambda col=col, bank=bank: nc.vector.tensor_copy(
                dst_buf.ap[:, col:col + QT], bank.ap[:, :]),
                reads=[bank.r()], writes=[dst_region], dbg=f"{dbg}.ev{oc}_{tt}")
        else:
            ctx.op("dve", lambda col=col, bank=bank: nc.vector.tensor_scalar(
                dst_buf.ap[:, col:col + QT], bank.ap[:, :], bias_ap(oc), None, ALU.add),
                reads=[bank.r(), *bias_reads], writes=[dst_region], dbg=f"{dbg}.ev{oc}_{tt}")
    return ep


def ep_residual(b, X, scale_col=None, scale_reads=(), dbg="res", sm_bias=None):
    ctx, nc = b.ctx, b.nc
    tb = b.banks[7]

    def ep(oc, tt, bank):
        s = (oc + tt) % 2
        stg = b.PB.ap[:, s * 512:(s + 1) * 512]
        stg_r = b.PB.r(f"{s}")
        if sm_bias is None:
            ctx.op("dve", lambda bank=bank, stg=stg: nc.vector.tensor_copy(stg, bank.ap[:, :]),
                   reads=[bank.r()], writes=[stg_r], dbg=f"{dbg}.ev{oc}_{tt}")
        else:
            ctx.op("dve", lambda bank=bank, stg=stg, oc=oc: nc.vector.tensor_scalar(
                stg, bank.ap[:, :], bias_ap(b, sm_bias, oc), None, ALU.add),
                reads=[bank.r(), b.SM.r(f"bias{sm_bias}")], writes=[stg_r],
                dbg=f"{dbg}.ev{oc}_{tt}")
        if RES_LVL < 2:
            return
        tbv = bf_half(tb, 0, 512)
        for q in range(4):
            ctx.op("pe", lambda tbv=tbv, stg=stg, q=q: nc.tensor.transpose(
                tbv[:, q * 128:(q + 1) * 128], stg[:, q * 128:(q + 1) * 128],
                b.cap("ident_bf")),
                reads=[stg_r, b.const_r["ident_bf"]], writes=[tb.r()],
                dbg=f"{dbg}.T{oc}_{tt}_{q}")
        if RES_LVL < 3:
            return
        rs = b.RS.ap[:, 0:512]
        rs_r = b.RS.r()
        if scale_col is None:
            ctx.op("dve", lambda tbv=tbv, rs=rs: nc.vector.tensor_copy(
                rs.rearrange("p (q v) -> p q v", v=128), tbv.rearrange("p (q v) -> p q v", v=128)),
                reads=[tb.r()], writes=[rs_r], dbg=f"{dbg}.Te{oc}_{tt}")
        else:
            for q in range(4):
                ctx.op("dve", lambda tbv=tbv, rs=rs, q=q, tt=tt: nc.vector.tensor_scalar(
                    rs[:, q * 128:(q + 1) * 128], tbv[:, q * 128:(q + 1) * 128],
                    scale_col(tt, q), None, ALU.mult),
                    reads=[tb.r(), *scale_reads], writes=[rs_r], dbg=f"{dbg}.Ts{oc}_{tt}_{q}")
        if RES_LVL < 4:
            return
        kw = {"accum_op": ALU.add} if RES_ACCUM else {}
        ctx.dma("pool", lambda X=X, tt=tt, oc=oc: X.ap[tt * 512:(tt + 1) * 512,
                                                       oc * 128:(oc + 1) * 128].rearrange(
            "(q p) v -> p q v", p=128),
            lambda rs=rs: rs.rearrange("p (q v) -> p q v", v=128),
            reads=[rs_r], writes=[X.r()], dbg=f"{dbg}.acc{oc}_{tt}", **kw)
    return ep


SM_LN = 0
SM_STAGE = 32
SM_QMAX = 64
SM_KMAX = 128
SM_LINV = 144
SM_B1 = 152
SM_NEGB = 156
SM_NEGBR = 160
SM_ROWM = 176
SM_TSC = 192
SM_BIAS = 208
SM_LROW = 512



def emit_v_repack(b, ccVo, vg2, n_kvh, featV, dbg):
    ctx, nc = b.ctx, b.nc
    for h in range(n_kvh):
        for rank in range(2):
            for t4 in range(4):
                s = t4 % 2
                stg = b.PB.ap[:, s * 512:(s + 1) * 512]
                stg_r = b.PB.r(f"{s}")
                ctx.dma("sp", lambda stg=stg: stg.rearrange("p (c v) -> p c v", v=128),
                        lambda h=h, rank=rank, t4=t4: ccVo.ap[
                            rank * TOK + t4 * 512:rank * TOK + (t4 + 1) * 512,
                            h * 128:(h + 1) * 128].rearrange("(c p) v -> p c v", p=128),
                        reads=[ccVo.r()], writes=[stg_r], dbg=f"{dbg}.vrl{h}_{rank}_{t4}")
                ctx.dma("sp", lambda h=h, rank=rank, t4=t4: vg2.ap[
                    :, h * 4096 + rank * 2048 + t4 * 512:h * 4096 + rank * 2048 + (t4 + 1) * 512],
                    lambda stg=stg: stg,
                    reads=[stg_r], writes=[vg2.r()], dbg=f"{dbg}.vrs{h}_{rank}_{t4}")


def emit_attn(b, *, kind, n_heads, scale, q_parts, k_parts, v_dram, vcol,
              kmax_col, qmax_col, o_sink, dbg):
    ctx, nc = b.ctx, b.nc
    plan, cls, _, _ = mask_tables(kind, 0)
    S0, S1, S2, OB, LB, TB = (b.banks[2], b.banks[3], b.banks[7], b.banks[4],
                              b.banks[5], b.banks[6])
    tt_idx, row_idx = {}, {}
    for j in range(NQT):
        for s in range(len(plan[j])):
            if cls[(j, s)] == "tt":
                tt_idx[(j, s)] = len(tt_idx)
            elif cls[(j, s)] == "row":
                row_idx[(j, s)] = len(row_idx)
    pmask_d = b.cdram(f"pmask_{kind}")
    rmask_d = b.cdram(f"rmask_{kind}")

    for j in range(NQT):
        mask_slots = {}
        rowm_slots = {}
        for s in range(len(plan[j])):
            if (j, s) in tt_idx:
                sl = len(mask_slots)
                assert sl < 16
                mask_slots[s] = sl
                ctx.dma("sp", lambda sl=sl: b.WB.ap[:, sl * 512:(sl + 1) * 512],
                        lambda i=tt_idx[(j, s)]: pmask_d.ap[i, :, :],
                        reads=[pmask_d.r()], writes=[b.WB.r("w0"), b.WB.r("w1")],
                        dbg=f"{dbg}.mask{j}_{s}")
            elif (j, s) in row_idx:
                sl = len(rowm_slots)
                assert sl < 16
                rowm_slots[s] = sl
                ctx.dma("sp", lambda sl=sl: b.SM.ap[:, SM_ROWM + sl:SM_ROWM + sl + 1],
                        lambda i=row_idx[(j, s)]: rmask_d.ap[i, :, :],
                        reads=[rmask_d.r()], writes=[b.SM.r(f"rowm{sl}")],
                        dbg=f"{dbg}.rowm{j}_{s}")

        for h in range(n_heads):
            qc = qmax_col(h, j)
            kc_ = kmax_col(h)
            ctx.op("dve", lambda qc=qc, kc_=kc_: nc.vector.tensor_tensor(
                b.SM.ap[0:1, SM_B1:SM_B1 + 1], b.SM.ap[0:1, qc:qc + 1],
                b.SM.ap[0:1, kc_:kc_ + 1], op=ALU.mult),
                reads=[b.SM.r("qmax"), b.SM.r("kmax")], writes=[b.SM.r("negB1")],
                dbg=f"{dbg}.b1_{h}_{j}")
            ctx.op("dve", lambda: nc.vector.tensor_scalar(
                b.SM.ap[0:1, SM_B1 + 1:SM_B1 + 2], b.SM.ap[0:1, SM_B1:SM_B1 + 1],
                -1.02 * scale, None, ALU.mult),
                reads=[b.SM.r("negB1")], writes=[b.SM.r("negB2")], dbg=f"{dbg}.b2_{h}_{j}")
            ctx.op("pe", MM(TB.ap[:, 0:1], b.cap("ones_row"),
                            b.SM.ap[0:1, SM_B1 + 1:SM_B1 + 2]),
                   reads=[b.SM.r("negB2"), b.const_r["ones_row"]], writes=[TB.r()],
                   dbg=f"{dbg}.bb_{h}_{j}")
            ctx.op("dve", lambda: nc.vector.tensor_copy(
                b.SM.ap[:, SM_NEGB:SM_NEGB + 1], TB.ap[:, 0:1]),
                reads=[TB.r()], writes=[b.SM.r("negB")], dbg=f"{dbg}.bc_{h}_{j}")
            for s, sl in rowm_slots.items():
                ctx.op("dve", lambda sl=sl: nc.vector.tensor_tensor(
                    b.SM.ap[:, SM_NEGBR + sl:SM_NEGBR + sl + 1],
                    b.SM.ap[:, SM_NEGB:SM_NEGB + 1],
                    b.SM.ap[:, SM_ROWM + sl:SM_ROWM + sl + 1], op=ALU.add),
                    reads=[b.SM.r("negB"), b.SM.r(f"rowm{sl}")],
                    writes=[b.SM.r(f"negBr{sl}")], dbg=f"{dbg}.bv_{h}_{j}_{sl}")

            chunks = plan[j]
            n_ch = len(chunks)
            assert n_ch % 4 == 0 or n_ch < 4
            kp = k_parts(h)
            qp = q_parts(h)
            for si, kchunk in enumerate(chunks):
                rank, kc16 = kchunk // 16, kchunk % 16
                sb_ = (S0, S1, S2)[si % 3]
                kv_s = (si // 4) % 2
                sub = si % 4
                kreads = []
                if sub == 0:
                    nw = min(4, n_ch - si) * 128
                    for pi, (kd, row0f, col0f, prt) in enumerate(kp):
                        qbase = qp[pi][3] if len(qp[pi]) > 3 else 0
                        kr = b.KVT.r(f"k{kv_s}_{pi}")
                        ctx.dma("sp", lambda kv_s=kv_s, pi=pi, prt=prt, qbase=qbase, nw=nw:
                                b.KVT.ap[qbase:qbase + prt,
                                         kv_s * 1536 + pi * 512:kv_s * 1536 + pi * 512 + nw],
                                lambda kd=kd, row0f=row0f, col0f=col0f, rank=rank,
                                kc16=kc16, prt=prt, nw=nw:
                                kd.ap[row0f(rank):row0f(rank) + prt,
                                      col0f(rank) + kc16 * 128:col0f(rank) + kc16 * 128 + nw],
                                reads=[kd.r()], writes=[kr], dbg=f"{dbg}.kld{j}_{h}_{si}")
                    vr = b.KVT.r(f"v{kv_s}")
                    ctx.dma("sp", lambda kv_s=kv_s, nw=nw: b.KVT.ap[:, kv_s * 1536 + 1024:
                                                                    kv_s * 1536 + 1024 + nw],
                            lambda rank=rank, kc16=kc16, h=h, nw=nw: v_dram.ap[
                                :, vcol(h) * 4096 // 128 + rank * 2048 + kc16 * 128:
                                vcol(h) * 4096 // 128 + rank * 2048 + kc16 * 128 + nw],
                            reads=[v_dram.r()], writes=[vr], dbg=f"{dbg}.vld{j}_{h}_{si}")
                kreads = [b.KVT.r(f"k{kv_s}_{pi}") for pi in range(len(kp))]
                vr = b.KVT.r(f"v{kv_s}")
                for pi, (kd, row0f, col0f, prt) in enumerate(kp):
                    tup = qp[pi]
                    qa, qprt, qread = tup[0], tup[1], tup[2]
                    qbase = tup[3] if len(tup) > 3 else 0
                    ctx.op("pe", MM(sb_.ap[:, :],
                                    b.KVT.ap[qbase:qbase + prt,
                                             kv_s * 1536 + pi * 512 + sub * 128:
                                             kv_s * 1536 + pi * 512 + (sub + 1) * 128],
                                    qa[:, j * QT:(j + 1) * QT],
                                    start=(pi == 0), stop=(pi == len(kp) - 1)),
                           reads=[kreads[pi], qread], writes=[sb_.r()],
                           dbg=f"{dbg}.smm{j}_{h}_{si}")
                if si in rowm_slots:
                    bias = b.SM.ap[:, SM_NEGBR + rowm_slots[si]:SM_NEGBR + rowm_slots[si] + 1]
                    bias_r = b.SM.r(f"negBr{rowm_slots[si]}")
                else:
                    bias = b.SM.ap[:, SM_NEGB:SM_NEGB + 1]
                    bias_r = b.SM.r("negB")
                ps = si % 3
                pb = b.PB.ap[:, ps * 512:(ps + 1) * 512]
                pb_r = b.PB.r(f"{ps}")
                ctx.op("act", lambda sb_=sb_, pb=pb, bias=bias: nc.scalar.activation(
                    out=pb, in_=sb_.ap[:, :], func=AF.Exp, bias=bias, scale=scale),
                    reads=[sb_.r(), bias_r], writes=[pb_r], dbg=f"{dbg}.exp{j}_{h}_{si}")
                if si in mask_slots:
                    sl = mask_slots[si]
                    ctx.op("dve", lambda pb=pb, sl=sl: nc.vector.tensor_mul(
                        pb, pb, b.WB.ap[:, sl * 512:(sl + 1) * 512]),
                        reads=[pb_r, b.WB.r("w0"), b.WB.r("w1")], writes=[pb_r],
                        dbg=f"{dbg}.pm{j}_{h}_{si}")
                ctx.op("pe", MM(LB.ap[0:1, 0:512], b.cap("ones_bf"), pb,
                                start=(si == 0), stop=(si == n_ch - 1)),
                       reads=[pb_r, b.const_r["ones_bf"]], writes=[LB.r()],
                       dbg=f"{dbg}.lmm{j}_{h}_{si}")
                ctx.op("pe", MM(OB.ap[:, :],
                                b.KVT.ap[:, kv_s * 1536 + 1024 + sub * 128:
                                         kv_s * 1536 + 1024 + (sub + 1) * 128],
                                pb,
                                start=(si == 0), stop=(si == n_ch - 1)),
                       reads=[pb_r, vr], writes=[OB.r()],
                       dbg=f"{dbg}.av{j}_{h}_{si}")
            ctx.op("dve", lambda: nc.vector.tensor_copy(
                b.SM.ap[0:1, SM_LROW:SM_LROW + 512], LB.ap[0:1, 0:512]),
                reads=[LB.r()], writes=[b.SM.r("lrow")], dbg=f"{dbg}.lc{j}_{h}")
            ctx.op("dve", lambda: nc.vector.reciprocal(
                b.SM.ap[0:1, SM_LROW:SM_LROW + 512], b.SM.ap[0:1, SM_LROW:SM_LROW + 512]),
                reads=[b.SM.r("lrow")], writes=[b.SM.r("lrow")], dbg=f"{dbg}.li{j}_{h}")
            ctx.op("pe", MM(TB.ap[:, 0:512], b.cap("ones_row"),
                            b.SM.ap[0:1, SM_LROW:SM_LROW + 512]),
                   reads=[b.SM.r("lrow"), b.const_r["ones_row"]], writes=[TB.r()],
                   dbg=f"{dbg}.lb{j}_{h}")
            ctx.op("dve", lambda: nc.vector.tensor_copy(
                b.WF.ap[:, 0:512], TB.ap[:, 0:512]),
                reads=[TB.r()], writes=[b.WF.r()], dbg=f"{dbg}.lbe{j}_{h}")
            o_sink(h, j, OB, b.WF.ap[:, 0:512], b.WF.r())


def osink_to_xnt(b, dbg="osink"):
    ctx, nc = b.ctx, b.nc

    def sink(h, j, ob_bank, linv_ap, linv_r):
        ctx.op("dve", lambda h=h, j=j, ob_bank=ob_bank, linv_ap=linv_ap:
               nc.vector.tensor_mul(
                   b.XNT.ap[:, h * TOK + j * 512:h * TOK + (j + 1) * 512],
                   ob_bank.ap[:, :], linv_ap),
               reads=[ob_bank.r(), linv_r], writes=[b.XNT.r()], dbg=f"{dbg}.ev{h}_{j}")
    return sink



QN_LVL = 9
RES_ACCUM = True
RES_LVL = 9


def emit_sq_norm(b, src_ap, src_r, nb_bank, dbg):
    ctx, nc = b.ctx, b.nc
    sq = b.XNB.ap[:, 0:512]
    sq_r = b.XNB.r()
    ctx.op("act", lambda src_ap=src_ap, sq=sq: nc.scalar.activation(
        out=sq, in_=src_ap, func=AF.Square),
        reads=[src_r], writes=[sq_r], dbg=dbg + ".sq")
    if QN_LVL < 2:
        return
    ctx.op("pe", MM(nb_bank.ap[0:1, 0:512], b.cap("ones_bf"), sq),
           reads=[sq_r, b.const_r["ones_bf"]], writes=[nb_bank.r()], dbg=dbg + ".nmm")


def emit_qmax_from_nb(b, nb_bank, h, j, scale_hint, dbg):
    ctx, nc = b.ctx, b.nc
    col = SM_QMAX + h * 4 + j
    if QN_LVL < 3:
        return
    ctx.op("dve", lambda: nc.vector.reduce_max(
        b.SM.ap[0:1, SM_STAGE:SM_STAGE + 1], nb_bank.ap[0:1, 0:512], axis=AX.X),
        reads=[nb_bank.r()], writes=[b.SM.r("stg0")], dbg=dbg + ".max")
    if QN_LVL < 4:
        return
    ctx.op("act", lambda col=col: nc.scalar.activation(
        out=b.SM.ap[0:1, col:col + 1], in_=b.SM.ap[0:1, SM_STAGE:SM_STAGE + 1],
        func=AF.Sqrt),
        reads=[b.SM.r("stg0")], writes=[b.SM.r("qmax")], dbg=dbg + ".sqrt")



def load_bias_cols(b, bias_dram, sm_col, dbg):
    ctx, nc = b.ctx, b.nc
    n_oc = bias_dram.ap.shape[0] // 128
    tb = b.banks[6]
    ctx.dma("sp", lambda n_oc=n_oc: b.WF.ap[0:n_oc, 0:128],
            lambda n_oc=n_oc: bias_dram.ap.rearrange("(o p) -> o p", p=128),
            reads=[bias_dram.r()], writes=[b.WF.r()], dbg=dbg + ".l")
    ctx.op("pe", lambda n_oc=n_oc: nc.tensor.transpose(
        tb.ap[:, 32:32 + n_oc], b.WF.ap[0:n_oc, 0:128],
        b.cap("ident_f32")[0:n_oc, 0:n_oc]),
        reads=[b.WF.r(), b.const_r["ident_f32"]], writes=[tb.r()], dbg=dbg + ".T")
    ctx.op("dve", lambda n_oc=n_oc: nc.vector.tensor_copy(
        b.SM.ap[:, sm_col:sm_col + n_oc], tb.ap[:, 32:32 + n_oc]),
        reads=[tb.r()], writes=[b.SM.r(f"bias{sm_col}")], dbg=dbg + ".e")


def bias_ap(b, sm_col, oc):
    if sm_col is None:
        return None
    return b.SM.ap[:, sm_col + oc:sm_col + oc + 1]


def ep_q_path(b, NB, dbg, sm_bias=None, qh_col=None):
    ctx, nc = b.ctx, b.nc
    qh_col = qh_col or (lambda h, tt: h * TOK + tt * 512)

    def ep(h, tt, bank):
        col = qh_col(h, tt)
        qr = b.QH.r()
        if sm_bias is None:
            ctx.op("dve", lambda bank=bank, col=col: nc.vector.tensor_copy(
                b.QH.ap[:, col:col + 512], bank.ap[:, :]),
                reads=[bank.r()], writes=[qr], dbg=f"{dbg}.qev{h}_{tt}")
        else:
            ctx.op("dve", lambda bank=bank, col=col, h=h: nc.vector.tensor_scalar(
                b.QH.ap[:, col:col + 512], bank.ap[:, :], bias_ap(b, sm_bias, h), None, ALU.add),
                reads=[bank.r(), b.SM.r(f"bias{sm_bias}")], writes=[qr], dbg=f"{dbg}.qev{h}_{tt}")
        emit_sq_norm(b, b.QH.ap[:, col:col + 512], qr, NB, f"{dbg}.qn{h}_{tt}")
        emit_qmax_from_nb(b, NB, h, tt, None, f"{dbg}.qm{h}_{tt}")
    return ep


def ep_k_path(b, NB, ccK, nrow0, dbg, sm_bias=None):
    ctx, nc = b.ctx, b.nc

    def ep(h, tt, bank):
        s = tt % 2
        stg = b.PB.ap[:, s * 512:(s + 1) * 512]
        stg_r = b.PB.r(f"{s}")
        if sm_bias is None:
            ctx.op("dve", lambda bank=bank, stg=stg: nc.vector.tensor_copy(stg, bank.ap[:, :]),
                   reads=[bank.r()], writes=[stg_r], dbg=f"{dbg}.kev{h}_{tt}")
        else:
            ctx.op("dve", lambda bank=bank, stg=stg, h=h: nc.vector.tensor_scalar(
                stg, bank.ap[:, :], bias_ap(b, sm_bias, h), None, ALU.add),
                reads=[bank.r(), b.SM.r(f"bias{sm_bias}")], writes=[stg_r], dbg=f"{dbg}.kev{h}_{tt}")
        ctx.dma("sp", lambda h=h, tt=tt: ccK.ap[h * 128:(h + 1) * 128, tt * 512:(tt + 1) * 512],
                lambda stg=stg: stg,
                reads=[stg_r], writes=[ccK.r()], dbg=f"{dbg}.kst{h}_{tt}")
        emit_sq_norm(b, stg, stg_r, NB, f"{dbg}.kn{h}_{tt}")
        ctx.op("dve", lambda: nc.vector.tensor_copy(
            b.XNB.ap[0:1, 512:1024], NB.ap[0:1, 0:512]),
            reads=[NB.r()], writes=[b.XNB.r()], dbg=f"{dbg}.kne{h}_{tt}")
        ctx.dma("sp", lambda h=h, tt=tt, nrow0=nrow0: ccK.ap[nrow0 + h:nrow0 + h + 1,
                                                            tt * 512:(tt + 1) * 512],
                lambda: b.XNB.ap[0:1, 512:1024],
                reads=[b.XNB.r()], writes=[ccK.r()], dbg=f"{dbg}.knst{h}_{tt}")
    return ep


def ep_v_path(b, ccV, dbg, sm_bias=None, vcol=None):
    ctx, nc = b.ctx, b.nc
    tb = b.banks[7]
    vcol = vcol or (lambda h: h * 128)

    def ep(h, tt, bank):
        s = tt % 2
        stg = b.PB.ap[:, s * 512:(s + 1) * 512]
        stg_r = b.PB.r(f"{s}")
        if sm_bias is None:
            ctx.op("dve", lambda bank=bank, stg=stg: nc.vector.tensor_copy(stg, bank.ap[:, :]),
                   reads=[bank.r()], writes=[stg_r], dbg=f"{dbg}.vev{h}_{tt}")
        else:
            ctx.op("dve", lambda bank=bank, stg=stg, h=h: nc.vector.tensor_scalar(
                stg, bank.ap[:, :], bias_ap(b, sm_bias, h), None, ALU.add),
                reads=[bank.r(), b.SM.r(f"bias{sm_bias}")], writes=[stg_r], dbg=f"{dbg}.vev{h}_{tt}")
        tbv = bf_half(tb, 0, 512)
        for q in range(4):
            ctx.op("pe", lambda tbv=tbv, stg=stg, q=q: nc.tensor.transpose(
                tbv[:, q * 128:(q + 1) * 128], stg[:, q * 128:(q + 1) * 128],
                b.cap("ident_bf")),
                reads=[stg_r, b.const_r["ident_bf"]], writes=[tb.r()],
                dbg=f"{dbg}.vT{h}_{tt}_{q}")
        s2 = 1 - s
        stg2 = b.PB.ap[:, s2 * 512:(s2 + 1) * 512]
        stg2_r = b.PB.r(f"{s2}")
        ctx.op("dve", lambda tbv=tbv, stg2=stg2: nc.vector.tensor_copy(stg2, tbv),
               reads=[tb.r()], writes=[stg2_r], dbg=f"{dbg}.vTe{h}_{tt}")
        ctx.dma("sp", lambda h=h, tt=tt: ccV.ap[tt * 512:(tt + 1) * 512,
                                                vcol(h):vcol(h) + 128].rearrange(
            "(sq p) v -> p sq v", p=128),
            lambda stg2=stg2: stg2.rearrange("p (sq v) -> p sq v", v=128),
            reads=[stg2_r], writes=[ccV.r()], dbg=f"{dbg}.vst{h}_{tt}")
    return ep



L0AA_SUB = 4
L0AA_PARTS = 5


def sub_l0aa(b):
    ctx, nc = b.ctx, b.nc
    X = b.X
    Wqkv = b.dram["L0_attn_a_w_qkv"]
    Wout = b.dram["L0_attn_a_w_out"]
    KR = 2064
    ccK = b.dram_buf("ccK_l0aa", [KR, TOK], BF16, "Internal")
    ccKo = b.dram_buf("ccKo_l0aa", [2 * KR, TOK], BF16, "Internal")
    ccV = b.dram_buf("ccV_l0aa", [TOK, D], BF16, "Internal")
    ccVo = b.dram_buf("ccVo_l0aa", [2 * TOK, D], BF16, "Internal")
    vg2 = b.dram_buf("vg2_l0aa", [128, 16 * 4096], BF16, "Internal")

    emit_ln(b, X, dbg="l0aa.ln")
    if L0AA_PARTS < 2:
        return
    NB = b.banks[2]
    epq = ep_q_path(b, NB, "l0aa")
    epk = ep_k_path(b, NB, ccK, 2048, "l0aa")
    epv = ep_v_path(b, ccV, "l0aa")

    def ep(oc, tt, bank):
        if oc < 16:
            epq(oc, tt, bank)
        elif oc < 32:
            epk(oc - 16, tt, bank)
        else:
            epv(oc - 32, tt, bank)

    emit_proj(b, Wqkv, D, 3 * D, ep, rhs_buf=b.XNT, rhs_read=b.XNT.r(), dbg="l0aa.qkv")

    if L0AA_PARTS < 3:
        return
    rg = RG_OVERRIDE or [[0, 1], [2, 3], [4, 5], [6, 7]]
    ctx.collective(ccK, ccKo, rg, dbg="l0aa_k")
    ctx.collective(ccV, ccVo, rg, dbg="l0aa_v")
    emit_v_repack(b, ccVo, vg2, 16, D, "l0aa.vr")

    emit_kmax(b, ccKo, KR, 2048, 16, "l0aa.kmax")

    if L0AA_PARTS < 4:
        return
    emit_attn(b, kind="causal", n_heads=16, scale=1.0 / math.sqrt(128.0),
              q_parts=lambda h: [(b.QH.ap[:, h * TOK:(h + 1) * TOK], 128, b.QH.r())],
              k_parts=lambda h: [(ccKo, lambda rank: rank * KR + h * 128, lambda rank: 0, 128)],
              v_dram=vg2,
              vcol=lambda h: h * 128,
              kmax_col=lambda h: SM_KMAX + h,
              qmax_col=lambda h, j: SM_QMAX + h * 4 + j,
              o_sink=osink_to_xnt(b, "l0aa.os"), dbg="l0aa.at")

    if L0AA_PARTS < 5:
        return
    emit_proj(b, Wout, D, D, ep_residual(b, X, dbg="l0aa.res"),
              rhs_buf=b.XNT, rhs_read=b.XNT.r(), dbg="l0aa.o")


def emit_kmax(b, ccKo, KR, nrow0, n_heads, dbg):
    ctx, nc = b.ctx, b.nc
    tb = b.banks[6]
    ctx.dma("sp", lambda n=n_heads: b.XNB.ap[0:n, 0:TOK],
            lambda: ccKo.ap[nrow0:nrow0 + n_heads, :],
            reads=[ccKo.r()], writes=[b.XNB.r()], dbg=dbg + ".knl0")
    ctx.op("dve", lambda: nc.vector.reduce_max(
        b.SM.ap[0:n_heads, SM_STAGE:SM_STAGE + 1], b.XNB.ap[0:n_heads, 0:TOK], axis=AX.X),
        reads=[b.XNB.r()], writes=[b.SM.r("kms0")], dbg=dbg + ".m0")
    ctx.dma("sp", lambda n=n_heads: b.XNB.ap[0:n, 0:TOK],
            lambda: ccKo.ap[KR + nrow0:KR + nrow0 + n_heads, :],
            reads=[ccKo.r()], writes=[b.XNB.r()], dbg=dbg + ".knl1")
    ctx.op("dve", lambda: nc.vector.reduce_max(
        b.SM.ap[0:n_heads, SM_STAGE + 1:SM_STAGE + 2], b.XNB.ap[0:n_heads, 0:TOK], axis=AX.X),
        reads=[b.XNB.r()], writes=[b.SM.r("kms1")], dbg=dbg + ".m1")
    ctx.op("dve", lambda: nc.vector.tensor_tensor(
        b.SM.ap[0:n_heads, SM_STAGE + 2:SM_STAGE + 3],
        b.SM.ap[0:n_heads, SM_STAGE:SM_STAGE + 1],
        b.SM.ap[0:n_heads, SM_STAGE + 1:SM_STAGE + 2], op=ALU.max),
        reads=[b.SM.r("kms0"), b.SM.r("kms1")], writes=[b.SM.r("kms2")], dbg=dbg + ".mx")
    ctx.op("act", lambda: nc.scalar.activation(
        out=b.SM.ap[0:n_heads, SM_STAGE + 3:SM_STAGE + 4],
        in_=b.SM.ap[0:n_heads, SM_STAGE + 2:SM_STAGE + 3], func=AF.Sqrt),
        reads=[b.SM.r("kms2")], writes=[b.SM.r("kms3")], dbg=dbg + ".sq")
    ctx.op("pe", lambda: nc.tensor.transpose(
        tb.ap[0:1, 16:16 + n_heads],
        b.SM.ap[0:n_heads, SM_STAGE + 3:SM_STAGE + 4],
        b.cap("ident_f32")[0:n_heads, 0:n_heads]),
        reads=[b.SM.r("kms3"), b.const_r["ident_f32"]], writes=[tb.r()], dbg=dbg + ".T")
    ctx.op("dve", lambda: nc.vector.tensor_copy(
        b.SM.ap[0:1, SM_KMAX:SM_KMAX + n_heads], tb.ap[0:1, 16:16 + n_heads]),
        reads=[tb.r()], writes=[b.SM.r("kmax")], dbg=dbg + ".Te")



SUB_WEIGHTS = {
    "l0aa": ["L0_attn_a_w_qkv", "L0_attn_a_w_out"],
    "l0ffn1": ["L0_ffn_w_gate_up", "L0_ffn_w_proj"],
    "l0ab": ["L0_attn_b_q_kernel", "L0_attn_b_kv_kernel", "L0_attn_b_out_kernel",
             "L0_attn_b_q_norm_scale", "L0_attn_b_k_norm_scale"],
    "l0ffn2": ["L0_ffn_w_gate_up", "L0_ffn_w_proj"],
    "l1aa": ["L1_attn_a_w_dq", "L1_attn_a_g_q", "L1_attn_a_w_uq", "L1_attn_a_w_dkv",
             "L1_attn_a_w_kr", "L1_attn_a_g_kv", "L1_attn_a_w_uk", "L1_attn_a_w_uv",
             "L1_attn_a_w_o"],
    "l1ffn1": ["L1_ffn_gate_w", "L1_ffn_w_gate", "L1_ffn_w_up", "L1_ffn_w_down",
               "L1_ffn_sw_gate", "L1_ffn_sw_up", "L1_ffn_sw_down", "L1_ffn_shared_gate_w"],
    "l1ab": ["L1_attn_b_Wq", "L1_attn_b_bq", "L1_attn_b_Wk", "L1_attn_b_bk",
             "L1_attn_b_Wv", "L1_attn_b_bv", "L1_attn_b_Wo", "L1_attn_b_bo"],
    "l1ffn2": ["L1_ffn_gate_w", "L1_ffn_w_gate", "L1_ffn_w_up", "L1_ffn_w_down",
               "L1_ffn_sw_gate", "L1_ffn_sw_up", "L1_ffn_sw_down", "L1_ffn_shared_gate_w"],
}

WEIGHT_SHAPES = {
    "L0_attn_a_w_qkv": (2048, 6144), "L0_attn_a_w_out": (2048, 2048),
    "L0_attn_b_q_kernel": (16, 2048, 128), "L0_attn_b_kv_kernel": (2, 8, 2048, 128),
    "L0_attn_b_out_kernel": (16, 128, 2048), "L0_attn_b_q_norm_scale": (128,),
    "L0_attn_b_k_norm_scale": (128,),
    "L0_ffn_w_gate_up": (16384, 2048), "L0_ffn_w_proj": (2048, 8192),
    "L1_attn_a_w_dq": (2048, 768), "L1_attn_a_g_q": (768,),
    "L1_attn_a_w_uq": (768, 3072), "L1_attn_a_w_dkv": (2048, 512),
    "L1_attn_a_w_kr": (2048, 64), "L1_attn_a_g_kv": (512,),
    "L1_attn_a_w_uk": (512, 2048), "L1_attn_a_w_uv": (512, 2048),
    "L1_attn_a_w_o": (2048, 2048),
    "L1_attn_b_Wq": (2048, 2048), "L1_attn_b_bq": (2048,),
    "L1_attn_b_Wk": (2048, 2048), "L1_attn_b_bk": (2048,),
    "L1_attn_b_Wv": (2048, 2048), "L1_attn_b_bv": (2048,),
    "L1_attn_b_Wo": (2048, 2048), "L1_attn_b_bo": (2048,),
    "L1_ffn_gate_w": (8, 2048), "L1_ffn_w_gate": (8, 2048, 1408),
    "L1_ffn_w_up": (8, 2048, 1408), "L1_ffn_w_down": (8, 1408, 2048),
    "L1_ffn_sw_gate": (2048, 5632), "L1_ffn_sw_up": (2048, 5632),
    "L1_ffn_sw_down": (5632, 2048), "L1_ffn_shared_gate_w": (2048, 1),
}


def build_nc(sub_names, verbose=False):
    nc = bass.Bass("TRN2", target_bir_lowering=False, debug=False)
    es = ExitStack()
    b = Builder(nc, es)
    b.parity = None
    x_in = b.dram_buf("x_in", [TOK, D], F32, "ExternalInput")
    x_out = b.dram_buf("x_out", [TOK, D], F32, "ExternalOutput")
    X = b.dram_buf("Xres", [TOK, D], F32, "Internal")
    b.X = X

    wnames = []
    for s in sub_names:
        for w in SUB_WEIGHTS[s]:
            if w not in wnames:
                wnames.append(w)
    for w in wnames:
        b.dram_buf(w, WEIGHT_SHAPES[w], F32, "ExternalInput")

    cshapes = build_constants(0)
    b.load_consts(cshapes)

    ctx = b.ctx
    for i in range(4):
        ctx.dma("pool", lambda i=i: X.ap[i * 512:(i + 1) * 512, :],
                lambda i=i: x_in.ap[i * 512:(i + 1) * 512, :],
                reads=[x_in.r()], writes=[X.r()], dbg=f"xin{i}")

    seen = {}
    for s in sub_names:
        seen[s] = seen.get(s, 0) + 1
        SUB_EMITTERS[s](b)

    for i in range(4):
        ctx.dma("pool", lambda i=i: x_out.ap[i * 512:(i + 1) * 512, :],
                lambda i=i: X.ap[i * 512:(i + 1) * 512, :],
                reads=[X.r()], writes=[x_out.r()], dbg=f"xout{i}")

    ctx.finalize()
    if verbose:
        tot = {e: len(ctx.ops[e]) for e in ENGINES}
        nw = sum(len(o.waits) for e in ENGINES for o in ctx.ops[e])
        print(f"[build] ops={tot} waits={nw} sems={ctx.n_sems}")
    with nc.Block() as block:
        ctx.replay(block)
    es.close()
    return nc



SUB_EMITTERS = {
    "l0aa": sub_l0aa,
}


def shard_x(x):
    out = []
    for c in range(NCORES):
        s, p = c // 2, c % 2
        xc = np.ascontiguousarray(x[s].reshape(8, 512, D)[p::2].reshape(TOK, D))
        out.append(xc)
    return out


def unshard_x(parts):
    x = np.zeros((4, T_GLOBAL_, D), parts[0].dtype)
    for c in range(NCORES):
        s, p = c // 2, c % 2
        x[s].reshape(8, 512, D)[p::2] = parts[c].reshape(NQT, 512, D)
    return x


T_GLOBAL_ = 4096

_NC_CACHE = {}


def _get_nc(subs):
    key = tuple(subs)
    if key not in _NC_CACHE:
        _NC_CACHE[key] = build_nc(list(subs), verbose=False)
    return _NC_CACHE[key]


def run_subs(subs, x, weights):
    nc = _get_nc(subs)
    shards = shard_x(np.asarray(x, np.float32))
    in_maps = []
    for c in range(NCORES):
        consts = build_constants(c)
        m = {"x_in": shards[c]}
        for s in subs:
            for w in SUB_WEIGHTS[s]:
                m[w] = np.asarray(weights[w], np.float32)
        for k, v in consts.items():
            m["c_" + k] = v
        in_maps.append(m)
    res = run_bass_kernel_spmd(nc, in_maps, core_ids=list(range(NCORES)))
    return unshard_x([r["x_out"] for r in res.results])


def kernel(**inputs):
    x = inputs["L0_attn_a_x"]
    subs = ["l0aa", "l0ffn1", "l0ab", "l0ffn2", "l1aa", "l1ffn1", "l1ab", "l1ffn2"]
    return run_subs(subs, x, inputs).astype(np.float32)



def sub_l1ab(b):
    ctx, nc = b.ctx, b.nc
    X = b.X
    KR = 2064
    ccK = b.dram_buf("ccK_l1ab", [KR, TOK], BF16, "Internal")
    ccKo = b.dram_buf("ccKo_l1ab", [2 * KR, TOK], BF16, "Internal")
    ccV = b.dram_buf("ccV_l1ab", [TOK, D], BF16, "Internal")
    ccVo = b.dram_buf("ccVo_l1ab", [2 * TOK, D], BF16, "Internal")
    vg2 = b.dram_buf("vg2_l1ab", [128, 16 * 4096], BF16, "Internal")

    emit_ln(b, X, dbg="l1ab.ln")
    NB = b.banks[2]
    load_bias_cols(b, b.dram["L1_attn_b_bq"], SM_BIAS, "l1ab.bq")
    load_bias_cols(b, b.dram["L1_attn_b_bk"], SM_BIAS + 16, "l1ab.bk")
    load_bias_cols(b, b.dram["L1_attn_b_bv"], SM_BIAS + 32, "l1ab.bv")
    load_bias_cols(b, b.dram["L1_attn_b_bo"], SM_BIAS + 48, "l1ab.bo")

    emit_proj(b, b.dram["L1_attn_b_Wq"], D, D, ep_q_path(b, NB, "l1ab", sm_bias=SM_BIAS),
              rhs_buf=b.XNT, rhs_read=b.XNT.r(), transposed_w=True, dbg="l1ab.q")
    emit_proj(b, b.dram["L1_attn_b_Wk"], D, D,
              ep_k_path(b, NB, ccK, 2048, "l1ab", sm_bias=SM_BIAS + 16),
              rhs_buf=b.XNT, rhs_read=b.XNT.r(), transposed_w=True, dbg="l1ab.k")
    emit_proj(b, b.dram["L1_attn_b_Wv"], D, D,
              ep_v_path(b, ccV, "l1ab", sm_bias=SM_BIAS + 32),
              rhs_buf=b.XNT, rhs_read=b.XNT.r(), transposed_w=True, dbg="l1ab.v")

    rg = RG_OVERRIDE or [[0, 1], [2, 3], [4, 5], [6, 7]]
    ctx.collective(ccK, ccKo, rg, dbg="l1ab_k")
    ctx.collective(ccV, ccVo, rg, dbg="l1ab_v")
    emit_v_repack(b, ccVo, vg2, 16, D, "l1ab.vr")
    emit_kmax(b, ccKo, KR, 2048, 16, "l1ab.kmax")

    emit_attn(b, kind="full", n_heads=16, scale=1.0 / math.sqrt(128.0),
              q_parts=lambda h: [(b.QH.ap[:, h * TOK:(h + 1) * TOK], 128, b.QH.r())],
              k_parts=lambda h: [(ccKo, lambda rank: rank * KR + h * 128, lambda rank: 0, 128)],
              v_dram=vg2,
              vcol=lambda h: h * 128,
              kmax_col=lambda h: SM_KMAX + h,
              qmax_col=lambda h, j: SM_QMAX + h * 4 + j,
              o_sink=osink_to_xnt(b, "l1ab.os"), dbg="l1ab.at")

    emit_proj(b, b.dram["L1_attn_b_Wo"], D, D,
              ep_residual(b, X, dbg="l1ab.res", sm_bias=SM_BIAS + 48),
              rhs_buf=b.XNT, rhs_read=b.XNT.r(), transposed_w=True, dbg="l1ab.o")


SUB_EMITTERS.update({
    "l1ab": sub_l1ab,
})



def emit_quant_cache(b, W, n_row_chunks, width, cache, cache_col_of, dbg):
    ctx, nc = b.ctx, b.nc
    n_cs = (width + 2047) // 2048
    t1, t2 = b.banks[5], b.banks[6]
    for r in range(n_row_chunks):
        for cs in range(n_cs):
            w0 = cs * 2048
            wd = min(2048, width - w0)
            ng = wd // 64
            nf = wd // 128
            wb = b.QH.ap[:, 0:wd]
            wb_r = b.QH.r()
            ctx.dma("pool", lambda wb=wb: wb,
                    lambda W=W, r=r, w0=w0, wd=wd: W.ap[r * 128:(r + 1) * 128, w0:w0 + wd],
                    reads=[W.r()], writes=[wb_r], dbg=f"{dbg}.wl{r}_{cs}")
            sgn = b.TMA.ap[:, 0:wd]
            a2 = b.TMA.ap[:, 2048:2048 + wd]
            ctx.op("act", lambda wb=wb, sgn=sgn: nc.scalar.activation(
                out=sgn, in_=wb, func=AF.Sign),
                reads=[wb_r], writes=[b.TMA.r("0")], dbg=f"{dbg}.sgn{r}_{cs}")
            ctx.op("act", lambda wb=wb, a2=a2: nc.scalar.activation(
                out=a2, in_=wb, func=AF.Abs, scale=2.0),
                reads=[wb_r], writes=[b.TMA.r("1")], dbg=f"{dbg}.abs{r}_{cs}")
            ctx.op("dve", lambda a2=a2, ng=ng: nc.vector.tensor_reduce(
                b.WF.ap[:, 0:ng], a2.rearrange("p (g k) -> p g k", k=64),
                axis=AX.X, op=ALU.add),
                reads=[b.TMA.r("1")], writes=[b.WF.r()], dbg=f"{dbg}.gs{r}_{cs}")
            ctx.op("dve", lambda ng=ng: nc.vector.tensor_scalar(
                b.WF.ap[:, 64:64 + ng], b.WF.ap[:, 0:ng], 1.0 / 128.0, 1e-8,
                ALU.mult, ALU.max),
                reads=[b.WF.r()], writes=[b.WF.r()], dbg=f"{dbg}.sc{r}_{cs}")
            ctx.op("dve", lambda ng=ng: nc.vector.tensor_copy(
                b.XNB.ap[:, 0:ng], b.WF.ap[:, 64:64 + ng]),
                reads=[b.WF.r()], writes=[b.XNB.r()], dbg=f"{dbg}.sb{r}_{cs}")
            ctx.op("dve", lambda ng=ng: nc.vector.tensor_copy(
                b.WF.ap[:, 128:128 + ng], b.XNB.ap[:, 0:ng]),
                reads=[b.XNB.r()], writes=[b.WF.r()], dbg=f"{dbg}.srt{r}_{cs}")
            ctx.op("dve", lambda a2=a2, ng=ng: nc.vector.tensor_tensor(
                a2.rearrange("p (g k) -> p g k", k=64),
                a2.rearrange("p (g k) -> p g k", k=64),
                b.WF.ap[:, 128:128 + ng].rearrange("p (g one) -> p g one", one=1).to_broadcast(
                    [128, ng, 64]), op=ALU.is_gt),
                reads=[b.TMA.r("1"), b.WF.r()], writes=[b.TMA.r("1")], dbg=f"{dbg}.msk{r}_{cs}")
            ctx.op("dve", lambda sgn=sgn, a2=a2: nc.vector.tensor_mul(sgn, a2, sgn),
                   reads=[b.TMA.r("0"), b.TMA.r("1")], writes=[b.TMA.r("0")],
                   dbg=f"{dbg}.qs{r}_{cs}")
            ctx.op("dve", lambda wb=wb, sgn=sgn, ng=ng: nc.vector.tensor_tensor(
                wb.rearrange("p (g k) -> p g k", k=64),
                sgn.rearrange("p (g k) -> p g k", k=64),
                b.WF.ap[:, 128:128 + ng].rearrange("p (g one) -> p g one", one=1).to_broadcast(
                    [128, ng, 64]), op=ALU.mult),
                reads=[b.TMA.r("0"), b.WF.r()], writes=[wb_r], dbg=f"{dbg}.wq{r}_{cs}")
            for g4 in range(nf // 4):
                tb = (t1, t2)[g4 % 2]
                tbv = bf_half(tb, 0, 512)
                for q in range(4):
                    ic = g4 * 4 + q
                    ctx.op("pe", lambda tbv=tbv, wb=wb, q=q, ic=ic: nc.tensor.transpose(
                        tbv[:, q * 128:(q + 1) * 128], wb[:, ic * 128:(ic + 1) * 128],
                        b.cap("ident_bf")),
                        reads=[wb_r, b.const_r["ident_bf"]], writes=[tb.r()],
                        dbg=f"{dbg}.T{r}_{cs}_{ic}")
                ctx.op("dve", lambda tbv=tbv, g4=g4: nc.vector.tensor_copy(
                    b.QH.ap[:, 2048 + g4 * 512:2048 + (g4 + 1) * 512], tbv),
                       reads=[tb.r()], writes=[b.QH.r()], dbg=f"{dbg}.Te{r}_{cs}_{g4}")
            ctx.dma("sp", lambda r=r, cs=cs, nf=nf: cache.ap[:, cache_col_of(r) + cs * 2048:
                                                             cache_col_of(r) + cs * 2048 + nf * 128],
                    lambda nf=nf: b.QH.ap[:, 2048:2048 + nf * 128],
                    reads=[b.QH.r()], writes=[cache.r()], dbg=f"{dbg}.cs{r}_{cs}")


def sub_l0ffn(b):
    ctx, nc = b.ctx, b.nc
    X = b.X
    Wgu = b.dram["L0_ffn_w_gate_up"]
    Wp = b.dram["L0_ffn_w_proj"]
    if not hasattr(b, "l0ffn_cache"):
        cgu = b.dram_buf("cache_gu", [128, 262144], BF16, "Internal")
        cgu.granule = (16, 4)
        cp = b.dram_buf("cache_p", [128, 131072], BF16, "Internal")
        cp.granule = (64, 1)
        emit_quant_cache(b, Wgu, 128, 2048, cgu,
                         lambda r: (2 * r if r < 64 else 2 * (r - 64) + 1) * 2048, "l0f.qgu")
        emit_quant_cache(b, Wp, 16, 8192, cp, lambda r: r * 8192, "l0f.qp")
        b.l0ffn_cache = (cgu, cp)
    cgu, cp = b.l0ffn_cache

    emit_ln(b, X, dbg="l0f.ln")
    NB = b.banks[2]
    EPS = 1.1920928955078125e-07

    for tt in range(NQT):
        state = {}

        def ep_gu(oc, tt_, bank, state=state):
            hc = oc // 2
            if oc % 2 == 0:
                s = hc % 2
                hg = b.PB.ap[:, s * 512:(s + 1) * 512]
                ctx.op("act", lambda bank=bank, hg=hg: nc.scalar.activation(
                    out=hg, in_=bank.ap[:, :], func=AF.Silu),
                    reads=[bank.r()], writes=[b.PB.r(f"{s}")], dbg=f"l0f.silu{hc}_{tt_}")
                state["hg"] = (hg, b.PB.r(f"{s}"))
            else:
                hg, hg_r = state["hg"]
                s2 = 1 - (hc % 2)
                hu = b.XNB.ap[:, s2 * 512:(s2 + 1) * 512]
                ctx.op("dve", lambda bank=bank, hu=hu: nc.vector.tensor_copy(hu, bank.ap[:, :]),
                       reads=[bank.r()], writes=[b.XNB.r()], dbg=f"l0f.uev{hc}_{tt_}")
                hcol = hc * 512
                ctx.op("dve", lambda hg=hg, hu=hu, hcol=hcol: nc.vector.tensor_mul(
                    b.QH.ap[:, hcol:hcol + 512], hg, hu),
                    reads=[hg_r, b.XNB.r()], writes=[b.QH.r()], dbg=f"l0f.h{hc}_{tt_}")
                sq = b.WF.ap[:, 0:512]
                ctx.op("act", lambda hcol=hcol, sq=sq: nc.scalar.activation(
                    out=sq, in_=b.QH.ap[:, hcol:hcol + 512], func=AF.Square),
                    reads=[b.QH.r()], writes=[b.WF.r()], dbg=f"l0f.h2{hc}_{tt_}")
                sqb = b.XNB.ap[:, 1024:1536]
                ctx.op("dve", lambda sq=sq, sqb=sqb: nc.vector.tensor_copy(sqb, sq),
                       reads=[b.WF.r()], writes=[b.XNB.r()], dbg=f"l0f.h2b{hc}_{tt_}")
                ctx.op("pe", MM(NB.ap[0:1, 0:512], b.cap("ones_bf"), sqb,
                                start=(hc == 0), stop=(hc == 63)),
                       reads=[b.XNB.r(), b.const_r["ones_bf"]], writes=[NB.r()],
                       dbg=f"l0f.nmm{hc}_{tt_}")

        emit_proj(b, cgu, D, 16384, ep_gu, rhs_buf=b.XNT, rhs_read=b.XNT.r(),
                  tt_list=[tt], dbg=f"l0f.gu{tt}")

        ctx.op("dve", lambda: nc.vector.tensor_scalar(
            b.SM.ap[0:1, SM_LROW:SM_LROW + 512], NB.ap[0:1, 0:512],
            1.0 / 8192.0, EPS, ALU.mult, ALU.add),
            reads=[NB.r()], writes=[b.SM.r("lrow")], dbg=f"l0f.ms{tt}")
        ctx.op("act", lambda: nc.scalar.activation(
            out=b.SM.ap[0:1, SM_LROW:SM_LROW + 512],
            in_=b.SM.ap[0:1, SM_LROW:SM_LROW + 512], func=AF.Sqrt),
            reads=[b.SM.r("lrow")], writes=[b.SM.r("lrow")], dbg=f"l0f.sq{tt}")
        ctx.op("dve", lambda: nc.vector.reciprocal(
            b.SM.ap[0:1, SM_LROW:SM_LROW + 512], b.SM.ap[0:1, SM_LROW:SM_LROW + 512]),
            reads=[b.SM.r("lrow")], writes=[b.SM.r("lrow")], dbg=f"l0f.rc{tt}")
        TB = b.banks[6]
        for qs in range(4):
            ctx.op("pe", lambda qs=qs: nc.tensor.transpose(
                TB.ap[:, 8 + qs:9 + qs],
                b.SM.ap[0:1, SM_LROW + qs * 128:SM_LROW + (qs + 1) * 128],
                b.cap("ident_f32")[0:1, 0:1]),
                reads=[b.SM.r("lrow"), b.const_r["ident_f32"]], writes=[TB.r()],
                dbg=f"l0f.rT{tt}_{qs}")
            ctx.op("dve", lambda qs=qs: nc.vector.tensor_copy(
                b.SM.ap[:, SM_TSC + qs:SM_TSC + qs + 1], TB.ap[:, 8 + qs:9 + qs]),
                reads=[TB.r()], writes=[b.SM.r("tsc")], dbg=f"l0f.rTe{tt}_{qs}")

        emit_proj(b, cp, 8192, D,
                  ep_residual(b, X, scale_col=lambda tt_, qs: b.SM.ap[:, SM_TSC + qs:SM_TSC + qs + 1],
                              scale_reads=(b.SM.r("tsc"),), dbg=f"l0f.res{tt}"),
                  rhs_buf=b.QH, rhs_read=b.QH.r(),
                  rhs_col_fn=lambda ic, tt_: ic * 512,
                  tt_list=[tt], dbg=f"l0f.dn{tt}")


SUB_EMITTERS.update({"l0ffn1": sub_l0ffn, "l0ffn2": sub_l0ffn})



def sub_l0ab(b):
    ctx, nc = b.ctx, b.nc
    X = b.X
    QK = b.dram["L0_attn_b_q_kernel"]
    KVK = b.dram["L0_attn_b_kv_kernel"]
    OK = b.dram["L0_attn_b_out_kernel"]
    KR = 1032
    ccK = b.dram_buf("ccK_l0ab", [KR, TOK], BF16, "Internal")
    ccKo = b.dram_buf("ccKo_l0ab", [2 * KR, TOK], BF16, "Internal")
    ccV = b.dram_buf("ccV_l0ab", [TOK, 1024], BF16, "Internal")
    ccVo = b.dram_buf("ccVo_l0ab", [2 * TOK, 1024], BF16, "Internal")
    vg2 = b.dram_buf("vg2_l0ab", [128, 8 * 4096], BF16, "Internal")

    emit_ln(b, X, dbg="l0ab.ln")
    ctx.dma("sp", lambda: b.SM.ap[:, SM_BIAS:SM_BIAS + 1],
            lambda: b.dram["L0_attn_b_q_norm_scale"].ap.unsqueeze(1),
            reads=[b.dram["L0_attn_b_q_norm_scale"].r()], writes=[b.SM.r("biasq")],
            dbg="l0ab.gq")
    ctx.dma("sp", lambda: b.SM.ap[:, SM_BIAS + 1:SM_BIAS + 2],
            lambda: b.dram["L0_attn_b_k_norm_scale"].ap.unsqueeze(1),
            reads=[b.dram["L0_attn_b_k_norm_scale"].r()], writes=[b.SM.r("biask")],
            dbg="l0ab.gk")
    NB = b.banks[2]

    def broadcast_gain2(sm_col, kv_col, reg, dbgn):
        T1 = b.banks[6]
        ctx.op("pe", lambda: nc.tensor.transpose(
            T1.ap[0:1, 0:128], b.SM.ap[:, sm_col:sm_col + 1], b.cap("ident_f32")),
            reads=[b.SM.r(reg), b.const_r["ident_f32"]], writes=[T1.r()], dbg=dbgn + ".T")
        ctx.op("dve", lambda: nc.vector.tensor_copy(
            b.SM.ap[0:1, SM_LROW:SM_LROW + 128], T1.ap[0:1, 0:128]),
            reads=[T1.r()], writes=[b.SM.r("grow")], dbg=dbgn + ".e")
        ctx.op("pe", lambda: nc.tensor.matmul(
            T1.ap[:, 0:128], b.cap("ones_row"), b.SM.ap[0:1, SM_LROW:SM_LROW + 128],
            start=True, stop=True),
            reads=[b.SM.r("grow"), b.const_r["ones_row"]], writes=[T1.r()], dbg=dbgn + ".b")
        ctx.op("dve", lambda kv_col=kv_col: nc.vector.tensor_copy(
            b.KVT.ap[:, kv_col:kv_col + 128], T1.ap[:, 0:128]),
            reads=[T1.r()], writes=[b.KVT.r(f"g{kv_col}")], dbg=dbgn + ".c")

    broadcast_gain2(SM_BIAS, 0, "biasq", "l0ab.gbq")
    broadcast_gain2(SM_BIAS + 1, 128, "biask", "l0ab.gbk")

    def q_fm_sink(h, tt):
        def sink(tbv2, treg):
            ctx.op("dve", lambda tbv2=tbv2, h=h, tt=tt: nc.vector.tensor_copy(
                b.QH.ap[:, h * TOK + tt * 512:h * TOK + (tt + 1) * 512], tbv2),
                reads=[treg], writes=[b.QH.r()], dbg=f"l0ab.qf{h}_{tt}")
        return sink

    def q_norm_sink(h, tt):
        def sink(row_ap, row_r):
            col = SM_QMAX + h * 4 + tt
            ctx.op("dve", lambda row_ap=row_ap: nc.vector.reduce_max(
                b.SM.ap[0:1, SM_STAGE:SM_STAGE + 1], row_ap, axis=AX.X),
                reads=[row_r], writes=[b.SM.r("stg0")], dbg=f"l0ab.qm{h}_{tt}")
            ctx.op("act", lambda col=col: nc.scalar.activation(
                out=b.SM.ap[0:1, col:col + 1], in_=b.SM.ap[0:1, SM_STAGE:SM_STAGE + 1],
                func=AF.Sqrt),
                reads=[b.SM.r("stg0")], writes=[b.SM.r("qmax")], dbg=f"l0ab.qs{h}_{tt}")
        return sink

    def k_fm_sink(h, tt):
        def sink(tbv2, treg):
            s2 = (tt + 1) % 2
            stg2 = b.PB.ap[:, s2 * 512:(s2 + 1) * 512]
            stg2_r = b.PB.r(f"{s2}")
            ctx.op("dve", lambda tbv2=tbv2, stg2=stg2: nc.vector.tensor_copy(stg2, tbv2),
                   reads=[treg], writes=[stg2_r], dbg=f"l0ab.kf{h}_{tt}")
            ctx.dma("sp", lambda h=h, tt=tt: ccK.ap[h * 128:(h + 1) * 128,
                                                    tt * 512:(tt + 1) * 512],
                    lambda stg2=stg2: stg2,
                    reads=[stg2_r], writes=[ccK.r()], dbg=f"l0ab.kst{h}_{tt}")
        return sink

    def k_norm_sink(h, tt):
        def sink(row_ap, row_r):
            ctx.op("dve", lambda row_ap=row_ap: nc.vector.tensor_copy(
                b.XNB.ap[0:1, 1024:1536], row_ap),
                reads=[row_r], writes=[b.XNB.r()], dbg=f"l0ab.knb{h}_{tt}")
            ctx.dma("sp", lambda h=h, tt=tt: ccK.ap[1024 + h:1025 + h,
                                                    tt * 512:(tt + 1) * 512],
                    lambda: b.XNB.ap[0:1, 1024:1536],
                    reads=[b.XNB.r()], writes=[ccK.r()], dbg=f"l0ab.knst{h}_{tt}")
        return sink

    def ep_q(h, tt, bank):
        emit_rms_rope_tm2(b, bank, None, None, tt, 128, "128",
                          q_fm_sink(h, tt), norm_sink=q_norm_sink(h, tt),
                          gain_col=0, dbg=f"l0ab.q{h}_{tt}")

    def ep_k(h, tt, bank):
        emit_rms_rope_tm2(b, bank, None, None, tt, 128, "128",
                          k_fm_sink(h, tt), norm_sink=k_norm_sink(h, tt),
                          gain_col=128, dbg=f"l0ab.k{h}_{tt}")

    for h in range(16):
        emit_proj(b, Buf(QK.ap[h], f"qk{h}"), D, 128,
                  lambda oc, tt, bank, h=h: ep_q(h, tt, bank),
                  rhs_buf=b.XNT, rhs_read=b.XNT.r(), dbg=f"l0ab.qp{h}")
    for h in range(8):
        emit_proj(b, Buf(KVK.ap[0, h], f"kk{h}"), D, 128,
                  lambda oc, tt, bank, h=h: ep_k(h, tt, bank),
                  rhs_buf=b.XNT, rhs_read=b.XNT.r(), dbg=f"l0ab.kp{h}")
    for h in range(8):
        emit_proj(b, Buf(KVK.ap[1, h], f"vk{h}"), D, 128,
                  lambda oc, tt, bank, h=h: ep_v_path(b, ccV, "l0ab", vcol=lambda _: h * 128)(h, tt, bank),
                  rhs_buf=b.XNT, rhs_read=b.XNT.r(), dbg=f"l0ab.vp{h}")

    rg = RG_OVERRIDE or [[0, 1], [2, 3], [4, 5], [6, 7]]
    ctx.collective(ccK, ccKo, rg, dbg="l0ab_k")
    ctx.collective(ccV, ccVo, rg, dbg="l0ab_v")
    emit_v_repack(b, ccVo, vg2, 8, 1024, "l0ab.vr")
    emit_kmax(b, ccKo, KR, 1024, 8, "l0ab.kmax")

    emit_attn(b, kind="swa", n_heads=16, scale=0.08838834764831845,
              q_parts=lambda h: [(b.QH.ap[:, h * TOK:(h + 1) * TOK], 128, b.QH.r())],
              k_parts=lambda h: [(ccKo, lambda rank, h=h: rank * KR + (h // 2) * 128, lambda rank: 0, 128)],
              v_dram=vg2,
              vcol=lambda h: (h // 2) * 128,
              kmax_col=lambda h: SM_KMAX + h // 2,
              qmax_col=lambda h, j: SM_QMAX + h * 4 + j,
              o_sink=osink_to_xnt(b, "l0ab.os"), dbg="l0ab.at")

    OKf = Buf(OK.ap.rearrange("n h d -> (n h) d"), "okf")
    emit_proj(b, OKf, D, D, ep_residual(b, X, dbg="l0ab.res"),
              rhs_buf=b.XNT, rhs_read=b.XNT.r(), dbg="l0ab.o")


def emit_rms_rope_tm2(b, bank, scale_col_ap, scale_read, tt, rope_dim, rope_name,
                      fm_sink, norm_sink=None, rms_eps=1e-6, gain_col=None,
                      do_rms=True, dbg=""):
    ctx, nc = b.ctx, b.nc
    T1 = b.banks[6]
    half = rope_dim // 2
    s = tt % 2
    stg = b.PB.ap[:, s * 512:(s + 1) * 512]
    stg_r = b.PB.r(f"{s}")
    ctx.op("dve", lambda bank=bank, stg=stg: nc.vector.tensor_copy(stg, bank.ap[:, :]),
           reads=[bank.r()], writes=[stg_r], dbg=f"{dbg}.ev")
    tbv = bf_half(T1, 0, 512)
    for q in range(4):
        ctx.op("pe", lambda tbv=tbv, stg=stg, q=q: nc.tensor.transpose(
            tbv[:, q * 128:(q + 1) * 128], stg[:, q * 128:(q + 1) * 128],
            b.cap("ident_bf")),
            reads=[stg_r, b.const_r["ident_bf"]], writes=[T1.r()], dbg=f"{dbg}.T{q}")
    tm = b.XNB.ap[:, 0:512]
    tm_r = b.XNB.r()
    ctx.op("dve", lambda tbv=tbv, tm=tm: nc.vector.tensor_copy(tm, tbv),
           reads=[T1.r()], writes=[tm_r], dbg=f"{dbg}.Te")
    c0 = SM_STAGE + 8
    for s4 in range(4):
        tile = tm[:, s4 * 128:s4 * 128 + 128]
        if do_rms:
            ctx.op("act", lambda tile=tile: nc.scalar.activation(
                out=b.WF.ap[:, 1536:1664], in_=tile, func=AF.Square,
                accum_out=b.SM.ap[:, c0:c0 + 1]),
                reads=[tm_r], writes=[b.WF.r(), b.SM.r("rr0")], dbg=f"{dbg}.sq{s4}")
            ctx.op("dve", lambda: nc.vector.tensor_scalar(
                b.SM.ap[:, c0 + 1:c0 + 2], b.SM.ap[:, c0:c0 + 1], 1.0 / 128.0, rms_eps,
                ALU.mult, ALU.add),
                reads=[b.SM.r("rr0")], writes=[b.SM.r("rr1")], dbg=f"{dbg}.ms{s4}")
            ctx.op("act", lambda: nc.scalar.activation(
                out=b.SM.ap[:, c0 + 2:c0 + 3], in_=b.SM.ap[:, c0 + 1:c0 + 2], func=AF.Sqrt),
                reads=[b.SM.r("rr1")], writes=[b.SM.r("rr2")], dbg=f"{dbg}.sd{s4}")
            ctx.op("dve", lambda: nc.vector.reciprocal(
                b.SM.ap[:, c0 + 3:c0 + 4], b.SM.ap[:, c0 + 2:c0 + 3]),
                reads=[b.SM.r("rr2")], writes=[b.SM.r("rr3")], dbg=f"{dbg}.rs{s4}")
            ctx.op("dve", lambda tile=tile: nc.vector.tensor_scalar(
                tile, tile, b.SM.ap[:, c0 + 3:c0 + 4], None, ALU.mult),
                reads=[tm_r, b.SM.r("rr3")], writes=[tm_r], dbg=f"{dbg}.n{s4}")
        if gain_col is not None:
            ctx.op("dve", lambda tile=tile, gain_col=gain_col: nc.vector.tensor_mul(
                tile, tile, b.KVT.ap[:, gain_col:gain_col + 128]),
                reads=[tm_r, b.KVT.r(f"g{gain_col}")], writes=[tm_r], dbg=f"{dbg}.g{s4}")
        if rope_dim > 0:
            cs = b.SC.ap[:, 0:half]
            sn = b.SC.ap[:, 128:128 + half]
            ctx.dma("sp", lambda cs=cs: cs,
                    lambda tt=tt, s4=s4: b.cdram("cos" + rope_name).ap[
                        tt * 512 + s4 * 128:tt * 512 + (s4 + 1) * 128, :],
                    reads=[b.cdram("cos" + rope_name).r()], writes=[b.SC.r("c")],
                    dbg=f"{dbg}.cl{s4}")
            ctx.dma("sp", lambda sn=sn: sn,
                    lambda tt=tt, s4=s4: b.cdram("sin" + rope_name).ap[
                        tt * 512 + s4 * 128:tt * 512 + (s4 + 1) * 128, :],
                    reads=[b.cdram("sin" + rope_name).r()], writes=[b.SC.r("s")],
                    dbg=f"{dbg}.sl{s4}")
            x1 = tile[:, 0:half]
            x2 = tile[:, half:2 * half]
            sc1 = b.XNB.ap[:, 512:512 + half]
            sc2 = b.XNB.ap[:, 640:640 + half]
            sc3 = b.XNB.ap[:, 768:768 + half]
            sc4 = b.XNB.ap[:, 896:896 + half]
            scr = b.XNB.r()
            ctx.op("dve", lambda sc1=sc1, x1=x1, cs=cs: nc.vector.tensor_mul(sc1, x1, cs),
                   reads=[tm_r, b.SC.r("c")], writes=[scr], dbg=f"{dbg}.r1{s4}")
            ctx.op("dve", lambda sc2=sc2, x2=x2, sn=sn: nc.vector.tensor_mul(sc2, x2, sn),
                   reads=[tm_r, b.SC.r("s")], writes=[scr], dbg=f"{dbg}.r2{s4}")
            ctx.op("dve", lambda sc3=sc3, x2=x2, cs=cs: nc.vector.tensor_mul(sc3, x2, cs),
                   reads=[tm_r, b.SC.r("c")], writes=[scr], dbg=f"{dbg}.r3{s4}")
            ctx.op("dve", lambda sc4=sc4, x1=x1, sn=sn: nc.vector.tensor_mul(sc4, x1, sn),
                   reads=[tm_r, b.SC.r("s")], writes=[scr], dbg=f"{dbg}.r4{s4}")
            ctx.op("dve", lambda x1=x1, sc1=sc1, sc2=sc2: nc.vector.tensor_sub(x1, sc1, sc2),
                   reads=[scr], writes=[tm_r], dbg=f"{dbg}.r5{s4}")
            ctx.op("dve", lambda x2=x2, sc3=sc3, sc4=sc4: nc.vector.tensor_add(x2, sc3, sc4),
                   reads=[scr], writes=[tm_r], dbg=f"{dbg}.r6{s4}")
        if norm_sink is not None:
            ctx.op("act", lambda tile=tile: nc.scalar.activation(
                out=b.WF.ap[:, 1536:1664], in_=tile, func=AF.Square,
                accum_out=b.SM.ap[:, c0 + 4:c0 + 5]),
                reads=[tm_r], writes=[b.WF.r(), b.SM.r("rr4")], dbg=f"{dbg}.nq{s4}")
            ctx.op("pe", lambda s4=s4: nc.tensor.transpose(
                b.banks[7].ap[0:1, s4 * 128:(s4 + 1) * 128],
                b.SM.ap[:, c0 + 4:c0 + 5], b.cap("ident_f32")),
                reads=[b.SM.r("rr4"), b.const_r["ident_f32"]], writes=[b.banks[7].r()],
                dbg=f"{dbg}.nT{s4}")
    if norm_sink is not None:
        ctx.op("dve", lambda: nc.vector.tensor_copy(
            b.SM.ap[0:1, SM_LROW:SM_LROW + 512], b.banks[7].ap[0:1, 0:512]),
            reads=[b.banks[7].r()], writes=[b.SM.r("lrow")], dbg=f"{dbg}.nr")
        norm_sink(b.SM.ap[0:1, SM_LROW:SM_LROW + 512], b.SM.r("lrow"))
    T2 = b.banks[7]
    tbv2 = bf_half(T2, 0, 512)
    for q in range(4):
        ctx.op("pe", lambda tbv2=tbv2, tm=tm, q=q: nc.tensor.transpose(
            tbv2[:, q * 128:(q + 1) * 128], tm[:, q * 128:(q + 1) * 128],
            b.cap("ident_bf")),
            reads=[tm_r, b.const_r["ident_bf"]], writes=[T2.r()], dbg=f"{dbg}.U{q}")
    fm_sink(tbv2, T2.r())


SUB_EMITTERS.update({"l0ab": sub_l0ab})



def emit_rope64(b, bank, prt0, tt, fm_sink, norm_max_col=None, dbg=""):
    ctx, nc = b.ctx, b.nc
    T1, T2 = b.banks[6], b.banks[7]
    s = tt % 2
    stg = b.PB.ap[0:64, s * 512:(s + 1) * 512]
    stg_r = b.PB.r(f"{s}")
    ctx.op("dve", lambda bank=bank, stg=stg: nc.vector.tensor_copy(
        stg, bank.ap[prt0:prt0 + 64, :]),
        reads=[bank.r()], writes=[stg_r], dbg=f"{dbg}.ev")
    tbv = bf_half(T1, 0, 512)
    for q in range(4):
        ctx.op("pe", lambda tbv=tbv, stg=stg, q=q: nc.tensor.transpose(
            tbv[:, q * 64:(q + 1) * 64], stg[:, q * 128:(q + 1) * 128],
            b.cap("ident_bf")[0:64, 0:64]),
            reads=[stg_r, b.const_r["ident_bf"]], writes=[T1.r()], dbg=f"{dbg}.T{q}")
    tm = b.XNB.ap[:, 0:256]
    tm_r = b.XNB.r()
    ctx.op("dve", lambda tbv=tbv, tm=tm: nc.vector.tensor_copy(tm, tbv[:, 0:256]),
           reads=[T1.r()], writes=[tm_r], dbg=f"{dbg}.Te")
    c0 = SM_STAGE + 8
    for s4 in range(4):
        tile = tm[:, s4 * 64:s4 * 64 + 64]
        half = 32
        cs = b.SC.ap[:, 0:half]
        sn = b.SC.ap[:, 128:128 + half]
        ctx.dma("sp", lambda cs=cs: cs,
                lambda tt=tt, s4=s4: b.cdram("cos64").ap[
                    tt * 512 + s4 * 128:tt * 512 + (s4 + 1) * 128, :],
                reads=[b.cdram("cos64").r()], writes=[b.SC.r("c")], dbg=f"{dbg}.cl{s4}")
        ctx.dma("sp", lambda sn=sn: sn,
                lambda tt=tt, s4=s4: b.cdram("sin64").ap[
                    tt * 512 + s4 * 128:tt * 512 + (s4 + 1) * 128, :],
                reads=[b.cdram("sin64").r()], writes=[b.SC.r("s")], dbg=f"{dbg}.sl{s4}")
        x1 = tile[:, 0:half]
        x2 = tile[:, half:2 * half]
        sc1 = b.XNB.ap[:, 512:512 + half]
        sc2 = b.XNB.ap[:, 640:640 + half]
        sc3 = b.XNB.ap[:, 768:768 + half]
        sc4 = b.XNB.ap[:, 896:896 + half]
        scr = b.XNB.r()
        ctx.op("dve", lambda sc1=sc1, x1=x1, cs=cs: nc.vector.tensor_mul(sc1, x1, cs),
               reads=[tm_r, b.SC.r("c")], writes=[scr], dbg=f"{dbg}.r1{s4}")
        ctx.op("dve", lambda sc2=sc2, x2=x2, sn=sn: nc.vector.tensor_mul(sc2, x2, sn),
               reads=[tm_r, b.SC.r("s")], writes=[scr], dbg=f"{dbg}.r2{s4}")
        ctx.op("dve", lambda sc3=sc3, x2=x2, cs=cs: nc.vector.tensor_mul(sc3, x2, cs),
               reads=[tm_r, b.SC.r("c")], writes=[scr], dbg=f"{dbg}.r3{s4}")
        ctx.op("dve", lambda sc4=sc4, x1=x1, sn=sn: nc.vector.tensor_mul(sc4, x1, sn),
               reads=[tm_r, b.SC.r("s")], writes=[scr], dbg=f"{dbg}.r4{s4}")
        ctx.op("dve", lambda x1=x1, sc1=sc1, sc2=sc2: nc.vector.tensor_sub(x1, sc1, sc2),
               reads=[scr], writes=[tm_r], dbg=f"{dbg}.r5{s4}")
        ctx.op("dve", lambda x2=x2, sc3=sc3, sc4=sc4: nc.vector.tensor_add(x2, sc3, sc4),
               reads=[scr], writes=[tm_r], dbg=f"{dbg}.r6{s4}")
        if norm_max_col is not None:
            ctx.op("act", lambda tile=tile: nc.scalar.activation(
                out=b.WF.ap[:, 1536:1600], in_=tile, func=AF.Square,
                accum_out=b.SM.ap[:, c0 + 4:c0 + 5]),
                reads=[tm_r], writes=[b.WF.r(), b.SM.r("rr4")], dbg=f"{dbg}.nq{s4}")
            ctx.op("pe", lambda s4=s4: nc.tensor.transpose(
                T2.ap[0:1, s4 * 128:(s4 + 1) * 128],
                b.SM.ap[:, c0 + 4:c0 + 5], b.cap("ident_f32")),
                reads=[b.SM.r("rr4"), b.const_r["ident_f32"]], writes=[T2.r()],
                dbg=f"{dbg}.nT{s4}")
    if norm_max_col is not None:
        ctx.op("dve", lambda: nc.vector.reduce_max(
            b.SM.ap[0:1, norm_max_col:norm_max_col + 1], T2.ap[0:1, 0:512], axis=AX.X),
            reads=[T2.r()], writes=[b.SM.r("pemax")], dbg=f"{dbg}.nx")
    tbv2 = bf_half(T1, 0, 512)
    for q in range(4):
        ctx.op("pe", lambda tbv2=tbv2, tm=tm, q=q: nc.tensor.transpose(
            tbv2[:, q * 128:(q + 1) * 128][0:64, :], tm[:, q * 64:(q + 1) * 64],
            b.cap("ident_bf")),
            reads=[tm_r, b.const_r["ident_bf"]], writes=[T1.r()], dbg=f"{dbg}.U{q}")
    fm_sink(tbv2, T1.r())


def emit_lora_norm(b, n_chunks, d_lora, gain_sm_col, gain_reg, qh_chunk0, W, dbg):
    ctx, nc = b.ctx, b.nc
    NB = b.banks[2]
    T1, T2 = b.banks[6], b.banks[7]
    for tt in range(NQT):
        banks = {}

        def ep(oc, tt_, bank, banks=banks):
            col = (qh_chunk0 + oc) * TOK + tt_ * 512
            ctx.op("dve", lambda bank=bank, col=col, oc=oc: nc.vector.tensor_scalar(
                b.QH.ap[:, col:col + 512], bank.ap[:, :],
                b.SM.ap[:, gain_sm_col + oc:gain_sm_col + oc + 1], None, ALU.mult),
                reads=[bank.r(), b.SM.r(gain_reg)], writes=[b.QH.r()],
                dbg=f"{dbg}.ev{oc}_{tt_}")
            sq = b.XNB.ap[:, 1024:1536]
            ctx.op("act", lambda bank=bank, sq=sq: nc.scalar.activation(
                out=sq, in_=bank.ap[:, :], func=AF.Square),
                reads=[bank.r()], writes=[b.XNB.r()], dbg=f"{dbg}.sq{oc}_{tt_}")
            ctx.op("pe", MM(NB.ap[0:1, 0:512], b.cap("ones_bf"), sq,
                            start=(oc == 0), stop=(oc == n_chunks - 1)),
                   reads=[b.XNB.r(), b.const_r["ones_bf"]], writes=[NB.r()],
                   dbg=f"{dbg}.nmm{oc}_{tt_}")

        emit_proj(b, W, D, d_lora, ep, rhs_buf=b.XNT, rhs_read=b.XNT.r(),
                  tt_list=[tt], dbg=f"{dbg}.p{tt}")
        ctx.op("dve", lambda: nc.vector.tensor_scalar(
            b.SM.ap[0:1, SM_LROW:SM_LROW + 512], NB.ap[0:1, 0:512],
            1.0 / d_lora, 1e-6, ALU.mult, ALU.add),
            reads=[NB.r()], writes=[b.SM.r("lrow")], dbg=f"{dbg}.ms{tt}")
        ctx.op("act", lambda: nc.scalar.activation(
            out=b.SM.ap[0:1, SM_LROW:SM_LROW + 512],
            in_=b.SM.ap[0:1, SM_LROW:SM_LROW + 512], func=AF.Sqrt),
            reads=[b.SM.r("lrow")], writes=[b.SM.r("lrow")], dbg=f"{dbg}.sd{tt}")
        ctx.op("dve", lambda: nc.vector.reciprocal(
            b.SM.ap[0:1, SM_LROW:SM_LROW + 512], b.SM.ap[0:1, SM_LROW:SM_LROW + 512]),
            reads=[b.SM.r("lrow")], writes=[b.SM.r("lrow")], dbg=f"{dbg}.rc{tt}")
        for qs in range(4):
            ctx.op("pe", lambda qs=qs: nc.tensor.transpose(
                T1.ap[:, qs:qs + 1],
                b.SM.ap[0:1, SM_LROW + qs * 128:SM_LROW + (qs + 1) * 128],
                b.cap("ident_f32")[0:1, 0:1]),
                reads=[b.SM.r("lrow"), b.const_r["ident_f32"]], writes=[T1.r()],
                dbg=f"{dbg}.rT{tt}_{qs}")
            ctx.op("dve", lambda qs=qs: nc.vector.tensor_copy(
                b.SM.ap[:, SM_TSC + qs:SM_TSC + qs + 1], T1.ap[:, qs:qs + 1]),
                reads=[T1.r()], writes=[b.SM.r("tsc")], dbg=f"{dbg}.rTe{tt}_{qs}")
        for oc in range(n_chunks):
            col = (qh_chunk0 + oc) * TOK + tt * 512
            s = oc % 2
            tbv = bf_half(T1, 0, 512)
            for q in range(4):
                ctx.op("pe", lambda tbv=tbv, col=col, q=q: nc.tensor.transpose(
                    tbv[:, q * 128:(q + 1) * 128], b.QH.ap[:, col + q * 128:col + (q + 1) * 128],
                    b.cap("ident_bf")),
                    reads=[b.QH.r(), b.const_r["ident_bf"]], writes=[T1.r()],
                    dbg=f"{dbg}.tT{tt}_{oc}_{q}")
            tmb = b.XNB.ap[:, 0:512]
            for q in range(4):
                ctx.op("dve", lambda tbv=tbv, tmb=tmb, q=q: nc.vector.tensor_scalar(
                    tmb[:, q * 128:(q + 1) * 128], tbv[:, q * 128:(q + 1) * 128],
                    b.SM.ap[:, SM_TSC + q:SM_TSC + q + 1], None, ALU.mult),
                    reads=[T1.r(), b.SM.r("tsc")], writes=[b.XNB.r()],
                    dbg=f"{dbg}.ts{tt}_{oc}_{q}")
            tbv2 = bf_half(T2, 0, 512)
            for q in range(4):
                ctx.op("pe", lambda tbv2=tbv2, tmb=tmb, q=q: nc.tensor.transpose(
                    tbv2[:, q * 128:(q + 1) * 128], tmb[:, q * 128:(q + 1) * 128],
                    b.cap("ident_bf")),
                    reads=[b.XNB.r(), b.const_r["ident_bf"]], writes=[T2.r()],
                    dbg=f"{dbg}.uT{tt}_{oc}_{q}")
            ctx.op("dve", lambda tbv2=tbv2, col=col: nc.vector.tensor_copy(
                b.QH.ap[:, col:col + 512], tbv2),
                reads=[T2.r()], writes=[b.QH.r()], dbg=f"{dbg}.uTe{tt}_{oc}")


L1AA_STOP = False
L1AA_EXP = 7


def sub_l1aa(b):
    ctx, nc = b.ctx, b.nc
    X = b.X
    KR = 576
    ccK = b.dram_buf("ccK_l1aa", [KR, TOK], BF16, "Internal")
    ccKo = b.dram_buf("ccKo_l1aa", [2 * KR, TOK], BF16, "Internal")
    kfull = b.dram_buf("kfull_l1aa", [2048, 2 * TOK], BF16, "Internal")
    vfull = b.dram_buf("vfull_l1aa", [128, 16 * 4096], BF16, "Internal")

    emit_ln(b, X, dbg="l1aa.ln")
    load_bias_cols(b, b.dram["L1_attn_a_g_q"], SM_BIAS, "l1aa.gq")
    load_bias_cols(b, b.dram["L1_attn_a_g_kv"], SM_BIAS + 8, "l1aa.gkv")

    emit_lora_norm(b, 6, 768, SM_BIAS, f"bias{SM_BIAS}", 0,
                   b.dram["L1_attn_a_w_dq"], "l1aa.cq")
    emit_lora_norm(b, 4, 512, SM_BIAS + 8, f"bias{SM_BIAS + 8}", 6,
                   b.dram["L1_attn_a_w_dkv"], "l1aa.ckv")

    for oc in range(4):
        for tt in range(NQT):
            ctx.dma("sp", lambda oc=oc, tt=tt: ccK.ap[oc * 128:(oc + 1) * 128,
                                                      tt * 512:(tt + 1) * 512],
                    lambda oc=oc, tt=tt: b.QH.ap[:, (6 + oc) * TOK + tt * 512:
                                                 (6 + oc) * TOK + (tt + 1) * 512],
                    reads=[b.QH.r()], writes=[ccK.r()], dbg=f"l1aa.kvst{oc}_{tt}")

    def kpe_sink(tt):
        def sink(tbv2, treg):
            s2 = (tt + 1) % 2
            stg2 = b.PB.ap[0:64, s2 * 512:(s2 + 1) * 512]
            stg2_r = b.PB.r(f"{s2}")
            ctx.op("dve", lambda tbv2=tbv2, stg2=stg2: nc.vector.tensor_copy(
                stg2, tbv2[0:64, :]),
                reads=[treg], writes=[stg2_r], dbg=f"l1aa.kpf{tt}")
            ctx.dma("sp", lambda tt=tt: ccK.ap[512:576, tt * 512:(tt + 1) * 512],
                    lambda stg2=stg2: stg2,
                    reads=[stg2_r], writes=[ccK.r()], dbg=f"l1aa.kpst{tt}")
        return sink

    def ep_kpe(oc, tt, bank):
        emit_rope64(b, bank, 0, tt, kpe_sink(tt), norm_max_col=None, dbg=f"l1aa.kpe{tt}")

    emit_proj(b, b.dram["L1_attn_a_w_kr"], D, 64, ep_kpe,
              rhs_buf=b.XNT, rhs_read=b.XNT.r(), dbg="l1aa.kr")

    NB = b.banks[2]

    def ep_qnope(h, tt, bank):
        col = h * TOK + tt * 512
        ctx.op("dve", lambda bank=bank, col=col: nc.vector.tensor_copy(
            b.XNT.ap[:, col:col + 512], bank.ap[:, :]),
            reads=[bank.r()], writes=[b.XNT.r()], dbg=f"l1aa.qnev{h}_{tt}")
        sq = b.XNB.ap[:, 1024:1536]
        ctx.op("act", lambda bank=bank, sq=sq: nc.scalar.activation(
            out=sq, in_=bank.ap[:, :], func=AF.Square),
            reads=[bank.r()], writes=[b.XNB.r()], dbg=f"l1aa.qn2{h}_{tt}")
        ctx.op("pe", MM(NB.ap[0:1, 0:512], b.cap("ones_bf"), sq),
               reads=[b.XNB.r(), b.const_r["ones_bf"]], writes=[NB.r()],
               dbg=f"l1aa.qnm{h}_{tt}")
        ctx.op("dve", lambda tt=tt: nc.vector.reduce_max(
            b.SM.ap[0:1, SM_STAGE + 2 + tt:SM_STAGE + 3 + tt], NB.ap[0:1, 0:512],
            axis=AX.X),
            reads=[NB.r()], writes=[b.SM.r("qnmax")], dbg=f"l1aa.qnr{h}_{tt}")

    def cqn_col(ic, tt):
        return ic * TOK + tt * 512

    for h in range(16):
        Wn = Buf(b.dram["L1_attn_a_w_uq"].ap[:, h * 192:h * 192 + 128], f"wuqn{h}")
        emit_proj(b, Wn, 768, 128,
                  lambda oc, tt, bank, h=h: ep_qnope(h, tt, bank),
                  rhs_buf=b.QH, rhs_read=b.QH.r(), rhs_col_fn=cqn_col, dbg=f"l1aa.uqn{h}")
        def qpe_sink(h=h):
            def sink2(tt):
                def sink(tbv2, treg):
                    chunk = 6 + h // 2
                    p0 = (h % 2) * 64
                    ctx.op("dve", lambda tbv2=tbv2, tt=tt, chunk=chunk, p0=p0:
                           nc.vector.tensor_copy(
                               b.QH.ap[p0:p0 + 64, chunk * TOK + tt * 512:
                                       chunk * TOK + (tt + 1) * 512], tbv2[0:64, :]),
                           reads=[treg], writes=[b.QH.r()], dbg=f"l1aa.qpf{h}_{tt}")
                return sink
            return sink2

        def ep_qpe(oc, tt, bank, h=h):
            emit_rope64(b, bank, 0, tt, qpe_sink(h)(tt),
                        norm_max_col=SM_STAGE + 6, dbg=f"l1aa.qpe{h}_{tt}")
            col = SM_QMAX + h * 4 + tt
            ctx.op("dve", lambda tt=tt: nc.vector.tensor_add(
                b.SM.ap[0:1, SM_STAGE:SM_STAGE + 1],
                b.SM.ap[0:1, SM_STAGE + 2 + tt:SM_STAGE + 3 + tt],
                b.SM.ap[0:1, SM_STAGE + 6:SM_STAGE + 7]),
                reads=[b.SM.r("qnmax"), b.SM.r("pemax")], writes=[b.SM.r("stg0")],
                dbg=f"l1aa.qm{h}_{tt}")
            ctx.op("act", lambda col=col: nc.scalar.activation(
                out=b.SM.ap[0:1, col:col + 1], in_=b.SM.ap[0:1, SM_STAGE:SM_STAGE + 1],
                func=AF.Sqrt),
                reads=[b.SM.r("stg0")], writes=[b.SM.r("qmax")], dbg=f"l1aa.qms{h}_{tt}")

        Wp = Buf(b.dram["L1_attn_a_w_uq"].ap[:, h * 192 + 128:h * 192 + 192], f"wuqp{h}")
        emit_proj(b, Wp, 768, 64, ep_qpe,
                  rhs_buf=b.QH, rhs_read=b.QH.r(), rhs_col_fn=cqn_col, dbg=f"l1aa.uqp{h}")

    rg = RG_OVERRIDE or [[0, 1], [2, 3], [4, 5], [6, 7]]
    ctx.collective(ccK, ccKo, rg, dbg="l1aa_k")

    NB2 = b.banks[3]
    for wname, sc in (("L1_attn_a_w_uk", 0), ("L1_attn_a_w_uv", 8192)):
        ctx.dma("pool", lambda sc=sc: b.WB.ap[:, sc:sc + 8192].rearrange(
            "p (i o) -> p i o", o=2048),
            lambda wname=wname: b.dram[wname].ap.rearrange("(i p) o -> p i o", p=128),
            reads=[b.dram[wname].r()],
            writes=[b.WB.r("w0" if sc == 0 else "w2"), b.WB.r("w1" if sc == 0 else "w3")],
            dbg=f"l1aa.wld{sc}")

    tb7 = b.banks[7]
    for rank in range(2):
        for tt8 in range(4):
            rhs_r = b.XNB.r()
            rhs3 = b.XNB.ap[:, 0:2048].rearrange("p (i t) -> p i t", t=512)
            ctx.dma("sp", lambda: b.XNB.ap[:, 0:2048].rearrange(
                "p (i t) -> p i t", t=512),
                lambda rank=rank, tt8=tt8: ccKo.ap[rank * KR:rank * KR + 512,
                                                   tt8 * 512:(tt8 + 1) * 512].rearrange(
                    "(i p) t -> p i t", p=128),
                reads=[ccKo.r()], writes=[rhs_r], dbg=f"l1aa.kvg{rank}_{tt8}")
            kpe_stg = b.PB.ap[0:64, 0:512]
            ctx.dma("sp", lambda rank=rank, tt8=tt8: b.PB.ap[0:64, 0:512],
                    lambda rank=rank, tt8=tt8: ccKo.ap[rank * KR + 512:rank * KR + 576,
                                                       tt8 * 512:(tt8 + 1) * 512],
                    reads=[ccKo.r()], writes=[b.PB.r("0")], dbg=f"l1aa.kpeg{rank}_{tt8}")
            sqp = b.PB.ap[0:64, 512:1024]
            ctx.op("act", lambda kpe_stg=kpe_stg, sqp=sqp: nc.scalar.activation(
                out=sqp, in_=kpe_stg, func=AF.Square),
                reads=[b.PB.r("0")], writes=[b.PB.r("1")], dbg=f"l1aa.kpq{rank}_{tt8}")
            ctx.op("pe", MM(NB2.ap[0:1, 0:512], b.cap("ones_bf")[0:64, :], sqp),
                   reads=[b.PB.r("1"), b.const_r["ones_bf"]], writes=[NB2.r()],
                   dbg=f"l1aa.kpm{rank}_{tt8}")
            ctx.op("dve", lambda: nc.vector.reduce_max(
                b.SM.ap[0:1, SM_STAGE + 6:SM_STAGE + 7], NB2.ap[0:1, 0:512], axis=AX.X),
                reads=[NB2.r()], writes=[b.SM.r("pemax")], dbg=f"l1aa.kpr{rank}_{tt8}")
            for oc in range(16):
                bank = b.banks[oc % 2]
                for ic in range(4):
                    ctx.op("pe", MM(bank.ap[:, :],
                                    b.WB.ap[:, (ic * 16 + oc) * 128:(ic * 16 + oc + 1) * 128],
                                    rhs3[:, ic, :],
                                    start=(ic == 0), stop=(ic == 3)),
                           reads=[b.WB.r("w0"), b.WB.r("w1"), rhs_r], writes=[bank.r()],
                           dbg=f"l1aa.kfm{rank}_{tt8}_{oc}_{ic}")
                s = oc % 2
                stg = b.PB.ap[:, s * 512:(s + 1) * 512]
                stg_r = b.PB.r(f"{s}")
                ctx.op("dve", lambda bank=bank, stg=stg: nc.vector.tensor_copy(stg, bank.ap[:, :]),
                       reads=[bank.r()], writes=[stg_r], dbg=f"l1aa.kfe{rank}_{tt8}_{oc}")
                ctx.dma("sp", lambda rank=rank, tt8=tt8, oc=oc: kfull.ap[
                    oc * 128:(oc + 1) * 128,
                    rank * TOK + tt8 * 512:rank * TOK + (tt8 + 1) * 512],
                    lambda stg=stg: stg,
                    reads=[stg_r], writes=[kfull.r()], dbg=f"l1aa.kfs{rank}_{tt8}_{oc}")
                if not (L1AA_EXP & 2):
                    continue
                sq = b.PB.ap[:, (1 - s) * 512:(2 - s) * 512]
                ctx.op("act", lambda stg=stg, sq=sq: nc.scalar.activation(
                    out=sq, in_=stg, func=AF.Square),
                    reads=[stg_r], writes=[b.PB.r(f"{1 - s}")], dbg=f"l1aa.knq{rank}_{tt8}_{oc}")
                ctx.op("pe", MM(NB.ap[0:1, 0:512], b.cap("ones_bf"), sq),
                       reads=[b.PB.r(f"{1 - s}"), b.const_r["ones_bf"]], writes=[NB.r()],
                       dbg=f"l1aa.knm{rank}_{tt8}_{oc}")
                ctx.op("dve", lambda: nc.vector.reduce_max(
                    b.SM.ap[0:1, SM_STAGE + 7:SM_STAGE + 8], NB.ap[0:1, 0:512], axis=AX.X),
                    reads=[NB.r()], writes=[b.SM.r("knmax")], dbg=f"l1aa.knx{rank}_{tt8}_{oc}")
                ctx.op("dve", lambda: nc.vector.tensor_add(
                    b.SM.ap[0:1, SM_STAGE:SM_STAGE + 1],
                    b.SM.ap[0:1, SM_STAGE + 7:SM_STAGE + 8],
                    b.SM.ap[0:1, SM_STAGE + 6:SM_STAGE + 7]),
                    reads=[b.SM.r("knmax"), b.SM.r("pemax")], writes=[b.SM.r("stg0")],
                    dbg=f"l1aa.kna{rank}_{tt8}_{oc}")
                first = (rank == 0 and tt8 == 0)
                if first:
                    ctx.op("dve", lambda oc=oc: nc.vector.tensor_copy(
                        b.SM.ap[0:1, SM_KMAX + oc:SM_KMAX + oc + 1],
                        b.SM.ap[0:1, SM_STAGE:SM_STAGE + 1]),
                        reads=[b.SM.r("stg0")], writes=[b.SM.r("kmax")],
                        dbg=f"l1aa.kni{oc}")
                else:
                    ctx.op("dve", lambda oc=oc: nc.vector.tensor_tensor(
                        b.SM.ap[0:1, SM_KMAX + oc:SM_KMAX + oc + 1],
                        b.SM.ap[0:1, SM_KMAX + oc:SM_KMAX + oc + 1],
                        b.SM.ap[0:1, SM_STAGE:SM_STAGE + 1], op=ALU.max),
                        reads=[b.SM.r("stg0"), b.SM.r("kmax")], writes=[b.SM.r("kmax")],
                        dbg=f"l1aa.knu{rank}_{tt8}_{oc}")
                if not (L1AA_EXP & 4):
                    continue
                bank2 = b.banks[4 + (oc % 2)]
                for ic in range(4):
                    ctx.op("pe", MM(bank2.ap[:, :],
                                    b.WB.ap[:, 8192 + (ic * 16 + oc) * 128:
                                            8192 + (ic * 16 + oc + 1) * 128],
                                    rhs3[:, ic, :],
                                    start=(ic == 0), stop=(ic == 3)),
                           reads=[b.WB.r("w2"), b.WB.r("w3"), rhs_r], writes=[bank2.r()],
                           dbg=f"l1aa.vfm{rank}_{tt8}_{oc}_{ic}")
                s2 = 1 - s
                stg2 = b.PB.ap[:, s2 * 512:(s2 + 1) * 512]
                stg2_r = b.PB.r(f"{s2}")
                ctx.op("dve", lambda bank2=bank2, stg2=stg2: nc.vector.tensor_copy(
                    stg2, bank2.ap[:, :]),
                    reads=[bank2.r()], writes=[stg2_r], dbg=f"l1aa.vfe{rank}_{tt8}_{oc}")
                tbv = bf_half(tb7, 0, 512)
                for q in range(4):
                    ctx.op("pe", lambda tbv=tbv, stg2=stg2, q=q: nc.tensor.transpose(
                        tbv[:, q * 128:(q + 1) * 128], stg2[:, q * 128:(q + 1) * 128],
                        b.cap("ident_bf")),
                        reads=[stg2_r, b.const_r["ident_bf"]], writes=[tb7.r()],
                        dbg=f"l1aa.vfT{rank}_{tt8}_{oc}_{q}")
                vst = b.QH.ap[:, 14 * TOK:14 * TOK + 512]
                ctx.op("dve", lambda tbv=tbv, vst=vst: nc.vector.tensor_copy(vst, tbv),
                       reads=[tb7.r()], writes=[b.QH.r()], dbg=f"l1aa.vfTe{rank}_{tt8}_{oc}")
                ctx.dma("sp", lambda rank=rank, tt8=tt8, oc=oc: vfull.ap[
                    :, oc * 4096 + rank * 2048 + tt8 * 512:
                    oc * 4096 + rank * 2048 + (tt8 + 1) * 512],
                    lambda vst=vst: vst,
                    reads=[b.QH.r()], writes=[vfull.r()], dbg=f"l1aa.vfs{rank}_{tt8}_{oc}")
    if L1AA_STOP:
        return
    ctx.op("act", lambda: nc.scalar.activation(
        out=b.SM.ap[0:1, SM_KMAX:SM_KMAX + 16], in_=b.SM.ap[0:1, SM_KMAX:SM_KMAX + 16],
        func=AF.Sqrt),
        reads=[b.SM.r("kmax")], writes=[b.SM.r("kmax")], dbg="l1aa.kms")

    emit_attn(b, kind="causal", n_heads=16, scale=192.0 ** -0.5,
              q_parts=lambda h: [
                  (b.XNT.ap[:, h * TOK:(h + 1) * TOK], 128, b.XNT.r(), 0),
                  (b.QH.ap[(h % 2) * 64:(h % 2) * 64 + 64,
                           (6 + h // 2) * TOK:(7 + h // 2) * TOK], 64, b.QH.r(),
                   (h % 2) * 64)],
              k_parts=lambda h: [
                  (kfull, lambda rank, h=h: h * 128, lambda rank: rank * TOK, 128),
                  (ccKo, lambda rank: rank * KR + 512, lambda rank: 0, 64)],
              v_dram=vfull,
              vcol=lambda h: h * 128,
              kmax_col=lambda h: SM_KMAX + h,
              qmax_col=lambda h, j: SM_QMAX + h * 4 + j,
              o_sink=osink_to_xnt(b, "l1aa.os"), dbg="l1aa.at")

    emit_proj(b, b.dram["L1_attn_a_w_o"], D, D, ep_residual(b, X, dbg="l1aa.res"),
              rhs_buf=b.XNT, rhs_read=b.XNT.r(), dbg="l1aa.o")


SUB_EMITTERS.update({"l1aa": sub_l1aa})



def sub_l1ffn(b):
    ctx, nc = b.ctx, b.nc
    X = b.X
    NB = b.banks[2]
    T1 = b.banks[6]
    LG = b.banks[3]

    gw = b.dram["L1_ffn_gate_w"]
    for ic in range(16):
        ctx.dma("sp", lambda ic=ic: b.WF.ap[0:8, 0:128],
                lambda ic=ic: gw.ap[:, ic * 128:(ic + 1) * 128],
                reads=[gw.r()], writes=[b.WF.r()], dbg=f"l1f.gwl{ic}")
        ctx.op("pe", lambda ic=ic: nc.tensor.transpose(
            T1.ap[:, 0:8], b.WF.ap[0:8, 0:128], b.cap("ident_f32")[0:8, 0:8]),
            reads=[b.WF.r(), b.const_r["ident_f32"]], writes=[T1.r()],
            dbg=f"l1f.gwT{ic}")
        ctx.op("dve", lambda ic=ic: nc.vector.tensor_copy(
            b.GT.ap[:, ic * 8:(ic + 1) * 8], T1.ap[:, 0:8]),
            reads=[T1.r()], writes=[b.GT.r("w")], dbg=f"l1f.gwTe{ic}")

    def routing_math(tc):
        c0 = SM_STAGE + 8
        L = b.SM.ap[:, c0 + 8:c0 + 16]
        ctx.op("dve", lambda L=L: nc.vector.tensor_copy(L, LG.ap[:, 0:8]),
               reads=[LG.r()], writes=[b.SM.r("rlog")], dbg=f"l1f.lc{tc}")
        ctx.op("dve", lambda L=L: nc.vector.reduce_max(
            b.SM.ap[:, c0:c0 + 1], L, axis=AX.X),
            reads=[b.SM.r("rlog")], writes=[b.SM.r("rm1")], dbg=f"l1f.m1{tc}")
        ctx.op("dve", lambda L=L: nc.vector.tensor_scalar(
            b.SM.ap[:, c0 + 16:c0 + 24], L, b.SM.ap[:, c0:c0 + 1], None, ALU.subtract),
            reads=[b.SM.r("rlog"), b.SM.r("rm1")], writes=[b.SM.r("rP0")], dbg=f"l1f.ls{tc}")
        ctx.op("act", lambda: nc.scalar.activation(
            out=b.SM.ap[:, c0 + 24:c0 + 32], in_=b.SM.ap[:, c0 + 16:c0 + 24], func=AF.Exp),
            reads=[b.SM.r("rP0")], writes=[b.SM.r("rP")], dbg=f"l1f.le{tc}")
        ctx.op("dve", lambda L=L: nc.vector.tensor_scalar(
            b.SM.ap[:, c0 + 32:c0 + 40], L, b.SM.ap[:, c0:c0 + 1], None, ALU.is_ge),
            reads=[b.SM.r("rlog"), b.SM.r("rm1")], writes=[b.SM.r("rind")], dbg=f"l1f.li{tc}")
        ctx.op("dve", lambda L=L: nc.vector.scalar_tensor_tensor(
            b.SM.ap[:, c0 + 40:c0 + 48], b.SM.ap[:, c0 + 32:c0 + 40], -1e30, L,
            ALU.mult, ALU.add),
            reads=[b.SM.r("rind"), b.SM.r("rlog")], writes=[b.SM.r("rmask")],
            dbg=f"l1f.lm{tc}")
        ctx.op("dve", lambda: nc.vector.reduce_max(
            b.SM.ap[:, c0 + 1:c0 + 2], b.SM.ap[:, c0 + 40:c0 + 48], axis=AX.X),
            reads=[b.SM.r("rmask")], writes=[b.SM.r("rm2")], dbg=f"l1f.m2{tc}")
        ctx.op("dve", lambda: nc.vector.tensor_tensor(
            b.SM.ap[:, c0 + 2:c0 + 3], b.SM.ap[:, c0 + 1:c0 + 2], b.SM.ap[:, c0:c0 + 1],
            op=ALU.subtract),
            reads=[b.SM.r("rm2"), b.SM.r("rm1")], writes=[b.SM.r("rd0")], dbg=f"l1f.d0{tc}")
        ctx.op("act", lambda: nc.scalar.activation(
            out=b.SM.ap[:, c0 + 3:c0 + 4], in_=b.SM.ap[:, c0 + 2:c0 + 3], func=AF.Exp),
            reads=[b.SM.r("rd0")], writes=[b.SM.r("rd1")], dbg=f"l1f.d1{tc}")
        ctx.op("dve", lambda: nc.vector.tensor_scalar(
            b.SM.ap[:, c0 + 4:c0 + 5], b.SM.ap[:, c0 + 3:c0 + 4], 1.0, None, ALU.add),
            reads=[b.SM.r("rd1")], writes=[b.SM.r("rd2")], dbg=f"l1f.d2{tc}")
        ctx.op("dve", lambda: nc.vector.reciprocal(
            b.SM.ap[:, c0 + 5:c0 + 6], b.SM.ap[:, c0 + 4:c0 + 5]),
            reads=[b.SM.r("rd2")], writes=[b.SM.r("rd3")], dbg=f"l1f.d3{tc}")
        ctx.op("dve", lambda L=L: nc.vector.tensor_scalar(
            b.SM.ap[:, c0 + 32:c0 + 40], L, b.SM.ap[:, c0 + 1:c0 + 2], None, ALU.is_ge),
            reads=[b.SM.r("rlog"), b.SM.r("rm2")], writes=[b.SM.r("rind")], dbg=f"l1f.se{tc}")
        ctx.op("dve", lambda: nc.vector.tensor_mul(
            b.SM.ap[:, c0 + 16:c0 + 24], b.SM.ap[:, c0 + 24:c0 + 32],
            b.SM.ap[:, c0 + 32:c0 + 40]),
            reads=[b.SM.r("rP"), b.SM.r("rind")], writes=[b.SM.r("rP0")], dbg=f"l1f.w0{tc}")
        ctx.op("dve", lambda tc=tc: nc.vector.tensor_scalar(
            b.RW.ap[:, tc * 8:(tc + 1) * 8], b.SM.ap[:, c0 + 16:c0 + 24],
            b.SM.ap[:, c0 + 5:c0 + 6], None, ALU.mult),
            reads=[b.SM.r("rP0"), b.SM.r("rd3")], writes=[b.RW.r()], dbg=f"l1f.w1{tc}")

    def gate_hook(tc, fc):
        ctx.op("pe", MM(LG.ap[:, 0:8],
                        b.GT.ap[:, 128:256],
                        b.GT.ap[:, fc * 8:(fc + 1) * 8],
                        start=(fc == 0), stop=(fc == 15)),
               reads=[b.GT.r("s"), b.GT.r("w")], writes=[LG.r()],
               dbg=f"l1f.lgm{tc}_{fc}")
        if fc == 15:
            routing_math(tc)

    emit_ln(b, X, dbg="l1f.ln", gate_hook=gate_hook)

    sgw = b.dram["L1_ffn_shared_gate_w"]
    sgwf = Buf(sgw.ap.rearrange("d one -> (d one)"), "sgwf")
    sgwf._regions = sgw._regions
    load_bias_cols(b, sgwf, SM_BIAS + 32, "l1f.sgl")
    ctx.op("dve", lambda: nc.vector.tensor_copy(
        b.CB.ap[:, 256:272], b.SM.ap[:, SM_BIAS + 32:SM_BIAS + 48]),
        reads=[b.SM.r(f"bias{SM_BIAS + 32}")], writes=[b.CB.r("sgw")], dbg="l1f.sgc")
    for tt in range(NQT):
        bank = b.banks[3]
        for ic in range(NFC):
            ctx.op("pe", MM(bank.ap[0:1, 0:512],
                            b.CB.ap[:, 256 + ic:257 + ic],
                            b.XNT.ap[:, ic * TOK + tt * 512:ic * TOK + (tt + 1) * 512],
                            start=(ic == 0), stop=(ic == NFC - 1)),
                   reads=[b.CB.r("sgw"), b.XNT.r()], writes=[bank.r()],
                   dbg=f"l1f.sg{tt}_{ic}")
        ctx.op("act", lambda bank=bank: nc.scalar.activation(
            out=b.SM.ap[0:1, SM_LROW:SM_LROW + 512], in_=bank.ap[0:1, 0:512],
            func=AF.Sigmoid),
            reads=[bank.r()], writes=[b.SM.r("lrow")], dbg=f"l1f.sgs{tt}")
        for qs in range(4):
            ctx.op("pe", lambda qs=qs: nc.tensor.transpose(
                T1.ap[:, 8 + qs:9 + qs],
                b.SM.ap[0:1, SM_LROW + qs * 128:SM_LROW + (qs + 1) * 128],
                b.cap("ident_f32")[0:1, 0:1]),
                reads=[b.SM.r("lrow"), b.const_r["ident_f32"]], writes=[T1.r()],
                dbg=f"l1f.sgT{tt}_{qs}")
            ctx.op("dve", lambda tt=tt, qs=qs: nc.vector.tensor_copy(
                b.RW.ap[:, 128 + tt * 4 + qs:129 + tt * 4 + qs], T1.ap[:, 8 + qs:9 + qs]),
                reads=[T1.r()], writes=[b.RW.r()], dbg=f"l1f.sgTe{tt}_{qs}")

    WG, WU, WD = b.dram["L1_ffn_w_gate"], b.dram["L1_ffn_w_up"], b.dram["L1_ffn_w_down"]
    SWG, SWU, SWD = b.dram["L1_ffn_sw_gate"], b.dram["L1_ffn_sw_up"], b.dram["L1_ffn_sw_down"]
    for e in range(12):
        routed = e < 8
        q4 = e - 8
        if routed:
            wg_ap = WG.ap[e]
            wu_ap = WU.ap[e]
            wd_ap = WD.ap[e]
            wg_r, wu_r, wd_r = WG.r(), WU.r(), WD.r()
        else:
            wg_ap = SWG.ap[:, q4 * 1408:(q4 + 1) * 1408]
            wu_ap = SWU.ap[:, q4 * 1408:(q4 + 1) * 1408]
            wd_ap = SWD.ap[q4 * 1408:(q4 + 1) * 1408, :]
            wg_r, wu_r, wd_r = SWG.r(), SWU.r(), SWD.r()
        for ioc in range(11):
            sl = (ioc // 2) % 2
            io2 = ioc % 2
            wgr = b.WB.r(f"w{sl}")
            wur = b.WB.r(f"w{2 + sl}")
            if io2 == 0:
                nio = min(2, 11 - ioc) * 128
                ctx.dma("pool", lambda sl=sl, nio=nio: b.WB.ap[:, sl * 4096:sl * 4096 + 16 * nio].rearrange(
                    "p (i c) -> p i c", c=nio),
                    lambda wg_ap=wg_ap, ioc=ioc, nio=nio: wg_ap.rearrange("(i p) o -> p i o", p=128)[
                        :, :, ioc * 128:ioc * 128 + nio],
                    reads=[wg_r], writes=[wgr], dbg=f"l1f.wg{e}_{ioc}")
                ctx.dma("pool", lambda sl=sl, nio=nio: b.WB.ap[:, 8192 + sl * 4096:8192 + sl * 4096 + 16 * nio].rearrange(
                    "p (i c) -> p i c", c=nio),
                    lambda wu_ap=wu_ap, ioc=ioc, nio=nio: wu_ap.rearrange("(i p) o -> p i o", p=128)[
                        :, :, ioc * 128:ioc * 128 + nio],
                    reads=[wu_r], writes=[wur], dbg=f"l1f.wu{e}_{ioc}")
            nio_cur = 256 if (ioc % 2 == 0 and ioc < 10) or (ioc % 2 == 1) else 128
            nio_of_load = 256 if (ioc - io2) < 10 else 128
            for tt in range(NQT):
                par = (ioc * NQT + tt) % 2
                bg = b.banks[0 if par == 0 else 4]
                bu = b.banks[1 if par == 0 else 5]
                for ic in range(NFC):
                    col = sl * 4096 + ic * nio_of_load + io2 * 128
                    ctx.op("pe", MM(bg.ap[:, :],
                                    b.WB.ap[:, col:col + 128],
                                    rhs_tile(b.XNT, ic, tt),
                                    start=(ic == 0), stop=(ic == NFC - 1)),
                           reads=[wgr, b.XNT.r()], writes=[bg.r()],
                           dbg=f"l1f.gm{e}_{ioc}_{tt}_{ic}")
                for ic in range(NFC):
                    col = 8192 + sl * 4096 + ic * nio_of_load + io2 * 128
                    ctx.op("pe", MM(bu.ap[:, :],
                                    b.WB.ap[:, col:col + 128],
                                    rhs_tile(b.XNT, ic, tt),
                                    start=(ic == 0), stop=(ic == NFC - 1)),
                           reads=[wur, b.XNT.r()], writes=[bu.r()],
                           dbg=f"l1f.um{e}_{ioc}_{tt}_{ic}")
                s = tt % 2
                hg = b.PB.ap[:, s * 512:(s + 1) * 512]
                hg_r = b.PB.r(f"{s}")
                ctx.op("act", lambda bg=bg, hg=hg: nc.scalar.activation(
                    out=hg, in_=bg.ap[:, :], func=AF.Silu),
                    reads=[bg.r()], writes=[hg_r], dbg=f"l1f.si{e}_{ioc}_{tt}")
                s2 = 1 - s
                hu = b.PB.ap[:, s2 * 512:(s2 + 1) * 512]
                hu_r = b.PB.r(f"{s2}")
                ctx.op("dve", lambda bu=bu, hu=hu: nc.vector.tensor_copy(hu, bu.ap[:, :]),
                       reads=[bu.r()], writes=[hu_r], dbg=f"l1f.ue{e}_{ioc}_{tt}")
                hcol = ioc * TOK + tt * 512
                ctx.op("dve", lambda hg=hg, hu=hu, hcol=hcol: nc.vector.tensor_mul(
                    b.QH.ap[:, hcol:hcol + 512], hg, hu),
                    reads=[hg_r, hu_r], writes=[b.QH.r()], dbg=f"l1f.h{e}_{ioc}_{tt}")
        if routed:
            def scol(tt, qs, e=e):
                return b.RW.ap[:, (tt * 4 + qs) * 8 + e:(tt * 4 + qs) * 8 + e + 1]
        else:
            def scol(tt, qs, e=e):
                return b.RW.ap[:, 128 + tt * 4 + qs:129 + tt * 4 + qs]
        scale_rd = b.RW.r()
        WdBuf = Buf(wd_ap, f"wd{e}")
        emit_proj(b, WdBuf, 1408, D,
                  ep_residual(b, X, scale_col=scol, scale_reads=(scale_rd,),
                              dbg=f"l1f.res{e}"),
                  rhs_buf=b.QH, rhs_read=b.QH.r(),
                  rhs_col_fn=lambda ic, tt: ic * TOK + tt * 512,
                  dbg=f"l1f.dn{e}")


SUB_EMITTERS.update({"l1ffn1": sub_l1ffn, "l1ffn2": sub_l1ffn})


# version 06 (final)
# speedup vs baseline: 1.1277x; 1.1260x over previous
import math
import numpy as np
import ml_dtypes
from contextlib import ExitStack
import concourse.bass as bass
import concourse.mybir as mybir
from concourse.bass_utils import run_bass_kernel_spmd

F32 = mybir.dt.float32
BF16 = mybir.dt.bfloat16
AF = mybir.ActivationFunctionType
ALU = mybir.AluOpType
AX = mybir.AxisListType

NCORES = 8
D = 2048
TOK = 2048
NTC = TOK // 128
NFC = D // 128
QT = 512
NQT = TOK // QT
NEG = -1e10

ENGINES = ("pe", "dve", "act", "pool", "sp")
RG_OVERRIDE = None



class Region:
    __slots__ = ("name", "last_w", "readers", "dma_sem", "is_psum")

    def __init__(self, name, is_psum=False):
        self.name = name
        self.last_w = None
        self.readers = []
        self.dma_sem = None
        self.is_psum = is_psum


class Op:
    __slots__ = ("eng", "fn", "idx", "waits", "inc_sem", "inc_amt", "inc_val",
                 "is_dma", "region_sem", "dbg")

    def __init__(self, eng, fn, idx, is_dma, dbg=""):
        self.eng = eng
        self.fn = fn
        self.idx = idx
        self.waits = []
        self.inc_sem = None
        self.inc_amt = 0
        self.inc_val = None
        self.is_dma = is_dma
        self.region_sem = None
        self.dbg = dbg


class _DmaSem:
    __slots__ = ("sem",)

    def __init__(self, sem):
        self.sem = sem


class Ctx:
    def __init__(self, nc, es):
        self.nc = nc
        self.es = es
        self.ops = {e: [] for e in ENGINES}
        self.eng_sem = {e: self.sem("prog_" + e) for e in ENGINES}
        self.covered = {}
        self.n_sems = 5

    def sem(self, name):
        self.n_sems = getattr(self, "n_sems", 0) + 1
        return self.es.enter_context(self.nc.semaphore(name))

    def _dep(self, op, producer):
        if producer is None or producer is op:
            return
        if producer.eng == op.eng and op.eng == "pe" and not producer.is_dma:
            return
        key = (op.eng, producer.eng, producer.is_dma,
               id(producer.region_sem) if producer.is_dma else 0)
        cov = self.covered.get(key, -1)
        if producer.idx <= cov:
            return
        self.covered[key] = producer.idx
        if producer.inc_amt == 0:
            if producer.is_dma:
                producer.inc_amt = 16
            else:
                producer.inc_amt = 1
                producer.inc_sem = self.eng_sem[producer.eng]
        op.waits.append(producer)

    @staticmethod
    def _newest_by_engine(ops):
        best = {}
        for rd in ops:
            k = (rd.eng, rd.is_dma, id(rd.region_sem) if rd.is_dma else 0)
            cur = best.get(k)
            if cur is None or rd.idx > cur.idx:
                best[k] = rd
        return best.values()

    def _track(self, o, reads, writes):
        for r in reads:
            self._dep(o, r.last_w)
            if r.is_psum:
                for rd in self._newest_by_engine(r.readers):
                    if rd.eng != o.eng:
                        self._dep(o, rd)
        for w in writes:
            self._dep(o, w.last_w)
            for rd in self._newest_by_engine(w.readers):
                self._dep(o, rd)
        for w in writes:
            w.readers = []
            w.last_w = o
        for r in reads:
            r.readers.append(o)

    def op(self, eng, fn, reads=(), writes=(), dbg=""):
        o = Op(eng, fn, len(self.ops[eng]), False, dbg)
        self._track(o, reads, writes)
        self.ops[eng].append(o)
        return o

    def dma(self, queue, out_ap_fn, in_ap_fn, reads=(), writes=(), dbg="", **kw):
        assert writes, "DMA needs a written region"
        w0 = writes[0]
        if w0.dma_sem is None:
            w0.dma_sem = {}
        if queue not in w0.dma_sem:
            w0.dma_sem[queue] = _DmaSem(self.sem(f"d{queue[0]}_" + w0.name[:23]))

        qeng = {"sp": self.nc.sync, "pool": self.nc.gpsimd, "act": self.nc.scalar}[queue]

        def fn(_out=out_ap_fn, _in=in_ap_fn, _kw=kw, _q=qeng):
            return _q.dma_start(out=_out(), in_=_in(), **_kw)

        o = Op(queue, fn, len(self.ops[queue]), True, dbg)
        o.region_sem = w0.dma_sem[queue]
        o.inc_sem = w0.dma_sem[queue].sem
        o.inc_amt = 16
        self._track(o, reads, writes)
        self.ops[queue].append(o)
        return o

    def collective(self, in_buf, out_buf, replica_groups, dbg="cc"):
        csem = self.sem("cc_" + dbg[:20])

        def fn(_i=in_buf, _o=out_buf, _rg=replica_groups, _g=self.nc.gpsimd):
            return _g.collective_compute(
                "AllGather", ALU.bypass, replica_groups=_rg,
                ins=[_i.ap.opt()], outs=[_o.ap.opt()])

        o = Op("pool", fn, len(self.ops["pool"]), False, dbg)
        o.inc_sem = csem
        o.inc_amt = 1
        self._track(o, [in_buf.r()], [out_buf.r()])
        self.ops["pool"].append(o)
        return o

    def finalize(self):
        run = {}
        for e in ENGINES:
            for o in self.ops[e]:
                if o.inc_amt:
                    run[o.inc_sem] = run.get(o.inc_sem, 0) + o.inc_amt
                    o.inc_val = run[o.inc_sem]

    def replay(self, block):
        def body(eng_name):
            def _f(eng):
                for o in self.ops[eng_name]:
                    for p in o.waits:
                        assert p.inc_val is not None, f"unresolved dep {p.dbg}"
                    for p in o.waits:
                        eng.wait_ge(p.inc_sem, p.inc_val)
                    try:
                        ins = o.fn()
                    except Exception as exc:
                        raise RuntimeError(f"emit failed at [{eng_name}] {o.dbg}: {exc}") from exc
                    if o.inc_amt and ins is not None:
                        ins.then_inc(o.inc_sem, o.inc_amt)
            return _f
        block.tensor(body("pe"))
        block.vector(body("dve"))
        block.scalar(body("act"))
        block.gpsimd(body("pool"))
        block.sync(body("sp"))


class Buf:
    def __init__(self, ap, name):
        self.ap_ = ap
        self.name = name
        self._regions = {}

    def r(self, key="*"):
        if key not in self._regions:
            self._regions[key] = Region(f"{self.name}.{key}", getattr(self, "psum", False))
        return self._regions[key]

    @property
    def ap(self):
        return self.ap_



def core_token_blocks(parity):
    return [2 * j + parity for j in range(NQT)]


def gathered_key_global_pos():
    pos = np.zeros(2 * TOK, np.int64)
    for j in range(NQT):
        pos[j * 512:(j + 1) * 512] = np.arange(512) + (2 * j) * 512
        pos[TOK + j * 512:TOK + (j + 1) * 512] = np.arange(512) + (2 * j + 1) * 512
    return pos


def attn_plan(kind):
    plan = []
    for j in range(NQT):
        if kind == "causal":
            cs = list(range(0, 4 * (j + 1))) + [16 + c for c in range(0, 4 * (j + 1))]
        elif kind == "swa":
            tiles = sorted(set(t for t in (j - 1, j) if t >= 0))
            cs = []
            for t in tiles:
                cs += list(range(4 * t, 4 * t + 4))
            for t in tiles:
                cs += [16 + c for c in range(4 * t, 4 * t + 4)]
        else:
            cs = list(range(32))
        plan.append(cs)
    return plan


def mask_tables(kind, parity):
    kpos = gathered_key_global_pos()
    plan = attn_plan(kind)
    cls = {}
    p_masks = {}
    row_masks = {}
    for j in range(NQT):
        for s, kc in enumerate(plan[j]):
            kk = kpos[kc * 128:(kc + 1) * 128]
            union_tt = False
            union_row = False
            my_valid = None
            for par in (0, 1):
                qq = np.arange(512) + (2 * j + par) * 512
                if kind == "full":
                    valid = np.ones((128, 512), bool)
                else:
                    valid = kk[:, None] <= qq[None, :]
                    if kind == "swa":
                        valid &= (qq[None, :] - kk[:, None]) < 1024
                row_const = np.all(valid == valid[:, :1], axis=1).all()
                if not row_const:
                    union_tt = True
                elif not valid.all():
                    union_row = True
                if par == parity:
                    my_valid = valid
            if union_tt:
                cls[(j, s)] = "tt"
                p_masks[(j, s)] = my_valid.astype(np.float32)
            elif union_row:
                cls[(j, s)] = "row"
                row_masks[(j, s)] = np.where(my_valid[:, 0], 0.0, NEG).astype(np.float32)
            else:
                cls[(j, s)] = None
    return plan, cls, p_masks, row_masks


def build_constants(core_id):
    parity = core_id % 2
    c = {}
    c["ident_bf"] = np.eye(128).astype(ml_dtypes.bfloat16)
    c["ident_f32"] = np.eye(128).astype(np.float32)
    c["ones_bf"] = np.ones((128, 1), ml_dtypes.bfloat16)
    c["ones_f32"] = np.ones((128, 1), np.float32)
    c["ones_row"] = np.ones((1, 128), np.float32)

    blocks = core_token_blocks(parity)
    pos = np.concatenate([np.arange(512) + b * 512 for b in blocks]).astype(np.float64)
    ts = 10000.0 ** (np.arange(0, 128, 2).astype(np.float64) / 128.0)
    sin128 = pos[:, None] / ts[None, :]
    c["cos128"] = np.cos(sin128).astype(ml_dtypes.bfloat16)
    c["sin128"] = np.sin(sin128).astype(ml_dtypes.bfloat16)
    inv = 1.0 / (10000.0 ** (np.arange(0, 64, 2).astype(np.float64) / 64.0))
    fr = pos[:, None] * inv[None, :]
    c["cos64"] = np.cos(fr).astype(ml_dtypes.bfloat16)
    c["sin64"] = np.sin(fr).astype(ml_dtypes.bfloat16)

    for kind in ("causal", "swa", "full"):
        plan, cls, pm, rm = mask_tables(kind, parity)
        tts, rows = [], []
        for j in range(NQT):
            for s in range(len(plan[j])):
                if cls[(j, s)] == "tt":
                    tts.append(pm[(j, s)])
                elif cls[(j, s)] == "row":
                    rows.append(rm[(j, s)])
        c[f"pmask_{kind}"] = (np.stack(tts).astype(ml_dtypes.bfloat16) if tts
                              else np.zeros((1, 128, 512), ml_dtypes.bfloat16))
        c[f"rmask_{kind}"] = (np.stack(rows).astype(np.float32)[:, :, None] if rows
                              else np.zeros((1, 128, 1), np.float32))
    return c



class Builder:
    def __init__(self, nc, es):
        self.nc = nc
        self.es = es
        self.ctx = Ctx(nc, es)
        self.dram = {}
        _NC_FOR_MM.clear()
        _NC_FOR_MM.append(nc)

        def sb(name, cols, dtype):
            h = es.enter_context(nc.sbuf_tensor(name, [128, cols], dtype))
            return Buf(h, name)

        self.XNT = sb("XNT", NFC * TOK, BF16)
        self.QH = sb("QH", 32768, BF16)
        self.WB = sb("WB", 16384, BF16)
        self.WF = sb("WF", 2048, F32)
        self.TMA = sb("TMA", 4096, F32)
        self.XNB = sb("XNB", 2048, BF16)
        self.KVT = sb("KVT", 3072, BF16)
        self.PB = sb("PB", 2048, BF16)
        self.RS = sb("RS", 512, F32)
        self.SM = sb("SM", 1024, F32)
        self.CC = sb("CC", 288, F32)
        self.CB = sb("CB", 288, BF16)
        self.SC = sb("SC", 256, BF16)
        self.RW = sb("RW", 160, F32)
        self.GT = sb("GT", 256, F32)

        self.banks = []
        for i in range(8):
            bb = Buf(es.enter_context(nc.psum_tensor(f"bank{i}", [128, 512], F32)), f"bank{i}")
            bb.psum = True
            self.banks.append(bb)
        self.const_ap = {}
        self.const_r = {}

    def dram_buf(self, name, shape, dtype, kind):
        h = self.nc.dram_tensor(name, list(shape), dtype, kind=kind)
        b = Buf(h.ap(), name)
        self.dram[name] = b
        return b

    def load_consts(self, cdict):
        ctx = self.ctx
        layout = {"ident_f32": (self.CC, 0, 128, F32),
                  "ones_f32": (self.CC, 128, 1, F32),
                  "ones_row": (self.CC, 129, 128, F32),
                  "ident_bf": (self.CB, 0, 128, BF16),
                  "ones_bf": (self.CB, 128, 1, BF16)}
        for name, arr in cdict.items():
            dt = BF16 if arr.dtype == ml_dtypes.bfloat16 else F32
            d = self.dram_buf("c_" + name, arr.shape, dt, "ExternalInput")
            if name in layout:
                buf, col, w, _ = layout[name]
                ap = buf.ap[:arr.shape[0], col:col + w]
                self.const_ap[name] = ap
                self.const_r[name] = buf.r(name)
                ctx.dma("sp", lambda ap=ap: ap, lambda d=d: d.ap[:, :],
                        reads=[d.r()], writes=[buf.r(name)], dbg="const_" + name)
            else:
                self.const_ap[name] = None

    def cap(self, name):
        return self.const_ap[name]

    def cdram(self, name):
        return self.dram["c_" + name]


_NC_FOR_MM = []


def MM(out_ap, lhsT_ap, rhs_ap, start=True, stop=True):
    return lambda: _NC_FOR_MM[0].tensor.matmul(out_ap, lhsT_ap, rhs_ap, start=start, stop=stop)


def bf_half(bank, n0=0, n=128):
    return bank.ap[:, n0 // 2:(n0 + n) // 2].bitcast(BF16)



def emit_ln(b, X, dbg="ln", gate_hook=None):
    ctx, nc = b.ctx, b.nc
    XNT, TMA, XNB, SM, WF = b.XNT, b.TMA, b.XNB, b.SM, b.WF
    tb = b.banks[6]
    for tc in range(NTC):
        s = tc % 2
        xa = TMA.ap[:, s * 2048:(s + 1) * 2048]
        xr = TMA.r(f"h{s}")
        c0 = s * 8
        ctx.dma("sp", lambda xa=xa: xa,
                lambda tc=tc: X.ap[tc * 128:(tc + 1) * 128, :],
                reads=[X.r()], writes=[xr], dbg=f"{dbg}.load{tc}")
        ctx.op("dve", lambda xa=xa, c0=c0: nc.vector.reduce_sum(
            SM.ap[:, c0:c0 + 1], xa, axis=AX.X),
            reads=[xr], writes=[SM.r(f"s{s}")], dbg=f"{dbg}.sum{tc}")
        ctx.op("act", lambda xa=xa, c0=c0: nc.scalar.activation(
            out=WF.ap[:, 0:2048], in_=xa, func=AF.Square,
            accum_out=SM.ap[:, c0 + 1:c0 + 2]),
            reads=[xr], writes=[WF.r(), SM.r(f"q{s}")], dbg=f"{dbg}.sq{tc}")
        ctx.op("dve", lambda c0=c0: nc.vector.tensor_scalar(
            SM.ap[:, c0:c0 + 1], SM.ap[:, c0:c0 + 1], 1.0 / D, None, ALU.mult),
            reads=[SM.r(f"s{s}")], writes=[SM.r(f"s{s}")], dbg=f"{dbg}.mean{tc}")
        ctx.op("dve", lambda c0=c0: nc.vector.tensor_scalar(
            SM.ap[:, c0 + 1:c0 + 2], SM.ap[:, c0 + 1:c0 + 2], 1.0 / D, None, ALU.mult),
            reads=[SM.r(f"q{s}")], writes=[SM.r(f"q{s}")], dbg=f"{dbg}.msq{tc}")
        ctx.op("dve", lambda c0=c0: nc.vector.tensor_tensor(
            SM.ap[:, c0 + 2:c0 + 3], SM.ap[:, c0:c0 + 1], SM.ap[:, c0:c0 + 1], op=ALU.mult),
            reads=[SM.r(f"s{s}")], writes=[SM.r(f"m2{s}")], dbg=f"{dbg}.m2{tc}")
        ctx.op("dve", lambda c0=c0: nc.vector.tensor_tensor(
            SM.ap[:, c0 + 3:c0 + 4], SM.ap[:, c0 + 1:c0 + 2], SM.ap[:, c0 + 2:c0 + 3],
            op=ALU.subtract),
            reads=[SM.r(f"q{s}"), SM.r(f"m2{s}")], writes=[SM.r(f"v{s}")], dbg=f"{dbg}.var{tc}")
        ctx.op("dve", lambda c0=c0: nc.vector.tensor_scalar(
            SM.ap[:, c0 + 3:c0 + 4], SM.ap[:, c0 + 3:c0 + 4], 1e-5, None, ALU.add),
            reads=[SM.r(f"v{s}")], writes=[SM.r(f"v{s}")], dbg=f"{dbg}.veps{tc}")
        ctx.op("act", lambda c0=c0: nc.scalar.activation(
            out=SM.ap[:, c0 + 4:c0 + 5], in_=SM.ap[:, c0 + 3:c0 + 4], func=AF.Sqrt),
            reads=[SM.r(f"v{s}")], writes=[SM.r(f"sd{s}")], dbg=f"{dbg}.sd{tc}")
        ctx.op("dve", lambda c0=c0: nc.vector.reciprocal(
            SM.ap[:, c0 + 5:c0 + 6], SM.ap[:, c0 + 4:c0 + 5]),
            reads=[SM.r(f"sd{s}")], writes=[SM.r(f"rs{s}")], dbg=f"{dbg}.rs{tc}")
        if gate_hook is None:
            ctx.op("dve", lambda xa=xa, c0=c0: nc.vector.tensor_scalar(
                XNB.ap[:, :], xa, SM.ap[:, c0:c0 + 1], SM.ap[:, c0 + 5:c0 + 6],
                ALU.subtract, ALU.mult),
                reads=[xr, SM.r(f"s{s}"), SM.r(f"rs{s}")], writes=[XNB.r()], dbg=f"{dbg}.n{tc}")
        else:
            ctx.op("dve", lambda xa=xa, c0=c0: nc.vector.tensor_scalar(
                WF.ap[:, 0:2048], xa, SM.ap[:, c0:c0 + 1], SM.ap[:, c0 + 5:c0 + 6],
                ALU.subtract, ALU.mult),
                reads=[xr, SM.r(f"s{s}"), SM.r(f"rs{s}")], writes=[WF.r()], dbg=f"{dbg}.nf{tc}")
            ctx.op("dve", lambda: nc.vector.tensor_copy(XNB.ap[:, :], WF.ap[:, 0:2048]),
                   reads=[WF.r()], writes=[XNB.r()], dbg=f"{dbg}.nc{tc}")
        for g in range(4):
            tbv = bf_half(tb, 0, 512)
            for q in range(4):
                fc = g * 4 + q
                ctx.op("pe", lambda tbv=tbv, q=q, fc=fc: nc.tensor.transpose(
                    tbv[:, q * 128:(q + 1) * 128], XNB.ap[:, fc * 128:(fc + 1) * 128],
                    b.cap("ident_bf")),
                    reads=[XNB.r(), b.const_r["ident_bf"]], writes=[tb.r()],
                    dbg=f"{dbg}.T{tc}_{fc}")
            ctx.op("dve", lambda tc=tc, g=g, tbv=tbv: nc.vector.tensor_copy(
                XNT.ap.rearrange("p (f t) -> p f t", t=TOK)[:, g * 4:(g + 1) * 4,
                                                            tc * 128:(tc + 1) * 128],
                tbv.rearrange("p (f t) -> p f t", t=128)),
                reads=[tb.r()], writes=[XNT.r()], dbg=f"{dbg}.Tev{tc}_{g}")
        if gate_hook is not None:
            t5 = b.banks[5]
            for fc in range(16):
                ctx.op("pe", lambda fc=fc: nc.tensor.transpose(
                    t5.ap[:, 0:128], WF.ap[:, fc * 128:(fc + 1) * 128],
                    b.cap("ident_f32")),
                    reads=[WF.r(), b.const_r["ident_f32"]], writes=[t5.r()],
                    dbg=f"{dbg}.gT{tc}_{fc}")
                ctx.op("dve", lambda: nc.vector.tensor_copy(
                    b.GT.ap[:, 128:256], t5.ap[:, 0:128]),
                    reads=[t5.r()], writes=[b.GT.r("s")], dbg=f"{dbg}.gTe{tc}_{fc}")
                gate_hook(tc, fc)



def rhs_tile(rhs_buf, ic, tt, w=QT):
    return rhs_buf.ap[:, ic * TOK + tt * w: ic * TOK + tt * w + w]


def emit_proj(b, W, d_in, d_out, epilogue, rhs_buf=None, rhs_read=None,
              transposed_w=False, tok_tiles=NQT, dbg="proj",
              oc_list=None, tt_list=None, rhs_col_fn=None):
    ctx, nc = b.ctx, b.nc
    n_ic = d_in // 128
    tw = min(128, d_out)
    n_oc = (d_out + 127) // 128
    oc_list = oc_list if oc_list is not None else list(range(n_oc))
    tt_list = tt_list if tt_list is not None else list(range(tok_tiles))
    if rhs_col_fn is None:
        rhs_col_fn = lambda ic, tt: ic * TOK + tt * QT
    cols_per_oc = n_ic * tw
    grp = max(1, min(8192 // cols_per_oc, len(oc_list)))
    mm_banks = (b.banks[0], b.banks[1], b.banks[4], b.banks[5])
    slot_cols = (0, 8192)
    groups = [oc_list[i:i + grp] for i in range(0, len(oc_list), grp)]
    w_bf = W.ap.dtype == BF16
    granule = getattr(W, "granule", None)
    if granule:
        assert granule == (n_ic, grp), (granule, n_ic, grp)
    for gi, g in enumerate(groups):
        sc = slot_cols[gi % 2]
        wr = b.WB.r(f"w{2 * (gi % 2)}")
        wr2 = b.WB.r(f"w{2 * (gi % 2) + 1}")
        if granule:
            ctx.dma("sp", lambda sc=sc, n=len(g), n_ic=n_ic: b.WB.ap[:, sc:sc + n_ic * n * 128],
                    lambda W=W, g=g, n=len(g), n_ic=n_ic: W.ap[
                        :, g[0] * n_ic * 128:(g[0] + n) * n_ic * 128],
                    reads=[W.r()], writes=[wr, wr2], dbg=f"{dbg}.w{gi}")
        elif not transposed_w:
            ctx.dma("sp" if w_bf else "pool",
                lambda sc=sc, n=len(g), n_ic=n_ic: b.WB.ap[:, sc:sc + n_ic * n * tw].rearrange(
                "p (i c) -> p i c", c=n * tw),
                lambda W=W, g=g, n=len(g): W.ap.rearrange("(i p) o -> p i o", p=128)[:, :, g[0] * tw:g[0] * tw + n * tw],
                reads=[W.r()], writes=[wr, wr2], dbg=f"{dbg}.w{gi}")
        else:
            assert d_in == 2048
            tb = b.banks[5]
            for oi, oc in enumerate(g):
                ldr = b.WF.r()
                ctx.dma("sp", lambda: b.WF.ap[:, :],
                        lambda W=W, oc=oc: W.ap[oc * 128:(oc + 1) * 128, :],
                        reads=[W.r()], writes=[ldr], dbg=f"{dbg}.wl{oc}")
                for g4 in range(n_ic // 4):
                    for q in range(4):
                        ic = g4 * 4 + q
                        ctx.op("pe", lambda q=q, ic=ic: nc.tensor.transpose(
                            tb.ap[:, q * 128:(q + 1) * 128],
                            b.WF.ap[:, ic * 128:(ic + 1) * 128],
                            b.cap("ident_f32")),
                            reads=[ldr, b.const_r["ident_f32"]], writes=[tb.r()],
                            dbg=f"{dbg}.wT{oc}_{ic}")
                    ctx.op("dve", lambda sc=sc, oi=oi, g4=g4, n=len(g): nc.vector.tensor_copy(
                        b.WB.ap[:, sc:sc + n_ic * n * 128].rearrange(
                            "p (i c) -> p i c", c=n * 128)[:, g4 * 4:(g4 + 1) * 4,
                                                           oi * 128:(oi + 1) * 128],
                        tb.ap.rearrange("p (i c) -> p i c", c=128)[:, 0:4, :]),
                        reads=[tb.r()], writes=[wr, wr2], dbg=f"{dbg}.wTe{oc}_{g4}")
        for oi, oc in enumerate(g):
            tt_banks = [mm_banks[(gi * len(g) * len(tt_list) + oi * len(tt_list) + ti) % 4]
                        for ti in range(len(tt_list))]
            for ic in range(n_ic):
                if granule:
                    col = sc + oi * n_ic * 128 + ic * 128
                else:
                    col = sc + ic * len(g) * tw + oi * tw
                for ti, tt in enumerate(tt_list):
                    bank = tt_banks[ti]
                    ctx.op("pe", MM(bank.ap[0:tw, :],
                                    b.WB.ap[:, col:col + tw],
                                    rhs_buf.ap[:, rhs_col_fn(ic, tt):rhs_col_fn(ic, tt) + QT],
                                    start=(ic == 0), stop=(ic == n_ic - 1)),
                           reads=[wr, wr2, rhs_read], writes=[bank.r()],
                           dbg=f"{dbg}.mm{oc}_{tt}_{ic}")
            for ti, tt in enumerate(tt_list):
                epilogue(oc, tt, tt_banks[ti])



def ep_to_sbuf_bf16(b, dst_buf, dst_region, col_fn, bias_ap=None, bias_reads=(), dbg="ep"):
    ctx, nc = b.ctx, b.nc

    def ep(oc, tt, bank):
        col = col_fn(oc, tt)
        if bias_ap is None:
            ctx.op("dve", lambda col=col, bank=bank: nc.vector.tensor_copy(
                dst_buf.ap[:, col:col + QT], bank.ap[:, :]),
                reads=[bank.r()], writes=[dst_region], dbg=f"{dbg}.ev{oc}_{tt}")
        else:
            ctx.op("dve", lambda col=col, bank=bank: nc.vector.tensor_scalar(
                dst_buf.ap[:, col:col + QT], bank.ap[:, :], bias_ap(oc), None, ALU.add),
                reads=[bank.r(), *bias_reads], writes=[dst_region], dbg=f"{dbg}.ev{oc}_{tt}")
    return ep


def ep_residual(b, X, scale_col=None, scale_reads=(), dbg="res", sm_bias=None):
    ctx, nc = b.ctx, b.nc
    tb = b.banks[7]

    def ep(oc, tt, bank):
        s = (oc + tt) % 2
        stg = b.PB.ap[:, s * 512:(s + 1) * 512]
        stg_r = b.PB.r(f"{s}")
        if sm_bias is None:
            ctx.op("dve", lambda bank=bank, stg=stg: nc.vector.tensor_copy(stg, bank.ap[:, :]),
                   reads=[bank.r()], writes=[stg_r], dbg=f"{dbg}.ev{oc}_{tt}")
        else:
            ctx.op("dve", lambda bank=bank, stg=stg, oc=oc: nc.vector.tensor_scalar(
                stg, bank.ap[:, :], bias_ap(b, sm_bias, oc), None, ALU.add),
                reads=[bank.r(), b.SM.r(f"bias{sm_bias}")], writes=[stg_r],
                dbg=f"{dbg}.ev{oc}_{tt}")
        if RES_LVL < 2:
            return
        tbv = bf_half(tb, 0, 512)
        for q in range(4):
            ctx.op("pe", lambda tbv=tbv, stg=stg, q=q: nc.tensor.transpose(
                tbv[:, q * 128:(q + 1) * 128], stg[:, q * 128:(q + 1) * 128],
                b.cap("ident_bf")),
                reads=[stg_r, b.const_r["ident_bf"]], writes=[tb.r()],
                dbg=f"{dbg}.T{oc}_{tt}_{q}")
        if RES_LVL < 3:
            return
        rs = b.RS.ap[:, 0:512]
        rs_r = b.RS.r()
        if scale_col is None:
            ctx.op("dve", lambda tbv=tbv, rs=rs: nc.vector.tensor_copy(
                rs.rearrange("p (q v) -> p q v", v=128), tbv.rearrange("p (q v) -> p q v", v=128)),
                reads=[tb.r()], writes=[rs_r], dbg=f"{dbg}.Te{oc}_{tt}")
        else:
            for q in range(4):
                ctx.op("dve", lambda tbv=tbv, rs=rs, q=q, tt=tt: nc.vector.tensor_scalar(
                    rs[:, q * 128:(q + 1) * 128], tbv[:, q * 128:(q + 1) * 128],
                    scale_col(tt, q), None, ALU.mult),
                    reads=[tb.r(), *scale_reads], writes=[rs_r], dbg=f"{dbg}.Ts{oc}_{tt}_{q}")
        if RES_LVL < 4:
            return
        kw = {"accum_op": ALU.add} if RES_ACCUM else {}
        ctx.dma("pool", lambda X=X, tt=tt, oc=oc: X.ap[tt * 512:(tt + 1) * 512,
                                                       oc * 128:(oc + 1) * 128].rearrange(
            "(q p) v -> p q v", p=128),
            lambda rs=rs: rs.rearrange("p (q v) -> p q v", v=128),
            reads=[rs_r], writes=[X.r()], dbg=f"{dbg}.acc{oc}_{tt}", **kw)
    return ep


SM_LN = 0
SM_STAGE = 32
SM_QMAX = 64
SM_KMAX = 128
SM_LINV = 144
SM_B1 = 152
SM_NEGB = 156
SM_NEGBR = 160
SM_ROWM = 176
SM_TSC = 192
SM_BIAS = 208
SM_LROW = 512



def emit_v_repack(b, ccVo, vg2, n_kvh, featV, dbg):
    ctx, nc = b.ctx, b.nc
    for h in range(n_kvh):
        for rank in range(2):
            for t4 in range(4):
                s = t4 % 2
                stg = b.PB.ap[:, s * 512:(s + 1) * 512]
                stg_r = b.PB.r(f"{s}")
                ctx.dma("sp", lambda stg=stg: stg.rearrange("p (c v) -> p c v", v=128),
                        lambda h=h, rank=rank, t4=t4: ccVo.ap[
                            rank * TOK + t4 * 512:rank * TOK + (t4 + 1) * 512,
                            h * 128:(h + 1) * 128].rearrange("(c p) v -> p c v", p=128),
                        reads=[ccVo.r()], writes=[stg_r], dbg=f"{dbg}.vrl{h}_{rank}_{t4}")
                ctx.dma("sp", lambda h=h, rank=rank, t4=t4: vg2.ap[
                    :, h * 4096 + rank * 2048 + t4 * 512:h * 4096 + rank * 2048 + (t4 + 1) * 512],
                    lambda stg=stg: stg,
                    reads=[stg_r], writes=[vg2.r()], dbg=f"{dbg}.vrs{h}_{rank}_{t4}")


def emit_attn(b, *, kind, n_heads, scale, q_parts, k_parts, v_dram, vcol,
              kmax_col, qmax_col, o_sink, dbg):
    ctx, nc = b.ctx, b.nc
    plan, cls, _, _ = mask_tables(kind, 0)
    S0, S1, S2, OB, LB, TB = (b.banks[2], b.banks[3], b.banks[7], b.banks[4],
                              b.banks[5], b.banks[6])
    tt_idx, row_idx = {}, {}
    for j in range(NQT):
        for s in range(len(plan[j])):
            if cls[(j, s)] == "tt":
                tt_idx[(j, s)] = len(tt_idx)
            elif cls[(j, s)] == "row":
                row_idx[(j, s)] = len(row_idx)
    pmask_d = b.cdram(f"pmask_{kind}")
    rmask_d = b.cdram(f"rmask_{kind}")

    for j in range(NQT):
        mask_slots = {}
        rowm_slots = {}
        for s in range(len(plan[j])):
            if (j, s) in tt_idx:
                sl = len(mask_slots)
                assert sl < 16
                mask_slots[s] = sl
                ctx.dma("sp", lambda sl=sl: b.WB.ap[:, sl * 512:(sl + 1) * 512],
                        lambda i=tt_idx[(j, s)]: pmask_d.ap[i, :, :],
                        reads=[pmask_d.r()], writes=[b.WB.r("w0"), b.WB.r("w1")],
                        dbg=f"{dbg}.mask{j}_{s}")
            elif (j, s) in row_idx:
                sl = len(rowm_slots)
                assert sl < 16
                rowm_slots[s] = sl
                ctx.dma("sp", lambda sl=sl: b.SM.ap[:, SM_ROWM + sl:SM_ROWM + sl + 1],
                        lambda i=row_idx[(j, s)]: rmask_d.ap[i, :, :],
                        reads=[rmask_d.r()], writes=[b.SM.r(f"rowm{sl}")],
                        dbg=f"{dbg}.rowm{j}_{s}")

        for h in range(n_heads):
            qc = qmax_col(h, j)
            kc_ = kmax_col(h)
            ctx.op("dve", lambda qc=qc, kc_=kc_: nc.vector.tensor_tensor(
                b.SM.ap[0:1, SM_B1:SM_B1 + 1], b.SM.ap[0:1, qc:qc + 1],
                b.SM.ap[0:1, kc_:kc_ + 1], op=ALU.mult),
                reads=[b.SM.r("qmax"), b.SM.r("kmax")], writes=[b.SM.r("negB1")],
                dbg=f"{dbg}.b1_{h}_{j}")
            ctx.op("dve", lambda: nc.vector.tensor_scalar(
                b.SM.ap[0:1, SM_B1 + 1:SM_B1 + 2], b.SM.ap[0:1, SM_B1:SM_B1 + 1],
                -1.02 * scale, None, ALU.mult),
                reads=[b.SM.r("negB1")], writes=[b.SM.r("negB2")], dbg=f"{dbg}.b2_{h}_{j}")
            ctx.op("pe", MM(TB.ap[:, 0:1], b.cap("ones_row"),
                            b.SM.ap[0:1, SM_B1 + 1:SM_B1 + 2]),
                   reads=[b.SM.r("negB2"), b.const_r["ones_row"]], writes=[TB.r()],
                   dbg=f"{dbg}.bb_{h}_{j}")
            ctx.op("dve", lambda: nc.vector.tensor_copy(
                b.SM.ap[:, SM_NEGB:SM_NEGB + 1], TB.ap[:, 0:1]),
                reads=[TB.r()], writes=[b.SM.r("negB")], dbg=f"{dbg}.bc_{h}_{j}")
            for s, sl in rowm_slots.items():
                ctx.op("dve", lambda sl=sl: nc.vector.tensor_tensor(
                    b.SM.ap[:, SM_NEGBR + sl:SM_NEGBR + sl + 1],
                    b.SM.ap[:, SM_NEGB:SM_NEGB + 1],
                    b.SM.ap[:, SM_ROWM + sl:SM_ROWM + sl + 1], op=ALU.add),
                    reads=[b.SM.r("negB"), b.SM.r(f"rowm{sl}")],
                    writes=[b.SM.r(f"negBr{sl}")], dbg=f"{dbg}.bv_{h}_{j}_{sl}")

            chunks = plan[j]
            n_ch = len(chunks)
            assert n_ch % 4 == 0 or n_ch < 4
            kp = k_parts(h)
            qp = q_parts(h)
            for si, kchunk in enumerate(chunks):
                rank, kc16 = kchunk // 16, kchunk % 16
                sb_ = (S0, S1, S2)[si % 3]
                kv_s = (si // 4) % 2
                sub = si % 4
                kreads = []
                if sub == 0:
                    nw = min(4, n_ch - si) * 128
                    for pi, (kd, row0f, col0f, prt) in enumerate(kp):
                        qbase = qp[pi][3] if len(qp[pi]) > 3 else 0
                        kr = b.KVT.r(f"k{kv_s}_{pi}")
                        ctx.dma("sp", lambda kv_s=kv_s, pi=pi, prt=prt, qbase=qbase, nw=nw:
                                b.KVT.ap[qbase:qbase + prt,
                                         kv_s * 1536 + pi * 512:kv_s * 1536 + pi * 512 + nw],
                                lambda kd=kd, row0f=row0f, col0f=col0f, rank=rank,
                                kc16=kc16, prt=prt, nw=nw:
                                kd.ap[row0f(rank):row0f(rank) + prt,
                                      col0f(rank) + kc16 * 128:col0f(rank) + kc16 * 128 + nw],
                                reads=[kd.r()], writes=[kr], dbg=f"{dbg}.kld{j}_{h}_{si}")
                    vr = b.KVT.r(f"v{kv_s}")
                    ctx.dma("sp", lambda kv_s=kv_s, nw=nw: b.KVT.ap[:, kv_s * 1536 + 1024:
                                                                    kv_s * 1536 + 1024 + nw],
                            lambda rank=rank, kc16=kc16, h=h, nw=nw: v_dram.ap[
                                :, vcol(h) * 4096 // 128 + rank * 2048 + kc16 * 128:
                                vcol(h) * 4096 // 128 + rank * 2048 + kc16 * 128 + nw],
                            reads=[v_dram.r()], writes=[vr], dbg=f"{dbg}.vld{j}_{h}_{si}")
                kreads = [b.KVT.r(f"k{kv_s}_{pi}") for pi in range(len(kp))]
                vr = b.KVT.r(f"v{kv_s}")
                for pi, (kd, row0f, col0f, prt) in enumerate(kp):
                    tup = qp[pi]
                    qa, qprt, qread = tup[0], tup[1], tup[2]
                    qbase = tup[3] if len(tup) > 3 else 0
                    ctx.op("pe", MM(sb_.ap[:, :],
                                    b.KVT.ap[qbase:qbase + prt,
                                             kv_s * 1536 + pi * 512 + sub * 128:
                                             kv_s * 1536 + pi * 512 + (sub + 1) * 128],
                                    qa[:, j * QT:(j + 1) * QT],
                                    start=(pi == 0), stop=(pi == len(kp) - 1)),
                           reads=[kreads[pi], qread], writes=[sb_.r()],
                           dbg=f"{dbg}.smm{j}_{h}_{si}")
                if si in rowm_slots:
                    bias = b.SM.ap[:, SM_NEGBR + rowm_slots[si]:SM_NEGBR + rowm_slots[si] + 1]
                    bias_r = b.SM.r(f"negBr{rowm_slots[si]}")
                else:
                    bias = b.SM.ap[:, SM_NEGB:SM_NEGB + 1]
                    bias_r = b.SM.r("negB")
                ps = si % 3
                pb = b.PB.ap[:, ps * 512:(ps + 1) * 512]
                pb_r = b.PB.r(f"{ps}")
                ctx.op("act", lambda sb_=sb_, pb=pb, bias=bias: nc.scalar.activation(
                    out=pb, in_=sb_.ap[:, :], func=AF.Exp, bias=bias, scale=scale),
                    reads=[sb_.r(), bias_r], writes=[pb_r], dbg=f"{dbg}.exp{j}_{h}_{si}")
                if si in mask_slots:
                    sl = mask_slots[si]
                    ctx.op("dve", lambda pb=pb, sl=sl: nc.vector.tensor_mul(
                        pb, pb, b.WB.ap[:, sl * 512:(sl + 1) * 512]),
                        reads=[pb_r, b.WB.r("w0"), b.WB.r("w1")], writes=[pb_r],
                        dbg=f"{dbg}.pm{j}_{h}_{si}")
                ctx.op("pe", MM(LB.ap[0:1, 0:512], b.cap("ones_bf"), pb,
                                start=(si == 0), stop=(si == n_ch - 1)),
                       reads=[pb_r, b.const_r["ones_bf"]], writes=[LB.r()],
                       dbg=f"{dbg}.lmm{j}_{h}_{si}")
                ctx.op("pe", MM(OB.ap[:, :],
                                b.KVT.ap[:, kv_s * 1536 + 1024 + sub * 128:
                                         kv_s * 1536 + 1024 + (sub + 1) * 128],
                                pb,
                                start=(si == 0), stop=(si == n_ch - 1)),
                       reads=[pb_r, vr], writes=[OB.r()],
                       dbg=f"{dbg}.av{j}_{h}_{si}")
            ctx.op("dve", lambda: nc.vector.tensor_copy(
                b.SM.ap[0:1, SM_LROW:SM_LROW + 512], LB.ap[0:1, 0:512]),
                reads=[LB.r()], writes=[b.SM.r("lrow")], dbg=f"{dbg}.lc{j}_{h}")
            ctx.op("dve", lambda: nc.vector.reciprocal(
                b.SM.ap[0:1, SM_LROW:SM_LROW + 512], b.SM.ap[0:1, SM_LROW:SM_LROW + 512]),
                reads=[b.SM.r("lrow")], writes=[b.SM.r("lrow")], dbg=f"{dbg}.li{j}_{h}")
            ctx.op("pe", MM(TB.ap[:, 0:512], b.cap("ones_row"),
                            b.SM.ap[0:1, SM_LROW:SM_LROW + 512]),
                   reads=[b.SM.r("lrow"), b.const_r["ones_row"]], writes=[TB.r()],
                   dbg=f"{dbg}.lb{j}_{h}")
            ctx.op("dve", lambda: nc.vector.tensor_copy(
                b.WF.ap[:, 0:512], TB.ap[:, 0:512]),
                reads=[TB.r()], writes=[b.WF.r()], dbg=f"{dbg}.lbe{j}_{h}")
            o_sink(h, j, OB, b.WF.ap[:, 0:512], b.WF.r())


def osink_to_xnt(b, dbg="osink"):
    ctx, nc = b.ctx, b.nc

    def sink(h, j, ob_bank, linv_ap, linv_r):
        ctx.op("dve", lambda h=h, j=j, ob_bank=ob_bank, linv_ap=linv_ap:
               nc.vector.tensor_mul(
                   b.XNT.ap[:, h * TOK + j * 512:h * TOK + (j + 1) * 512],
                   ob_bank.ap[:, :], linv_ap),
               reads=[ob_bank.r(), linv_r], writes=[b.XNT.r()], dbg=f"{dbg}.ev{h}_{j}")
    return sink



QN_LVL = 9
RES_ACCUM = True
RES_LVL = 9


def emit_sq_norm(b, src_ap, src_r, nb_bank, dbg):
    ctx, nc = b.ctx, b.nc
    sq = b.XNB.ap[:, 0:512]
    sq_r = b.XNB.r()
    ctx.op("act", lambda src_ap=src_ap, sq=sq: nc.scalar.activation(
        out=sq, in_=src_ap, func=AF.Square),
        reads=[src_r], writes=[sq_r], dbg=dbg + ".sq")
    if QN_LVL < 2:
        return
    ctx.op("pe", MM(nb_bank.ap[0:1, 0:512], b.cap("ones_bf"), sq),
           reads=[sq_r, b.const_r["ones_bf"]], writes=[nb_bank.r()], dbg=dbg + ".nmm")


def emit_qmax_from_nb(b, nb_bank, h, j, scale_hint, dbg):
    ctx, nc = b.ctx, b.nc
    col = SM_QMAX + h * 4 + j
    if QN_LVL < 3:
        return
    ctx.op("dve", lambda: nc.vector.reduce_max(
        b.SM.ap[0:1, SM_STAGE:SM_STAGE + 1], nb_bank.ap[0:1, 0:512], axis=AX.X),
        reads=[nb_bank.r()], writes=[b.SM.r("stg0")], dbg=dbg + ".max")
    if QN_LVL < 4:
        return
    ctx.op("act", lambda col=col: nc.scalar.activation(
        out=b.SM.ap[0:1, col:col + 1], in_=b.SM.ap[0:1, SM_STAGE:SM_STAGE + 1],
        func=AF.Sqrt),
        reads=[b.SM.r("stg0")], writes=[b.SM.r("qmax")], dbg=dbg + ".sqrt")



def load_bias_cols(b, bias_dram, sm_col, dbg):
    ctx, nc = b.ctx, b.nc
    n_oc = bias_dram.ap.shape[0] // 128
    tb = b.banks[6]
    ctx.dma("sp", lambda n_oc=n_oc: b.WF.ap[0:n_oc, 0:128],
            lambda n_oc=n_oc: bias_dram.ap.rearrange("(o p) -> o p", p=128),
            reads=[bias_dram.r()], writes=[b.WF.r()], dbg=dbg + ".l")
    ctx.op("pe", lambda n_oc=n_oc: nc.tensor.transpose(
        tb.ap[:, 32:32 + n_oc], b.WF.ap[0:n_oc, 0:128],
        b.cap("ident_f32")[0:n_oc, 0:n_oc]),
        reads=[b.WF.r(), b.const_r["ident_f32"]], writes=[tb.r()], dbg=dbg + ".T")
    ctx.op("dve", lambda n_oc=n_oc: nc.vector.tensor_copy(
        b.SM.ap[:, sm_col:sm_col + n_oc], tb.ap[:, 32:32 + n_oc]),
        reads=[tb.r()], writes=[b.SM.r(f"bias{sm_col}")], dbg=dbg + ".e")


def bias_ap(b, sm_col, oc):
    if sm_col is None:
        return None
    return b.SM.ap[:, sm_col + oc:sm_col + oc + 1]


def ep_q_path(b, NB, dbg, sm_bias=None, qh_col=None):
    ctx, nc = b.ctx, b.nc
    qh_col = qh_col or (lambda h, tt: h * TOK + tt * 512)

    def ep(h, tt, bank):
        col = qh_col(h, tt)
        qr = b.QH.r()
        if sm_bias is None:
            ctx.op("dve", lambda bank=bank, col=col: nc.vector.tensor_copy(
                b.QH.ap[:, col:col + 512], bank.ap[:, :]),
                reads=[bank.r()], writes=[qr], dbg=f"{dbg}.qev{h}_{tt}")
        else:
            ctx.op("dve", lambda bank=bank, col=col, h=h: nc.vector.tensor_scalar(
                b.QH.ap[:, col:col + 512], bank.ap[:, :], bias_ap(b, sm_bias, h), None, ALU.add),
                reads=[bank.r(), b.SM.r(f"bias{sm_bias}")], writes=[qr], dbg=f"{dbg}.qev{h}_{tt}")
        emit_sq_norm(b, b.QH.ap[:, col:col + 512], qr, NB, f"{dbg}.qn{h}_{tt}")
        emit_qmax_from_nb(b, NB, h, tt, None, f"{dbg}.qm{h}_{tt}")
    return ep


def ep_k_path(b, NB, ccK, nrow0, dbg, sm_bias=None):
    ctx, nc = b.ctx, b.nc

    def ep(h, tt, bank):
        s = tt % 2
        stg = b.PB.ap[:, s * 512:(s + 1) * 512]
        stg_r = b.PB.r(f"{s}")
        if sm_bias is None:
            ctx.op("dve", lambda bank=bank, stg=stg: nc.vector.tensor_copy(stg, bank.ap[:, :]),
                   reads=[bank.r()], writes=[stg_r], dbg=f"{dbg}.kev{h}_{tt}")
        else:
            ctx.op("dve", lambda bank=bank, stg=stg, h=h: nc.vector.tensor_scalar(
                stg, bank.ap[:, :], bias_ap(b, sm_bias, h), None, ALU.add),
                reads=[bank.r(), b.SM.r(f"bias{sm_bias}")], writes=[stg_r], dbg=f"{dbg}.kev{h}_{tt}")
        ctx.dma("sp", lambda h=h, tt=tt: ccK.ap[h * 128:(h + 1) * 128, tt * 512:(tt + 1) * 512],
                lambda stg=stg: stg,
                reads=[stg_r], writes=[ccK.r()], dbg=f"{dbg}.kst{h}_{tt}")
        emit_sq_norm(b, stg, stg_r, NB, f"{dbg}.kn{h}_{tt}")
        ctx.op("dve", lambda: nc.vector.tensor_copy(
            b.XNB.ap[0:1, 512:1024], NB.ap[0:1, 0:512]),
            reads=[NB.r()], writes=[b.XNB.r()], dbg=f"{dbg}.kne{h}_{tt}")
        ctx.dma("sp", lambda h=h, tt=tt, nrow0=nrow0: ccK.ap[nrow0 + h:nrow0 + h + 1,
                                                            tt * 512:(tt + 1) * 512],
                lambda: b.XNB.ap[0:1, 512:1024],
                reads=[b.XNB.r()], writes=[ccK.r()], dbg=f"{dbg}.knst{h}_{tt}")
    return ep


def ep_v_path(b, ccV, dbg, sm_bias=None, vcol=None):
    ctx, nc = b.ctx, b.nc
    tb = b.banks[7]
    vcol = vcol or (lambda h: h * 128)

    def ep(h, tt, bank):
        s = tt % 2
        stg = b.PB.ap[:, s * 512:(s + 1) * 512]
        stg_r = b.PB.r(f"{s}")
        if sm_bias is None:
            ctx.op("dve", lambda bank=bank, stg=stg: nc.vector.tensor_copy(stg, bank.ap[:, :]),
                   reads=[bank.r()], writes=[stg_r], dbg=f"{dbg}.vev{h}_{tt}")
        else:
            ctx.op("dve", lambda bank=bank, stg=stg, h=h: nc.vector.tensor_scalar(
                stg, bank.ap[:, :], bias_ap(b, sm_bias, h), None, ALU.add),
                reads=[bank.r(), b.SM.r(f"bias{sm_bias}")], writes=[stg_r], dbg=f"{dbg}.vev{h}_{tt}")
        tbv = bf_half(tb, 0, 512)
        for q in range(4):
            ctx.op("pe", lambda tbv=tbv, stg=stg, q=q: nc.tensor.transpose(
                tbv[:, q * 128:(q + 1) * 128], stg[:, q * 128:(q + 1) * 128],
                b.cap("ident_bf")),
                reads=[stg_r, b.const_r["ident_bf"]], writes=[tb.r()],
                dbg=f"{dbg}.vT{h}_{tt}_{q}")
        s2 = 1 - s
        stg2 = b.PB.ap[:, s2 * 512:(s2 + 1) * 512]
        stg2_r = b.PB.r(f"{s2}")
        ctx.op("dve", lambda tbv=tbv, stg2=stg2: nc.vector.tensor_copy(stg2, tbv),
               reads=[tb.r()], writes=[stg2_r], dbg=f"{dbg}.vTe{h}_{tt}")
        ctx.dma("sp", lambda h=h, tt=tt: ccV.ap[tt * 512:(tt + 1) * 512,
                                                vcol(h):vcol(h) + 128].rearrange(
            "(sq p) v -> p sq v", p=128),
            lambda stg2=stg2: stg2.rearrange("p (sq v) -> p sq v", v=128),
            reads=[stg2_r], writes=[ccV.r()], dbg=f"{dbg}.vst{h}_{tt}")
    return ep



L0AA_SUB = 4
L0AA_PARTS = 5


def sub_l0aa(b):
    ctx, nc = b.ctx, b.nc
    X = b.X
    Wqkv = b.dram["L0_attn_a_w_qkv"]
    Wout = b.dram["L0_attn_a_w_out"]
    KR = 2064
    ccK = b.dram_buf("ccK_l0aa", [KR, TOK], BF16, "Internal")
    ccKo = b.dram_buf("ccKo_l0aa", [2 * KR, TOK], BF16, "Internal")
    ccV = b.dram_buf("ccV_l0aa", [TOK, D], BF16, "Internal")
    ccVo = b.dram_buf("ccVo_l0aa", [2 * TOK, D], BF16, "Internal")
    vg2 = b.dram_buf("vg2_l0aa", [128, 16 * 4096], BF16, "Internal")

    emit_ln(b, X, dbg="l0aa.ln")
    if L0AA_PARTS < 2:
        return
    NB = b.banks[2]
    epq = ep_q_path(b, NB, "l0aa")
    epk = ep_k_path(b, NB, ccK, 2048, "l0aa")
    epv = ep_v_path(b, ccV, "l0aa")

    def ep(oc, tt, bank):
        if oc < 16:
            epq(oc, tt, bank)
        elif oc < 32:
            epk(oc - 16, tt, bank)
        else:
            epv(oc - 32, tt, bank)

    emit_proj(b, Wqkv, D, 3 * D, ep, rhs_buf=b.XNT, rhs_read=b.XNT.r(), dbg="l0aa.qkv")

    if L0AA_PARTS < 3:
        return
    rg = RG_OVERRIDE or [[0, 1], [2, 3], [4, 5], [6, 7]]
    ctx.collective(ccK, ccKo, rg, dbg="l0aa_k")
    ctx.collective(ccV, ccVo, rg, dbg="l0aa_v")
    emit_v_repack(b, ccVo, vg2, 16, D, "l0aa.vr")

    emit_kmax(b, ccKo, KR, 2048, 16, "l0aa.kmax")

    if L0AA_PARTS < 4:
        return
    emit_attn(b, kind="causal", n_heads=16, scale=1.0 / math.sqrt(128.0),
              q_parts=lambda h: [(b.QH.ap[:, h * TOK:(h + 1) * TOK], 128, b.QH.r())],
              k_parts=lambda h: [(ccKo, lambda rank: rank * KR + h * 128, lambda rank: 0, 128)],
              v_dram=vg2,
              vcol=lambda h: h * 128,
              kmax_col=lambda h: SM_KMAX + h,
              qmax_col=lambda h, j: SM_QMAX + h * 4 + j,
              o_sink=osink_to_xnt(b, "l0aa.os"), dbg="l0aa.at")

    if L0AA_PARTS < 5:
        return
    emit_proj(b, Wout, D, D, ep_residual(b, X, dbg="l0aa.res"),
              rhs_buf=b.XNT, rhs_read=b.XNT.r(), dbg="l0aa.o")


def emit_kmax(b, ccKo, KR, nrow0, n_heads, dbg):
    ctx, nc = b.ctx, b.nc
    tb = b.banks[6]
    ctx.dma("sp", lambda n=n_heads: b.XNB.ap[0:n, 0:TOK],
            lambda: ccKo.ap[nrow0:nrow0 + n_heads, :],
            reads=[ccKo.r()], writes=[b.XNB.r()], dbg=dbg + ".knl0")
    ctx.op("dve", lambda: nc.vector.reduce_max(
        b.SM.ap[0:n_heads, SM_STAGE:SM_STAGE + 1], b.XNB.ap[0:n_heads, 0:TOK], axis=AX.X),
        reads=[b.XNB.r()], writes=[b.SM.r("kms0")], dbg=dbg + ".m0")
    ctx.dma("sp", lambda n=n_heads: b.XNB.ap[0:n, 0:TOK],
            lambda: ccKo.ap[KR + nrow0:KR + nrow0 + n_heads, :],
            reads=[ccKo.r()], writes=[b.XNB.r()], dbg=dbg + ".knl1")
    ctx.op("dve", lambda: nc.vector.reduce_max(
        b.SM.ap[0:n_heads, SM_STAGE + 1:SM_STAGE + 2], b.XNB.ap[0:n_heads, 0:TOK], axis=AX.X),
        reads=[b.XNB.r()], writes=[b.SM.r("kms1")], dbg=dbg + ".m1")
    ctx.op("dve", lambda: nc.vector.tensor_tensor(
        b.SM.ap[0:n_heads, SM_STAGE + 2:SM_STAGE + 3],
        b.SM.ap[0:n_heads, SM_STAGE:SM_STAGE + 1],
        b.SM.ap[0:n_heads, SM_STAGE + 1:SM_STAGE + 2], op=ALU.max),
        reads=[b.SM.r("kms0"), b.SM.r("kms1")], writes=[b.SM.r("kms2")], dbg=dbg + ".mx")
    ctx.op("act", lambda: nc.scalar.activation(
        out=b.SM.ap[0:n_heads, SM_STAGE + 3:SM_STAGE + 4],
        in_=b.SM.ap[0:n_heads, SM_STAGE + 2:SM_STAGE + 3], func=AF.Sqrt),
        reads=[b.SM.r("kms2")], writes=[b.SM.r("kms3")], dbg=dbg + ".sq")
    ctx.op("pe", lambda: nc.tensor.transpose(
        tb.ap[0:1, 16:16 + n_heads],
        b.SM.ap[0:n_heads, SM_STAGE + 3:SM_STAGE + 4],
        b.cap("ident_f32")[0:n_heads, 0:n_heads]),
        reads=[b.SM.r("kms3"), b.const_r["ident_f32"]], writes=[tb.r()], dbg=dbg + ".T")
    ctx.op("dve", lambda: nc.vector.tensor_copy(
        b.SM.ap[0:1, SM_KMAX:SM_KMAX + n_heads], tb.ap[0:1, 16:16 + n_heads]),
        reads=[tb.r()], writes=[b.SM.r("kmax")], dbg=dbg + ".Te")



SUB_WEIGHTS = {
    "l0aa": ["L0_attn_a_w_qkv", "L0_attn_a_w_out"],
    "l0ffn1": ["L0_ffn_w_gate_up", "L0_ffn_w_proj"],
    "l0ab": ["L0_attn_b_q_kernel", "L0_attn_b_kv_kernel", "L0_attn_b_out_kernel",
             "L0_attn_b_q_norm_scale", "L0_attn_b_k_norm_scale"],
    "l0ffn2": ["L0_ffn_w_gate_up", "L0_ffn_w_proj"],
    "l1aa": ["L1_attn_a_w_dq", "L1_attn_a_g_q", "L1_attn_a_w_uq", "L1_attn_a_w_dkv",
             "L1_attn_a_w_kr", "L1_attn_a_g_kv", "L1_attn_a_w_uk", "L1_attn_a_w_uv",
             "L1_attn_a_w_o"],
    "l1ffn1": ["L1_ffn_gate_w", "L1_ffn_w_gate", "L1_ffn_w_up", "L1_ffn_w_down",
               "L1_ffn_sw_gate", "L1_ffn_sw_up", "L1_ffn_sw_down", "L1_ffn_shared_gate_w"],
    "l1ab": ["L1_attn_b_Wq", "L1_attn_b_bq", "L1_attn_b_Wk", "L1_attn_b_bk",
             "L1_attn_b_Wv", "L1_attn_b_bv", "L1_attn_b_Wo", "L1_attn_b_bo"],
    "l1ffn2": ["L1_ffn_gate_w", "L1_ffn_w_gate", "L1_ffn_w_up", "L1_ffn_w_down",
               "L1_ffn_sw_gate", "L1_ffn_sw_up", "L1_ffn_sw_down", "L1_ffn_shared_gate_w"],
}

WEIGHT_SHAPES = {
    "L0_attn_a_w_qkv": (2048, 6144), "L0_attn_a_w_out": (2048, 2048),
    "L0_attn_b_q_kernel": (16, 2048, 128), "L0_attn_b_kv_kernel": (2, 8, 2048, 128),
    "L0_attn_b_out_kernel": (16, 128, 2048), "L0_attn_b_q_norm_scale": (128,),
    "L0_attn_b_k_norm_scale": (128,),
    "L0_ffn_w_gate_up": (16384, 2048), "L0_ffn_w_proj": (2048, 8192),
    "L1_attn_a_w_dq": (2048, 768), "L1_attn_a_g_q": (768,),
    "L1_attn_a_w_uq": (768, 3072), "L1_attn_a_w_dkv": (2048, 512),
    "L1_attn_a_w_kr": (2048, 64), "L1_attn_a_g_kv": (512,),
    "L1_attn_a_w_uk": (512, 2048), "L1_attn_a_w_uv": (512, 2048),
    "L1_attn_a_w_o": (2048, 2048),
    "L1_attn_b_Wq": (2048, 2048), "L1_attn_b_bq": (2048,),
    "L1_attn_b_Wk": (2048, 2048), "L1_attn_b_bk": (2048,),
    "L1_attn_b_Wv": (2048, 2048), "L1_attn_b_bv": (2048,),
    "L1_attn_b_Wo": (2048, 2048), "L1_attn_b_bo": (2048,),
    "L1_ffn_gate_w": (8, 2048), "L1_ffn_w_gate": (8, 2048, 1408),
    "L1_ffn_w_up": (8, 2048, 1408), "L1_ffn_w_down": (8, 1408, 2048),
    "L1_ffn_sw_gate": (2048, 5632), "L1_ffn_sw_up": (2048, 5632),
    "L1_ffn_sw_down": (5632, 2048), "L1_ffn_shared_gate_w": (2048, 1),
}


def build_nc(sub_names, verbose=False):
    nc = bass.Bass("TRN2", target_bir_lowering=False, debug=False)
    es = ExitStack()
    b = Builder(nc, es)
    b.parity = None
    x_in = b.dram_buf("x_in", [TOK, D], F32, "ExternalInput")
    x_out = b.dram_buf("x_out", [TOK, D], F32, "ExternalOutput")
    X = b.dram_buf("Xres", [TOK, D], F32, "Internal")
    b.X = X

    wnames = []
    for s in sub_names:
        for w in SUB_WEIGHTS[s]:
            if w not in wnames:
                wnames.append(w)
    for w in wnames:
        b.dram_buf(w, WEIGHT_SHAPES[w], F32, "ExternalInput")

    cshapes = build_constants(0)
    b.load_consts(cshapes)

    ctx = b.ctx
    for i in range(4):
        ctx.dma("pool", lambda i=i: X.ap[i * 512:(i + 1) * 512, :],
                lambda i=i: x_in.ap[i * 512:(i + 1) * 512, :],
                reads=[x_in.r()], writes=[X.r()], dbg=f"xin{i}")

    seen = {}
    for s in sub_names:
        seen[s] = seen.get(s, 0) + 1
        SUB_EMITTERS[s](b)

    for i in range(4):
        ctx.dma("pool", lambda i=i: x_out.ap[i * 512:(i + 1) * 512, :],
                lambda i=i: X.ap[i * 512:(i + 1) * 512, :],
                reads=[X.r()], writes=[x_out.r()], dbg=f"xout{i}")

    ctx.finalize()
    if verbose:
        tot = {e: len(ctx.ops[e]) for e in ENGINES}
        nw = sum(len(o.waits) for e in ENGINES for o in ctx.ops[e])
        print(f"[build] ops={tot} waits={nw} sems={ctx.n_sems}")
    with nc.Block() as block:
        ctx.replay(block)
    es.close()
    return nc



SUB_EMITTERS = {
    "l0aa": sub_l0aa,
}


def shard_x(x):
    out = []
    for c in range(NCORES):
        s, p = c // 2, c % 2
        xc = np.ascontiguousarray(x[s].reshape(8, 512, D)[p::2].reshape(TOK, D))
        out.append(xc)
    return out


def unshard_x(parts):
    x = np.zeros((4, T_GLOBAL_, D), parts[0].dtype)
    for c in range(NCORES):
        s, p = c // 2, c % 2
        x[s].reshape(8, 512, D)[p::2] = parts[c].reshape(NQT, 512, D)
    return x


T_GLOBAL_ = 4096

_NC_CACHE = {}


def _get_nc(subs):
    key = tuple(subs)
    if key not in _NC_CACHE:
        _NC_CACHE[key] = build_nc(list(subs), verbose=False)
    return _NC_CACHE[key]


def run_subs(subs, x, weights):
    nc = _get_nc(subs)
    shards = shard_x(np.asarray(x, np.float32))
    in_maps = []
    for c in range(NCORES):
        consts = build_constants(c)
        m = {"x_in": shards[c]}
        for s in subs:
            for w in SUB_WEIGHTS[s]:
                m[w] = np.asarray(weights[w], np.float32)
        for k, v in consts.items():
            m["c_" + k] = v
        in_maps.append(m)
    res = run_bass_kernel_spmd(nc, in_maps, core_ids=list(range(NCORES)))
    return unshard_x([r["x_out"] for r in res.results])


def kernel(**inputs):
    x = inputs["L0_attn_a_x"]
    subs = ["l0aa", "l0ffn1", "l0ab", "l0ffn2", "l1aa", "l1ffn1", "l1ab", "l1ffn2"]
    return run_subs(subs, x, inputs).astype(np.float32)



def sub_l1ab(b):
    ctx, nc = b.ctx, b.nc
    X = b.X
    KR = 2064
    ccK = b.dram_buf("ccK_l1ab", [KR, TOK], BF16, "Internal")
    ccKo = b.dram_buf("ccKo_l1ab", [2 * KR, TOK], BF16, "Internal")
    ccV = b.dram_buf("ccV_l1ab", [TOK, D], BF16, "Internal")
    ccVo = b.dram_buf("ccVo_l1ab", [2 * TOK, D], BF16, "Internal")
    vg2 = b.dram_buf("vg2_l1ab", [128, 16 * 4096], BF16, "Internal")

    emit_ln(b, X, dbg="l1ab.ln")
    NB = b.banks[2]
    load_bias_cols(b, b.dram["L1_attn_b_bq"], SM_BIAS, "l1ab.bq")
    load_bias_cols(b, b.dram["L1_attn_b_bk"], SM_BIAS + 16, "l1ab.bk")
    load_bias_cols(b, b.dram["L1_attn_b_bv"], SM_BIAS + 32, "l1ab.bv")
    load_bias_cols(b, b.dram["L1_attn_b_bo"], SM_BIAS + 48, "l1ab.bo")

    emit_proj(b, b.dram["L1_attn_b_Wq"], D, D, ep_q_path(b, NB, "l1ab", sm_bias=SM_BIAS),
              rhs_buf=b.XNT, rhs_read=b.XNT.r(), transposed_w=True, dbg="l1ab.q")
    emit_proj(b, b.dram["L1_attn_b_Wk"], D, D,
              ep_k_path(b, NB, ccK, 2048, "l1ab", sm_bias=SM_BIAS + 16),
              rhs_buf=b.XNT, rhs_read=b.XNT.r(), transposed_w=True, dbg="l1ab.k")
    emit_proj(b, b.dram["L1_attn_b_Wv"], D, D,
              ep_v_path(b, ccV, "l1ab", sm_bias=SM_BIAS + 32),
              rhs_buf=b.XNT, rhs_read=b.XNT.r(), transposed_w=True, dbg="l1ab.v")

    rg = RG_OVERRIDE or [[0, 1], [2, 3], [4, 5], [6, 7]]
    ctx.collective(ccK, ccKo, rg, dbg="l1ab_k")
    ctx.collective(ccV, ccVo, rg, dbg="l1ab_v")
    emit_v_repack(b, ccVo, vg2, 16, D, "l1ab.vr")
    emit_kmax(b, ccKo, KR, 2048, 16, "l1ab.kmax")

    emit_attn(b, kind="full", n_heads=16, scale=1.0 / math.sqrt(128.0),
              q_parts=lambda h: [(b.QH.ap[:, h * TOK:(h + 1) * TOK], 128, b.QH.r())],
              k_parts=lambda h: [(ccKo, lambda rank: rank * KR + h * 128, lambda rank: 0, 128)],
              v_dram=vg2,
              vcol=lambda h: h * 128,
              kmax_col=lambda h: SM_KMAX + h,
              qmax_col=lambda h, j: SM_QMAX + h * 4 + j,
              o_sink=osink_to_xnt(b, "l1ab.os"), dbg="l1ab.at")

    emit_proj(b, b.dram["L1_attn_b_Wo"], D, D,
              ep_residual(b, X, dbg="l1ab.res", sm_bias=SM_BIAS + 48),
              rhs_buf=b.XNT, rhs_read=b.XNT.r(), transposed_w=True, dbg="l1ab.o")


SUB_EMITTERS.update({
    "l1ab": sub_l1ab,
})



def emit_quant_cache(b, W, n_row_chunks, width, cache, cache_col_of, dbg):
    ctx, nc = b.ctx, b.nc
    n_cs = (width + 2047) // 2048
    t1, t2 = b.banks[5], b.banks[6]
    for r in range(n_row_chunks):
        for cs in range(n_cs):
            w0 = cs * 2048
            wd = min(2048, width - w0)
            ng = wd // 64
            nf = wd // 128
            wb = b.QH.ap[:, 0:wd]
            wb_r = b.QH.r()
            ctx.dma("pool", lambda wb=wb: wb,
                    lambda W=W, r=r, w0=w0, wd=wd: W.ap[r * 128:(r + 1) * 128, w0:w0 + wd],
                    reads=[W.r()], writes=[wb_r], dbg=f"{dbg}.wl{r}_{cs}")
            sgn = b.TMA.ap[:, 0:wd]
            a2 = b.TMA.ap[:, 2048:2048 + wd]
            ctx.op("act", lambda wb=wb, sgn=sgn: nc.scalar.activation(
                out=sgn, in_=wb, func=AF.Sign),
                reads=[wb_r], writes=[b.TMA.r("0")], dbg=f"{dbg}.sgn{r}_{cs}")
            ctx.op("act", lambda wb=wb, a2=a2: nc.scalar.activation(
                out=a2, in_=wb, func=AF.Abs, scale=2.0),
                reads=[wb_r], writes=[b.TMA.r("1")], dbg=f"{dbg}.abs{r}_{cs}")
            ctx.op("dve", lambda a2=a2, ng=ng: nc.vector.tensor_reduce(
                b.WF.ap[:, 0:ng], a2.rearrange("p (g k) -> p g k", k=64),
                axis=AX.X, op=ALU.add),
                reads=[b.TMA.r("1")], writes=[b.WF.r()], dbg=f"{dbg}.gs{r}_{cs}")
            ctx.op("dve", lambda ng=ng: nc.vector.tensor_scalar(
                b.WF.ap[:, 64:64 + ng], b.WF.ap[:, 0:ng], 1.0 / 128.0, 1e-8,
                ALU.mult, ALU.max),
                reads=[b.WF.r()], writes=[b.WF.r()], dbg=f"{dbg}.sc{r}_{cs}")
            ctx.op("dve", lambda ng=ng: nc.vector.tensor_copy(
                b.XNB.ap[:, 0:ng], b.WF.ap[:, 64:64 + ng]),
                reads=[b.WF.r()], writes=[b.XNB.r()], dbg=f"{dbg}.sb{r}_{cs}")
            ctx.op("dve", lambda ng=ng: nc.vector.tensor_copy(
                b.WF.ap[:, 128:128 + ng], b.XNB.ap[:, 0:ng]),
                reads=[b.XNB.r()], writes=[b.WF.r()], dbg=f"{dbg}.srt{r}_{cs}")
            ctx.op("dve", lambda a2=a2, ng=ng: nc.vector.tensor_tensor(
                a2.rearrange("p (g k) -> p g k", k=64),
                a2.rearrange("p (g k) -> p g k", k=64),
                b.WF.ap[:, 128:128 + ng].rearrange("p (g one) -> p g one", one=1).to_broadcast(
                    [128, ng, 64]), op=ALU.is_gt),
                reads=[b.TMA.r("1"), b.WF.r()], writes=[b.TMA.r("1")], dbg=f"{dbg}.msk{r}_{cs}")
            ctx.op("dve", lambda sgn=sgn, a2=a2: nc.vector.tensor_mul(sgn, a2, sgn),
                   reads=[b.TMA.r("0"), b.TMA.r("1")], writes=[b.TMA.r("0")],
                   dbg=f"{dbg}.qs{r}_{cs}")
            ctx.op("dve", lambda wb=wb, sgn=sgn, ng=ng: nc.vector.tensor_tensor(
                wb.rearrange("p (g k) -> p g k", k=64),
                sgn.rearrange("p (g k) -> p g k", k=64),
                b.WF.ap[:, 128:128 + ng].rearrange("p (g one) -> p g one", one=1).to_broadcast(
                    [128, ng, 64]), op=ALU.mult),
                reads=[b.TMA.r("0"), b.WF.r()], writes=[wb_r], dbg=f"{dbg}.wq{r}_{cs}")
            for g4 in range(nf // 4):
                tb = (t1, t2)[g4 % 2]
                tbv = bf_half(tb, 0, 512)
                for q in range(4):
                    ic = g4 * 4 + q
                    ctx.op("pe", lambda tbv=tbv, wb=wb, q=q, ic=ic: nc.tensor.transpose(
                        tbv[:, q * 128:(q + 1) * 128], wb[:, ic * 128:(ic + 1) * 128],
                        b.cap("ident_bf")),
                        reads=[wb_r, b.const_r["ident_bf"]], writes=[tb.r()],
                        dbg=f"{dbg}.T{r}_{cs}_{ic}")
                ctx.op("dve", lambda tbv=tbv, g4=g4: nc.vector.tensor_copy(
                    b.QH.ap[:, 2048 + g4 * 512:2048 + (g4 + 1) * 512], tbv),
                       reads=[tb.r()], writes=[b.QH.r()], dbg=f"{dbg}.Te{r}_{cs}_{g4}")
            ctx.dma("sp", lambda r=r, cs=cs, nf=nf: cache.ap[:, cache_col_of(r) + cs * 2048:
                                                             cache_col_of(r) + cs * 2048 + nf * 128],
                    lambda nf=nf: b.QH.ap[:, 2048:2048 + nf * 128],
                    reads=[b.QH.r()], writes=[cache.r()], dbg=f"{dbg}.cs{r}_{cs}")


def sub_l0ffn(b):
    ctx, nc = b.ctx, b.nc
    X = b.X
    Wgu = b.dram["L0_ffn_w_gate_up"]
    Wp = b.dram["L0_ffn_w_proj"]
    if not hasattr(b, "l0ffn_cache"):
        cgu = b.dram_buf("cache_gu", [128, 262144], BF16, "Internal")
        cgu.granule = (16, 4)
        cp = b.dram_buf("cache_p", [128, 131072], BF16, "Internal")
        cp.granule = (64, 1)
        emit_quant_cache(b, Wgu, 128, 2048, cgu,
                         lambda r: (2 * r if r < 64 else 2 * (r - 64) + 1) * 2048, "l0f.qgu")
        emit_quant_cache(b, Wp, 16, 8192, cp, lambda r: r * 8192, "l0f.qp")
        b.l0ffn_cache = (cgu, cp)
    cgu, cp = b.l0ffn_cache

    emit_ln(b, X, dbg="l0f.ln")
    NB = b.banks[2]
    EPS = 1.1920928955078125e-07

    for tt in range(NQT):
        state = {}

        def ep_gu(oc, tt_, bank, state=state):
            hc = oc // 2
            if oc % 2 == 0:
                s = hc % 2
                hg = b.PB.ap[:, s * 512:(s + 1) * 512]
                ctx.op("act", lambda bank=bank, hg=hg: nc.scalar.activation(
                    out=hg, in_=bank.ap[:, :], func=AF.Silu),
                    reads=[bank.r()], writes=[b.PB.r(f"{s}")], dbg=f"l0f.silu{hc}_{tt_}")
                state["hg"] = (hg, b.PB.r(f"{s}"))
            else:
                hg, hg_r = state["hg"]
                s2 = 1 - (hc % 2)
                hu = b.XNB.ap[:, s2 * 512:(s2 + 1) * 512]
                ctx.op("dve", lambda bank=bank, hu=hu: nc.vector.tensor_copy(hu, bank.ap[:, :]),
                       reads=[bank.r()], writes=[b.XNB.r()], dbg=f"l0f.uev{hc}_{tt_}")
                hcol = hc * 512
                ctx.op("dve", lambda hg=hg, hu=hu, hcol=hcol: nc.vector.tensor_mul(
                    b.QH.ap[:, hcol:hcol + 512], hg, hu),
                    reads=[hg_r, b.XNB.r()], writes=[b.QH.r()], dbg=f"l0f.h{hc}_{tt_}")
                sq = b.WF.ap[:, 0:512]
                ctx.op("act", lambda hcol=hcol, sq=sq: nc.scalar.activation(
                    out=sq, in_=b.QH.ap[:, hcol:hcol + 512], func=AF.Square),
                    reads=[b.QH.r()], writes=[b.WF.r()], dbg=f"l0f.h2{hc}_{tt_}")
                sqb = b.XNB.ap[:, 1024:1536]
                ctx.op("dve", lambda sq=sq, sqb=sqb: nc.vector.tensor_copy(sqb, sq),
                       reads=[b.WF.r()], writes=[b.XNB.r()], dbg=f"l0f.h2b{hc}_{tt_}")
                ctx.op("pe", MM(NB.ap[0:1, 0:512], b.cap("ones_bf"), sqb,
                                start=(hc == 0), stop=(hc == 63)),
                       reads=[b.XNB.r(), b.const_r["ones_bf"]], writes=[NB.r()],
                       dbg=f"l0f.nmm{hc}_{tt_}")

        emit_proj(b, cgu, D, 16384, ep_gu, rhs_buf=b.XNT, rhs_read=b.XNT.r(),
                  tt_list=[tt], dbg=f"l0f.gu{tt}")

        ctx.op("dve", lambda: nc.vector.tensor_scalar(
            b.SM.ap[0:1, SM_LROW:SM_LROW + 512], NB.ap[0:1, 0:512],
            1.0 / 8192.0, EPS, ALU.mult, ALU.add),
            reads=[NB.r()], writes=[b.SM.r("lrow")], dbg=f"l0f.ms{tt}")
        ctx.op("act", lambda: nc.scalar.activation(
            out=b.SM.ap[0:1, SM_LROW:SM_LROW + 512],
            in_=b.SM.ap[0:1, SM_LROW:SM_LROW + 512], func=AF.Sqrt),
            reads=[b.SM.r("lrow")], writes=[b.SM.r("lrow")], dbg=f"l0f.sq{tt}")
        ctx.op("dve", lambda: nc.vector.reciprocal(
            b.SM.ap[0:1, SM_LROW:SM_LROW + 512], b.SM.ap[0:1, SM_LROW:SM_LROW + 512]),
            reads=[b.SM.r("lrow")], writes=[b.SM.r("lrow")], dbg=f"l0f.rc{tt}")
        TB = b.banks[6]
        for qs in range(4):
            ctx.op("pe", lambda qs=qs: nc.tensor.transpose(
                TB.ap[:, 8 + qs:9 + qs],
                b.SM.ap[0:1, SM_LROW + qs * 128:SM_LROW + (qs + 1) * 128],
                b.cap("ident_f32")[0:1, 0:1]),
                reads=[b.SM.r("lrow"), b.const_r["ident_f32"]], writes=[TB.r()],
                dbg=f"l0f.rT{tt}_{qs}")
            ctx.op("dve", lambda qs=qs: nc.vector.tensor_copy(
                b.SM.ap[:, SM_TSC + qs:SM_TSC + qs + 1], TB.ap[:, 8 + qs:9 + qs]),
                reads=[TB.r()], writes=[b.SM.r("tsc")], dbg=f"l0f.rTe{tt}_{qs}")

        emit_proj(b, cp, 8192, D,
                  ep_residual(b, X, scale_col=lambda tt_, qs: b.SM.ap[:, SM_TSC + qs:SM_TSC + qs + 1],
                              scale_reads=(b.SM.r("tsc"),), dbg=f"l0f.res{tt}"),
                  rhs_buf=b.QH, rhs_read=b.QH.r(),
                  rhs_col_fn=lambda ic, tt_: ic * 512,
                  tt_list=[tt], dbg=f"l0f.dn{tt}")


SUB_EMITTERS.update({"l0ffn1": sub_l0ffn, "l0ffn2": sub_l0ffn})



def sub_l0ab(b):
    ctx, nc = b.ctx, b.nc
    X = b.X
    QK = b.dram["L0_attn_b_q_kernel"]
    KVK = b.dram["L0_attn_b_kv_kernel"]
    OK = b.dram["L0_attn_b_out_kernel"]
    KR = 1032
    ccK = b.dram_buf("ccK_l0ab", [KR, TOK], BF16, "Internal")
    ccKo = b.dram_buf("ccKo_l0ab", [2 * KR, TOK], BF16, "Internal")
    ccV = b.dram_buf("ccV_l0ab", [TOK, 1024], BF16, "Internal")
    ccVo = b.dram_buf("ccVo_l0ab", [2 * TOK, 1024], BF16, "Internal")
    vg2 = b.dram_buf("vg2_l0ab", [128, 8 * 4096], BF16, "Internal")

    emit_ln(b, X, dbg="l0ab.ln")
    ctx.dma("sp", lambda: b.SM.ap[:, SM_BIAS:SM_BIAS + 1],
            lambda: b.dram["L0_attn_b_q_norm_scale"].ap.unsqueeze(1),
            reads=[b.dram["L0_attn_b_q_norm_scale"].r()], writes=[b.SM.r("biasq")],
            dbg="l0ab.gq")
    ctx.dma("sp", lambda: b.SM.ap[:, SM_BIAS + 1:SM_BIAS + 2],
            lambda: b.dram["L0_attn_b_k_norm_scale"].ap.unsqueeze(1),
            reads=[b.dram["L0_attn_b_k_norm_scale"].r()], writes=[b.SM.r("biask")],
            dbg="l0ab.gk")
    NB = b.banks[2]

    def broadcast_gain2(sm_col, kv_col, reg, dbgn):
        T1 = b.banks[6]
        ctx.op("pe", lambda: nc.tensor.transpose(
            T1.ap[0:1, 0:128], b.SM.ap[:, sm_col:sm_col + 1], b.cap("ident_f32")),
            reads=[b.SM.r(reg), b.const_r["ident_f32"]], writes=[T1.r()], dbg=dbgn + ".T")
        ctx.op("dve", lambda: nc.vector.tensor_copy(
            b.SM.ap[0:1, SM_LROW:SM_LROW + 128], T1.ap[0:1, 0:128]),
            reads=[T1.r()], writes=[b.SM.r("grow")], dbg=dbgn + ".e")
        ctx.op("pe", lambda: nc.tensor.matmul(
            T1.ap[:, 0:128], b.cap("ones_row"), b.SM.ap[0:1, SM_LROW:SM_LROW + 128],
            start=True, stop=True),
            reads=[b.SM.r("grow"), b.const_r["ones_row"]], writes=[T1.r()], dbg=dbgn + ".b")
        ctx.op("dve", lambda kv_col=kv_col: nc.vector.tensor_copy(
            b.KVT.ap[:, kv_col:kv_col + 128], T1.ap[:, 0:128]),
            reads=[T1.r()], writes=[b.KVT.r(f"g{kv_col}")], dbg=dbgn + ".c")

    broadcast_gain2(SM_BIAS, 0, "biasq", "l0ab.gbq")
    broadcast_gain2(SM_BIAS + 1, 128, "biask", "l0ab.gbk")

    def q_fm_sink(h, tt):
        def sink(tbv2, treg):
            ctx.op("dve", lambda tbv2=tbv2, h=h, tt=tt: nc.vector.tensor_copy(
                b.QH.ap[:, h * TOK + tt * 512:h * TOK + (tt + 1) * 512], tbv2),
                reads=[treg], writes=[b.QH.r()], dbg=f"l0ab.qf{h}_{tt}")
        return sink

    def q_norm_sink(h, tt):
        def sink(row_ap, row_r):
            col = SM_QMAX + h * 4 + tt
            ctx.op("dve", lambda row_ap=row_ap: nc.vector.reduce_max(
                b.SM.ap[0:1, SM_STAGE:SM_STAGE + 1], row_ap, axis=AX.X),
                reads=[row_r], writes=[b.SM.r("stg0")], dbg=f"l0ab.qm{h}_{tt}")
            ctx.op("act", lambda col=col: nc.scalar.activation(
                out=b.SM.ap[0:1, col:col + 1], in_=b.SM.ap[0:1, SM_STAGE:SM_STAGE + 1],
                func=AF.Sqrt),
                reads=[b.SM.r("stg0")], writes=[b.SM.r("qmax")], dbg=f"l0ab.qs{h}_{tt}")
        return sink

    def k_fm_sink(h, tt):
        def sink(tbv2, treg):
            s2 = (tt + 1) % 2
            stg2 = b.PB.ap[:, s2 * 512:(s2 + 1) * 512]
            stg2_r = b.PB.r(f"{s2}")
            ctx.op("dve", lambda tbv2=tbv2, stg2=stg2: nc.vector.tensor_copy(stg2, tbv2),
                   reads=[treg], writes=[stg2_r], dbg=f"l0ab.kf{h}_{tt}")
            ctx.dma("sp", lambda h=h, tt=tt: ccK.ap[h * 128:(h + 1) * 128,
                                                    tt * 512:(tt + 1) * 512],
                    lambda stg2=stg2: stg2,
                    reads=[stg2_r], writes=[ccK.r()], dbg=f"l0ab.kst{h}_{tt}")
        return sink

    def k_norm_sink(h, tt):
        def sink(row_ap, row_r):
            ctx.op("dve", lambda row_ap=row_ap: nc.vector.tensor_copy(
                b.XNB.ap[0:1, 1024:1536], row_ap),
                reads=[row_r], writes=[b.XNB.r()], dbg=f"l0ab.knb{h}_{tt}")
            ctx.dma("sp", lambda h=h, tt=tt: ccK.ap[1024 + h:1025 + h,
                                                    tt * 512:(tt + 1) * 512],
                    lambda: b.XNB.ap[0:1, 1024:1536],
                    reads=[b.XNB.r()], writes=[ccK.r()], dbg=f"l0ab.knst{h}_{tt}")
        return sink

    def ep_q(h, tt, bank):
        emit_rms_rope_tm2(b, bank, None, None, tt, 128, "128",
                          q_fm_sink(h, tt), norm_sink=q_norm_sink(h, tt),
                          gain_col=0, dbg=f"l0ab.q{h}_{tt}")

    def ep_k(h, tt, bank):
        emit_rms_rope_tm2(b, bank, None, None, tt, 128, "128",
                          k_fm_sink(h, tt), norm_sink=k_norm_sink(h, tt),
                          gain_col=128, dbg=f"l0ab.k{h}_{tt}")

    for h in range(16):
        emit_proj(b, Buf(QK.ap[h], f"qk{h}"), D, 128,
                  lambda oc, tt, bank, h=h: ep_q(h, tt, bank),
                  rhs_buf=b.XNT, rhs_read=b.XNT.r(), dbg=f"l0ab.qp{h}")
    for h in range(8):
        emit_proj(b, Buf(KVK.ap[0, h], f"kk{h}"), D, 128,
                  lambda oc, tt, bank, h=h: ep_k(h, tt, bank),
                  rhs_buf=b.XNT, rhs_read=b.XNT.r(), dbg=f"l0ab.kp{h}")
    for h in range(8):
        emit_proj(b, Buf(KVK.ap[1, h], f"vk{h}"), D, 128,
                  lambda oc, tt, bank, h=h: ep_v_path(b, ccV, "l0ab", vcol=lambda _: h * 128)(h, tt, bank),
                  rhs_buf=b.XNT, rhs_read=b.XNT.r(), dbg=f"l0ab.vp{h}")

    rg = RG_OVERRIDE or [[0, 1], [2, 3], [4, 5], [6, 7]]
    ctx.collective(ccK, ccKo, rg, dbg="l0ab_k")
    ctx.collective(ccV, ccVo, rg, dbg="l0ab_v")
    emit_v_repack(b, ccVo, vg2, 8, 1024, "l0ab.vr")
    emit_kmax(b, ccKo, KR, 1024, 8, "l0ab.kmax")

    emit_attn(b, kind="swa", n_heads=16, scale=0.08838834764831845,
              q_parts=lambda h: [(b.QH.ap[:, h * TOK:(h + 1) * TOK], 128, b.QH.r())],
              k_parts=lambda h: [(ccKo, lambda rank, h=h: rank * KR + (h // 2) * 128, lambda rank: 0, 128)],
              v_dram=vg2,
              vcol=lambda h: (h // 2) * 128,
              kmax_col=lambda h: SM_KMAX + h // 2,
              qmax_col=lambda h, j: SM_QMAX + h * 4 + j,
              o_sink=osink_to_xnt(b, "l0ab.os"), dbg="l0ab.at")

    OKf = Buf(OK.ap.rearrange("n h d -> (n h) d"), "okf")
    emit_proj(b, OKf, D, D, ep_residual(b, X, dbg="l0ab.res"),
              rhs_buf=b.XNT, rhs_read=b.XNT.r(), dbg="l0ab.o")


def emit_rms_rope_tm2(b, bank, scale_col_ap, scale_read, tt, rope_dim, rope_name,
                      fm_sink, norm_sink=None, rms_eps=1e-6, gain_col=None,
                      do_rms=True, dbg=""):
    ctx, nc = b.ctx, b.nc
    T1 = b.banks[6]
    half = rope_dim // 2
    s = tt % 2
    stg = b.PB.ap[:, s * 512:(s + 1) * 512]
    stg_r = b.PB.r(f"{s}")
    ctx.op("dve", lambda bank=bank, stg=stg: nc.vector.tensor_copy(stg, bank.ap[:, :]),
           reads=[bank.r()], writes=[stg_r], dbg=f"{dbg}.ev")
    tbv = bf_half(T1, 0, 512)
    for q in range(4):
        ctx.op("pe", lambda tbv=tbv, stg=stg, q=q: nc.tensor.transpose(
            tbv[:, q * 128:(q + 1) * 128], stg[:, q * 128:(q + 1) * 128],
            b.cap("ident_bf")),
            reads=[stg_r, b.const_r["ident_bf"]], writes=[T1.r()], dbg=f"{dbg}.T{q}")
    tm = b.XNB.ap[:, 0:512]
    tm_r = b.XNB.r()
    ctx.op("dve", lambda tbv=tbv, tm=tm: nc.vector.tensor_copy(tm, tbv),
           reads=[T1.r()], writes=[tm_r], dbg=f"{dbg}.Te")
    c0 = SM_STAGE + 8
    for s4 in range(4):
        tile = tm[:, s4 * 128:s4 * 128 + 128]
        if do_rms:
            ctx.op("act", lambda tile=tile: nc.scalar.activation(
                out=b.WF.ap[:, 1536:1664], in_=tile, func=AF.Square,
                accum_out=b.SM.ap[:, c0:c0 + 1]),
                reads=[tm_r], writes=[b.WF.r(), b.SM.r("rr0")], dbg=f"{dbg}.sq{s4}")
            ctx.op("dve", lambda: nc.vector.tensor_scalar(
                b.SM.ap[:, c0 + 1:c0 + 2], b.SM.ap[:, c0:c0 + 1], 1.0 / 128.0, rms_eps,
                ALU.mult, ALU.add),
                reads=[b.SM.r("rr0")], writes=[b.SM.r("rr1")], dbg=f"{dbg}.ms{s4}")
            ctx.op("act", lambda: nc.scalar.activation(
                out=b.SM.ap[:, c0 + 2:c0 + 3], in_=b.SM.ap[:, c0 + 1:c0 + 2], func=AF.Sqrt),
                reads=[b.SM.r("rr1")], writes=[b.SM.r("rr2")], dbg=f"{dbg}.sd{s4}")
            ctx.op("dve", lambda: nc.vector.reciprocal(
                b.SM.ap[:, c0 + 3:c0 + 4], b.SM.ap[:, c0 + 2:c0 + 3]),
                reads=[b.SM.r("rr2")], writes=[b.SM.r("rr3")], dbg=f"{dbg}.rs{s4}")
            ctx.op("dve", lambda tile=tile: nc.vector.tensor_scalar(
                tile, tile, b.SM.ap[:, c0 + 3:c0 + 4], None, ALU.mult),
                reads=[tm_r, b.SM.r("rr3")], writes=[tm_r], dbg=f"{dbg}.n{s4}")
        if gain_col is not None:
            ctx.op("dve", lambda tile=tile, gain_col=gain_col: nc.vector.tensor_mul(
                tile, tile, b.KVT.ap[:, gain_col:gain_col + 128]),
                reads=[tm_r, b.KVT.r(f"g{gain_col}")], writes=[tm_r], dbg=f"{dbg}.g{s4}")
        if rope_dim > 0:
            cs = b.SC.ap[:, 0:half]
            sn = b.SC.ap[:, 128:128 + half]
            ctx.dma("sp", lambda cs=cs: cs,
                    lambda tt=tt, s4=s4: b.cdram("cos" + rope_name).ap[
                        tt * 512 + s4 * 128:tt * 512 + (s4 + 1) * 128, :],
                    reads=[b.cdram("cos" + rope_name).r()], writes=[b.SC.r("c")],
                    dbg=f"{dbg}.cl{s4}")
            ctx.dma("sp", lambda sn=sn: sn,
                    lambda tt=tt, s4=s4: b.cdram("sin" + rope_name).ap[
                        tt * 512 + s4 * 128:tt * 512 + (s4 + 1) * 128, :],
                    reads=[b.cdram("sin" + rope_name).r()], writes=[b.SC.r("s")],
                    dbg=f"{dbg}.sl{s4}")
            x1 = tile[:, 0:half]
            x2 = tile[:, half:2 * half]
            sc1 = b.XNB.ap[:, 512:512 + half]
            sc2 = b.XNB.ap[:, 640:640 + half]
            sc3 = b.XNB.ap[:, 768:768 + half]
            sc4 = b.XNB.ap[:, 896:896 + half]
            scr = b.XNB.r()
            ctx.op("dve", lambda sc1=sc1, x1=x1, cs=cs: nc.vector.tensor_mul(sc1, x1, cs),
                   reads=[tm_r, b.SC.r("c")], writes=[scr], dbg=f"{dbg}.r1{s4}")
            ctx.op("dve", lambda sc2=sc2, x2=x2, sn=sn: nc.vector.tensor_mul(sc2, x2, sn),
                   reads=[tm_r, b.SC.r("s")], writes=[scr], dbg=f"{dbg}.r2{s4}")
            ctx.op("dve", lambda sc3=sc3, x2=x2, cs=cs: nc.vector.tensor_mul(sc3, x2, cs),
                   reads=[tm_r, b.SC.r("c")], writes=[scr], dbg=f"{dbg}.r3{s4}")
            ctx.op("dve", lambda sc4=sc4, x1=x1, sn=sn: nc.vector.tensor_mul(sc4, x1, sn),
                   reads=[tm_r, b.SC.r("s")], writes=[scr], dbg=f"{dbg}.r4{s4}")
            ctx.op("dve", lambda x1=x1, sc1=sc1, sc2=sc2: nc.vector.tensor_sub(x1, sc1, sc2),
                   reads=[scr], writes=[tm_r], dbg=f"{dbg}.r5{s4}")
            ctx.op("dve", lambda x2=x2, sc3=sc3, sc4=sc4: nc.vector.tensor_add(x2, sc3, sc4),
                   reads=[scr], writes=[tm_r], dbg=f"{dbg}.r6{s4}")
        if norm_sink is not None:
            ctx.op("act", lambda tile=tile: nc.scalar.activation(
                out=b.WF.ap[:, 1536:1664], in_=tile, func=AF.Square,
                accum_out=b.SM.ap[:, c0 + 4:c0 + 5]),
                reads=[tm_r], writes=[b.WF.r(), b.SM.r("rr4")], dbg=f"{dbg}.nq{s4}")
            ctx.op("pe", lambda s4=s4: nc.tensor.transpose(
                b.banks[7].ap[0:1, s4 * 128:(s4 + 1) * 128],
                b.SM.ap[:, c0 + 4:c0 + 5], b.cap("ident_f32")),
                reads=[b.SM.r("rr4"), b.const_r["ident_f32"]], writes=[b.banks[7].r()],
                dbg=f"{dbg}.nT{s4}")
    if norm_sink is not None:
        ctx.op("dve", lambda: nc.vector.tensor_copy(
            b.SM.ap[0:1, SM_LROW:SM_LROW + 512], b.banks[7].ap[0:1, 0:512]),
            reads=[b.banks[7].r()], writes=[b.SM.r("lrow")], dbg=f"{dbg}.nr")
        norm_sink(b.SM.ap[0:1, SM_LROW:SM_LROW + 512], b.SM.r("lrow"))
    T2 = b.banks[7]
    tbv2 = bf_half(T2, 0, 512)
    for q in range(4):
        ctx.op("pe", lambda tbv2=tbv2, tm=tm, q=q: nc.tensor.transpose(
            tbv2[:, q * 128:(q + 1) * 128], tm[:, q * 128:(q + 1) * 128],
            b.cap("ident_bf")),
            reads=[tm_r, b.const_r["ident_bf"]], writes=[T2.r()], dbg=f"{dbg}.U{q}")
    fm_sink(tbv2, T2.r())


SUB_EMITTERS.update({"l0ab": sub_l0ab})



def emit_rope64(b, bank, prt0, tt, fm_sink, norm_max_col=None, dbg=""):
    ctx, nc = b.ctx, b.nc
    T1, T2 = b.banks[6], b.banks[7]
    s = tt % 2
    stg = b.PB.ap[0:64, s * 512:(s + 1) * 512]
    stg_r = b.PB.r(f"{s}")
    ctx.op("dve", lambda bank=bank, stg=stg: nc.vector.tensor_copy(
        stg, bank.ap[prt0:prt0 + 64, :]),
        reads=[bank.r()], writes=[stg_r], dbg=f"{dbg}.ev")
    tbv = bf_half(T1, 0, 512)
    for q in range(4):
        ctx.op("pe", lambda tbv=tbv, stg=stg, q=q: nc.tensor.transpose(
            tbv[:, q * 64:(q + 1) * 64], stg[:, q * 128:(q + 1) * 128],
            b.cap("ident_bf")[0:64, 0:64]),
            reads=[stg_r, b.const_r["ident_bf"]], writes=[T1.r()], dbg=f"{dbg}.T{q}")
    tm = b.XNB.ap[:, 0:256]
    tm_r = b.XNB.r()
    ctx.op("dve", lambda tbv=tbv, tm=tm: nc.vector.tensor_copy(tm, tbv[:, 0:256]),
           reads=[T1.r()], writes=[tm_r], dbg=f"{dbg}.Te")
    c0 = SM_STAGE + 8
    for s4 in range(4):
        tile = tm[:, s4 * 64:s4 * 64 + 64]
        half = 32
        cs = b.SC.ap[:, 0:half]
        sn = b.SC.ap[:, 128:128 + half]
        ctx.dma("sp", lambda cs=cs: cs,
                lambda tt=tt, s4=s4: b.cdram("cos64").ap[
                    tt * 512 + s4 * 128:tt * 512 + (s4 + 1) * 128, :],
                reads=[b.cdram("cos64").r()], writes=[b.SC.r("c")], dbg=f"{dbg}.cl{s4}")
        ctx.dma("sp", lambda sn=sn: sn,
                lambda tt=tt, s4=s4: b.cdram("sin64").ap[
                    tt * 512 + s4 * 128:tt * 512 + (s4 + 1) * 128, :],
                reads=[b.cdram("sin64").r()], writes=[b.SC.r("s")], dbg=f"{dbg}.sl{s4}")
        x1 = tile[:, 0:half]
        x2 = tile[:, half:2 * half]
        sc1 = b.XNB.ap[:, 512:512 + half]
        sc2 = b.XNB.ap[:, 640:640 + half]
        sc3 = b.XNB.ap[:, 768:768 + half]
        sc4 = b.XNB.ap[:, 896:896 + half]
        scr = b.XNB.r()
        ctx.op("dve", lambda sc1=sc1, x1=x1, cs=cs: nc.vector.tensor_mul(sc1, x1, cs),
               reads=[tm_r, b.SC.r("c")], writes=[scr], dbg=f"{dbg}.r1{s4}")
        ctx.op("dve", lambda sc2=sc2, x2=x2, sn=sn: nc.vector.tensor_mul(sc2, x2, sn),
               reads=[tm_r, b.SC.r("s")], writes=[scr], dbg=f"{dbg}.r2{s4}")
        ctx.op("dve", lambda sc3=sc3, x2=x2, cs=cs: nc.vector.tensor_mul(sc3, x2, cs),
               reads=[tm_r, b.SC.r("c")], writes=[scr], dbg=f"{dbg}.r3{s4}")
        ctx.op("dve", lambda sc4=sc4, x1=x1, sn=sn: nc.vector.tensor_mul(sc4, x1, sn),
               reads=[tm_r, b.SC.r("s")], writes=[scr], dbg=f"{dbg}.r4{s4}")
        ctx.op("dve", lambda x1=x1, sc1=sc1, sc2=sc2: nc.vector.tensor_sub(x1, sc1, sc2),
               reads=[scr], writes=[tm_r], dbg=f"{dbg}.r5{s4}")
        ctx.op("dve", lambda x2=x2, sc3=sc3, sc4=sc4: nc.vector.tensor_add(x2, sc3, sc4),
               reads=[scr], writes=[tm_r], dbg=f"{dbg}.r6{s4}")
        if norm_max_col is not None:
            ctx.op("act", lambda tile=tile: nc.scalar.activation(
                out=b.WF.ap[:, 1536:1600], in_=tile, func=AF.Square,
                accum_out=b.SM.ap[:, c0 + 4:c0 + 5]),
                reads=[tm_r], writes=[b.WF.r(), b.SM.r("rr4")], dbg=f"{dbg}.nq{s4}")
            ctx.op("pe", lambda s4=s4: nc.tensor.transpose(
                T2.ap[0:1, s4 * 128:(s4 + 1) * 128],
                b.SM.ap[:, c0 + 4:c0 + 5], b.cap("ident_f32")),
                reads=[b.SM.r("rr4"), b.const_r["ident_f32"]], writes=[T2.r()],
                dbg=f"{dbg}.nT{s4}")
    if norm_max_col is not None:
        ctx.op("dve", lambda: nc.vector.reduce_max(
            b.SM.ap[0:1, norm_max_col:norm_max_col + 1], T2.ap[0:1, 0:512], axis=AX.X),
            reads=[T2.r()], writes=[b.SM.r("pemax")], dbg=f"{dbg}.nx")
    tbv2 = bf_half(T1, 0, 512)
    for q in range(4):
        ctx.op("pe", lambda tbv2=tbv2, tm=tm, q=q: nc.tensor.transpose(
            tbv2[:, q * 128:(q + 1) * 128][0:64, :], tm[:, q * 64:(q + 1) * 64],
            b.cap("ident_bf")),
            reads=[tm_r, b.const_r["ident_bf"]], writes=[T1.r()], dbg=f"{dbg}.U{q}")
    fm_sink(tbv2, T1.r())


def emit_lora_norm(b, n_chunks, d_lora, gain_sm_col, gain_reg, qh_chunk0, W, dbg):
    ctx, nc = b.ctx, b.nc
    NB = b.banks[2]
    T1, T2 = b.banks[6], b.banks[7]
    for tt in range(NQT):
        banks = {}

        def ep(oc, tt_, bank, banks=banks):
            col = (qh_chunk0 + oc) * TOK + tt_ * 512
            ctx.op("dve", lambda bank=bank, col=col, oc=oc: nc.vector.tensor_scalar(
                b.QH.ap[:, col:col + 512], bank.ap[:, :],
                b.SM.ap[:, gain_sm_col + oc:gain_sm_col + oc + 1], None, ALU.mult),
                reads=[bank.r(), b.SM.r(gain_reg)], writes=[b.QH.r()],
                dbg=f"{dbg}.ev{oc}_{tt_}")
            sq = b.XNB.ap[:, 1024:1536]
            ctx.op("act", lambda bank=bank, sq=sq: nc.scalar.activation(
                out=sq, in_=bank.ap[:, :], func=AF.Square),
                reads=[bank.r()], writes=[b.XNB.r()], dbg=f"{dbg}.sq{oc}_{tt_}")
            ctx.op("pe", MM(NB.ap[0:1, 0:512], b.cap("ones_bf"), sq,
                            start=(oc == 0), stop=(oc == n_chunks - 1)),
                   reads=[b.XNB.r(), b.const_r["ones_bf"]], writes=[NB.r()],
                   dbg=f"{dbg}.nmm{oc}_{tt_}")

        emit_proj(b, W, D, d_lora, ep, rhs_buf=b.XNT, rhs_read=b.XNT.r(),
                  tt_list=[tt], dbg=f"{dbg}.p{tt}")
        ctx.op("dve", lambda: nc.vector.tensor_scalar(
            b.SM.ap[0:1, SM_LROW:SM_LROW + 512], NB.ap[0:1, 0:512],
            1.0 / d_lora, 1e-6, ALU.mult, ALU.add),
            reads=[NB.r()], writes=[b.SM.r("lrow")], dbg=f"{dbg}.ms{tt}")
        ctx.op("act", lambda: nc.scalar.activation(
            out=b.SM.ap[0:1, SM_LROW:SM_LROW + 512],
            in_=b.SM.ap[0:1, SM_LROW:SM_LROW + 512], func=AF.Sqrt),
            reads=[b.SM.r("lrow")], writes=[b.SM.r("lrow")], dbg=f"{dbg}.sd{tt}")
        ctx.op("dve", lambda: nc.vector.reciprocal(
            b.SM.ap[0:1, SM_LROW:SM_LROW + 512], b.SM.ap[0:1, SM_LROW:SM_LROW + 512]),
            reads=[b.SM.r("lrow")], writes=[b.SM.r("lrow")], dbg=f"{dbg}.rc{tt}")
        for qs in range(4):
            ctx.op("pe", lambda qs=qs: nc.tensor.transpose(
                T1.ap[:, qs:qs + 1],
                b.SM.ap[0:1, SM_LROW + qs * 128:SM_LROW + (qs + 1) * 128],
                b.cap("ident_f32")[0:1, 0:1]),
                reads=[b.SM.r("lrow"), b.const_r["ident_f32"]], writes=[T1.r()],
                dbg=f"{dbg}.rT{tt}_{qs}")
            ctx.op("dve", lambda qs=qs: nc.vector.tensor_copy(
                b.SM.ap[:, SM_TSC + qs:SM_TSC + qs + 1], T1.ap[:, qs:qs + 1]),
                reads=[T1.r()], writes=[b.SM.r("tsc")], dbg=f"{dbg}.rTe{tt}_{qs}")
        for oc in range(n_chunks):
            col = (qh_chunk0 + oc) * TOK + tt * 512
            s = oc % 2
            tbv = bf_half(T1, 0, 512)
            for q in range(4):
                ctx.op("pe", lambda tbv=tbv, col=col, q=q: nc.tensor.transpose(
                    tbv[:, q * 128:(q + 1) * 128], b.QH.ap[:, col + q * 128:col + (q + 1) * 128],
                    b.cap("ident_bf")),
                    reads=[b.QH.r(), b.const_r["ident_bf"]], writes=[T1.r()],
                    dbg=f"{dbg}.tT{tt}_{oc}_{q}")
            tmb = b.XNB.ap[:, 0:512]
            for q in range(4):
                ctx.op("dve", lambda tbv=tbv, tmb=tmb, q=q: nc.vector.tensor_scalar(
                    tmb[:, q * 128:(q + 1) * 128], tbv[:, q * 128:(q + 1) * 128],
                    b.SM.ap[:, SM_TSC + q:SM_TSC + q + 1], None, ALU.mult),
                    reads=[T1.r(), b.SM.r("tsc")], writes=[b.XNB.r()],
                    dbg=f"{dbg}.ts{tt}_{oc}_{q}")
            tbv2 = bf_half(T2, 0, 512)
            for q in range(4):
                ctx.op("pe", lambda tbv2=tbv2, tmb=tmb, q=q: nc.tensor.transpose(
                    tbv2[:, q * 128:(q + 1) * 128], tmb[:, q * 128:(q + 1) * 128],
                    b.cap("ident_bf")),
                    reads=[b.XNB.r(), b.const_r["ident_bf"]], writes=[T2.r()],
                    dbg=f"{dbg}.uT{tt}_{oc}_{q}")
            ctx.op("dve", lambda tbv2=tbv2, col=col: nc.vector.tensor_copy(
                b.QH.ap[:, col:col + 512], tbv2),
                reads=[T2.r()], writes=[b.QH.r()], dbg=f"{dbg}.uTe{tt}_{oc}")


L1AA_STOP = False
L1AA_EXP = 7


def sub_l1aa(b):
    ctx, nc = b.ctx, b.nc
    X = b.X
    KR = 576
    ccK = b.dram_buf("ccK_l1aa", [KR, TOK], BF16, "Internal")
    ccKo = b.dram_buf("ccKo_l1aa", [2 * KR, TOK], BF16, "Internal")
    kfull = b.dram_buf("kfull_l1aa", [2048, 2 * TOK], BF16, "Internal")
    vfull = b.dram_buf("vfull_l1aa", [128, 16 * 4096], BF16, "Internal")

    emit_ln(b, X, dbg="l1aa.ln")
    load_bias_cols(b, b.dram["L1_attn_a_g_q"], SM_BIAS, "l1aa.gq")
    load_bias_cols(b, b.dram["L1_attn_a_g_kv"], SM_BIAS + 8, "l1aa.gkv")

    emit_lora_norm(b, 6, 768, SM_BIAS, f"bias{SM_BIAS}", 0,
                   b.dram["L1_attn_a_w_dq"], "l1aa.cq")
    emit_lora_norm(b, 4, 512, SM_BIAS + 8, f"bias{SM_BIAS + 8}", 6,
                   b.dram["L1_attn_a_w_dkv"], "l1aa.ckv")

    for oc in range(4):
        for tt in range(NQT):
            ctx.dma("sp", lambda oc=oc, tt=tt: ccK.ap[oc * 128:(oc + 1) * 128,
                                                      tt * 512:(tt + 1) * 512],
                    lambda oc=oc, tt=tt: b.QH.ap[:, (6 + oc) * TOK + tt * 512:
                                                 (6 + oc) * TOK + (tt + 1) * 512],
                    reads=[b.QH.r()], writes=[ccK.r()], dbg=f"l1aa.kvst{oc}_{tt}")

    def kpe_sink(tt):
        def sink(tbv2, treg):
            s2 = (tt + 1) % 2
            stg2 = b.PB.ap[0:64, s2 * 512:(s2 + 1) * 512]
            stg2_r = b.PB.r(f"{s2}")
            ctx.op("dve", lambda tbv2=tbv2, stg2=stg2: nc.vector.tensor_copy(
                stg2, tbv2[0:64, :]),
                reads=[treg], writes=[stg2_r], dbg=f"l1aa.kpf{tt}")
            ctx.dma("sp", lambda tt=tt: ccK.ap[512:576, tt * 512:(tt + 1) * 512],
                    lambda stg2=stg2: stg2,
                    reads=[stg2_r], writes=[ccK.r()], dbg=f"l1aa.kpst{tt}")
        return sink

    def ep_kpe(oc, tt, bank):
        emit_rope64(b, bank, 0, tt, kpe_sink(tt), norm_max_col=None, dbg=f"l1aa.kpe{tt}")

    emit_proj(b, b.dram["L1_attn_a_w_kr"], D, 64, ep_kpe,
              rhs_buf=b.XNT, rhs_read=b.XNT.r(), dbg="l1aa.kr")

    NB = b.banks[2]

    def ep_qnope(h, tt, bank):
        col = h * TOK + tt * 512
        ctx.op("dve", lambda bank=bank, col=col: nc.vector.tensor_copy(
            b.XNT.ap[:, col:col + 512], bank.ap[:, :]),
            reads=[bank.r()], writes=[b.XNT.r()], dbg=f"l1aa.qnev{h}_{tt}")
        sq = b.XNB.ap[:, 1024:1536]
        ctx.op("act", lambda bank=bank, sq=sq: nc.scalar.activation(
            out=sq, in_=bank.ap[:, :], func=AF.Square),
            reads=[bank.r()], writes=[b.XNB.r()], dbg=f"l1aa.qn2{h}_{tt}")
        ctx.op("pe", MM(NB.ap[0:1, 0:512], b.cap("ones_bf"), sq),
               reads=[b.XNB.r(), b.const_r["ones_bf"]], writes=[NB.r()],
               dbg=f"l1aa.qnm{h}_{tt}")
        ctx.op("dve", lambda tt=tt: nc.vector.reduce_max(
            b.SM.ap[0:1, SM_STAGE + 2 + tt:SM_STAGE + 3 + tt], NB.ap[0:1, 0:512],
            axis=AX.X),
            reads=[NB.r()], writes=[b.SM.r("qnmax")], dbg=f"l1aa.qnr{h}_{tt}")

    def cqn_col(ic, tt):
        return ic * TOK + tt * 512

    for h in range(16):
        Wn = Buf(b.dram["L1_attn_a_w_uq"].ap[:, h * 192:h * 192 + 128], f"wuqn{h}")
        emit_proj(b, Wn, 768, 128,
                  lambda oc, tt, bank, h=h: ep_qnope(h, tt, bank),
                  rhs_buf=b.QH, rhs_read=b.QH.r(), rhs_col_fn=cqn_col, dbg=f"l1aa.uqn{h}")
        def qpe_sink(h=h):
            def sink2(tt):
                def sink(tbv2, treg):
                    chunk = 6 + h // 2
                    p0 = (h % 2) * 64
                    ctx.op("dve", lambda tbv2=tbv2, tt=tt, chunk=chunk, p0=p0:
                           nc.vector.tensor_copy(
                               b.QH.ap[p0:p0 + 64, chunk * TOK + tt * 512:
                                       chunk * TOK + (tt + 1) * 512], tbv2[0:64, :]),
                           reads=[treg], writes=[b.QH.r()], dbg=f"l1aa.qpf{h}_{tt}")
                return sink
            return sink2

        def ep_qpe(oc, tt, bank, h=h):
            emit_rope64(b, bank, 0, tt, qpe_sink(h)(tt),
                        norm_max_col=SM_STAGE + 6, dbg=f"l1aa.qpe{h}_{tt}")
            col = SM_QMAX + h * 4 + tt
            ctx.op("dve", lambda tt=tt: nc.vector.tensor_add(
                b.SM.ap[0:1, SM_STAGE:SM_STAGE + 1],
                b.SM.ap[0:1, SM_STAGE + 2 + tt:SM_STAGE + 3 + tt],
                b.SM.ap[0:1, SM_STAGE + 6:SM_STAGE + 7]),
                reads=[b.SM.r("qnmax"), b.SM.r("pemax")], writes=[b.SM.r("stg0")],
                dbg=f"l1aa.qm{h}_{tt}")
            ctx.op("act", lambda col=col: nc.scalar.activation(
                out=b.SM.ap[0:1, col:col + 1], in_=b.SM.ap[0:1, SM_STAGE:SM_STAGE + 1],
                func=AF.Sqrt),
                reads=[b.SM.r("stg0")], writes=[b.SM.r("qmax")], dbg=f"l1aa.qms{h}_{tt}")

        Wp = Buf(b.dram["L1_attn_a_w_uq"].ap[:, h * 192 + 128:h * 192 + 192], f"wuqp{h}")
        emit_proj(b, Wp, 768, 64, ep_qpe,
                  rhs_buf=b.QH, rhs_read=b.QH.r(), rhs_col_fn=cqn_col, dbg=f"l1aa.uqp{h}")

    rg = RG_OVERRIDE or [[0, 1], [2, 3], [4, 5], [6, 7]]
    ctx.collective(ccK, ccKo, rg, dbg="l1aa_k")

    NB2 = b.banks[3]
    for wname, sc in (("L1_attn_a_w_uk", 0), ("L1_attn_a_w_uv", 8192)):
        ctx.dma("pool", lambda sc=sc: b.WB.ap[:, sc:sc + 8192].rearrange(
            "p (i o) -> p i o", o=2048),
            lambda wname=wname: b.dram[wname].ap.rearrange("(i p) o -> p i o", p=128),
            reads=[b.dram[wname].r()],
            writes=[b.WB.r("w0" if sc == 0 else "w2"), b.WB.r("w1" if sc == 0 else "w3")],
            dbg=f"l1aa.wld{sc}")

    tb7 = b.banks[7]
    for rank in range(2):
        for tt8 in range(4):
            rhs_r = b.XNB.r()
            rhs3 = b.XNB.ap[:, 0:2048].rearrange("p (i t) -> p i t", t=512)
            ctx.dma("sp", lambda: b.XNB.ap[:, 0:2048].rearrange(
                "p (i t) -> p i t", t=512),
                lambda rank=rank, tt8=tt8: ccKo.ap[rank * KR:rank * KR + 512,
                                                   tt8 * 512:(tt8 + 1) * 512].rearrange(
                    "(i p) t -> p i t", p=128),
                reads=[ccKo.r()], writes=[rhs_r], dbg=f"l1aa.kvg{rank}_{tt8}")
            kpe_stg = b.PB.ap[0:64, 0:512]
            ctx.dma("sp", lambda rank=rank, tt8=tt8: b.PB.ap[0:64, 0:512],
                    lambda rank=rank, tt8=tt8: ccKo.ap[rank * KR + 512:rank * KR + 576,
                                                       tt8 * 512:(tt8 + 1) * 512],
                    reads=[ccKo.r()], writes=[b.PB.r("0")], dbg=f"l1aa.kpeg{rank}_{tt8}")
            sqp = b.PB.ap[0:64, 512:1024]
            ctx.op("act", lambda kpe_stg=kpe_stg, sqp=sqp: nc.scalar.activation(
                out=sqp, in_=kpe_stg, func=AF.Square),
                reads=[b.PB.r("0")], writes=[b.PB.r("1")], dbg=f"l1aa.kpq{rank}_{tt8}")
            ctx.op("pe", MM(NB2.ap[0:1, 0:512], b.cap("ones_bf")[0:64, :], sqp),
                   reads=[b.PB.r("1"), b.const_r["ones_bf"]], writes=[NB2.r()],
                   dbg=f"l1aa.kpm{rank}_{tt8}")
            ctx.op("dve", lambda: nc.vector.reduce_max(
                b.SM.ap[0:1, SM_STAGE + 6:SM_STAGE + 7], NB2.ap[0:1, 0:512], axis=AX.X),
                reads=[NB2.r()], writes=[b.SM.r("pemax")], dbg=f"l1aa.kpr{rank}_{tt8}")
            for oc in range(16):
                bank = b.banks[oc % 2]
                for ic in range(4):
                    ctx.op("pe", MM(bank.ap[:, :],
                                    b.WB.ap[:, (ic * 16 + oc) * 128:(ic * 16 + oc + 1) * 128],
                                    rhs3[:, ic, :],
                                    start=(ic == 0), stop=(ic == 3)),
                           reads=[b.WB.r("w0"), b.WB.r("w1"), rhs_r], writes=[bank.r()],
                           dbg=f"l1aa.kfm{rank}_{tt8}_{oc}_{ic}")
                s = oc % 2
                stg = b.PB.ap[:, s * 512:(s + 1) * 512]
                stg_r = b.PB.r(f"{s}")
                ctx.op("dve", lambda bank=bank, stg=stg: nc.vector.tensor_copy(stg, bank.ap[:, :]),
                       reads=[bank.r()], writes=[stg_r], dbg=f"l1aa.kfe{rank}_{tt8}_{oc}")
                ctx.dma("sp", lambda rank=rank, tt8=tt8, oc=oc: kfull.ap[
                    oc * 128:(oc + 1) * 128,
                    rank * TOK + tt8 * 512:rank * TOK + (tt8 + 1) * 512],
                    lambda stg=stg: stg,
                    reads=[stg_r], writes=[kfull.r()], dbg=f"l1aa.kfs{rank}_{tt8}_{oc}")
                if not (L1AA_EXP & 2):
                    continue
                sq = b.PB.ap[:, (1 - s) * 512:(2 - s) * 512]
                ctx.op("act", lambda stg=stg, sq=sq: nc.scalar.activation(
                    out=sq, in_=stg, func=AF.Square),
                    reads=[stg_r], writes=[b.PB.r(f"{1 - s}")], dbg=f"l1aa.knq{rank}_{tt8}_{oc}")
                ctx.op("pe", MM(NB.ap[0:1, 0:512], b.cap("ones_bf"), sq),
                       reads=[b.PB.r(f"{1 - s}"), b.const_r["ones_bf"]], writes=[NB.r()],
                       dbg=f"l1aa.knm{rank}_{tt8}_{oc}")
                ctx.op("dve", lambda: nc.vector.reduce_max(
                    b.SM.ap[0:1, SM_STAGE + 7:SM_STAGE + 8], NB.ap[0:1, 0:512], axis=AX.X),
                    reads=[NB.r()], writes=[b.SM.r("knmax")], dbg=f"l1aa.knx{rank}_{tt8}_{oc}")
                ctx.op("dve", lambda: nc.vector.tensor_add(
                    b.SM.ap[0:1, SM_STAGE:SM_STAGE + 1],
                    b.SM.ap[0:1, SM_STAGE + 7:SM_STAGE + 8],
                    b.SM.ap[0:1, SM_STAGE + 6:SM_STAGE + 7]),
                    reads=[b.SM.r("knmax"), b.SM.r("pemax")], writes=[b.SM.r("stg0")],
                    dbg=f"l1aa.kna{rank}_{tt8}_{oc}")
                first = (rank == 0 and tt8 == 0)
                if first:
                    ctx.op("dve", lambda oc=oc: nc.vector.tensor_copy(
                        b.SM.ap[0:1, SM_KMAX + oc:SM_KMAX + oc + 1],
                        b.SM.ap[0:1, SM_STAGE:SM_STAGE + 1]),
                        reads=[b.SM.r("stg0")], writes=[b.SM.r("kmax")],
                        dbg=f"l1aa.kni{oc}")
                else:
                    ctx.op("dve", lambda oc=oc: nc.vector.tensor_tensor(
                        b.SM.ap[0:1, SM_KMAX + oc:SM_KMAX + oc + 1],
                        b.SM.ap[0:1, SM_KMAX + oc:SM_KMAX + oc + 1],
                        b.SM.ap[0:1, SM_STAGE:SM_STAGE + 1], op=ALU.max),
                        reads=[b.SM.r("stg0"), b.SM.r("kmax")], writes=[b.SM.r("kmax")],
                        dbg=f"l1aa.knu{rank}_{tt8}_{oc}")
                if not (L1AA_EXP & 4):
                    continue
                bank2 = b.banks[4 + (oc % 2)]
                for ic in range(4):
                    ctx.op("pe", MM(bank2.ap[:, :],
                                    b.WB.ap[:, 8192 + (ic * 16 + oc) * 128:
                                            8192 + (ic * 16 + oc + 1) * 128],
                                    rhs3[:, ic, :],
                                    start=(ic == 0), stop=(ic == 3)),
                           reads=[b.WB.r("w2"), b.WB.r("w3"), rhs_r], writes=[bank2.r()],
                           dbg=f"l1aa.vfm{rank}_{tt8}_{oc}_{ic}")
                s2 = 1 - s
                stg2 = b.PB.ap[:, s2 * 512:(s2 + 1) * 512]
                stg2_r = b.PB.r(f"{s2}")
                ctx.op("dve", lambda bank2=bank2, stg2=stg2: nc.vector.tensor_copy(
                    stg2, bank2.ap[:, :]),
                    reads=[bank2.r()], writes=[stg2_r], dbg=f"l1aa.vfe{rank}_{tt8}_{oc}")
                tbv = bf_half(tb7, 0, 512)
                for q in range(4):
                    ctx.op("pe", lambda tbv=tbv, stg2=stg2, q=q: nc.tensor.transpose(
                        tbv[:, q * 128:(q + 1) * 128], stg2[:, q * 128:(q + 1) * 128],
                        b.cap("ident_bf")),
                        reads=[stg2_r, b.const_r["ident_bf"]], writes=[tb7.r()],
                        dbg=f"l1aa.vfT{rank}_{tt8}_{oc}_{q}")
                vst = b.QH.ap[:, 14 * TOK:14 * TOK + 512]
                ctx.op("dve", lambda tbv=tbv, vst=vst: nc.vector.tensor_copy(vst, tbv),
                       reads=[tb7.r()], writes=[b.QH.r()], dbg=f"l1aa.vfTe{rank}_{tt8}_{oc}")
                ctx.dma("sp", lambda rank=rank, tt8=tt8, oc=oc: vfull.ap[
                    :, oc * 4096 + rank * 2048 + tt8 * 512:
                    oc * 4096 + rank * 2048 + (tt8 + 1) * 512],
                    lambda vst=vst: vst,
                    reads=[b.QH.r()], writes=[vfull.r()], dbg=f"l1aa.vfs{rank}_{tt8}_{oc}")
    if L1AA_STOP:
        return
    ctx.op("act", lambda: nc.scalar.activation(
        out=b.SM.ap[0:1, SM_KMAX:SM_KMAX + 16], in_=b.SM.ap[0:1, SM_KMAX:SM_KMAX + 16],
        func=AF.Sqrt),
        reads=[b.SM.r("kmax")], writes=[b.SM.r("kmax")], dbg="l1aa.kms")

    emit_attn(b, kind="causal", n_heads=16, scale=192.0 ** -0.5,
              q_parts=lambda h: [
                  (b.XNT.ap[:, h * TOK:(h + 1) * TOK], 128, b.XNT.r(), 0),
                  (b.QH.ap[(h % 2) * 64:(h % 2) * 64 + 64,
                           (6 + h // 2) * TOK:(7 + h // 2) * TOK], 64, b.QH.r(),
                   (h % 2) * 64)],
              k_parts=lambda h: [
                  (kfull, lambda rank, h=h: h * 128, lambda rank: rank * TOK, 128),
                  (ccKo, lambda rank: rank * KR + 512, lambda rank: 0, 64)],
              v_dram=vfull,
              vcol=lambda h: h * 128,
              kmax_col=lambda h: SM_KMAX + h,
              qmax_col=lambda h, j: SM_QMAX + h * 4 + j,
              o_sink=osink_to_xnt(b, "l1aa.os"), dbg="l1aa.at")

    emit_proj(b, b.dram["L1_attn_a_w_o"], D, D, ep_residual(b, X, dbg="l1aa.res"),
              rhs_buf=b.XNT, rhs_read=b.XNT.r(), dbg="l1aa.o")


SUB_EMITTERS.update({"l1aa": sub_l1aa})



def sub_l1ffn(b):
    ctx, nc = b.ctx, b.nc
    X = b.X
    NB = b.banks[2]
    T1 = b.banks[6]
    LG = b.banks[3]

    gw = b.dram["L1_ffn_gate_w"]
    for ic in range(16):
        ctx.dma("sp", lambda ic=ic: b.WF.ap[0:8, 0:128],
                lambda ic=ic: gw.ap[:, ic * 128:(ic + 1) * 128],
                reads=[gw.r()], writes=[b.WF.r()], dbg=f"l1f.gwl{ic}")
        ctx.op("pe", lambda ic=ic: nc.tensor.transpose(
            T1.ap[:, 0:8], b.WF.ap[0:8, 0:128], b.cap("ident_f32")[0:8, 0:8]),
            reads=[b.WF.r(), b.const_r["ident_f32"]], writes=[T1.r()],
            dbg=f"l1f.gwT{ic}")
        ctx.op("dve", lambda ic=ic: nc.vector.tensor_copy(
            b.GT.ap[:, ic * 8:(ic + 1) * 8], T1.ap[:, 0:8]),
            reads=[T1.r()], writes=[b.GT.r("w")], dbg=f"l1f.gwTe{ic}")

    def routing_math(tc):
        c0 = SM_STAGE + 8
        L = b.SM.ap[:, c0 + 8:c0 + 16]
        ctx.op("dve", lambda L=L: nc.vector.tensor_copy(L, LG.ap[:, 0:8]),
               reads=[LG.r()], writes=[b.SM.r("rlog")], dbg=f"l1f.lc{tc}")
        ctx.op("dve", lambda L=L: nc.vector.reduce_max(
            b.SM.ap[:, c0:c0 + 1], L, axis=AX.X),
            reads=[b.SM.r("rlog")], writes=[b.SM.r("rm1")], dbg=f"l1f.m1{tc}")
        ctx.op("dve", lambda L=L: nc.vector.tensor_scalar(
            b.SM.ap[:, c0 + 16:c0 + 24], L, b.SM.ap[:, c0:c0 + 1], None, ALU.subtract),
            reads=[b.SM.r("rlog"), b.SM.r("rm1")], writes=[b.SM.r("rP0")], dbg=f"l1f.ls{tc}")
        ctx.op("act", lambda: nc.scalar.activation(
            out=b.SM.ap[:, c0 + 24:c0 + 32], in_=b.SM.ap[:, c0 + 16:c0 + 24], func=AF.Exp),
            reads=[b.SM.r("rP0")], writes=[b.SM.r("rP")], dbg=f"l1f.le{tc}")
        ctx.op("dve", lambda L=L: nc.vector.tensor_scalar(
            b.SM.ap[:, c0 + 32:c0 + 40], L, b.SM.ap[:, c0:c0 + 1], None, ALU.is_ge),
            reads=[b.SM.r("rlog"), b.SM.r("rm1")], writes=[b.SM.r("rind")], dbg=f"l1f.li{tc}")
        ctx.op("dve", lambda L=L: nc.vector.scalar_tensor_tensor(
            b.SM.ap[:, c0 + 40:c0 + 48], b.SM.ap[:, c0 + 32:c0 + 40], -1e30, L,
            ALU.mult, ALU.add),
            reads=[b.SM.r("rind"), b.SM.r("rlog")], writes=[b.SM.r("rmask")],
            dbg=f"l1f.lm{tc}")
        ctx.op("dve", lambda: nc.vector.reduce_max(
            b.SM.ap[:, c0 + 1:c0 + 2], b.SM.ap[:, c0 + 40:c0 + 48], axis=AX.X),
            reads=[b.SM.r("rmask")], writes=[b.SM.r("rm2")], dbg=f"l1f.m2{tc}")
        ctx.op("dve", lambda: nc.vector.tensor_tensor(
            b.SM.ap[:, c0 + 2:c0 + 3], b.SM.ap[:, c0 + 1:c0 + 2], b.SM.ap[:, c0:c0 + 1],
            op=ALU.subtract),
            reads=[b.SM.r("rm2"), b.SM.r("rm1")], writes=[b.SM.r("rd0")], dbg=f"l1f.d0{tc}")
        ctx.op("act", lambda: nc.scalar.activation(
            out=b.SM.ap[:, c0 + 3:c0 + 4], in_=b.SM.ap[:, c0 + 2:c0 + 3], func=AF.Exp),
            reads=[b.SM.r("rd0")], writes=[b.SM.r("rd1")], dbg=f"l1f.d1{tc}")
        ctx.op("dve", lambda: nc.vector.tensor_scalar(
            b.SM.ap[:, c0 + 4:c0 + 5], b.SM.ap[:, c0 + 3:c0 + 4], 1.0, None, ALU.add),
            reads=[b.SM.r("rd1")], writes=[b.SM.r("rd2")], dbg=f"l1f.d2{tc}")
        ctx.op("dve", lambda: nc.vector.reciprocal(
            b.SM.ap[:, c0 + 5:c0 + 6], b.SM.ap[:, c0 + 4:c0 + 5]),
            reads=[b.SM.r("rd2")], writes=[b.SM.r("rd3")], dbg=f"l1f.d3{tc}")
        ctx.op("dve", lambda L=L: nc.vector.tensor_scalar(
            b.SM.ap[:, c0 + 32:c0 + 40], L, b.SM.ap[:, c0 + 1:c0 + 2], None, ALU.is_ge),
            reads=[b.SM.r("rlog"), b.SM.r("rm2")], writes=[b.SM.r("rind")], dbg=f"l1f.se{tc}")
        ctx.op("dve", lambda: nc.vector.tensor_mul(
            b.SM.ap[:, c0 + 16:c0 + 24], b.SM.ap[:, c0 + 24:c0 + 32],
            b.SM.ap[:, c0 + 32:c0 + 40]),
            reads=[b.SM.r("rP"), b.SM.r("rind")], writes=[b.SM.r("rP0")], dbg=f"l1f.w0{tc}")
        ctx.op("dve", lambda tc=tc: nc.vector.tensor_scalar(
            b.RW.ap[:, tc * 8:(tc + 1) * 8], b.SM.ap[:, c0 + 16:c0 + 24],
            b.SM.ap[:, c0 + 5:c0 + 6], None, ALU.mult),
            reads=[b.SM.r("rP0"), b.SM.r("rd3")], writes=[b.RW.r()], dbg=f"l1f.w1{tc}")

    def gate_hook(tc, fc):
        ctx.op("pe", MM(LG.ap[:, 0:8],
                        b.GT.ap[:, 128:256],
                        b.GT.ap[:, fc * 8:(fc + 1) * 8],
                        start=(fc == 0), stop=(fc == 15)),
               reads=[b.GT.r("s"), b.GT.r("w")], writes=[LG.r()],
               dbg=f"l1f.lgm{tc}_{fc}")
        if fc == 15:
            routing_math(tc)

    emit_ln(b, X, dbg="l1f.ln", gate_hook=gate_hook)

    sgw = b.dram["L1_ffn_shared_gate_w"]
    sgwf = Buf(sgw.ap.rearrange("d one -> (d one)"), "sgwf")
    sgwf._regions = sgw._regions
    load_bias_cols(b, sgwf, SM_BIAS + 32, "l1f.sgl")
    ctx.op("dve", lambda: nc.vector.tensor_copy(
        b.CB.ap[:, 256:272], b.SM.ap[:, SM_BIAS + 32:SM_BIAS + 48]),
        reads=[b.SM.r(f"bias{SM_BIAS + 32}")], writes=[b.CB.r("sgw")], dbg="l1f.sgc")
    for tt in range(NQT):
        bank = b.banks[3]
        for ic in range(NFC):
            ctx.op("pe", MM(bank.ap[0:1, 0:512],
                            b.CB.ap[:, 256 + ic:257 + ic],
                            b.XNT.ap[:, ic * TOK + tt * 512:ic * TOK + (tt + 1) * 512],
                            start=(ic == 0), stop=(ic == NFC - 1)),
                   reads=[b.CB.r("sgw"), b.XNT.r()], writes=[bank.r()],
                   dbg=f"l1f.sg{tt}_{ic}")
        ctx.op("act", lambda bank=bank: nc.scalar.activation(
            out=b.SM.ap[0:1, SM_LROW:SM_LROW + 512], in_=bank.ap[0:1, 0:512],
            func=AF.Sigmoid),
            reads=[bank.r()], writes=[b.SM.r("lrow")], dbg=f"l1f.sgs{tt}")
        for qs in range(4):
            ctx.op("pe", lambda qs=qs: nc.tensor.transpose(
                T1.ap[:, 8 + qs:9 + qs],
                b.SM.ap[0:1, SM_LROW + qs * 128:SM_LROW + (qs + 1) * 128],
                b.cap("ident_f32")[0:1, 0:1]),
                reads=[b.SM.r("lrow"), b.const_r["ident_f32"]], writes=[T1.r()],
                dbg=f"l1f.sgT{tt}_{qs}")
            ctx.op("dve", lambda tt=tt, qs=qs: nc.vector.tensor_copy(
                b.RW.ap[:, 128 + tt * 4 + qs:129 + tt * 4 + qs], T1.ap[:, 8 + qs:9 + qs]),
                reads=[T1.r()], writes=[b.RW.r()], dbg=f"l1f.sgTe{tt}_{qs}")

    WG, WU, WD = b.dram["L1_ffn_w_gate"], b.dram["L1_ffn_w_up"], b.dram["L1_ffn_w_down"]
    SWG, SWU, SWD = b.dram["L1_ffn_sw_gate"], b.dram["L1_ffn_sw_up"], b.dram["L1_ffn_sw_down"]
    for e in range(12):
        routed = e < 8
        q4 = e - 8
        if routed:
            wg_ap = WG.ap[e]
            wu_ap = WU.ap[e]
            wd_ap = WD.ap[e]
            wg_r, wu_r, wd_r = WG.r(), WU.r(), WD.r()
        else:
            wg_ap = SWG.ap[:, q4 * 1408:(q4 + 1) * 1408]
            wu_ap = SWU.ap[:, q4 * 1408:(q4 + 1) * 1408]
            wd_ap = SWD.ap[q4 * 1408:(q4 + 1) * 1408, :]
            wg_r, wu_r, wd_r = SWG.r(), SWU.r(), SWD.r()
        for ioc in range(11):
            sl = (ioc // 2) % 2
            io2 = ioc % 2
            wgr = b.WB.r(f"w{sl}")
            wur = b.WB.r(f"w{2 + sl}")
            if io2 == 0:
                nio = min(2, 11 - ioc) * 128
                ctx.dma("pool", lambda sl=sl, nio=nio: b.WB.ap[:, sl * 4096:sl * 4096 + 16 * nio].rearrange(
                    "p (i c) -> p i c", c=nio),
                    lambda wg_ap=wg_ap, ioc=ioc, nio=nio: wg_ap.rearrange("(i p) o -> p i o", p=128)[
                        :, :, ioc * 128:ioc * 128 + nio],
                    reads=[wg_r], writes=[wgr], dbg=f"l1f.wg{e}_{ioc}")
                ctx.dma("pool", lambda sl=sl, nio=nio: b.WB.ap[:, 8192 + sl * 4096:8192 + sl * 4096 + 16 * nio].rearrange(
                    "p (i c) -> p i c", c=nio),
                    lambda wu_ap=wu_ap, ioc=ioc, nio=nio: wu_ap.rearrange("(i p) o -> p i o", p=128)[
                        :, :, ioc * 128:ioc * 128 + nio],
                    reads=[wu_r], writes=[wur], dbg=f"l1f.wu{e}_{ioc}")
            nio_cur = 256 if (ioc % 2 == 0 and ioc < 10) or (ioc % 2 == 1) else 128
            nio_of_load = 256 if (ioc - io2) < 10 else 128
            for tt in range(NQT):
                par = (ioc * NQT + tt) % 2
                bg = b.banks[0 if par == 0 else 4]
                bu = b.banks[1 if par == 0 else 5]
                for ic in range(NFC):
                    col = sl * 4096 + ic * nio_of_load + io2 * 128
                    ctx.op("pe", MM(bg.ap[:, :],
                                    b.WB.ap[:, col:col + 128],
                                    rhs_tile(b.XNT, ic, tt),
                                    start=(ic == 0), stop=(ic == NFC - 1)),
                           reads=[wgr, b.XNT.r()], writes=[bg.r()],
                           dbg=f"l1f.gm{e}_{ioc}_{tt}_{ic}")
                for ic in range(NFC):
                    col = 8192 + sl * 4096 + ic * nio_of_load + io2 * 128
                    ctx.op("pe", MM(bu.ap[:, :],
                                    b.WB.ap[:, col:col + 128],
                                    rhs_tile(b.XNT, ic, tt),
                                    start=(ic == 0), stop=(ic == NFC - 1)),
                           reads=[wur, b.XNT.r()], writes=[bu.r()],
                           dbg=f"l1f.um{e}_{ioc}_{tt}_{ic}")
                s = tt % 2
                hg = b.PB.ap[:, s * 512:(s + 1) * 512]
                hg_r = b.PB.r(f"{s}")
                ctx.op("act", lambda bg=bg, hg=hg: nc.scalar.activation(
                    out=hg, in_=bg.ap[:, :], func=AF.Silu),
                    reads=[bg.r()], writes=[hg_r], dbg=f"l1f.si{e}_{ioc}_{tt}")
                s2 = 1 - s
                hu = b.PB.ap[:, s2 * 512:(s2 + 1) * 512]
                hu_r = b.PB.r(f"{s2}")
                ctx.op("dve", lambda bu=bu, hu=hu: nc.vector.tensor_copy(hu, bu.ap[:, :]),
                       reads=[bu.r()], writes=[hu_r], dbg=f"l1f.ue{e}_{ioc}_{tt}")
                hcol = ioc * TOK + tt * 512
                ctx.op("dve", lambda hg=hg, hu=hu, hcol=hcol: nc.vector.tensor_mul(
                    b.QH.ap[:, hcol:hcol + 512], hg, hu),
                    reads=[hg_r, hu_r], writes=[b.QH.r()], dbg=f"l1f.h{e}_{ioc}_{tt}")
        if routed:
            def scol(tt, qs, e=e):
                return b.RW.ap[:, (tt * 4 + qs) * 8 + e:(tt * 4 + qs) * 8 + e + 1]
        else:
            def scol(tt, qs, e=e):
                return b.RW.ap[:, 128 + tt * 4 + qs:129 + tt * 4 + qs]
        scale_rd = b.RW.r()
        WdBuf = Buf(wd_ap, f"wd{e}")
        emit_proj(b, WdBuf, 1408, D,
                  ep_residual(b, X, scale_col=scol, scale_reads=(scale_rd,),
                              dbg=f"l1f.res{e}"),
                  rhs_buf=b.QH, rhs_read=b.QH.r(),
                  rhs_col_fn=lambda ic, tt: ic * TOK + tt * 512,
                  dbg=f"l1f.dn{e}")


SUB_EMITTERS.update({"l1ffn1": sub_l1ffn, "l1ffn2": sub_l1ffn})
